# version 08 (final)
# baseline (speedup 1.0000x reference)
import numpy as np
import ml_dtypes
import concourse.bass as bass
import concourse.mybir as mybir
from concourse.bass_utils import run_bass_kernel_spmd

F32 = mybir.dt.float32
BF16 = mybir.dt.bfloat16
I32 = mybir.dt.int32
AF = mybir.ActivationFunctionType
ALU = mybir.AluOpType
AXX = mybir.AxisListType.X
BFNP = ml_dtypes.bfloat16

NCORE = 8
HID = 4096
TOK = 8192
TPC = TOK // NCORE
H, P, NS, G = 128, 64, 128, 8
HPC = H // NCORE
INTER = H * P
IPC = INTER // NCORE
CONVCH = INTER + 2 * G * NS
CCPC = CONVCH // NCORE
PROJ = INTER + CONVCH + H
RPC = PROJ // NCORE
FFN_I = 11008
CS = 128
EPS = 1e-5
NB = 6
NBLK_GU = 43
GU_BLOCKS = [6, 6, 6, 5, 5, 5, 5, 5]
GU_B0 = [0, 6, 12, 18, 23, 28, 33, 38]


def cdiv(a, b):
    return (a + b - 1) // b


class Sem:
    def __init__(self, handle, step):
        self.h = handle
        self.step = step
        self.val = 0

    def inc(self, inst, n=1):
        inst.then_inc(self.h, n * self.step)
        self.val += n * self.step
        return self.val


_CHAIN_OPS = {
    'memset', 'memzero', 'tensor_copy', 'tensor_tensor', 'tensor_add',
    'tensor_sub', 'tensor_mul', 'tensor_max', 'tensor_scalar',
    'tensor_single_scalar', 'tensor_scalar_add', 'tensor_scalar_sub',
    'tensor_scalar_mul', 'tensor_scalar_max', 'tensor_scalar_min',
    'tensor_relu', 'tensor_reduce', 'reduce_max', 'reduce_sum',
    'scalar_tensor_tensor', 'tensor_tensor_scan', 'reciprocal', 'select',
    'copy_predicated', 'transpose', 'bn_stats', 'bn_aggr', 'pool', 'pool_avg',
    'pool_max', 'stream_shuffle', 'activation', 'sign', 'square', 'sqrt',
    'mul', 'add', 'copy', 'iota', 'affine_select', 'partition_broadcast',
    'partition_all_reduce',
}
_DMA_OPS = {'dma_start', 'dma_start_transpose', 'indirect_dma_start'}


class DmaFamily:

    def __init__(self, sems):
        self.sems = sems
        self.cnt = [0] * len(sems)
        self.i = 0

    def issue(self, raw_eng, fn, *a, **k):
        j = self.i % len(self.sems)
        if self.cnt[j] > 0:
            raw_eng.wait_ge(self.sems[j], 16 * self.cnt[j])
        inst = fn(*a, **k)
        inst.then_inc(self.sems[j], 16)
        self.cnt[j] += 1
        self.i += 1
        return inst

    def token(self):
        return tuple((s, 16 * c) for s, c in zip(self.sems, self.cnt) if c > 0)


class ChainEng:

    def __init__(self, eng, sem, fam):
        self._e = eng
        self._sem = sem
        self._fam = fam
        self._lastwait = -1

    def __getattr__(self, name):
        fn = getattr(self._e, name)
        if name in _DMA_OPS:
            def wdma(*a, **k):
                return self._fam.issue(self._e, fn, *a, **k)
            return wdma
        if name in _CHAIN_OPS:
            def wop(*a, **k):
                if self._sem.val > self._lastwait:
                    self._e.wait_ge(self._sem.h, self._sem.val)
                    self._lastwait = self._sem.val
                inst = fn(*a, **k)
                self._sem.inc(inst)
                return inst
            return wop
        return fn

    def wait_tok(self, tok):
        for s, v in tok:
            self._e.wait_ge(s, v)


class Prog:
    def __init__(self, nc, block, hw_sems, sw_sems, th, vh, sh, gh):
        self.nc = nc
        self.block = block
        self.hw = DmaFamily(hw_sems)
        self.sw = DmaFamily(sw_sems)
        self.tsem = Sem(th, 1)
        self.vsem = Sem(vh, 1)
        self.ssem = Sem(sh, 1)
        self.gsem = Sem(gh, 1)
        self.engsem = {'vector': self.vsem, 'scalar': self.ssem,
                       'gpsimd': self.gsem, 'sync': Sem(th, 0)}

    def on(self, eng, fn):
        def wrap(raw):
            if eng == 'tensor':
                fn(raw)
            else:
                fam = self.sw if eng == 'gpsimd' else self.hw
                fn(ChainEng(raw, self.engsem[eng], fam))
        getattr(self.block, eng)(wrap)

    def dtok(self):
        return self.hw.token() + self.sw.token()


def wait_tok(eng, tok):
    e = eng._e if isinstance(eng, ChainEng) else eng
    for s, v in tok:
        e.wait_ge(s, v)


CONSTS = [
    ('ones_bf', [128, 128], BF16, 'sbuf'),
    ('ones_f', [128, 128], F32, 'sbuf'),
    ('idbf', [128, 128], BF16, 'sbuf'),
    ('idf', [128, 128], F32, 'sbuf'),
    ('triu', [128, 128], F32, 'sbuf'),
    ('mneg', [128, 128], F32, 'sbuf'),
]


def alloc(nc, want, stack=None, prefix=''):
    out = {}
    for name, shape, dt, kind in want:
        g = (nc.sbuf_tensor(prefix + name, shape, dt) if kind == 'sbuf'
             else nc.psum_tensor(prefix + name, shape, dt))
        out[name] = stack.enter_context(g) if stack is not None else g.__enter__()
    return out


def emit_barrier(pr):
    vals = [(pr.tsem.h, pr.tsem.val), (pr.vsem.h, pr.vsem.val),
            (pr.ssem.h, pr.ssem.val), (pr.gsem.h, pr.gsem.val)]
    toks = pr.hw.token() + pr.sw.token()

    def mk(eng):
        def f(e):
            raw = e._e if isinstance(e, ChainEng) else e
            for h, v in vals:
                if v > 0:
                    raw.wait_ge(h, v)
            for h, v in toks:
                raw.wait_ge(h, v)
        return f
    for eng in ('sync', 'tensor', 'vector', 'scalar', 'gpsimd'):
        pr.on(eng, mk(eng))


def emit_consts(pr, sb):
    def f(g):
        g.memset(sb['ones_bf'][:, :], 1.0)
        g.memset(sb['ones_f'][:, :], 1.0)
        g.affine_select(sb['idbf'][:, :], sb['ones_bf'][:, 0:128], [[1, 128]],
                        ALU.is_equal, 0.0, base=0, channel_multiplier=-1)
        g.affine_select(sb['idf'][:, :], sb['ones_f'][:, 0:128], [[1, 128]],
                        ALU.is_equal, 0.0, base=0, channel_multiplier=-1)
        g.affine_select(sb['triu'][:, :], sb['ones_f'][:, 0:128], [[1, 128]],
                        ALU.is_ge, 0.0, base=0, channel_multiplier=-1)
        g.memset(sb['mneg'][:, :], 0.0)
        g.affine_select(sb['mneg'][:, :], sb['mneg'][:, :], [[1, 128]],
                        ALU.is_ge, -1e30, base=0, channel_multiplier=-1)
    pr.on('gpsimd', f)



PREP_TENSORS = [
    ('lda', [128, 4096], F32, 'sbuf'),
    ('ldb', [128, 4096], F32, 'sbuf'),
    ('cba', [128, 4096], BF16, 'sbuf'),
    ('cbb', [128, 4096], BF16, 'sbuf'),
    ('sq', [128, 4096], F32, 'sbuf'),
    ('cols', [128, 64], F32, 'sbuf'),
    ('rowa', [1, 4096], F32, 'sbuf'),
    ('rowb', [1, 4096], F32, 'sbuf'),
    ('qwa', [128, 512], I32, 'sbuf'),
    ('qwb', [128, 512], I32, 'sbuf'),
    ('nib', [128, 512], I32, 'sbuf'),
    ('dqt', [128, 512], F32, 'sbuf'),
    ('dqw', [128, 512], BF16, 'sbuf'),
    ('dqw2', [128, 512], BF16, 'sbuf'),
    ('srep', [128, 512], F32, 'sbuf'),
    ('srb', [128, 512], F32, 'sbuf'),
    ('brep', [128, 512], F32, 'sbuf'),
    ('brb', [128, 512], F32, 'sbuf'),
    ('scz', [86, 1536], F32, 'sbuf'),
    ('zqz', [86, 192], I32, 'sbuf'),
    ('zzi', [86, 1536], I32, 'sbuf'),
    ('zz', [86, 1536], F32, 'sbuf'),
    ('beta', [86, 1536], F32, 'sbuf'),
    ('w2T', [128, 32 * 512], BF16, 'sbuf'),
    ('mk', [128, 512], BF16, 'sbuf'),
    ('att', [128, 512], BF16, 'sbuf'),
    ('att2', [128, 256], BF16, 'sbuf'),
    ('psA', [128, 512], F32, 'psum'),
    ('psB', [128, 512], F32, 'psum'),
    ('psC', [128, 512], F32, 'psum'),
    ('psD', [128, 512], F32, 'psum'),
    ('psT', [128, 128], BF16, 'psum'),
    ('psT2', [128, 128], BF16, 'psum'),
    ('psTF', [128, 128], F32, 'psum'),
    ('psR', [1, 512], F32, 'psum'),
]

def emit_prep_p1(nc, pr, sb, ck, *, xs, o_xbt, o_lnrow, o_lncol):
    tsem, vsem, ssem, gsem = pr.tsem, pr.vsem, pr.ssem, pr.gsem

    for t in range(8):
        def load(q, t=t):
            if t >= 2:
                q.wait_ge(vsem.h, ck[('xcast', t - 2)])
            q.dma_start(out=sb['lda' if t % 2 == 0 else 'ldb'][:, :],
                        in_=xs[128 * t:128 * (t + 1), :])
            ck[('xld', t)] = pr.hw.token()
        pr.on('sync', load)

        def square(s, t=t):
            wait_tok(s, ck[('xld', t)])
            s.activation(sb['sq'][:, :], sb['lda' if t % 2 == 0 else 'ldb'][:, :],
                         AF.Square, accum_out=sb['cols'][:, 32:33])
            ck[('xsq', t)] = ssem.val
        pr.on('scalar', square)

        def stats(v, t=t):
            fl = sb['lda' if t % 2 == 0 else 'ldb']
            wait_tok(v, ck[('xld', t)])
            v.reduce_sum(sb['cols'][:, 33:34], fl[:, :], AXX)
            v.wait_ge(ssem.h, ck[('xsq', t)])
            c = sb['cols']
            v.tensor_scalar(c[:, 34:35], c[:, 33:34], 1.0 / HID, None, ALU.mult)
            v.tensor_scalar(c[:, 35:36], c[:, 32:33], 1.0 / HID, None, ALU.mult)
            v.tensor_tensor(c[:, 36:37], c[:, 34:35], c[:, 34:35], ALU.mult)
            v.tensor_scalar(c[:, 36:37], c[:, 36:37], -1.0, EPS, ALU.mult, ALU.add)
            v.tensor_tensor(c[:, 36:37], c[:, 35:36], c[:, 36:37], ALU.add)
            v.reciprocal(c[:, 37:38], c[:, 36:37])
            v.tensor_scalar(c[:, 8 + t:9 + t], c[:, 34:35], -1.0, None, ALU.mult)
            if t >= 2:
                wait_tok(v, ck[('xob', t - 2)])
            v.tensor_copy(sb['cba' if t % 2 == 0 else 'cbb'][:, :], fl[:, :])
            ck[('xcast', t)] = vsem.val
        pr.on('vector', stats)

        def rstd(s, t=t):
            s.wait_ge(vsem.h, ck[('xcast', t)])
            s.activation(sb['cols'][:, 0 + t:1 + t], sb['cols'][:, 37:38], AF.Sqrt)
            ck[('xr', t)] = ssem.val
        pr.on('scalar', rstd)

        def out(g, t=t):
            g.wait_ge(vsem.h, ck[('xcast', t)])
            g.dma_start(out=o_xbt[128 * t:128 * (t + 1), :],
                        in_=sb['cba' if t % 2 == 0 else 'cbb'][:, :])
            ck[('xob', t)] = pr.sw.token()
        pr.on('gpsimd', out)

    def lnr_t(tt):
        tt.wait_ge(ssem.h, ck[('xr', 7)])
        tt.wait_ge(gsem.h, gsem.val)
        ins = tt.transpose(sb['psTF'][0:16, 0:128], sb['cols'][:, 0:16], sb['idf'][:, :])
        tsem.inc(ins)
        ck['lnr'] = tsem.val
    pr.on('tensor', lnr_t)

    def lnr_v(v):
        v.wait_ge(tsem.h, ck['lnr'])
        v.tensor_copy(sb['sq'][0:16, 0:128], sb['psTF'][0:16, 0:128])
        ck['lnrv'] = vsem.val
    pr.on('vector', lnr_v)

    def lnr_g(g):
        g.wait_ge(vsem.h, ck['lnrv'])
        for t in range(8):
            g.dma_start(out=o_lnrow[0:1, 128 * t:128 * (t + 1)], in_=sb['sq'][t:t + 1, 0:128])
            g.dma_start(out=o_lnrow[1:2, 128 * t:128 * (t + 1)], in_=sb['sq'][8 + t:9 + t, 0:128])
        g.dma_start(out=o_lncol[:, :], in_=sb['cols'][:, 0:16])
        ck['lnc'] = pr.sw.token()
    pr.on('gpsimd', lnr_g)


def emit_prep_rest(nc, pr, sb, ck, *, wi, wo, gq, gsc, gz, uq, usc, uz, dq, dsc, dz, w1, w2s, masks, wbs, wr, lA, lB, o_wib, o_wsum, o_wob, o_wgu, o_wgs, o_wus, o_wdp, o_attnT, o_asum, o_wbb, o_wbsum, o_wrb, o_rsum, o_lab, o_acs, o_bcb, w2c, mkc):
    tsem, vsem, ssem, gsem = pr.tsem, pr.vsem, pr.ssem, pr.gsem

    def cast_pass(src, dst, R, C, sum_row_dst, tag):
        nt = cdiv(R, 128)
        for i in range(nt):
            r0 = 128 * i
            rr = min(R, r0 + 128) - r0

            def load(q, i=i, r0=r0, rr=rr):
                if i >= 2:
                    q.wait_ge(ssem.h, ck[(tag, 'c', i - 2)])
                if i == 0:
                    q.wait_ge(vsem.h, vsem.val)
                    q.wait_ge(ssem.h, ssem.val)
                    q.wait_ge(tsem.h, tsem.val)
                    wait_tok(q, pr.dtok())
                q.dma_start(out=sb['lda' if i % 2 == 0 else 'ldb'][0:rr, 0:C],
                            in_=src[r0:r0 + rr, :])
                ck[(tag, 'l', i)] = pr.hw.token()
            pr.on('sync', load)

            def cast(s, i=i, rr=rr):
                wait_tok(s, ck[(tag, 'l', i)])
                if i >= 2:
                    wait_tok(s, ck[(tag, 'd', i - 2)])
                    if sum_row_dst is not None:
                        s.wait_ge(tsem.h, ck[(tag, 't', i - 2)])
                acc = (sb['cols'][0:rr, 34 + (i % 2):35 + (i % 2)]
                       if sum_row_dst is not None else None)
                s.activation(sb['cba' if i % 2 == 0 else 'cbb'][0:rr, 0:C],
                             sb['lda' if i % 2 == 0 else 'ldb'][0:rr, 0:C],
                             AF.Identity, accum_out=acc)
                ck[(tag, 'c', i)] = ssem.val
            pr.on('scalar', cast)

            if sum_row_dst is not None:
                def trow(tt, i=i, rr=rr, vg=None):
                    tt.wait_ge(ssem.h, ck[(tag, 'c', i)])
                    tt.wait_ge(vsem.h, ck[(tag, 'r', i - 1)] if i >= 1 else vg)
                    ins = tt.transpose(sb['psTF'][0:1, 0:rr],
                                       sb['cols'][0:rr, 34 + (i % 2):35 + (i % 2)],
                                       sb['idf'][0:rr, 0:rr])
                    tsem.inc(ins)
                    ck[(tag, 't', i)] = tsem.val
                pr.on('tensor', lambda tt, f=trow, vg=vsem.val: f(tt, vg=vg))

                def vrow(v, i=i, r0=r0, rr=rr):
                    v.wait_ge(tsem.h, ck[(tag, 't', i)])
                    v.tensor_copy(sb['rowa'][0:1, r0:r0 + rr], sb['psTF'][0:1, 0:rr])
                    ck[(tag, 'r', i)] = vsem.val
                pr.on('vector', vrow)

            def outd(g, i=i, r0=r0, rr=rr):
                g.wait_ge(ssem.h, ck[(tag, 'c', i)])
                g.dma_start(out=dst[r0:r0 + rr, :],
                            in_=sb['cba' if i % 2 == 0 else 'cbb'][0:rr, 0:C])
                ck[(tag, 'd', i)] = pr.sw.token()
            pr.on('gpsimd', outd)

        if sum_row_dst is not None:
            def sout(g, nt=nt):
                g.wait_ge(vsem.h, ck[(tag, 'r', nt - 1)])
                g.dma_start(out=sum_row_dst, in_=sb['rowa'][0:1, 0:R])
                ck[(tag, 'sum')] = pr.sw.token()
            pr.on('gpsimd', sout)

    def cast_pass_colsum(src, dst, R, C, sum_dst, tag):
        nt = cdiv(R, 128)
        for i in range(nt):
            r0 = 128 * i

            def load(q, i=i, r0=r0):
                if i >= 2:
                    q.wait_ge(ssem.h, ck[(tag, 'c', i - 2)])
                if i == 0:
                    q.wait_ge(vsem.h, vsem.val)
                    q.wait_ge(ssem.h, ssem.val)
                    q.wait_ge(tsem.h, tsem.val)
                    wait_tok(q, pr.dtok())
                q.dma_start(out=sb['lda' if i % 2 == 0 else 'ldb'][:, 0:C],
                            in_=src[r0:r0 + 128, :])
                ck[(tag, 'l', i)] = pr.hw.token()
            pr.on('sync', load)

            def cast(s, i=i):
                wait_tok(s, ck[(tag, 'l', i)])
                if i >= 2:
                    wait_tok(s, ck[(tag, 'd', i - 2)])
                    s.wait_ge(tsem.h, ck[(tag, 't', i - 2)])
                s.activation(sb['cba' if i % 2 == 0 else 'cbb'][:, 0:C],
                             sb['lda' if i % 2 == 0 else 'ldb'][:, 0:C], AF.Identity)
                ck[(tag, 'c', i)] = ssem.val
            pr.on('scalar', cast)

            def csmm(tt, i=i):
                tt.wait_ge(ssem.h, ck[(tag, 'c', i)])
                ins = tt.matmul(sb['psR'][0:1, 0:C], sb['ones_bf'][:, 0:1],
                                sb['cba' if i % 2 == 0 else 'cbb'][:, 0:C],
                                start=(i == 0), stop=(i == nt - 1))
                tsem.inc(ins)
                ck[(tag, 't', i)] = tsem.val
            pr.on('tensor', csmm)

            def outd(g, i=i, r0=r0):
                g.wait_ge(ssem.h, ck[(tag, 'c', i)])
                g.dma_start(out=dst[r0:r0 + 128, :],
                            in_=sb['cba' if i % 2 == 0 else 'cbb'][:, 0:C])
                ck[(tag, 'd', i)] = pr.sw.token()
            pr.on('gpsimd', outd)

        def sev(v, nt=nt):
            v.wait_ge(tsem.h, ck[(tag, 't', nt - 1)])
            v.tensor_copy(sb['rowa'][0:1, 0:C], sb['psR'][0:1, 0:C])
            ck[(tag, 'sv')] = vsem.val
        pr.on('vector', sev)

        def sout(g):
            g.wait_ge(vsem.h, ck[(tag, 'sv')])
            g.dma_start(out=sum_dst, in_=sb['rowa'][0:1, 0:C])
            ck[(tag, 'sum')] = pr.sw.token()
        pr.on('gpsimd', sout)

    cast_pass(w2s, w2c, 512, HID, None, 'w2')
    cast_pass(masks, mkc, HID, 512, None, 'mk')
    cast_pass(wi, o_wib, RPC, HID, o_wsum, 'wi')
    cast_pass(wo, o_wob, IPC, HID, None, 'wo')
    cast_pass_colsum(wbs, o_wbb, HID, 512, o_wbsum, 'wb')
    cast_pass(wr, o_wrb, 8, HID, o_rsum, 'wr')
    cast_pass(lA, o_lab, 128, HID, o_acs, 'la')

    for e in range(8):
        for ot in range(32):
            i = e * 32 + ot

            def load(q, e=e, ot=ot, i=i):
                if i == 0:
                    q.wait_ge(ssem.h, ssem.val)
                    q.wait_ge(vsem.h, vsem.val)
                    q.wait_ge(tsem.h, tsem.val)
                    wait_tok(q, pr.dtok())
                if i >= 2:
                    q.wait_ge(tsem.h, ck[('bc', 't', i - 2)])
                q.dma_start(out=sb['lda' if i % 2 == 0 else 'ldb'][:, 0:16],
                            in_=lB[4096 * e + 128 * ot:4096 * e + 128 * (ot + 1), :])
                ck[('bc', 'l', i)] = pr.hw.token()
            pr.on('sync', load)

            def casts(s, i=i):
                wait_tok(s, ck[('bc', 'l', i)])
                if i >= 1:
                    s.wait_ge(tsem.h, ck[('bc', 't', i - 1)])
                s.activation(sb['cbb'][:, 0:16],
                             sb['lda' if i % 2 == 0 else 'ldb'][:, 0:16],
                             AF.Identity, scale=2.0)
                ck[('bc', 'c', i)] = ssem.val
            pr.on('scalar', casts)

            def tr(tt, i=i):
                tt.wait_ge(ssem.h, ck[('bc', 'c', i)])
                if i >= 1:
                    tt.wait_ge(vsem.h, ck[('bc', 'v', i - 1)])
                ins = tt.transpose(sb['psT'][0:16, 0:128], sb['cbb'][:, 0:16],
                                   sb['idbf'][:, :])
                tsem.inc(ins)
                ck[('bc', 't', i)] = tsem.val
            pr.on('tensor', tr)

            def ev(v, i=i):
                v.wait_ge(tsem.h, ck[('bc', 't', i)])
                if i >= 2:
                    wait_tok(v, ck[('bc', 'd', i - 2)])
                buf = sb['cba'] if i % 2 == 0 else sb['cbb']
                v.tensor_copy(buf[0:16, 2048:2176], sb['psT'][0:16, 0:128])
                ck[('bc', 'v', i)] = vsem.val
            pr.on('vector', ev)

            def outp(g, e=e, ot=ot, i=i):
                g.wait_ge(vsem.h, ck[('bc', 'v', i)])
                buf = sb['cba'] if i % 2 == 0 else sb['cbb']
                g.dma_start(out=o_bcb[16 * e:16 * (e + 1), 128 * ot:128 * (ot + 1)],
                            in_=buf[0:16, 2048:2176])
                ck[('bc', 'd', i)] = pr.sw.token()
            pr.on('gpsimd', outp)

    entry_tok = pr.sw.token()
    dq_units = []
    dq_units += emit_dequant_u(pr, sb, ck, 'dqg', gq, gsc, gz, 512, NB * 256,
                               lambda j, rb, n0: (o_wgu, HID * (n0 // 256) + 512 * j + 128 * rb, n0 % 256),
                               o_wgs, prevq=None, prevw=None)
    dq_units += emit_dequant_u(pr, sb, ck, 'dqu', uq, usc, uz, 512, NB * 256,
                               lambda j, rb, n0: (o_wgu, HID * (n0 // 256) + 512 * j + 128 * rb, 256 + n0 % 256),
                               o_wus, prevq=('dqg', 'q', 23), prevw=('dqg', 'wsd'))
    dq_units += emit_dequant_u(pr, sb, ck, 'dqd', dq, dsc, dz, FFN_I // 8, 512,
                               lambda j, rb, n0: (o_wdp, 11264 * (n0 // 256) + 1408 * j + 128 * rb, n0 % 256),
                               None, prevq=('dqu', 'q', 23), prevw=None)
    at_units = emit_attn_prep_u(pr, sb, ck, w1, w2c, mkc, o_attnT, o_asum, entry_tok)
    for i in range(max(len(dq_units), len(at_units))):
        if i < len(dq_units):
            dq_units[i]()
        if i < len(at_units):
            at_units[i]()


def emit_prep_body(nc, pr, sb, ck, **kw):
    emit_prep_p1(nc, pr, sb, ck, xs=kw.pop('xs'), o_xbt=kw.pop('o_xbt'),
                 o_lnrow=kw.pop('o_lnrow'), o_lncol=kw.pop('o_lncol'))
    emit_prep_rest(nc, pr, sb, ck, **kw)



def build_prep():
    nc = bass.Bass("TRN2", target_bir_lowering=False, debug=False)
    Iap = lambda name, shape, dt=F32: nc.dram_tensor(name, shape, dt, kind="ExternalInput").ap()
    Oap = lambda name, shape, dt=F32: nc.dram_tensor(name, shape, dt, kind="ExternalOutput").ap()

    xs = Iap("xs", [TPC, HID])
    wi = Iap("wi", [RPC, HID])
    wo = Iap("wo", [IPC, HID])
    gq = Iap("gq", [512, NB * 256], I32)
    gsc = Iap("gsc", [32, NB * 256])
    gz = Iap("gz", [32, NB * 32], I32)
    uq = Iap("uq", [512, NB * 256], I32)
    usc = Iap("usc", [32, NB * 256])
    uz = Iap("uz", [32, NB * 32], I32)
    dq = Iap("dq", [FFN_I // 8, 512], I32)
    dsc = Iap("dsc", [86, 512])
    dz = Iap("dz", [86, 64], I32)
    w1 = Iap("w1", [HID, HID])
    w2s = Iap("w2s", [512, HID])
    masks = Iap("masks", [HID, 512])
    wbs = Iap("wbs", [HID, 512])
    wr = Iap("wr", [8, HID])
    lA = Iap("lA", [128, HID])
    lB = Iap("lB", [8 * HID, 16])

    o_xbt = Oap("o_xbt", [TPC, HID], BF16)
    o_lnrow = Oap("o_lnrow", [2, TPC])
    o_lncol = Oap("o_lncol", [128, 16])
    o_wib = Oap("o_wib", [RPC, HID], BF16)
    o_wsum = Oap("o_wsum", [1, RPC])
    o_wob = Oap("o_wob", [IPC, HID], BF16)
    o_wgu = Oap("o_wgu", [NB * HID, 512], BF16)
    o_wgs = Oap("o_wgs", [1, NB * 256])
    o_wus = Oap("o_wus", [1, NB * 256])
    o_wdp = Oap("o_wdp", [2 * 11264, 256], BF16)
    o_attnT = Oap("o_attnT", [HID, 512], BF16)
    o_asum = Oap("o_asum", [1, 512])
    o_wbb = Oap("o_wbb", [HID, 512], BF16)
    o_wbsum = Oap("o_wbsum", [1, 512])
    o_wrb = Oap("o_wrb", [8, HID], BF16)
    o_rsum = Oap("o_rsum", [1, 8])
    o_lab = Oap("o_lab", [128, HID], BF16)
    o_acs = Oap("o_acs", [1, 128])
    o_bcb = Oap("o_bcb", [128, HID], BF16)
    w2c = nc.dram_tensor("w2c", [512, HID], BF16, kind="Internal").ap()
    mkc = nc.dram_tensor("mkc", [HID, 512], BF16, kind="Internal").ap()

    tensors = CONSTS + PREP_TENSORS

    with nc.Block() as block:
        hw_sems = [nc.semaphore(f"hw{i}").__enter__() for i in range(8)]
        sw_sems = [nc.semaphore(f"sw{i}").__enter__() for i in range(8)]
        th = nc.semaphore("tsem").__enter__()
        vh = nc.semaphore("vsem").__enter__()
        sh = nc.semaphore("ssem").__enter__()
        gh = nc.semaphore("gsem").__enter__()
        sb = alloc(nc, tensors)
        pr = Prog(nc, block, hw_sems, sw_sems, th, vh, sh, gh)
        tsem, vsem, ssem, gsem = pr.tsem, pr.vsem, pr.ssem, pr.gsem
        emit_consts(pr, sb)
        ck = {}
        emit_prep_body(nc, pr, sb, ck, xs=xs, wi=wi, wo=wo, gq=gq, gsc=gsc, gz=gz, uq=uq, usc=usc, uz=uz, dq=dq, dsc=dsc, dz=dz, w1=w1, w2s=w2s, masks=masks, wbs=wbs, wr=wr, lA=lA, lB=lB, o_xbt=o_xbt, o_lnrow=o_lnrow, o_lncol=o_lncol, o_wib=o_wib, o_wsum=o_wsum, o_wob=o_wob, o_wgu=o_wgu, o_wgs=o_wgs, o_wus=o_wus, o_wdp=o_wdp, o_attnT=o_attnT, o_asum=o_asum, o_wbb=o_wbb, o_wbsum=o_wbsum, o_wrb=o_wrb, o_rsum=o_rsum, o_lab=o_lab, o_acs=o_acs, o_bcb=o_bcb, w2c=w2c, mkc=mkc)
        def fin(g):
            wait_tok(g, pr.dtok())
            g.wait_ge(vsem.h, vsem.val)
            g.wait_ge(ssem.h, ssem.val)
            g.wait_ge(tsem.h, tsem.val)
        pr.on('gpsimd', fin)

    return nc


def emit_dequant(pr, sb, ck, tag, qw, sc, zq, QROWS, NC, dst_fn, o_ws):
    tsem, vsem, ssem, gsem = pr.tsem, pr.vsem, pr.ssem, pr.gsem
    NGRP = QROWS // 16
    nzc = cdiv(NC, 8)

    def ldsz(q):
        q.wait_ge(vsem.h, vsem.val)
        q.wait_ge(ssem.h, ssem.val)
        q.wait_ge(tsem.h, tsem.val)
        wait_tok(q, pr.dtok())
        q.dma_start(out=sb['scz'][0:NGRP, 0:NC], in_=sc)
        q.dma_start(out=sb['zqz'][0:NGRP, 0:nzc], in_=zq)
        ck[(tag, 'sz')] = pr.hw.token()
    pr.on('sync', ldsz)

    def zprep(v):
        wait_tok(v, ck[(tag, 'sz')])
        zview = sb['zzi'][0:NGRP, 0:NC].rearrange("g (c l) -> g c l", l=8)
        for l in range(8):
            v.tensor_scalar(zview[:, :, l], sb['zqz'][0:NGRP, 0:nzc], 4 * l, 15,
                            ALU.logical_shift_right, ALU.bitwise_and)
        v.tensor_copy(sb['zz'][0:NGRP, 0:NC], sb['zzi'][0:NGRP, 0:NC])
        v.tensor_scalar(sb['beta'][0:NGRP, 0:NC], sb['zz'][0:NGRP, 0:NC],
                        1.0, -1.0, ALU.add, ALU.mult)
        v.tensor_tensor(sb['beta'][0:NGRP, 0:NC], sb['beta'][0:NGRP, 0:NC],
                        sb['scz'][0:NGRP, 0:NC], ALU.mult)
        ck[(tag, 'beta')] = vsem.val
    pr.on('vector', zprep)

    if o_ws is not None:
        def wsz(g):
            g.wait_ge(vsem.h, ck[(tag, 'beta')])
            g.memset(sb['rowb'][0:1, 0:NC], 0.0)
            ck[(tag, 'ws0')] = gsem.val
        pr.on('gpsimd', wsz)

    nblk = cdiv(QROWS, 128)
    nsubs = [(256 * i, min(NC, 256 * (i + 1))) for i in range(cdiv(NC, 256))]
    step = 0
    for b in range(nblk):
        rb0 = 128 * b
        rbn = min(QROWS, rb0 + 128) - rb0
        for (n0, n1) in nsubs:
            nn = n1 - n0

            def ldq(q, b=b, rb0=rb0, rbn=rbn, n0=n0, nn=nn, step=step):
                if step > 0:
                    q.wait_ge(vsem.h, ck[(tag, 'u', step - 1, 7)])
                q.wait_ge(vsem.h, ck[(tag, 'beta')])
                q.dma_start(out=sb['qwa'][0:rbn, 0:nn], in_=qw[rb0:rb0 + rbn, n0:n1])
                g0 = rb0 // 16
                ng = cdiv(rbn, 16)
                sap = sb['scz'][g0:g0 + ng, n0:n1].unsqueeze(1).to_broadcast([ng, 16, nn])
                bap = sb['beta'][g0:g0 + ng, n0:n1].unsqueeze(1).to_broadcast([ng, 16, nn])
                q.dma_start(out=sb['srep'][0:rbn, 0:nn], in_=sap)
                q.dma_start(out=sb['brep'][0:rbn, 0:nn], in_=bap)
                ck[(tag, 'q', step)] = pr.hw.token()
            pr.on('sync', ldq)

            for j in range(8):
                def unpack(v, j=j, rbn=rbn, nn=nn, step=step):
                    wait_tok(v, ck[(tag, 'q', step)])
                    if j >= 1:
                        wait_tok(v, ck[(tag, 'd', step, j - 1)])
                    elif step > 0:
                        wait_tok(v, ck[(tag, 'd', step - 1, 7)])
                    v.tensor_scalar(sb['nib'][0:rbn, 0:nn], sb['qwa'][0:rbn, 0:nn],
                                    4 * j, 15, ALU.logical_shift_right, ALU.bitwise_and)
                    v.tensor_tensor(sb['dqt'][0:rbn, 0:nn], sb['nib'][0:rbn, 0:nn],
                                    sb['srep'][0:rbn, 0:nn], ALU.mult)
                    v.tensor_tensor(sb['dqw'][0:rbn, 0:nn], sb['dqt'][0:rbn, 0:nn],
                                    sb['brep'][0:rbn, 0:nn], ALU.add)
                    ck[(tag, 'u', step, j)] = vsem.val
                pr.on('vector', unpack)

                if o_ws is not None:
                    def wsmm(tt, j=j, rbn=rbn, nn=nn, step=step):
                        tt.wait_ge(vsem.h, ck[(tag, 'u', step, j)])
                        if (tag, 'm', step, j - 1) in ck:
                            tt.wait_ge(vsem.h, ck[(tag, 'm', step, j - 1)])
                        elif step > 0:
                            tt.wait_ge(vsem.h, ck[(tag, 'm', step - 1, 7)])
                        else:
                            tt.wait_ge(gsem.h, ck[(tag, 'ws0')])
                        ins = tt.matmul(sb['psR'][0:1, 0:nn], sb['ones_bf'][0:rbn, 0:1],
                                        sb['dqw'][0:rbn, 0:nn], start=True, stop=True)
                        tsem.inc(ins)
                        ck[(tag, 'mm', step, j)] = tsem.val
                    pr.on('tensor', wsmm)

                    def wsacc(v, n0=n0, nn=nn, step=step, j=j):
                        v.wait_ge(tsem.h, ck[(tag, 'mm', step, j)])
                        v.tensor_tensor(sb['rowb'][0:1, n0:n0 + nn],
                                        sb['rowb'][0:1, n0:n0 + nn],
                                        sb['psR'][0:1, 0:nn], ALU.add)
                        ck[(tag, 'm', step, j)] = vsem.val
                    pr.on('vector', wsacc)

                def wout(g, b=b, j=j, rbn=rbn, n0=n0, nn=nn, step=step):
                    g.wait_ge(vsem.h, ck[(tag, 'u', step, j)])
                    dst, row0, col0 = dst_fn(j, b, n0)
                    g.dma_start(out=dst[row0:row0 + rbn, col0:col0 + nn],
                                in_=sb['dqw'][0:rbn, 0:nn])
                    ck[(tag, 'd', step, j)] = pr.sw.token()
                pr.on('gpsimd', wout)
            step += 1

    if o_ws is not None:
        laststep = step - 1

        def wsout(g):
            g.wait_ge(vsem.h, ck[(tag, 'm', laststep, 7)])
            g.dma_start(out=o_ws, in_=sb['rowb'][0:1, 0:NC])
            ck[(tag, 'wsd')] = pr.sw.token()
        pr.on('gpsimd', wsout)


def emit_attn_prep(pr, sb, ck, w1, w2c, mkc, o_attnT, o_asum):
    tsem, vsem, ssem, gsem = pr.tsem, pr.vsem, pr.ssem, pr.gsem

    def ldw2(q):
        q.wait_ge(vsem.h, vsem.val)
        q.wait_ge(ssem.h, ssem.val)
        q.wait_ge(tsem.h, tsem.val)
        wait_tok(q, pr.dtok())
        for d in range(32):
            q.dma_start_transpose(sb['w2T'][:, 512 * d:512 * (d + 1)],
                                  w2c[0:512, 128 * d:128 * (d + 1)])
        ck['at_w2'] = pr.hw.token()
    pr.on('sync', ldw2)

    def az(g):
        wait_tok(g, ck['at_w2'])
        g.memset(sb['cols'][:, 56:60], 0.0)
        ck['at_az'] = gsem.val
    pr.on('gpsimd', az)

    banks = ['psA', 'psB', 'psC', 'psD']
    for sn in range(8):
        for d in range(32):
            def ldp(g, sn=sn, d=d):
                i = sn * 32 + d
                if i >= 2:
                    g.wait_ge(tsem.h, ck[('at', 'mm', i - 2)])
                if i == 0:
                    g.wait_ge(gsem.h, ck['at_az'])
                buf = sb['cba' if d % 2 == 0 else 'cbb']
                g.dma_start(out=buf[:, 0:512],
                            in_=w1[128 * d:128 * (d + 1), 512 * sn:512 * (sn + 1)])
                ck[('at', 'p', i)] = pr.sw.token()
            pr.on('gpsimd', ldp)

            def mm(tt, sn=sn, d=d):
                i = sn * 32 + d
                wait_tok(tt, ck[('at', 'p', i)])
                if i == 0:
                    wait_tok(tt, ck['at_w2'])
                if d == 0 and sn > 0:
                    tt.wait_ge(vsem.h, ck[('at', 'ev', sn - 1, 3)])
                buf = sb['cba' if d % 2 == 0 else 'cbb']
                for m in range(4):
                    ins = tt.matmul(sb[banks[m]][:, 0:512],
                                    sb['w2T'][:, 512 * d + 128 * m:512 * d + 128 * (m + 1)],
                                    buf[:, 0:512], start=(d == 0), stop=(d == 31))
                tsem.inc(ins)
                ck[('at', 'mm', i)] = tsem.val
            pr.on('tensor', mm)

        for m in range(4):
            def ldmk(q, sn=sn, m=m):
                q.wait_ge(tsem.h, ck[('at', 'mm', sn * 32 + 31)])
                if (sn, m) != (0, 0):
                    pv = ('at', 'ev', sn, m - 1) if m > 0 else ('at', 'ev', sn - 1, 3)
                    q.wait_ge(vsem.h, ck[pv])
                q.dma_start_transpose(sb['mk'][:, 0:512],
                                      mkc[512 * sn:512 * (sn + 1), 128 * m:128 * (m + 1)])
                ck[('at', 'mk', sn, m)] = pr.hw.token()
            pr.on('sync', ldmk)

            def ev(v, sn=sn, m=m):
                wait_tok(v, ck[('at', 'mk', sn, m)])
                v.wait_ge(gsem.h, ck['at_az'])
                if (sn, m) != (0, 0):
                    pv = ('at', 'tr', sn, m - 1, 3) if m > 0 else ('at', 'tr', sn - 1, 3, 3)
                    tv = ck.get(pv)
                    if tv is not None:
                        v.wait_ge(tsem.h, tv)
                v.scalar_tensor_tensor(sb['att'][:, 0:512], sb[banks[m]][:, 0:512],
                                       1.0, sb['mk'][:, 0:512], ALU.mult, ALU.mult)
                v.reduce_sum(sb['cols'][:, 60:61], sb['att'][:, 0:512], AXX)
                v.tensor_tensor(sb['cols'][:, 56 + m:57 + m], sb['cols'][:, 56 + m:57 + m],
                                sb['cols'][:, 60:61], ALU.add)
                ck[('at', 'ev', sn, m)] = vsem.val
            pr.on('vector', ev)

            for j in range(4):
                def atr(tt, sn=sn, m=m, j=j):
                    tt.wait_ge(vsem.h, ck[('at', 'ev', sn, m)])
                    pv = ck.get(('at', 'te', sn, m, j - 2),
                                ck.get(('at', 'te', sn, m - 1, j + 2),
                                       ck.get(('at', 'te', sn - 1, 3, j + 2))))
                    if pv is not None:
                        tt.wait_ge(vsem.h, pv)
                    ins = tt.transpose(sb['psT' if j % 2 == 0 else 'psT2'][:, 0:128],
                                       sb['att'][:, 128 * j:128 * (j + 1)],
                                       sb['idbf'][:, :])
                    tsem.inc(ins)
                    ck[('at', 'tr', sn, m, j)] = tsem.val
                pr.on('tensor', atr)

                def ate(v, sn=sn, m=m, j=j):
                    v.wait_ge(tsem.h, ck[('at', 'tr', sn, m, j)])
                    pv = ck.get(('at', 'ad', sn, m, j - 2),
                                ck.get(('at', 'ad', sn, m - 1, j + 2),
                                       ck.get(('at', 'ad', sn - 1, 3, j + 2))))
                    if pv is not None:
                        wait_tok(v, pv)
                    v.tensor_copy(sb['att2'][:, 128 * (j % 2):128 * (j % 2 + 1)],
                                  sb['psT' if j % 2 == 0 else 'psT2'][:, 0:128])
                    ck[('at', 'te', sn, m, j)] = vsem.val
                pr.on('vector', ate)

                def outd(g, sn=sn, m=m, j=j):
                    g.wait_ge(vsem.h, ck[('at', 'te', sn, m, j)])
                    g.dma_start(out=o_attnT[512 * sn + 128 * j:512 * sn + 128 * (j + 1),
                                            128 * m:128 * (m + 1)],
                                in_=sb['att2'][:, 128 * (j % 2):128 * (j % 2 + 1)])
                    ck[('at', 'ad', sn, m, j)] = pr.sw.token()
                pr.on('gpsimd', outd)

    def asum_t(tt):
        tt.wait_ge(vsem.h, ck[('at', 'ev', 7, 3)])
        ins = tt.transpose(sb['psTF'][0:4, 0:128], sb['cols'][:, 56:60], sb['idf'][:, :])
        tsem.inc(ins)
        ck['at_str'] = tsem.val
    pr.on('tensor', asum_t)

    def asum_v(v):
        v.wait_ge(tsem.h, ck['at_str'])
        v.tensor_copy(sb['sq'][0:4, 0:128], sb['psTF'][0:4, 0:128])
        ck['at_srow'] = vsem.val
    pr.on('vector', asum_v)

    def asum_g(g):
        g.wait_ge(vsem.h, ck['at_srow'])
        for m in range(4):
            g.dma_start(out=o_asum[0:1, 128 * m:128 * (m + 1)], in_=sb['sq'][m:m + 1, 0:128])
        ck['at_sd'] = pr.sw.token()
    pr.on('gpsimd', asum_g)


def emit_dequant_u(pr, sb, ck, tag, qw, sc, zq, QROWS, NC, dst_fn, o_ws,
                   prevq=None, prevw=None):
    tsem, vsem, ssem, gsem = pr.tsem, pr.vsem, pr.ssem, pr.gsem
    NGRP = QROWS // 16
    nzc = cdiv(NC, 8)
    units = []

    def u_head():
        def ldsz(q):
            if prevq is not None:
                wait_tok(q, ck[prevq])
            q.dma_start(out=sb['scz'][0:NGRP, 0:NC], in_=sc)
            q.dma_start(out=sb['zqz'][0:NGRP, 0:nzc], in_=zq)
            ck[(tag, 'sz')] = pr.hw.token()
        pr.on('sync', ldsz)

        def zprep(v):
            wait_tok(v, ck[(tag, 'sz')])
            zview = sb['zzi'][0:NGRP, 0:NC].rearrange("g (c l) -> g c l", l=8)
            for l in range(8):
                v.tensor_scalar(zview[:, :, l], sb['zqz'][0:NGRP, 0:nzc], 4 * l, 15,
                                ALU.logical_shift_right, ALU.bitwise_and)
            v.tensor_copy(sb['zz'][0:NGRP, 0:NC], sb['zzi'][0:NGRP, 0:NC])
            v.tensor_scalar(sb['beta'][0:NGRP, 0:NC], sb['zz'][0:NGRP, 0:NC],
                            1.0, -1.0, ALU.add, ALU.mult)
            v.tensor_tensor(sb['beta'][0:NGRP, 0:NC], sb['beta'][0:NGRP, 0:NC],
                            sb['scz'][0:NGRP, 0:NC], ALU.mult)
            ck[(tag, 'beta')] = vsem.val
        pr.on('vector', zprep)

        if o_ws is not None:
            def wsz(g):
                if prevw is not None:
                    wait_tok(g, ck[prevw])
                g.memset(sb['rowb'][0:1, 0:NC], 0.0)
                ck[(tag, 'ws0')] = gsem.val
            pr.on('gpsimd', wsz)
    units.append(u_head)

    nblk = cdiv(QROWS, 128)
    nsubs = [(256 * i, min(NC, 256 * (i + 1))) for i in range(cdiv(NC, 256))]
    plan = [(b, n0, n1) for b in range(nblk) for (n0, n1) in nsubs]
    for step, (b, n0, n1) in enumerate(plan):
        rb0 = 128 * b
        rbn = min(QROWS, rb0 + 128) - rb0
        nn = n1 - n0

        def u_step(b=b, rb0=rb0, rbn=rbn, n0=n0, nn=nn, step=step):
            qb = 'qwa' if step % 2 == 0 else 'qwb'
            srb = 'srep' if step % 2 == 0 else 'srb'
            brb = 'brep' if step % 2 == 0 else 'brb'

            def ldq(q):
                if step >= 2:
                    q.wait_ge(vsem.h, ck[(tag, 'u', step - 2, 7)])
                q.wait_ge(vsem.h, ck[(tag, 'beta')])
                q.dma_start(out=sb[qb][0:rbn, 0:nn], in_=qw[rb0:rb0 + rbn, n0:n0 + nn])
                g0 = rb0 // 16
                ng = cdiv(rbn, 16)
                sap = sb['scz'][g0:g0 + ng, n0:n0 + nn].unsqueeze(1).to_broadcast([ng, 16, nn])
                bap = sb['beta'][g0:g0 + ng, n0:n0 + nn].unsqueeze(1).to_broadcast([ng, 16, nn])
                q.dma_start(out=sb[srb][0:rbn, 0:nn], in_=sap)
                q.dma_start(out=sb[brb][0:rbn, 0:nn], in_=bap)
                ck[(tag, 'q', step)] = pr.hw.token()
            pr.on('sync', ldq)

            for j in range(8):
                dwb = 'dqw' if j % 2 == 0 else 'dqw2'

                def unpack(v, j=j, dwb=dwb):
                    wait_tok(v, ck[(tag, 'q', step)])
                    pd = ((tag, 'd', step, j - 2) if j >= 2 else
                          ((tag, 'd', step - 1, 6 + j) if step > 0 else None))
                    if pd is not None:
                        wait_tok(v, ck[pd])
                    if o_ws is not None:
                        pm = ((tag, 'mm', step, j - 2) if j >= 2 else
                              ((tag, 'mm', step - 1, 6 + j) if step > 0 else None))
                        if pm is not None:
                            v.wait_ge(tsem.h, ck[pm])
                    v.tensor_scalar(sb['nib'][0:rbn, 0:nn], sb[qb][0:rbn, 0:nn],
                                    4 * j, 15, ALU.logical_shift_right, ALU.bitwise_and)
                    v.tensor_tensor(sb['dqt'][0:rbn, 0:nn], sb['nib'][0:rbn, 0:nn],
                                    sb[srb][0:rbn, 0:nn], ALU.mult)
                    v.tensor_tensor(sb[dwb][0:rbn, 0:nn], sb['dqt'][0:rbn, 0:nn],
                                    sb[brb][0:rbn, 0:nn], ALU.add)
                    ck[(tag, 'u', step, j)] = vsem.val
                pr.on('vector', unpack)

                if o_ws is not None:
                    def wsmm(tt, j=j, dwb=dwb):
                        tt.wait_ge(vsem.h, ck[(tag, 'u', step, j)])
                        if (tag, 'm', step, j - 1) in ck:
                            tt.wait_ge(vsem.h, ck[(tag, 'm', step, j - 1)])
                        elif step > 0:
                            tt.wait_ge(vsem.h, ck[(tag, 'm', step - 1, 7)])
                        else:
                            tt.wait_ge(gsem.h, ck[(tag, 'ws0')])
                        ins = tt.matmul(sb['psR'][0:1, 0:nn], sb['ones_bf'][0:rbn, 0:1],
                                        sb[dwb][0:rbn, 0:nn], start=True, stop=True)
                        tsem.inc(ins)
                        ck[(tag, 'mm', step, j)] = tsem.val
                    pr.on('tensor', wsmm)

                    def wsacc(v, j=j):
                        v.wait_ge(tsem.h, ck[(tag, 'mm', step, j)])
                        v.tensor_tensor(sb['rowb'][0:1, n0:n0 + nn],
                                        sb['rowb'][0:1, n0:n0 + nn],
                                        sb['psR'][0:1, 0:nn], ALU.add)
                        ck[(tag, 'm', step, j)] = vsem.val
                    pr.on('vector', wsacc)

                def wout(g, b=b, j=j, dwb=dwb):
                    g.wait_ge(vsem.h, ck[(tag, 'u', step, j)])
                    dst, row0, col0 = dst_fn(j, b, n0)
                    g.dma_start(out=dst[row0:row0 + rbn, col0:col0 + nn],
                                in_=sb[dwb][0:rbn, 0:nn])
                    ck[(tag, 'd', step, j)] = pr.sw.token()
                pr.on('gpsimd', wout)
        units.append(u_step)

    if o_ws is not None:
        laststep = len(plan) - 1

        def u_tail():
            def wsout(g):
                g.wait_ge(vsem.h, ck[(tag, 'm', laststep, 7)])
                g.dma_start(out=o_ws, in_=sb['rowb'][0:1, 0:NC])
                ck[(tag, 'wsd')] = pr.sw.token()
            pr.on('gpsimd', wsout)
        units.append(u_tail)
    return units


def emit_attn_prep_u(pr, sb, ck, w1, w2c, mkc, o_attnT, o_asum, entry_tok):
    tsem, vsem, ssem, gsem = pr.tsem, pr.vsem, pr.ssem, pr.gsem
    units = []

    def u_head():
        def ldw2(q):
            wait_tok(q, entry_tok)
            for d in range(32):
                q.dma_start_transpose(sb['w2T'][:, 512 * d:512 * (d + 1)],
                                      w2c[0:512, 128 * d:128 * (d + 1)])
            ck['at_w2'] = pr.hw.token()
        pr.on('sync', ldw2)

        def az(g):
            wait_tok(g, ck['at_w2'])
            g.memset(sb['cols'][:, 56:60], 0.0)
            ck['at_az'] = gsem.val
        pr.on('gpsimd', az)
    units.append(u_head)

    banks = ['psA', 'psB', 'psC', 'psD']
    for sn in range(8):
        for dg in range(4):
            def u_mm(sn=sn, dg=dg):
                for d in range(8 * dg, 8 * dg + 8):
                    def ldp(g, sn=sn, d=d):
                        i = sn * 32 + d
                        if i >= 2:
                            g.wait_ge(tsem.h, ck[('at', 'mm', i - 2)])
                        if i == 0:
                            g.wait_ge(gsem.h, ck['at_az'])
                        buf = sb['cba' if d % 2 == 0 else 'cbb']
                        g.dma_start(out=buf[:, 0:512],
                                    in_=w1[128 * d:128 * (d + 1), 512 * sn:512 * (sn + 1)])
                        ck[('at', 'p', i)] = pr.sw.token()
                    pr.on('gpsimd', ldp)

                    def mm(tt, sn=sn, d=d):
                        i = sn * 32 + d
                        wait_tok(tt, ck[('at', 'p', i)])
                        if i == 0:
                            wait_tok(tt, ck['at_w2'])
                        if d == 0 and sn > 0:
                            tt.wait_ge(vsem.h, ck[('at', 'ev', sn - 1, 3)])
                        buf = sb['cba' if d % 2 == 0 else 'cbb']
                        for m in range(4):
                            ins = tt.matmul(sb[banks[m]][:, 0:512],
                                            sb['w2T'][:, 512 * d + 128 * m:512 * d + 128 * (m + 1)],
                                            buf[:, 0:512], start=(d == 0), stop=(d == 31))
                        tsem.inc(ins)
                        ck[('at', 'mm', i)] = tsem.val
                    pr.on('tensor', mm)
            units.append(u_mm)

        for m in range(4):
            def u_ev(sn=sn, m=m):
                def ldmk(q, sn=sn, m=m):
                    q.wait_ge(tsem.h, ck[('at', 'mm', sn * 32 + 31)])
                    if (sn, m) != (0, 0):
                        pv = ('at', 'ev', sn, m - 1) if m > 0 else ('at', 'ev', sn - 1, 3)
                        q.wait_ge(vsem.h, ck[pv])
                    q.dma_start_transpose(sb['mk'][:, 0:512],
                                          mkc[512 * sn:512 * (sn + 1), 128 * m:128 * (m + 1)])
                    ck[('at', 'mk', sn, m)] = pr.hw.token()
                pr.on('sync', ldmk)

                def ev(v, sn=sn, m=m):
                    wait_tok(v, ck[('at', 'mk', sn, m)])
                    v.wait_ge(gsem.h, ck['at_az'])
                    if (sn, m) != (0, 0):
                        pv = ('at', 'tr', sn, m - 1, 3) if m > 0 else ('at', 'tr', sn - 1, 3, 3)
                        tv = ck.get(pv)
                        if tv is not None:
                            v.wait_ge(tsem.h, tv)
                    v.scalar_tensor_tensor(sb['att'][:, 0:512], sb[banks[m]][:, 0:512],
                                           1.0, sb['mk'][:, 0:512], ALU.mult, ALU.mult)
                    v.reduce_sum(sb['cols'][:, 60:61], sb['att'][:, 0:512], AXX)
                    v.tensor_tensor(sb['cols'][:, 56 + m:57 + m], sb['cols'][:, 56 + m:57 + m],
                                    sb['cols'][:, 60:61], ALU.add)
                    ck[('at', 'ev', sn, m)] = vsem.val
                pr.on('vector', ev)

                for j in range(4):
                    def atr(tt, sn=sn, m=m, j=j):
                        tt.wait_ge(vsem.h, ck[('at', 'ev', sn, m)])
                        pv = ck.get(('at', 'te', sn, m, j - 2),
                                    ck.get(('at', 'te', sn, m - 1, j + 2),
                                           ck.get(('at', 'te', sn - 1, 3, j + 2))))
                        if pv is not None:
                            tt.wait_ge(vsem.h, pv)
                        ins = tt.transpose(sb['psT' if j % 2 == 0 else 'psT2'][:, 0:128],
                                           sb['att'][:, 128 * j:128 * (j + 1)],
                                           sb['idbf'][:, :])
                        tsem.inc(ins)
                        ck[('at', 'tr', sn, m, j)] = tsem.val
                    pr.on('tensor', atr)

                    def ate(v, sn=sn, m=m, j=j):
                        v.wait_ge(tsem.h, ck[('at', 'tr', sn, m, j)])
                        pv = ck.get(('at', 'ad', sn, m, j - 2),
                                    ck.get(('at', 'ad', sn, m - 1, j + 2),
                                           ck.get(('at', 'ad', sn - 1, 3, j + 2))))
                        if pv is not None:
                            wait_tok(v, pv)
                        v.tensor_copy(sb['att2'][:, 128 * (j % 2):128 * (j % 2 + 1)],
                                      sb['psT' if j % 2 == 0 else 'psT2'][:, 0:128])
                        ck[('at', 'te', sn, m, j)] = vsem.val
                    pr.on('vector', ate)

                    def outd(g, sn=sn, m=m, j=j):
                        g.wait_ge(vsem.h, ck[('at', 'te', sn, m, j)])
                        g.dma_start(out=o_attnT[512 * sn + 128 * j:512 * sn + 128 * (j + 1),
                                                128 * m:128 * (m + 1)],
                                    in_=sb['att2'][:, 128 * (j % 2):128 * (j % 2 + 1)])
                        ck[('at', 'ad', sn, m, j)] = pr.sw.token()
                    pr.on('gpsimd', outd)
            units.append(u_ev)

    def u_tail():
        def asum_t(tt):
            tt.wait_ge(vsem.h, ck[('at', 'ev', 7, 3)])
            ins = tt.transpose(sb['psTF'][0:4, 0:128], sb['cols'][:, 56:60], sb['idf'][:, :])
            tsem.inc(ins)
            ck['at_str'] = tsem.val
        pr.on('tensor', asum_t)

        def asum_v(v):
            v.wait_ge(tsem.h, ck['at_str'])
            v.tensor_copy(sb['sq'][0:4, 0:128], sb['psTF'][0:4, 0:128])
            ck['at_srow'] = vsem.val
        pr.on('vector', asum_v)

        def asum_g(g):
            g.wait_ge(vsem.h, ck['at_srow'])
            for m in range(4):
                g.dma_start(out=o_asum[0:1, 128 * m:128 * (m + 1)], in_=sb['sq'][m:m + 1, 0:128])
            ck['at_sd'] = pr.sw.token()
        pr.on('gpsimd', asum_g)
    units.append(u_tail)
    return units



def prep_in_maps(inp):
    x2 = np.ascontiguousarray(inp['L0_attn_input_states'].reshape(TOK, HID))
    Wi = inp['L0_attn_in_proj_w']
    Wo = inp['L0_attn_out_w']
    maps = []
    for c in range(NCORE):
        b0, nb = GU_B0[c], GU_BLOCKS[c]

        def pad_cols(a, colw):
            sl = a[:, b0 * colw:(b0 + nb) * colw]
            if nb < NB:
                sl = np.concatenate([sl, a[:, :colw * (NB - nb)]], axis=1)
            return np.ascontiguousarray(sl)
        m = dict(
            xs=x2[TPC * c:TPC * (c + 1)],
            wi=np.ascontiguousarray(np.concatenate([
                Wi[IPC * c:IPC * (c + 1)],
                Wi[INTER + IPC * c:INTER + IPC * (c + 1)],
                Wi[2 * INTER + 128 * c:2 * INTER + 128 * (c + 1)],
                Wi[2 * INTER + G * NS + 128 * c:2 * INTER + G * NS + 128 * (c + 1)],
                Wi[PROJ - H + HPC * c:PROJ - H + HPC * (c + 1)],
            ], axis=0)),
            wo=np.ascontiguousarray(Wo[:, IPC * c:IPC * (c + 1)].T),
            gq=pad_cols(inp['L0_ffn_gate_qweight'], 256),
            gsc=pad_cols(inp['L0_ffn_gate_scales'], 256),
            gz=pad_cols(inp['L0_ffn_gate_qzeros'], 32),
            uq=pad_cols(inp['L0_ffn_up_qweight'], 256),
            usc=pad_cols(inp['L0_ffn_up_scales'], 256),
            uz=pad_cols(inp['L0_ffn_up_qzeros'], 32),
            dq=np.ascontiguousarray(inp['L0_ffn_down_qweight'][:, 512 * c:512 * (c + 1)]),
            dsc=np.ascontiguousarray(inp['L0_ffn_down_scales'][:, 512 * c:512 * (c + 1)]),
            dz=np.ascontiguousarray(inp['L0_ffn_down_qzeros'][:, 64 * c:64 * (c + 1)]),
            w1=inp['L1_attn_w1'],
            w2s=np.ascontiguousarray(inp['L1_attn_w2'][512 * c:512 * (c + 1), :]),
            masks=np.ascontiguousarray(inp['L1_attn_sparse_mask'][:, 512 * c:512 * (c + 1)]),
            wbs=np.ascontiguousarray(inp['L1_ffn_W_base'][512 * c:512 * (c + 1), :].T),
            wr=np.ascontiguousarray(inp['L1_ffn_W_router'].reshape(8, HID)),
            lA=np.ascontiguousarray(inp['L1_ffn_lora_A'].reshape(128, HID)),
            lB=np.ascontiguousarray(inp['L1_ffn_lora_B'].reshape(8 * HID, 16)),
        )
        maps.append(m)
    return maps


def run_prep(inp):
    nc = build_prep()
    maps = prep_in_maps(inp)
    res = run_bass_kernel_spmd(nc, maps, core_ids=list(range(NCORE)))
    return res.results




ATTN0_TENSORS = [
    ('dts', [128, 1024], F32, 'sbuf'),
    ('hrep', [128, 192], F32, 'sbuf'),
    ('acol', [128, 64], F32, 'sbuf'),
    ('nwrep', [128, 1024], F32, 'sbuf'),
    ('nwrow', [1, 1024], F32, 'sbuf'),
    ('cwc', [128, 40], F32, 'sbuf'),
    ('cbc', [128, 10], F32, 'sbuf'),
    ('grow', [1, 16], F32, 'sbuf'),
]

ATTN0A_TENSORS = [
    ('xseg', [128, 32768], BF16, 'sbuf'),
    ('wgT', [128, 33280], BF16, 'sbuf'),
    ('wTa', [128, 512], BF16, 'sbuf'),
    ('wTb', [128, 512], BF16, 'sbuf'),
    ('arepa', [128, 512], F32, 'sbuf'),
    ('arepb', [128, 512], F32, 'sbuf'),
    ('evb', [128, 1024], BF16, 'sbuf'),
    ('evb2', [128, 1024], BF16, 'sbuf'),
    ('rows', [1, 1024], F32, 'sbuf'),
    ('rown', [1, 1024], F32, 'sbuf'),
    ('wsr', [1, RPC], F32, 'sbuf'),
    ('psA', [128, 512], F32, 'psum'),
    ('psB', [128, 512], F32, 'psum'),
    ('psC', [128, 512], F32, 'psum'),
    ('psD', [128, 512], F32, 'psum'),
    ('psE', [128, 512], F32, 'psum'),
    ('psF', [128, 512], F32, 'psum'),
    ('psG', [128, 512], F32, 'psum'),
    ('psT', [128, 512], BF16, 'psum'),
]

ATTN0A2_TENSORS = [
    ('xin', [128, 2640], BF16, 'sbuf'),
    ('cva', [128, 2560], F32, 'sbuf'),
    ('cvb', [128, 2560], F32, 'sbuf'),
    ('svo', [128, 2560], BF16, 'sbuf'),
]

ATTN0B_TENSORS = [
    ('bcb16', [128, 512], BF16, 'sbuf'),
    ('btm', [128, 256], BF16, 'sbuf'),
    ('xctm', [128, 2048], BF16, 'sbuf'),
    ('hcols', [128, 192], F32, 'sbuf'),
    ('hctp', [128, 32], F32, 'sbuf'),
    ('hctq', [128, 32], F32, 'sbuf'),
    ('csrow', [1, 32], F32, 'sbuf'),
    ('tp4', [128, 1024], F32, 'sbuf'),
    ('texp4', [128, 1024], F32, 'sbuf'),
    ('sct', [128, 256], BF16, 'sbuf'),
    ('xcw', [128, 128], BF16, 'sbuf'),
    ('y1t', [128, 128], F32, 'sbuf'),
    ('pfa', [128, 1024], F32, 'sbuf'),
    ('pfb', [128, 1024], F32, 'sbuf'),
    ('pbf', [128, 1024], BF16, 'sbuf'),
    ('ybf', [128, 2048], BF16, 'sbuf'),
    ('pCB', [128, 512], F32, 'psum'),
    ('pQ0', [128, 512], F32, 'psum'),
    ('pQ1', [128, 512], F32, 'psum'),
    ('pQ2', [128, 512], F32, 'psum'),
    ('pF', [128, 512], F32, 'psum'),
    ('pC', [128, 512], F32, 'psum'),
    ('pD', [128, 512], F32, 'psum'),
    ('pE', [128, 512], F32, 'psum'),
]

ATTN0C_TENSORS = [
    ('gld', [128, 4096], BF16, 'sbuf'),
    ('yld', [128, 4096], BF16, 'sbuf'),
    ('sg4', [128, 4096], F32, 'sbuf'),
    ('gg4', [128, 4096], F32, 'sbuf'),
    ('stat', [128, 16], F32, 'sbuf'),
    ('evc', [128, 2048], BF16, 'sbuf'),
    ('tev', [128, 512], BF16, 'sbuf'),
    ('psT0', [128, 512], BF16, 'psum'),
    ('psT1', [128, 512], BF16, 'psum'),
    ('psT2', [128, 512], BF16, 'psum'),
    ('psT3', [128, 512], BF16, 'psum'),
]

def emit_attn0_body(nc, pr, sb, ck, *, xbt, lnrow, lncola, wib, wsum, nwc, cw, cb, dtb, alog, dd, o_ghat, xbc_s, gate_s, cvs_s, ys_s, post_a=None):
    import contextlib
    tsem, vsem, ssem, gsem = pr.tsem, pr.vsem, pr.ssem, pr.gsem

    with contextlib.ExitStack() as _stA:
        sba = alloc(nc, ATTN0A_TENSORS, stack=_stA, prefix='a0a_')
        sba.update(sb)
        _emit_attn0_A(nc, pr, sba, ck, xbt=xbt, lnrow=lnrow, lncola=lncola,
                      wib=wib, wsum=wsum, nwc=nwc, cw=cw, cb=cb, dtb=dtb,
                      alog=alog, dd=dd, xbc_s=xbc_s, gate_s=gate_s)
        emit_barrier(pr)
    if post_a is not None:
        post_a()

    with contextlib.ExitStack() as _st2:
        sb2 = alloc(nc, ATTN0A2_TENSORS, stack=_st2, prefix='a0c_')
        sb2.update(sb)
        _emit_attn0_A2(nc, pr, sb2, ck, xbc_s=xbc_s, cvs_s=cvs_s)
        emit_barrier(pr)

    with contextlib.ExitStack() as _stB:
        sbb = alloc(nc, ATTN0B_TENSORS, stack=_stB, prefix='a0b_')
        sbb.update(sb)
        _emit_attn0_B(nc, pr, sbb, ck, cvs_s=cvs_s, ys_s=ys_s)
        emit_barrier(pr)

    with contextlib.ExitStack() as _stC:
        sbc = alloc(nc, ATTN0C_TENSORS, stack=_stC, prefix='a0g_')
        sbc.update(sb)
        _emit_attn0_C(nc, pr, sbc, ck, gate_s=gate_s, ys_s=ys_s, o_ghat=o_ghat)


def _emit_attn0_A(nc, pr, sb, ck, *, xbt, lnrow, lncola, wib, wsum, nwc, cw, cb, dtb, alog, dd, xbc_s, gate_s):
    tsem, vsem, ssem, gsem = pr.tsem, pr.vsem, pr.ssem, pr.gsem
    banks = ['psA', 'psB', 'psC', 'psD', 'psE', 'psF', 'psG', 'psT']

    def ld0(q):
        q.dma_start(out=sb['wsr'][0:1, :], in_=wsum)
        q.dma_start(out=sb['acol'][:, :], in_=lncola)
        q.dma_start(out=sb['cbc'][:, 0:10],
                    in_=bass.AP(cb.tensor, 0, [[1, 128], [128, 10]]),
                    allow_slow_non_contiguous=True)
        for ct in range(10):
            q.dma_start(out=sb['cwc'][:, 4 * ct:4 * (ct + 1)],
                        in_=cw[128 * ct:128 * (ct + 1), :])
        q.dma_start(out=sb['grow'][0:1, 0:16], in_=alog)
        q.dma_start(out=sb['hrep'][:, 16:32],
                    in_=bass.AP(dtb.tensor, 0, [[0, 128], [1, 16]]))
        q.dma_start(out=sb['hrep'][:, 32:48],
                    in_=bass.AP(dd.tensor, 0, [[0, 128], [1, 16]]))
        q.dma_start(out=sb['nwrow'][0:1, 0:1024], in_=nwc)
        ck['ld0'] = pr.hw.token()
        for k in range(32):
            q.dma_start_transpose(sb['wgT'][:, 1040 * k:1040 * k + 512],
                                  wib[0:512, 128 * k:128 * (k + 1)])
            q.dma_start_transpose(sb['wgT'][:, 1040 * k + 512:1040 * k + 1024],
                                  wib[512:1024, 128 * k:128 * (k + 1)])
            q.dma_start_transpose(sb['wgT'][:, 1040 * k + 1024:1040 * k + 1040],
                                  wib[2304:2320, 128 * k:128 * (k + 1)])
        ck['wgT'] = pr.hw.token()
    pr.on('sync', ld0)

    def c_t1(tt):
        wait_tok(tt, ck['ld0'])
        tt.wait_ge(gsem.h, gsem.val)
        ins = tt.matmul(sb['psF'][:, 0:16], sb['ones_f'][0:1, 0:128],
                        sb['grow'][0:1, 0:16], start=True, stop=True)
        tsem.inc(ins)
        ck['c_mm1'] = tsem.val
    pr.on('tensor', c_t1)

    def c_s1(s):
        s.wait_ge(tsem.h, ck['c_mm1'])
        s.activation(sb['hrep'][:, 0:16], sb['psF'][:, 0:16], AF.Exp)
        ck['c_exp'] = ssem.val
    pr.on('scalar', c_s1)

    def c_v1(v):
        v.wait_ge(ssem.h, ck['c_exp'])
        v.tensor_scalar(sb['hrep'][:, 0:16], sb['hrep'][:, 0:16], -1.0, None, ALU.mult)
        ck['c_arep'] = vsem.val
    pr.on('vector', c_v1)

    for half in range(2):
        def nwt(tt, half=half):
            if half == 0:
                tt.wait_ge(vsem.h, ck['c_arep'])
            else:
                tt.wait_ge(vsem.h, ck[('nwev', 0)])
            ins = tt.matmul(sb['psF'][:, 0:512], sb['ones_f'][0:1, 0:128],
                            sb['nwrow'][0:1, 512 * half:512 * (half + 1)],
                            start=True, stop=True)
            tsem.inc(ins)
            ck[('nwmm', half)] = tsem.val
        pr.on('tensor', nwt)

        def nwv(v, half=half):
            v.wait_ge(tsem.h, ck[('nwmm', half)])
            v.tensor_copy(sb['nwrep'][:, 512 * half:512 * (half + 1)], sb['psF'][:, 0:512])
            ck[('nwev', half)] = vsem.val
        pr.on('vector', nwv)

    for seg in range(8):
        t0 = 1024 * seg

        def xload(q, seg=seg, t0=t0):
            if seg == 0:
                q.wait_ge(vsem.h, ck[('nwev', 1)])
            else:
                q.wait_ge(tsem.h, ck[('a2mm', 8 * seg - 1)])
            q.dma_start(out=sb['rows'][0:1, :], in_=lnrow[0:1, t0:t0 + 1024])
            q.dma_start(out=sb['rown'][0:1, :], in_=lnrow[1:2, t0:t0 + 1024])
            for kf in range(32):
                for p2 in range(2):
                    q.dma_start_transpose(
                        sb['xseg'][:, 1024 * kf + 512 * p2:1024 * kf + 512 * (p2 + 1)],
                        xbt[t0 + 512 * p2:t0 + 512 * (p2 + 1), 128 * kf:128 * (kf + 1)])
            ck[('xs', seg)] = pr.hw.token()
        pr.on('sync', xload)

        for tq in range(2):
            def armm(tt, seg=seg, tq=tq):
                pv = ck.get(('ar_ev', seg - 1, tq) if tq == 0 else ('ar_ev', seg, tq - 1))
                if pv is None:
                    tt.wait_ge(vsem.h, ck[('nwev', 1)])
                else:
                    tt.wait_ge(vsem.h, pv)
                wait_tok(tt, ck[('xs', seg)])
                ins = tt.matmul(sb['psG'][:, 0:512], sb['ones_f'][0:1, 0:128],
                                sb['rows'][0:1, 512 * tq:512 * (tq + 1)],
                                start=True, stop=True)
                tsem.inc(ins)
                ck[('ar_mm', seg, tq)] = tsem.val
            pr.on('tensor', armm)

            def arev(v, seg=seg, tq=tq):
                v.wait_ge(tsem.h, ck[('ar_mm', seg, tq)])
                v.tensor_copy((sb['arepa'] if tq == 0 else sb['arepb'])[:, 0:512],
                              sb['psG'][:, 0:512])
                ck[('ar_ev', seg, tq)] = vsem.val
            pr.on('vector', arev)

        for gi, (mg0, mgn) in enumerate([(0, 3), (3, 3), (6, 2), (8, 2)]):
            for k in range(32):
                def wld(q, seg=seg, gi=gi, mg0=mg0, mgn=mgn, k=k):
                    wait_tok(q, ck[('xs', seg)])
                    pv = ck.get(('a1mme', seg, gi, k - 2))
                    if pv is not None:
                        q.wait_ge(tsem.h, pv)
                    elif k < 2 and (seg, gi) != (0, 0):
                        pseg, pgi = (seg, gi - 1) if gi > 0 else (seg - 1, 3)
                        q.wait_ge(tsem.h, ck[('a1mme', pseg, pgi, 30 + k)])
                    buf = sb['wTa' if k % 2 == 0 else 'wTb']
                    q.dma_start_transpose(
                        buf[:, 0:128 * mgn],
                        wib[1024 + 128 * mg0:1024 + 128 * (mg0 + mgn), 128 * k:128 * (k + 1)])
                    ck[('a1w', seg, gi, k)] = pr.hw.token()
                pr.on('sync', wld)

                def a1mm(tt, seg=seg, gi=gi, mg0=mg0, mgn=mgn, k=k):
                    wait_tok(tt, ck[('a1w', seg, gi, k)])
                    if k == 0:
                        pv = (ck.get(('a1evl', seg, gi - 1)) if gi > 0
                              else ck.get(('a1evl', seg - 1, 3)))
                        if pv is not None:
                            tt.wait_ge(vsem.h, pv)
                        if gi == 0 and seg > 0:
                            tt.wait_ge(vsem.h, ck[('a2evl', 8 * seg - 1)])
                    buf = sb['wTa' if k % 2 == 0 else 'wTb']
                    for mi in range(mgn):
                        for tq in range(2):
                            ins = tt.matmul(
                                sb[banks[mi * 2 + tq]][:, 0:512],
                                buf[:, 128 * mi:128 * (mi + 1)],
                                sb['xseg'][:, 1024 * k + 512 * tq:1024 * k + 512 * (tq + 1)],
                                start=(k == 0), stop=False)
                    tsem.inc(ins)
                    ck[('a1mme', seg, gi, k)] = tsem.val
                pr.on('tensor', a1mm)

            def a1bf(tt, seg=seg, gi=gi, mg0=mg0, mgn=mgn):
                for mi in range(mgn):
                    for tq in range(2):
                        ins = tt.matmul(
                            sb[banks[mi * 2 + tq]][:, 0:512],
                            sb['wsr'][0:1, 1024 + 128 * (mg0 + mi):1024 + 128 * (mg0 + mi + 1)],
                            sb['rown'][0:1, 512 * tq:512 * (tq + 1)],
                            start=False, stop=True)
                tsem.inc(ins)
                ck[('a1bf', seg, gi)] = tsem.val
            pr.on('tensor', a1bf)

            for mi in range(mgn):
                def a1ev(v, seg=seg, gi=gi, mg0=mg0, mgn=mgn, mi=mi):
                    v.wait_ge(tsem.h, ck[('a1bf', seg, gi)])
                    for tq in range(2):
                        v.wait_ge(vsem.h, ck[('ar_ev', seg, tq)])
                    pv = ck.get(('a1dma_last',))
                    if pv is not None:
                        wait_tok(v, pv)
                    for tq in range(2):
                        arep = sb['arepa'] if tq == 0 else sb['arepb']
                        v.tensor_tensor(sb['evb'][:, 512 * tq:512 * (tq + 1)],
                                        sb[banks[mi * 2 + tq]][:, 0:512],
                                        arep[:, 0:512], ALU.mult)
                    ck[('a1ev', seg, gi, mi)] = vsem.val
                    if mi == mgn - 1:
                        ck[('a1evl', seg, gi)] = vsem.val
                pr.on('vector', a1ev)

                def a1dma(g, seg=seg, gi=gi, mg0=mg0, mi=mi):
                    g.wait_ge(vsem.h, ck[('a1ev', seg, gi, mi)])
                    g.dma_start(out=xbc_s[128 * (mg0 + mi):128 * (mg0 + mi + 1),
                                          1024 * seg:1024 * (seg + 1)],
                                in_=sb['evb'][:, 0:1024])
                    ck[('a1dma_last',)] = pr.sw.token()
                pr.on('gpsimd', a1dma)

        for tt8 in range(8):
            tau = 8 * seg + tt8
            idx = tau

            def a2mm(tt, seg=seg, tt8=tt8, tau=tau, idx=idx):
                wait_tok(tt, ck[('xs', seg)])
                if (seg, tt8) == (0, 0):
                    wait_tok(tt, ck['wgT'])
                pv = ck.get(('a2evl', idx - 1))
                if pv is not None:
                    tt.wait_ge(vsem.h, pv)
                if tt8 == 0:
                    tt.wait_ge(vsem.h, ck[('a1evl', seg, 3)])
                for k in range(32):
                    xap = sb['xseg'][:, 1024 * k + 128 * tt8:1024 * k + 128 * (tt8 + 1)]
                    tt.matmul(sb['psA'][:, 0:512], xap, sb['wgT'][:, 1040 * k:1040 * k + 512],
                              start=(k == 0), stop=False)
                    tt.matmul(sb['psB'][:, 0:512], xap,
                              sb['wgT'][:, 1040 * k + 512:1040 * k + 1024],
                              start=(k == 0), stop=False)
                    ins = tt.matmul(sb['psC'][:, 0:16], xap,
                                    sb['wgT'][:, 1040 * k + 1024:1040 * k + 1040],
                                    start=(k == 0), stop=False)
                tt.matmul(sb['psA'][:, 0:512],
                          sb['rown'][0:1, 128 * tt8:128 * (tt8 + 1)],
                          sb['wsr'][0:1, 0:512], start=False, stop=True)
                tt.matmul(sb['psB'][:, 0:512],
                          sb['rown'][0:1, 128 * tt8:128 * (tt8 + 1)],
                          sb['wsr'][0:1, 512:1024], start=False, stop=True)
                ins = tt.matmul(sb['psC'][:, 0:16],
                                sb['rown'][0:1, 128 * tt8:128 * (tt8 + 1)],
                                sb['wsr'][0:1, 2304:2320], start=False, stop=True)
                tsem.inc(ins)
                ck[('a2mm', idx)] = tsem.val
            pr.on('tensor', a2mm)

            def a2ev(v, tau=tau, idx=idx):
                v.wait_ge(tsem.h, ck[('a2mm', idx)])
                pv = ck.get(('a2dma', idx - 1))
                if pv is not None:
                    wait_tok(v, pv)
                ac = sb['acol'][:, tau:tau + 1]
                v.tensor_scalar(sb['evb2'][:, 0:512], sb['psA'][:, 0:512], ac, None, ALU.mult)
                v.tensor_scalar(sb['evb2'][:, 512:1024], sb['psB'][:, 0:512], ac, None, ALU.mult)
                v.tensor_scalar(sb['dts'][:, 16 * (tau % 64):16 * (tau % 64) + 16],
                                sb['psC'][:, 0:16], ac, None, ALU.mult)
                ck[('a2evl', idx)] = vsem.val
            pr.on('vector', a2ev)

            def a2dma(g, tau=tau, idx=idx):
                g.wait_ge(vsem.h, ck[('a2evl', idx)])
                g.dma_start(out=gate_s[128 * tau:128 * (tau + 1), :], in_=sb['evb2'][:, :])
                ck[('a2dma', idx)] = pr.sw.token()
            pr.on('gpsimd', a2dma)


def _emit_attn0_A2(nc, pr, sb, ck, *, xbc_s, cvs_s):
    tsem, vsem, ssem, gsem = pr.tsem, pr.vsem, pr.ssem, pr.gsem
    for tau in range(64):
        pi = tau % 2
        X0, C0 = 1320 * pi, 1280 * pi
        bstart = (tau % 32 == 0)

        def cld(q, tau=tau, pi=pi, X0=X0, bstart=bstart):
            pv = ck.get(('cv_tap', tau - 2))
            if pv is not None:
                q.wait_ge(vsem.h, pv)
                q.wait_ge(ssem.h, ck[('cv_t3', tau - 2)])
            for ct in range(10):
                if bstart:
                    q.dma_start(out=sb['xin'][:, X0 + 132 * ct + 3:X0 + 132 * ct + 131],
                                in_=xbc_s[128 * ct:128 * (ct + 1), 128 * tau:128 * (tau + 1)])
                else:
                    q.dma_start(out=sb['xin'][:, X0 + 132 * ct:X0 + 132 * ct + 131],
                                in_=xbc_s[128 * ct:128 * (ct + 1), 128 * tau - 3:128 * (tau + 1)])
            ck[('cv_ld', tau)] = pr.hw.token()
        pr.on('sync', cld)

        if bstart:
            def czero(g, tau=tau, X0=X0):
                pv = ck.get(('cv_tap', tau - 2))
                if pv is not None:
                    g.wait_ge(vsem.h, pv)
                    g.wait_ge(ssem.h, ck[('cv_t3', tau - 2)])
                for ct in range(10):
                    g.memset(sb['xin'][:, X0 + 132 * ct:X0 + 132 * ct + 3], 0.0)
                ck[('cv_z', tau)] = gsem.val
            pr.on('gpsimd', czero)

        def tap3(s, tau=tau, pi=pi, X0=X0, C0=C0, bstart=bstart):
            wait_tok(s, ck[('cv_ld', tau)])
            if bstart:
                s.wait_ge(gsem.h, ck[('cv_z', tau)])
            pv = ck.get(('cv_tap', tau - 2))
            if pv is not None:
                s.wait_ge(vsem.h, pv)
            for ct in range(10):
                s.activation(sb['cva'][:, C0 + 128 * ct:C0 + 128 * (ct + 1)],
                             sb['xin'][:, X0 + 132 * ct + 3:X0 + 132 * ct + 131],
                             AF.Identity, bias=sb['cbc'][:, ct:ct + 1],
                             scale=sb['cwc'][:, 4 * ct + 3:4 * ct + 4])
            ck[('cv_t3', tau)] = ssem.val
        pr.on('scalar', tap3)

        def taps(v, tau=tau, pi=pi, X0=X0, C0=C0):
            v.wait_ge(ssem.h, ck[('cv_t3', tau)])
            pv = ck.get(('cv_silu', tau - 2))
            if pv is not None:
                v.wait_ge(ssem.h, pv)
            for ct in range(10):
                a = sb['cva'][:, C0 + 128 * ct:C0 + 128 * (ct + 1)]
                b = sb['cvb'][:, C0 + 128 * ct:C0 + 128 * (ct + 1)]
                x = sb['xin'][:, X0 + 132 * ct:X0 + 132 * ct + 132]
                v.scalar_tensor_tensor(b, x[:, 2:130], sb['cwc'][:, 4 * ct + 2:4 * ct + 3],
                                       a, ALU.mult, ALU.add)
                v.scalar_tensor_tensor(a, x[:, 1:129], sb['cwc'][:, 4 * ct + 1:4 * ct + 2],
                                       b, ALU.mult, ALU.add)
                v.scalar_tensor_tensor(b, x[:, 0:128], sb['cwc'][:, 4 * ct:4 * ct + 1],
                                       a, ALU.mult, ALU.add)
            ck[('cv_tap', tau)] = vsem.val
        pr.on('vector', taps)

        def silu(s, tau=tau, pi=pi, C0=C0):
            s.wait_ge(vsem.h, ck[('cv_tap', tau)])
            pv = ck.get(('cv_dma', tau - 2))
            if pv is not None:
                wait_tok(s, pv)
            s.activation(sb['svo'][:, C0:C0 + 1280], sb['cvb'][:, C0:C0 + 1280], AF.Silu)
            ck[('cv_silu', tau)] = ssem.val
        pr.on('scalar', silu)

        def cstore(g, tau=tau, pi=pi, C0=C0):
            g.wait_ge(ssem.h, ck[('cv_silu', tau)])
            g.dma_start(
                out=cvs_s[:, 128 * tau:128 * (tau + 1)].rearrange("(k p) n -> p k n", p=128),
                in_=sb['svo'][:, C0:C0 + 1280].rearrange("p (k n) -> p k n", k=10))
            ck[('cv_dma', tau)] = pr.sw.token()
        pr.on('gpsimd', cstore)


def _emit_attn0_B(nc, pr, sb, ck, *, cvs_s, ys_s):
    tsem, vsem, ssem, gsem = pr.tsem, pr.vsem, pr.ssem, pr.gsem
    pQ = ['pQ0', 'pQ1', 'pQ2']

    def init(g):
        g.memset(sb['pfa'][:, :], 0.0)
        ck['b_pfz'] = gsem.val
    pr.on('gpsimd', init)

    for tau in range(64):
        pi = tau % 2
        H0, HC0 = 96 * pi, 16 * pi
        bstart = (tau % 32 == 0)
        pcur = 'pfa' if tau % 2 == 0 else 'pfb'
        pnew = 'pfb' if tau % 2 == 0 else 'pfa'

        def bld(q, tau=tau, pi=pi):
            pv = ck.get(('b_ymm', tau - 2, 15))
            if pv is not None:
                q.wait_ge(tsem.h, pv)
                q.wait_ge(vsem.h, ck[('b_evu', tau - 2, 15)])
            q.dma_start(out=sb['bcb16'][:, 256 * pi:256 * pi + 128],
                        in_=cvs_s[1024:1152, 128 * tau:128 * (tau + 1)])
            q.dma_start(out=sb['bcb16'][:, 256 * pi + 128:256 * pi + 256],
                        in_=cvs_s[1152:1280, 128 * tau:128 * (tau + 1)])
            q.dma_start_transpose(sb['btm'][:, 128 * pi:128 * (pi + 1)],
                                  cvs_s[1024:1152, 128 * tau:128 * (tau + 1)])
            for f in range(8):
                q.dma_start_transpose(
                    sb['xctm'][:, 1024 * pi + 128 * f:1024 * pi + 128 * (f + 1)],
                    cvs_s[128 * f:128 * (f + 1), 128 * tau:128 * (tau + 1)])
            ck[('b_ld', tau)] = pr.hw.token()
        pr.on('sync', bld)

        def dtm_v(v, tau=tau, pi=pi, HC0=HC0):
            pv = ck.get(('b_evu', tau - 2, 15))
            if pv is not None:
                v.wait_ge(vsem.h, pv)
            sl = sb['dts'][:, 16 * tau:16 * tau + 16]
            v.tensor_tensor(sb['hctp'][:, HC0:HC0 + 16], sl, sb['hrep'][:, 16:32], ALU.add)
            v.tensor_scalar(sb['hctq'][:, HC0:HC0 + 16], sb['hctp'][:, HC0:HC0 + 16],
                            -1.0, None, ALU.mult)
            v.tensor_tensor(sb['hctq'][:, HC0:HC0 + 16], sb['hctp'][:, HC0:HC0 + 16],
                            sb['hctq'][:, HC0:HC0 + 16], ALU.max)
            ck[('b_dt1', tau)] = vsem.val
        pr.on('vector', dtm_v)

        def dtm_s(s, tau=tau, pi=pi, H0=H0, HC0=HC0):
            s.wait_ge(vsem.h, ck[('b_dt1', tau)])
            s.activation(sb['hcols'][:, H0 + 80:H0 + 96], sb['hctq'][:, HC0:HC0 + 16],
                         AF.Exp, scale=-1.0)
            s.activation(sb['hcols'][:, H0 + 80:H0 + 96], sb['hcols'][:, H0 + 80:H0 + 96],
                         AF.Ln, bias=1.0)
            ck[('b_dt2', tau)] = ssem.val
        pr.on('scalar', dtm_s)

        def dtm_v2(v, tau=tau, pi=pi, H0=H0, HC0=HC0):
            v.wait_ge(ssem.h, ck[('b_dt2', tau)])
            v.tensor_scalar(sb['hctq'][:, HC0:HC0 + 16], sb['hctp'][:, HC0:HC0 + 16],
                            0.0, None, ALU.max)
            v.tensor_tensor(sb['hctp'][:, HC0:HC0 + 16], sb['hctq'][:, HC0:HC0 + 16],
                            sb['hcols'][:, H0 + 80:H0 + 96], ALU.add)
            v.tensor_scalar(sb['hctp'][:, HC0:HC0 + 16], sb['hctp'][:, HC0:HC0 + 16],
                            1e-30, None, ALU.max)
            v.tensor_tensor(sb['hctq'][:, HC0:HC0 + 16], sb['hctp'][:, HC0:HC0 + 16],
                            sb['hrep'][:, 0:16], ALU.mult)
            ck[('b_dt3', tau)] = vsem.val
        pr.on('vector', dtm_v2)

        def cum_t(tt, tau=tau, pi=pi, HC0=HC0):
            tt.wait_ge(vsem.h, ck[('b_dt3', tau)])
            pv = ck.get(('b_cdec', tau - 1))
            if pv is not None:
                tt.wait_ge(ssem.h, pv)
                tt.wait_ge(vsem.h, ck[('b_w1', tau - 1)])
            tt.matmul(sb['pF'][:, 0:16], sb['triu'][:, :], sb['hctq'][:, HC0:HC0 + 16],
                      start=True, stop=True)
            ins = tt.matmul(sb['pQ2'][0:1, 16:32], sb['ones_f'][:, 0:1],
                            sb['hctq'][:, HC0:HC0 + 16], start=True, stop=True)
            tsem.inc(ins)
            ck[('b_cumt', tau)] = tsem.val
        pr.on('tensor', cum_t)

        def cum_v(v, tau=tau, pi=pi, H0=H0):
            v.wait_ge(tsem.h, ck[('b_cumt', tau)])
            v.tensor_copy(sb['hcols'][:, H0:H0 + 16], sb['pF'][:, 0:16])
            v.tensor_copy(sb['csrow'][0:1, 16 * pi:16 * pi + 16], sb['pQ2'][0:1, 16:32])
            ck[('b_cum', tau)] = vsem.val
        pr.on('vector', cum_v)

        def cl_t(tt, tau=tau, pi=pi):
            tt.wait_ge(vsem.h, ck[('b_cum', tau)])
            ins = tt.matmul(sb['pF'][:, 32:48], sb['ones_f'][0:1, 0:128],
                            sb['csrow'][0:1, 16 * pi:16 * pi + 16], start=True, stop=True)
            tsem.inc(ins)
            ck[('b_clt', tau)] = tsem.val
        pr.on('tensor', cl_t)

        def cb_t(tt, tau=tau, pi=pi):
            wait_tok(tt, ck[('b_ld', tau)])
            pv = ck.get(('b_sct_done', tau - 1, 15))
            if pv is not None:
                tt.wait_ge(vsem.h, pv)
            ins = tt.matmul(sb['pCB'][:, 0:128], sb['bcb16'][:, 256 * pi:256 * pi + 128],
                            sb['bcb16'][:, 256 * pi + 128:256 * pi + 256],
                            start=True, stop=True)
            tsem.inc(ins)
            ck[('b_cb', tau)] = tsem.val
        pr.on('tensor', cb_t)

        def post_s(s, tau=tau, pi=pi, H0=H0, HC0=HC0):
            s.wait_ge(vsem.h, ck[('b_cum', tau)])
            s.activation(sb['hcols'][:, H0 + 16:H0 + 32], sb['hcols'][:, H0:H0 + 16], AF.Exp)
            s.activation(sb['hcols'][:, H0 + 80:H0 + 96], sb['hctp'][:, HC0:HC0 + 16], AF.Ln)
            s.wait_ge(tsem.h, ck[('b_clt', tau)])
            s.activation(sb['hcols'][:, H0 + 64:H0 + 80], sb['pF'][:, 32:48], AF.Exp)
            ck[('b_cdec', tau)] = ssem.val
        pr.on('scalar', post_s)

        def post_v(v, tau=tau, pi=pi, H0=H0, HC0=HC0):
            v.wait_ge(ssem.h, ck[('b_cdec', tau)])
            v.tensor_scalar(sb['hcols'][:, H0 + 80:H0 + 96], sb['hcols'][:, H0 + 80:H0 + 96],
                            -60.0, None, ALU.max)
            v.tensor_tensor(sb['hcols'][:, H0 + 32:H0 + 48], sb['hcols'][:, H0:H0 + 16],
                            sb['hcols'][:, H0 + 80:H0 + 96], ALU.subtract)
            v.scalar_tensor_tensor(sb['hctq'][:, HC0:HC0 + 16], sb['hcols'][:, H0:H0 + 16],
                                   -1.0, sb['pF'][:, 32:48], ALU.mult, ALU.add)
            ck[('b_w1', tau)] = vsem.val
        pr.on('vector', post_v)

        def post_s2(s, tau=tau, pi=pi, HC0=HC0):
            s.wait_ge(vsem.h, ck[('b_w1', tau)])
            s.activation(sb['hctq'][:, HC0:HC0 + 16], sb['hctq'][:, HC0:HC0 + 16], AF.Exp)
            ck[('b_w2', tau)] = ssem.val
        pr.on('scalar', post_s2)

        def post_v2(v, tau=tau, pi=pi, H0=H0, HC0=HC0, pcur=pcur, bstart=bstart):
            v.wait_ge(ssem.h, ck[('b_w2', tau)])
            v.tensor_tensor(sb['hcols'][:, H0 + 48:H0 + 64], sb['hctq'][:, HC0:HC0 + 16],
                            sb['hctp'][:, HC0:HC0 + 16], ALU.mult)
            pv = ck.get(('b_ymm', tau - 1, 15))
            if pv is not None:
                v.wait_ge(tsem.h, pv)
            if tau == 0:
                v.wait_ge(gsem.h, ck['b_pfz'])
            if bstart and tau > 0:
                v.memset(sb[pcur][:, :], 0.0)
            v.tensor_copy(sb['pbf'][:, :], sb[pcur][:, :])
            ck[('b_wdone', tau)] = vsem.val
        pr.on('vector', post_v2)

        def em_cumtp(g4, tau=tau, pi=pi, H0=H0):
            for i in range(4):
                h = 4 * g4 + i

                def fct(tt, h=h, g4=g4):
                    if h == 0:
                        tt.wait_ge(vsem.h, ck[('b_cum', tau)])
                    pv = ck.get(('b_tpu', tau, h - 3), ck.get(('b_tpu', tau - 1, h + 13)))
                    if pv is not None:
                        tt.wait_ge(vsem.h, pv)
                    ins = tt.matmul(sb[pQ[h % 3]][:, 0:128],
                                    sb['hcols'][:, H0 + h:H0 + h + 1].to_broadcast([128, 128]),
                                    sb['idf'][:, :], start=True, stop=True)
                    tsem.inc(ins)
                    ck[('b_cumi', tau, h)] = tsem.val
                pr.on('tensor', fct)

                def ftp(v, h=h, i=i, g4=g4):
                    if i == 0:
                        pv = ck.get(('b_texp', tau, g4 - 2), ck.get(('b_texp', tau - 1, g4 + 2)))
                        if pv is not None:
                            v.wait_ge(ssem.h, pv)
                    v.wait_ge(tsem.h, ck[('b_cumi', tau, h)])
                    v.scalar_tensor_tensor(sb['tp4'][:, 512 * (g4 % 2) + 128 * i:
                                                     512 * (g4 % 2) + 128 * (i + 1)],
                                           sb[pQ[h % 3]][:, 0:128],
                                           sb['hcols'][:, H0 + 32 + h:H0 + 33 + h],
                                           sb['mneg'][:, :], ALU.subtract, ALU.add)
                    ck[('b_tpu', tau, h)] = vsem.val
                pr.on('vector', ftp)

        def em_texp(g4, tau=tau):
            def f(s):
                s.wait_ge(vsem.h, ck[('b_tpu', tau, 4 * g4 + 3)])
                pv = ck.get(('b_sct_done', tau, 4 * g4 - 5),
                            ck.get(('b_sct_done', tau - 1, 4 * g4 + 11)))
                if pv is not None:
                    s.wait_ge(vsem.h, pv)
                s.activation(sb['texp4'][:, 512 * (g4 % 2):512 * (g4 % 2 + 1)],
                             sb['tp4'][:, 512 * (g4 % 2):512 * (g4 % 2 + 1)], AF.Exp)
                ck[('b_texp', tau, g4)] = ssem.val
            pr.on('scalar', f)

        def em_scty(g4, tau=tau, pi=pi, H0=H0, pcur=pcur, pnew=pnew):
            for i in range(4):
                h = 4 * g4 + i
                hb = h % 2

                def fsct(v, h=h, i=i, g4=g4):
                    if i == 0:
                        v.wait_ge(ssem.h, ck[('b_texp', tau, g4)])
                        v.wait_ge(tsem.h, ck[('b_cb', tau)])
                        if g4 == 0:
                            wait_tok(v, ck[('b_ld', tau)])
                    pv = ck.get(('b_ymm', tau, h - 2), ck.get(('b_ymm', tau - 1, h + 14)))
                    if pv is not None:
                        v.wait_ge(tsem.h, pv)
                    v.tensor_tensor(sb['sct'][:, 128 * (h % 2):128 * (h % 2 + 1)],
                                    sb['texp4'][:, 512 * (g4 % 2) + 128 * i:
                                                512 * (g4 % 2) + 128 * (i + 1)],
                                    sb['pCB'][:, 0:128], ALU.mult)
                    v.tensor_scalar(sb['xcw'][:, 64 * (h % 2):64 * (h % 2 + 1)],
                                    sb['xctm'][:, 1024 * pi + 64 * h:1024 * pi + 64 * (h + 1)],
                                    sb['hcols'][:, H0 + 48 + h:H0 + 49 + h], None, ALU.mult)
                    ck[('b_sct_done', tau, h)] = vsem.val
                pr.on('vector', fsct)

                def fy(tt, h=h, hb=hb):
                    tt.wait_ge(vsem.h, ck[('b_sct_done', tau, h)])
                    if h == 0:
                        tt.wait_ge(vsem.h, ck[('b_wdone', tau)])
                    if hb == 0:
                        pv = ck.get(('b_evu', tau, h - 1), ck.get(('b_evu', tau - 1, 15)))
                        if pv is not None:
                            tt.wait_ge(vsem.h, pv)
                    tt.matmul(sb['pC'][:, 64 * hb:64 * (hb + 1)],
                              sb['sct'][:, 128 * (h % 2):128 * (h % 2 + 1)],
                              sb['xctm'][:, 1024 * pi + 64 * h:1024 * pi + 64 * (h + 1)],
                              start=(hb == 0), stop=(hb == 1))
                    tt.matmul(sb['pD'][:, 64 * hb:64 * (hb + 1)],
                              sb['bcb16'][:, 256 * pi + 128:256 * pi + 256],
                              sb['pbf'][:, 64 * h:64 * (h + 1)],
                              start=(hb == 0), stop=(hb == 1))
                    ins = tt.matmul(sb['pE'][:, 64 * hb:64 * (hb + 1)],
                                    sb['btm'][:, 128 * pi:128 * (pi + 1)],
                                    sb['xcw'][:, 64 * (h % 2):64 * (h % 2 + 1)],
                                    start=(hb == 0), stop=(hb == 1))
                    tsem.inc(ins)
                    ck[('b_ymm', tau, h)] = tsem.val
                pr.on('tensor', fy)

                if hb == 1:
                    def fyev(v, h=h):
                        for h2 in (h - 1, h):
                            hb2 = h2 % 2
                            v.wait_ge(tsem.h, ck[('b_ymm', tau, h2 | 1)])
                            if h2 == 0:
                                pv = ck.get(('b_ydma', tau - 2))
                                if pv is not None:
                                    wait_tok(v, pv)
                            pp = (h2 // 2) % 2
                            v.scalar_tensor_tensor(sb['y1t'][:, 64 * pp:64 * (pp + 1)],
                                                   sb['xctm'][:, 1024 * pi + 64 * h2:
                                                               1024 * pi + 64 * (h2 + 1)],
                                                   sb['hrep'][:, 32 + h2:33 + h2],
                                                   sb['pC'][:, 64 * hb2:64 * (hb2 + 1)],
                                                   ALU.mult, ALU.add)
                            v.scalar_tensor_tensor(sb['ybf'][:, 1024 * pi + 64 * h2:
                                                             1024 * pi + 64 * (h2 + 1)],
                                                   sb['pD'][:, 64 * hb2:64 * (hb2 + 1)],
                                                   sb['hcols'][:, H0 + 16 + h2:H0 + 17 + h2],
                                                   sb['y1t'][:, 64 * pp:64 * (pp + 1)],
                                                   ALU.mult, ALU.add)
                            v.scalar_tensor_tensor(sb[pnew][:, 64 * h2:64 * (h2 + 1)],
                                                   sb[pcur][:, 64 * h2:64 * (h2 + 1)],
                                                   sb['hcols'][:, H0 + 64 + h2:H0 + 65 + h2],
                                                   sb['pE'][:, 64 * hb2:64 * (hb2 + 1)],
                                                   ALU.mult, ALU.add)
                            ck[('b_evu', tau, h2)] = vsem.val
                            ck[('b_stu', tau, h2)] = vsem.val
                    pr.on('vector', fyev)

        em_cumtp(0)
        em_texp(0)
        for g4 in range(1, 4):
            em_cumtp(g4)
            em_scty(g4 - 1)
            em_texp(g4)
        em_scty(3)

        def ydma(g, tau=tau, pi=pi):
            g.wait_ge(vsem.h, ck[('b_stu', tau, 15)])
            g.dma_start(out=ys_s[128 * tau:128 * (tau + 1), :],
                        in_=sb['ybf'][:, 1024 * pi:1024 * (pi + 1)])
            ck[('b_ydma', tau)] = pr.sw.token()
        pr.on('gpsimd', ydma)


def _emit_attn0_C(nc, pr, sb, ck, *, gate_s, ys_s, o_ghat):
    tsem, vsem, ssem, gsem = pr.tsem, pr.vsem, pr.ssem, pr.gsem
    for b in range(16):
        def gload(q, b=b):
            pv = ck.get(('g_gg', b - 1))
            if pv is not None:
                q.wait_ge(vsem.h, pv)
            for j in range(4):
                tau = 4 * b + j
                q.dma_start(out=sb['gld'][:, 1024 * j:1024 * (j + 1)],
                            in_=gate_s[128 * tau:128 * (tau + 1), :])
                q.dma_start(out=sb['yld'][:, 1024 * j:1024 * (j + 1)],
                            in_=ys_s[128 * tau:128 * (tau + 1), :])
            ck[('g_ld', b)] = pr.hw.token()
        pr.on('sync', gload)

        def gsilu(s, b=b):
            wait_tok(s, ck[('g_ld', b)])
            pv = ck.get(('g_sc', b - 1))
            if pv is not None:
                s.wait_ge(vsem.h, pv)
            for j in range(4):
                s.activation(sb['sg4'][:, 1024 * j:1024 * (j + 1)],
                             sb['gld'][:, 1024 * j:1024 * (j + 1)], AF.Silu)
            ck[('g_silu', b)] = ssem.val
        pr.on('scalar', gsilu)

        def ggate(v, b=b):
            v.wait_ge(ssem.h, ck[('g_silu', b)])
            pv = ck.get(('g_sc', b - 1))
            if pv is not None:
                v.wait_ge(vsem.h, pv)
            for j in range(4):
                v.tensor_tensor(sb['gg4'][:, 1024 * j:1024 * (j + 1)],
                                sb['yld'][:, 1024 * j:1024 * (j + 1)],
                                sb['sg4'][:, 1024 * j:1024 * (j + 1)], ALU.mult)
            ck[('g_gg', b)] = vsem.val
        pr.on('vector', ggate)

        def gsq(s, b=b):
            s.wait_ge(vsem.h, ck[('g_gg', b)])
            for j in range(4):
                s.activation(sb['sg4'][:, 1024 * j:1024 * (j + 1)],
                             sb['gg4'][:, 1024 * j:1024 * (j + 1)], AF.Square,
                             accum_out=sb['stat'][:, j:j + 1])
            ck[('g_sq', b)] = ssem.val
        pr.on('scalar', gsq)

        def gms(v, b=b):
            v.wait_ge(ssem.h, ck[('g_sq', b)])
            for j in range(4):
                v.tensor_scalar(sb['stat'][:, 4 + j:5 + j], sb['stat'][:, j:j + 1],
                                1.0 / IPC, EPS, ALU.mult, ALU.add)
                v.reciprocal(sb['stat'][:, 8 + j:9 + j], sb['stat'][:, 4 + j:5 + j])
            ck[('g_ms', b)] = vsem.val
        pr.on('vector', gms)

        def grs(s, b=b):
            s.wait_ge(vsem.h, ck[('g_ms', b)])
            for j in range(4):
                s.activation(sb['stat'][:, 12 + j:13 + j], sb['stat'][:, 8 + j:9 + j], AF.Sqrt)
            ck[('g_rs', b)] = ssem.val
        pr.on('scalar', grs)

        for j in range(4):
            tau = 4 * b + j
            pi = tau % 2

            def gscale(v, b=b, j=j, tau=tau, pi=pi):
                v.wait_ge(ssem.h, ck[('g_rs', b)])
                pv = ck.get(('g_tr', tau - 2, 7))
                if pv is not None:
                    v.wait_ge(tsem.h, pv)
                v.tensor_scalar(sb['sg4'][:, 1024 * j:1024 * (j + 1)],
                                sb['gg4'][:, 1024 * j:1024 * (j + 1)],
                                sb['stat'][:, 12 + j:13 + j], None, ALU.mult)
                v.tensor_tensor(sb['evc'][:, 1024 * pi:1024 * (pi + 1)],
                                sb['sg4'][:, 1024 * j:1024 * (j + 1)],
                                sb['nwrep'][:, :], ALU.mult)
                ck[('g_sc2', tau)] = vsem.val
                if j == 3:
                    ck[('g_sc', b)] = vsem.val
            pr.on('vector', gscale)

            for f in range(8):
                def gtr(tt, tau=tau, pi=pi, f=f):
                    tt.wait_ge(vsem.h, ck[('g_sc2', tau)])
                    pv = ck.get(('g_te', tau, f - 4), ck.get(('g_te', tau - 1, f + 4)))
                    if pv is not None:
                        tt.wait_ge(vsem.h, pv)
                    ins = tt.transpose(sb['psT%d' % (f % 4)][:, 0:128],
                                       sb['evc'][:, 1024 * pi + 128 * f:
                                                 1024 * pi + 128 * (f + 1)],
                                       sb['idbf'][:, :])
                    tsem.inc(ins)
                    ck[('g_tr', tau, f)] = tsem.val
                pr.on('tensor', gtr)

                def gte(v, tau=tau, f=f):
                    v.wait_ge(tsem.h, ck[('g_tr', tau, f)])
                    pv = ck.get(('g_td', tau, f - 4), ck.get(('g_td', tau - 1, f + 4)))
                    if pv is not None:
                        wait_tok(v, pv)
                    v.tensor_copy(sb['tev'][:, 128 * (f % 4):128 * (f % 4 + 1)],
                                  sb['psT%d' % (f % 4)][:, 0:128])
                    ck[('g_te', tau, f)] = vsem.val
                pr.on('vector', gte)

                def gdma(g, tau=tau, f=f):
                    g.wait_ge(vsem.h, ck[('g_te', tau, f)])
                    g.dma_start(out=o_ghat[128 * f:128 * (f + 1), 128 * tau:128 * (tau + 1)],
                                in_=sb['tev'][:, 128 * (f % 4):128 * (f % 4 + 1)])
                    ck[('g_td', tau, f)] = pr.sw.token()
                pr.on('gpsimd', gdma)


def build_attn0():
    nc = bass.Bass("TRN2", target_bir_lowering=False, debug=False)
    Iap = lambda name, shape, dt=F32: nc.dram_tensor(name, shape, dt, kind="ExternalInput").ap()

    xbt = Iap("xbt", [TOK, HID], BF16)
    lnrow = Iap("lnrow", [2, TOK])
    lncola = Iap("lncola", [128, 64])
    wib = Iap("wib", [RPC, HID], BF16)
    wsum = Iap("wsum", [1, RPC])
    nwc = Iap("nwc", [1, IPC])
    cw = Iap("cw", [CCPC, 4])
    cb = Iap("cb", [1, CCPC])
    dtb = Iap("dtb", [1, HPC])
    alog = Iap("alog", [1, HPC])
    dd = Iap("dd", [1, HPC])
    o_ghat = nc.dram_tensor("o_ghat", [IPC, TOK], BF16, kind="ExternalOutput").ap()
    xbc_s = nc.dram_tensor("xbc_s", [CCPC, TOK], BF16, kind="Internal").ap()
    gate_s = nc.dram_tensor("gate_s", [TOK, IPC], BF16, kind="Internal").ap()
    cvs_s = nc.dram_tensor("cvs_s", [CCPC, TOK], BF16, kind="Internal").ap()
    ys_s = nc.dram_tensor("ys_s", [TOK, IPC], BF16, kind="Internal").ap()

    tensors = CONSTS + ATTN0_TENSORS

    with nc.Block() as block:
        hw_sems = [nc.semaphore(f"hw{i}").__enter__() for i in range(8)]
        sw_sems = [nc.semaphore(f"sw{i}").__enter__() for i in range(8)]
        th = nc.semaphore("tsem").__enter__()
        vh = nc.semaphore("vsem").__enter__()
        sh = nc.semaphore("ssem").__enter__()
        gh = nc.semaphore("gsem").__enter__()
        sb = alloc(nc, tensors)
        pr = Prog(nc, block, hw_sems, sw_sems, th, vh, sh, gh)
        tsem, vsem, ssem, gsem = pr.tsem, pr.vsem, pr.ssem, pr.gsem
        emit_consts(pr, sb)
        ck = {}
        emit_attn0_body(nc, pr, sb, ck, xbt=xbt, lnrow=lnrow, lncola=lncola, wib=wib, wsum=wsum, nwc=nwc, cw=cw, cb=cb, dtb=dtb, alog=alog, dd=dd, o_ghat=o_ghat, xbc_s=xbc_s, gate_s=gate_s, cvs_s=cvs_s, ys_s=ys_s)
        def fin(g):
            wait_tok(g, pr.dtok())
            g.wait_ge(vsem.h, vsem.val)
            g.wait_ge(ssem.h, ssem.val)
            g.wait_ge(tsem.h, tsem.val)
        pr.on('gpsimd', fin)

    return nc


def attn0_in_maps(inp, prep):
    xbt = np.concatenate([np.asarray(prep[c]['o_xbt']) for c in range(NCORE)], axis=0)
    lnrow = np.concatenate([np.asarray(prep[c]['o_lnrow']) for c in range(NCORE)], axis=1)
    lncola = np.concatenate([np.asarray(prep[c]['o_lncol'][:, 0:8]) for c in range(NCORE)], axis=1)
    cw2 = inp['L0_attn_conv_w'][:, 0, :]
    cbv = inp['L0_attn_conv_b']
    maps = []
    for c in range(NCORE):
        chsl = np.r_[IPC * c:IPC * (c + 1),
                     INTER + 128 * c:INTER + 128 * (c + 1),
                     INTER + G * NS + 128 * c:INTER + G * NS + 128 * (c + 1)]
        m = dict(
            xbt=xbt, lnrow=lnrow, lncola=lncola,
            wib=np.asarray(prep[c]['o_wib']),
            wsum=np.asarray(prep[c]['o_wsum']),
            nwc=np.ascontiguousarray(inp['L0_attn_norm_w'][IPC * c:IPC * (c + 1)].reshape(1, IPC)),
            cw=np.ascontiguousarray(cw2[chsl]),
            cb=np.ascontiguousarray(cbv[chsl].reshape(1, CCPC)),
            dtb=np.ascontiguousarray(inp['L0_attn_dt_bias'][HPC * c:HPC * (c + 1)].reshape(1, HPC)),
            alog=np.ascontiguousarray(inp['L0_attn_A_log'][HPC * c:HPC * (c + 1)].reshape(1, HPC)),
            dd=np.ascontiguousarray(inp['L0_attn_D'][HPC * c:HPC * (c + 1)].reshape(1, HPC)),
        )
        maps.append(m)
    return maps


OUTPROJ_TENSORS = [
    ('gres', [128, INTER * 512 // 128], BF16, 'sbuf'),
    ('wblk', [128, 16384], BF16, 'sbuf'),
    ('ev', [128, 512], F32, 'sbuf'),
    ('sq', [128, 512], F32, 'sbuf'),
    ('arep', [128, 1024], F32, 'sbuf'),
    ('brep', [128, 1024], F32, 'sbuf'),
    ('xfm', [128, 1024], F32, 'sbuf'),
    ('aof', [128, 1024], F32, 'sbuf'),
    ('r1f', [128, 1024], F32, 'sbuf'),
    ('x0t', [128, 128], F32, 'sbuf'),
    ('srows', [1, 6 * TPC], F32, 'sbuf'),
    ('psA', [128, 512], F32, 'psum'),
    ('psB', [128, 512], F32, 'psum'),
    ('psC', [128, 512], F32, 'psum'),
    ('psD', [128, 512], F32, 'psum'),
    ('psS0', [1, 512], F32, 'psum'),
    ('psS1', [1, 512], F32, 'psum'),
    ('psS2', [1, 512], F32, 'psum'),
    ('psS3', [1, 512], F32, 'psum'),
]

def emit_outproj_body(nc, pr, sb, ck, *, ghat, wob, x0, o_r1, ao_s, cid=None):
    tsem, vsem, ssem, gsem = pr.tsem, pr.vsem, pr.ssem, pr.gsem

    banks = ['psA', 'psB', 'psC', 'psD']
    for th in range(2):
        def gload(q, th=th):
            if th == 1:
                q.wait_ge(tsem.h, ck[('mml', 0, 31)])
            if cid is None:
                for k in range(64):
                    q.dma_start(out=sb['gres'][:, 512 * k:512 * (k + 1)],
                                in_=ghat[128 * k:128 * (k + 1), 512 * th:512 * (th + 1)])
            else:
                c2 = q.snap(q.partition_id() * 2 + th, min_val=0, max_val=2 * NCORE - 1)
                q.dma_start(
                    out=sb['gres'][:, :].rearrange("p (k n) -> p k n", k=64),
                    in_=ghat[:, bass.ts(c2, 512)].rearrange("(k p) n -> p k n", p=128))
            ck[('g', th)] = pr.hw.token()
        pr.on('sync', gload)

        for m1 in range(32):
            idx = th * 32 + m1

            def wld(q, th=th, m1=m1, idx=idx):
                pv = ck.get(('mml', (idx - 2) // 32, (idx - 2) % 32))
                if pv is not None:
                    q.wait_ge(tsem.h, pv)
                q.dma_start(
                    out=sb['wblk'][:, 8192 * (idx % 2):8192 * (idx % 2 + 1)]
                    .rearrange("p (k n) -> p k n", k=64),
                    in_=wob[0:INTER, 128 * m1:128 * (m1 + 1)]
                    .rearrange("(k p) n -> p k n", p=128))
                ck[('w', idx)] = pr.hw.token()
            pr.on('sync', wld)

            def mm(tt, th=th, m1=m1, idx=idx):
                wait_tok(tt, ck[('w', idx)])
                if m1 == 0:
                    wait_tok(tt, ck[('g', th)])
                pv = ck.get(('ev', idx - 2))
                if pv is not None:
                    tt.wait_ge(vsem.h, pv)
                W0 = 8192 * (idx % 2)
                for k in range(64):
                    ins = tt.matmul(sb[banks[idx % 4]][:, 0:512],
                                    sb['wblk'][:, W0 + 128 * k:W0 + 128 * (k + 1)],
                                    sb['gres'][:, 512 * k:512 * (k + 1)],
                                    start=(k == 0), stop=(k == 63))
                tsem.inc(ins)
                ck[('mml', th, m1)] = tsem.val
                ck[('mme', idx)] = tsem.val
            pr.on('tensor', mm)

            def ev(v, th=th, m1=m1, idx=idx):
                v.wait_ge(tsem.h, ck[('mme', idx)])
                pv = ck.get(('aodma', idx - 1))
                if pv is not None:
                    wait_tok(v, pv)
                v.tensor_copy(sb['ev'][:, 0:512], sb[banks[idx % 4]][:, 0:512])
                ck[('ev', idx)] = vsem.val
            pr.on('vector', ev)

            def sqp(s, idx=idx):
                s.wait_ge(vsem.h, ck[('ev', idx)])
                if idx >= 1:
                    s.wait_ge(tsem.h, ck[('smm', idx - 1)])
                s.activation(sb['sq'][:, 0:512], sb['ev'][:, 0:512], AF.Square)
                ck[('sq', idx)] = ssem.val
            pr.on('scalar', sqp)

            def smm(tt, th=th, m1=m1, idx=idx):
                tt.wait_ge(ssem.h, ck[('sq', idx)])
                tt.matmul(sb['psS' + str(th)][0:1, 0:512], sb['ones_f'][:, 0:1],
                          sb['ev'][:, 0:512], start=(m1 == 0), stop=(m1 == 31))
                ins = tt.matmul(sb['psS' + str(2 + th)][0:1, 0:512], sb['ones_f'][:, 0:1],
                                sb['sq'][:, 0:512], start=(m1 == 0), stop=(m1 == 31))
                tsem.inc(ins)
                ck[('smm', idx)] = tsem.val
            pr.on('tensor', smm)

            def aodma(g, th=th, m1=m1, idx=idx):
                g.wait_ge(tsem.h, ck[('smm', idx)])
                g.dma_start(out=ao_s[128 * m1:128 * (m1 + 1), 512 * th:512 * (th + 1)],
                            in_=sb['ev'][:, 0:512])
                ck[('aodma', idx)] = pr.sw.token()
            pr.on('gpsimd', aodma)

    def strows(v):
        v.wait_ge(tsem.h, ck[('smm', 63)])
        R = sb['srows']
        for t in range(2):
            v.tensor_copy(R[0:1, 512 * t:512 * (t + 1)], sb['psS' + str(t)][0:1, 0:512])
            v.tensor_copy(R[0:1, TPC + 512 * t:TPC + 512 * (t + 1)],
                          sb['psS' + str(2 + t)][0:1, 0:512])
        v.tensor_scalar(R[0:1, 2 * TPC:3 * TPC], R[0:1, 0:TPC], 1.0 / HID, None, ALU.mult)
        v.tensor_scalar(R[0:1, 3 * TPC:4 * TPC], R[0:1, TPC:2 * TPC], 1.0 / HID, None, ALU.mult)
        v.tensor_tensor(R[0:1, 4 * TPC:5 * TPC], R[0:1, 2 * TPC:3 * TPC],
                        R[0:1, 2 * TPC:3 * TPC], ALU.mult)
        v.tensor_scalar(R[0:1, 4 * TPC:5 * TPC], R[0:1, 4 * TPC:5 * TPC], -1.0, EPS,
                        ALU.mult, ALU.add)
        v.tensor_tensor(R[0:1, 3 * TPC:4 * TPC], R[0:1, 3 * TPC:4 * TPC],
                        R[0:1, 4 * TPC:5 * TPC], ALU.add)
        v.reciprocal(R[0:1, 4 * TPC:5 * TPC], R[0:1, 3 * TPC:4 * TPC])
        ck['st1'] = vsem.val
    pr.on('vector', strows)

    def strows2(s):
        s.wait_ge(vsem.h, ck['st1'])
        R = sb['srows']
        s.activation(R[0:1, 5 * TPC:6 * TPC], R[0:1, 4 * TPC:5 * TPC], AF.Sqrt)
        ck['st2'] = ssem.val
    pr.on('scalar', strows2)

    def strows3(v):
        v.wait_ge(ssem.h, ck['st2'])
        R = sb['srows']
        v.tensor_tensor(R[0:1, 2 * TPC:3 * TPC], R[0:1, 2 * TPC:3 * TPC],
                        R[0:1, 5 * TPC:6 * TPC], ALU.mult)
        v.tensor_scalar(R[0:1, 2 * TPC:3 * TPC], R[0:1, 2 * TPC:3 * TPC], -1.0, None, ALU.mult)
        ck['st3'] = vsem.val
    pr.on('vector', strows3)

    for t in range(2):
        def repmm(tt, t=t, which=0):
            tt.wait_ge(vsem.h, ck['st3'])
            pv = ck.get(('repev', t - 1, 1))
            if pv is not None:
                tt.wait_ge(vsem.h, pv)
            ins = tt.matmul(sb['psA'][:, 0:512], sb['ones_f'][0:1, 0:128],
                            sb['srows'][0:1, 5 * TPC + 512 * t:5 * TPC + 512 * (t + 1)],
                            start=True, stop=True)
            tsem.inc(ins)
            ck[('repmm', t, 0)] = tsem.val
        pr.on('tensor', repmm)

        def repev(v, t=t):
            v.wait_ge(tsem.h, ck[('repmm', t, 0)])
            v.tensor_copy(sb['arep'][:, 512 * t:512 * (t + 1)], sb['psA'][:, 0:512])
            ck[('repev', t, 0)] = vsem.val
        pr.on('vector', repev)

        def repmm2(tt, t=t):
            tt.wait_ge(vsem.h, ck[('repev', t, 0)])
            ins = tt.matmul(sb['psA'][:, 0:512], sb['ones_f'][0:1, 0:128],
                            sb['srows'][0:1, 2 * TPC + 512 * t:2 * TPC + 512 * (t + 1)],
                            start=True, stop=True)
            tsem.inc(ins)
            ck[('repmm', t, 1)] = tsem.val
        pr.on('tensor', repmm2)

        def repev2(v, t=t):
            v.wait_ge(tsem.h, ck[('repmm', t, 1)])
            v.tensor_copy(sb['brep'][:, 512 * t:512 * (t + 1)], sb['psA'][:, 0:512])
            ck[('repev', t, 1)] = vsem.val
        pr.on('vector', repev2)

    for f in range(32):
        for t in range(8):
            def x0ld(q, f=f, t=t):
                i = f * 8 + t
                pv = ck.get(('x0t', f, t - 1)) if t >= 1 else (
                    ck.get(('x0t', f - 1, 7)) if f > 0 else None)
                if pv is not None:
                    q.wait_ge(tsem.h, pv)
                if (f, t) == (0, 0):
                    q.wait_ge(vsem.h, ck[('repev', 1, 1)])
                q.dma_start(out=sb['x0t'][:, :],
                            in_=x0[128 * t:128 * (t + 1), 128 * f:128 * (f + 1)])
                ck[('x0l', f, t)] = pr.hw.token()
            pr.on('sync', x0ld)

            def x0tr(tt, f=f, t=t):
                wait_tok(tt, ck[('x0l', f, t)])
                pv = ck.get(('xfme', f, t - 1)) if t > 0 else (
                    ck.get(('xfme', f - 1, 7)) if f > 0 else None)
                if pv is not None:
                    tt.wait_ge(vsem.h, pv)
                tt.wait_ge(gsem.h, gsem.val)
                ins = tt.transpose(sb['psB'][:, 0:128], sb['x0t'][:, :], sb['idf'][:, :])
                tsem.inc(ins)
                ck[('x0t', f, t)] = tsem.val
            pr.on('tensor', x0tr)

            def xfme(v, f=f, t=t):
                v.wait_ge(tsem.h, ck[('x0t', f, t)])
                v.tensor_copy(sb['xfm'][:, 128 * t:128 * (t + 1)], sb['psB'][:, 0:128])
                ck[('xfme', f, t)] = vsem.val
            pr.on('vector', xfme)

        def aold(q, f=f):
            wait_tok(q, ck[('aodma', 63)])
            pv = ck.get(('r1u', f - 1))
            if pv is not None:
                q.wait_ge(vsem.h, pv)
            q.dma_start(out=sb['aof'][:, :], in_=ao_s[128 * f:128 * (f + 1), :])
            ck[('aol', f)] = pr.hw.token()
        pr.on('sync', aold)

        def applyv(v, f=f):
            wait_tok(v, ck[('aol', f)])
            v.wait_ge(vsem.h, ck[('xfme', f, 7)])
            pv = ck.get(('r1d', f - 1))
            if pv is not None:
                wait_tok(v, pv)
            v.tensor_tensor(sb['r1f'][:, :], sb['aof'][:, :], sb['arep'][:, :], ALU.mult)
            v.tensor_tensor(sb['r1f'][:, :], sb['r1f'][:, :], sb['brep'][:, :], ALU.add)
            v.tensor_tensor(sb['r1f'][:, :], sb['r1f'][:, :], sb['xfm'][:, :], ALU.add)
            ck[('r1u', f)] = vsem.val
        pr.on('vector', applyv)

        def r1dma(g, f=f):
            g.wait_ge(vsem.h, ck[('r1u', f)])
            g.dma_start(out=o_r1[128 * f:128 * (f + 1), :], in_=sb['r1f'][:, :])
            ck[('r1d', f)] = pr.sw.token()
        pr.on('gpsimd', r1dma)



def build_outproj():
    nc = bass.Bass("TRN2", target_bir_lowering=False, debug=False)
    ghat = nc.dram_tensor("ghat", [INTER, TPC], BF16, kind="ExternalInput").ap()
    wob = nc.dram_tensor("wob", [INTER, HID], BF16, kind="ExternalInput").ap()
    x0 = nc.dram_tensor("x0", [TPC, HID], F32, kind="ExternalInput").ap()
    o_r1 = nc.dram_tensor("o_r1", [HID, TPC], F32, kind="ExternalOutput").ap()
    ao_s = nc.dram_tensor("ao_s", [HID, TPC], F32, kind="Internal").ap()

    tensors = CONSTS + OUTPROJ_TENSORS

    with nc.Block() as block:
        hw_sems = [nc.semaphore(f"hw{i}").__enter__() for i in range(8)]
        sw_sems = [nc.semaphore(f"sw{i}").__enter__() for i in range(8)]
        th = nc.semaphore("tsem").__enter__()
        vh = nc.semaphore("vsem").__enter__()
        sh = nc.semaphore("ssem").__enter__()
        gh = nc.semaphore("gsem").__enter__()
        sb = alloc(nc, tensors)
        pr = Prog(nc, block, hw_sems, sw_sems, th, vh, sh, gh)
        tsem, vsem, ssem, gsem = pr.tsem, pr.vsem, pr.ssem, pr.gsem
        emit_consts(pr, sb)
        ck = {}
        emit_outproj_body(nc, pr, sb, ck, ghat=ghat, wob=wob, x0=x0, o_r1=o_r1, ao_s=ao_s)
        def fin(g):
            wait_tok(g, pr.dtok())
            g.wait_ge(vsem.h, vsem.val)
            g.wait_ge(ssem.h, ssem.val)
            g.wait_ge(tsem.h, tsem.val)
        pr.on('gpsimd', fin)

    return nc


def outproj_in_maps(inp, prep, ghats):
    x2 = np.ascontiguousarray(inp['L0_attn_input_states'].reshape(TOK, HID))
    wob = np.concatenate([np.asarray(prep[c]['o_wob']) for c in range(NCORE)], axis=0)
    maps = []
    for c in range(NCORE):
        gin = np.ascontiguousarray(np.concatenate(
            [np.asarray(ghats[j])[:, TPC * c:TPC * (c + 1)] for j in range(NCORE)], axis=0))
        maps.append(dict(ghat=gin, wob=wob, x0=x2[TPC * c:TPC * (c + 1)]))
    return maps


DBLK = 11264


FFN0_TENSORS = [
    ('big', [128, 45056], BF16, 'sbuf'),
    ('wblk', [128, 22528], BF16, 'sbuf'),
    ('ld', [128, 1024], F32, 'sbuf'),
    ('sq', [128, 1024], F32, 'sbuf'),
    ('ev', [128, 1024], F32, 'sbuf'),
    ('ev2', [128, 1024], F32, 'sbuf'),
    ('hbf', [128, 1024], BF16, 'sbuf'),
    ('arep', [128, 1024], F32, 'sbuf'),
    ('brep', [128, 1024], F32, 'sbuf'),
    ('r1t', [128, 1024], F32, 'sbuf'),
    ('r2t', [128, 1024], F32, 'sbuf'),
    ('srows', [1, 6 * TPC], F32, 'sbuf'),
    ('wsblk', [1, 512], F32, 'sbuf'),
    ('psA', [128, 512], F32, 'psum'),
    ('psB', [128, 512], F32, 'psum'),
    ('psC', [128, 512], F32, 'psum'),
    ('psD', [128, 512], F32, 'psum'),
    ('psE', [128, 512], F32, 'psum'),
    ('psF', [128, 512], F32, 'psum'),
    ('psG', [128, 512], F32, 'psum'),
    ('psH', [128, 512], F32, 'psum'),
]

def emit_ffn0_body(nc, pr, sb, ck, *, r1, wgu, wgs, wus, wdp, o_r2, h_s, do_s, o_dbg):
    tsem, vsem, ssem, gsem = pr.tsem, pr.vsem, pr.ssem, pr.gsem



    for f in range(32):
        def ldr(q, f=f):
            if f >= 1:
                q.wait_ge(tsem.h, ck[('stmm', f - 1)])
            q.dma_start(out=sb['ld'][:, :], in_=r1[128 * f:128 * (f + 1), :])
            ck[('r1l', f)] = pr.hw.token()
        pr.on('sync', ldr)

        def sqp(s, f=f):
            wait_tok(s, ck[('r1l', f)])
            if f >= 1:
                s.wait_ge(tsem.h, ck[('stmm', f - 1)])
            s.activation(sb['sq'][:, :], sb['ld'][:, :], AF.Square)
            ck[('r1sq', f)] = ssem.val
        pr.on('scalar', sqp)

        def stmm(tt, f=f):
            tt.wait_ge(ssem.h, ck[('r1sq', f)])
            tt.wait_ge(gsem.h, gsem.val)
            for t in range(2):
                tt.matmul(sb['psE' if t == 0 else 'psF'][0:1, 0:512], sb['ones_f'][:, 0:1],
                          sb['ld'][:, 512 * t:512 * (t + 1)], start=(f == 0), stop=(f == 31))
                ins = tt.matmul(sb['psG' if t == 0 else 'psH'][0:1, 0:512], sb['ones_f'][:, 0:1],
                                sb['sq'][:, 512 * t:512 * (t + 1)], start=(f == 0), stop=(f == 31))
            tsem.inc(ins)
            ck[('stmm', f)] = tsem.val
        pr.on('tensor', stmm)

    def xpl(g):
        for i in range(32):
            j, rb = i // 4, i % 4
            src = bass.AP(r1.tensor, (8 * 128 * rb + j) * TPC, [[8 * TPC, 128], [1, TPC]])
            g.dma_start(out=sb['big'][:, 1024 * i:1024 * (i + 1)], in_=src)
        ck['xp'] = pr.sw.token()
    pr.on('gpsimd', xpl)

    def strows(v):
        v.wait_ge(tsem.h, ck[('stmm', 31)])
        R = sb['srows']
        v.tensor_copy(R[0:1, 0:512], sb['psE'][0:1, 0:512])
        v.tensor_copy(R[0:1, 512:1024], sb['psF'][0:1, 0:512])
        v.tensor_copy(R[0:1, TPC:TPC + 512], sb['psG'][0:1, 0:512])
        v.tensor_copy(R[0:1, TPC + 512:TPC + 1024], sb['psH'][0:1, 0:512])
        v.tensor_scalar(R[0:1, 2 * TPC:3 * TPC], R[0:1, 0:TPC], 1.0 / HID, None, ALU.mult)
        v.tensor_scalar(R[0:1, 3 * TPC:4 * TPC], R[0:1, TPC:2 * TPC], 1.0 / HID, None, ALU.mult)
        v.tensor_tensor(R[0:1, 4 * TPC:5 * TPC], R[0:1, 2 * TPC:3 * TPC],
                        R[0:1, 2 * TPC:3 * TPC], ALU.mult)
        v.tensor_scalar(R[0:1, 4 * TPC:5 * TPC], R[0:1, 4 * TPC:5 * TPC], -1.0, EPS,
                        ALU.mult, ALU.add)
        v.tensor_tensor(R[0:1, 3 * TPC:4 * TPC], R[0:1, 3 * TPC:4 * TPC],
                        R[0:1, 4 * TPC:5 * TPC], ALU.add)
        v.reciprocal(R[0:1, 4 * TPC:5 * TPC], R[0:1, 3 * TPC:4 * TPC])
        ck['st1'] = vsem.val
    pr.on('vector', strows)

    def dbgout(g):
        g.wait_ge(vsem.h, ck['st1'])
        g.dma_start(out=o_dbg, in_=sb['srows'][0:1, 0:2048])
        ck['dbg'] = pr.sw.token()
    pr.on('gpsimd', dbgout)

    def strows2(s):
        s.wait_ge(vsem.h, ck['st1'])
        R = sb['srows']
        s.activation(R[0:1, 5 * TPC:6 * TPC], R[0:1, 4 * TPC:5 * TPC], AF.Sqrt)
        ck['st2'] = ssem.val
    pr.on('scalar', strows2)

    def strows3(v):
        v.wait_ge(ssem.h, ck['st2'])
        R = sb['srows']
        v.tensor_scalar(R[0:1, 2 * TPC:3 * TPC], R[0:1, 2 * TPC:3 * TPC], -1.0, None, ALU.mult)
        ck['st3'] = vsem.val
    pr.on('vector', strows3)

    for t in range(2):
        for which, rowoff, dst in [(0, 5 * TPC, 'arep'), (1, 2 * TPC, 'brep')]:
            def rmm(tt, t=t, rowoff=rowoff, which=which):
                tt.wait_ge(vsem.h, ck['st3'])
                pv = ck.get(('repev', t, which - 1)) or (ck.get(('repev', t - 1, 1)) if t else None)
                if pv is not None:
                    tt.wait_ge(vsem.h, pv)
                ins = tt.matmul(sb['psA'][:, 0:512], sb['ones_f'][0:1, 0:128],
                                sb['srows'][0:1, rowoff + 512 * t:rowoff + 512 * (t + 1)],
                                start=True, stop=True)
                tsem.inc(ins)
                ck[('repmm', t, which)] = tsem.val
            pr.on('tensor', rmm)

            def rev(v, t=t, which=which, dst=dst):
                v.wait_ge(tsem.h, ck[('repmm', t, which)])
                v.tensor_copy(sb[dst][:, 512 * t:512 * (t + 1)], sb['psA'][:, 0:512])
                ck[('repev', t, which)] = vsem.val
            pr.on('vector', rev)

    banks = ['psA', 'psB', 'psC', 'psD', 'psE', 'psF', 'psG', 'psH']
    for mg in range(NBLK_GU):
        def blkld(q, mg=mg):
            if mg == 0:
                q.wait_ge(vsem.h, ck[('repev', 1, 1)])
            else:
                q.wait_ge(tsem.h, ck[('gumm', mg - 1)])
            src = wgu[HID * mg:HID * (mg + 1), :].rearrange("(k p) n -> p k n", p=128)
            q.dma_start(out=sb['wblk'][:, 0:16384].rearrange("p (k n) -> p k n", k=32), in_=src)
            q.dma_start(out=sb['wsblk'][0:1, 0:256], in_=wgs[0:1, 256 * mg:256 * (mg + 1)])
            q.dma_start(out=sb['wsblk'][0:1, 256:512], in_=wus[0:1, 256 * mg:256 * (mg + 1)])
            ck[('blk', mg)] = pr.hw.token()
        pr.on('sync', blkld)

        def gumm(tt, mg=mg):
            wait_tok(tt, ck[('blk', mg)])
            wait_tok(tt, ck['xp'])
            pv = ck.get(('guevl', mg - 1))
            if pv is not None:
                tt.wait_ge(vsem.h, pv)
            for k in range(32):
                for sub in range(4):
                    half, mi = sub // 2, sub % 2
                    lhs = sb['wblk'][:, 512 * k + 256 * half + 128 * mi:
                                     512 * k + 256 * half + 128 * (mi + 1)]
                    for t in range(2):
                        tt.matmul(sb[banks[sub * 2 + t]][:, 0:512], lhs,
                                  sb['big'][:, 1024 * k + 512 * t:1024 * k + 512 * (t + 1)],
                                  start=(k == 0), stop=False)
            for sub in range(4):
                half, mi = sub // 2, sub % 2
                gcol = 256 * half + 128 * mi
                for t in range(2):
                    ins = tt.matmul(sb[banks[sub * 2 + t]][:, 0:512],
                                    sb['wsblk'][0:1, gcol:gcol + 128],
                                    sb['srows'][0:1, 2 * TPC + 512 * t:2 * TPC + 512 * (t + 1)],
                                    start=False, stop=True)
            tsem.inc(ins)
            ck[('gumm', mg)] = tsem.val
        pr.on('tensor', gumm)

        for mi in range(2):
            def guev(v, mg=mg, mi=mi):
                v.wait_ge(tsem.h, ck[('gumm', mg)])
                pv = ck.get(('hdma', mg, mi - 1)) or (ck.get(('hdma', mg - 1, 1)) if mg else None)
                if pv is not None:
                    wait_tok(v, pv)
                for t in range(2):
                    v.tensor_tensor(sb['ev'][:, 512 * t:512 * (t + 1)],
                                    sb[banks[mi * 2 + t]][:, 0:512],
                                    sb['arep'][:, 512 * t:512 * (t + 1)], ALU.mult)
                    v.tensor_tensor(sb['ev2'][:, 512 * t:512 * (t + 1)],
                                    sb[banks[4 + mi * 2 + t]][:, 0:512],
                                    sb['arep'][:, 512 * t:512 * (t + 1)], ALU.mult)
                ck[('guev', mg, mi)] = vsem.val
                if mi == 1:
                    ck[('guevl', mg)] = vsem.val
            pr.on('vector', guev)

            def gusl(s, mg=mg, mi=mi):
                s.wait_ge(vsem.h, ck[('guev', mg, mi)])
                s.activation(sb['sq'][:, :], sb['ev'][:, :], AF.Silu)
                ck[('gusl', mg, mi)] = ssem.val
            pr.on('scalar', gusl)

            def humul(v, mg=mg, mi=mi):
                v.wait_ge(ssem.h, ck[('gusl', mg, mi)])
                v.tensor_tensor(sb['hbf'][:, :], sb['sq'][:, :], sb['ev2'][:, :], ALU.mult)
                ck[('hh', mg, mi)] = vsem.val
            pr.on('vector', humul)

            def hdma(g, mg=mg, mi=mi):
                g.wait_ge(vsem.h, ck[('hh', mg, mi)])
                g.dma_start(out=h_s[256 * mg + 128 * mi:256 * mg + 128 * (mi + 1), :],
                            in_=sb['hbf'][:, :])
                ck[('hdma', mg, mi)] = pr.sw.token()
            pr.on('gpsimd', hdma)

    KT = []
    for j in range(8):
        for rb in range(11):
            KT.append((j, rb, 96 if rb == 10 else 128))
    for thh in range(2):
        def hload(q, thh=thh):
            wait_tok(q, ck[('hdma', NBLK_GU - 1, 1)])
            if thh == 1:
                q.wait_ge(tsem.h, ck[('dmm', 0, 15)])
            for i, (j, rb, rbn) in enumerate(KT):
                src = bass.AP(h_s.tensor, (8 * 128 * rb + j) * TPC + 512 * thh,
                              [[8 * TPC, rbn], [1, 512]])
                q.dma_start(out=sb['big'][0:rbn, 512 * i:512 * (i + 1)], in_=src)
            ck[('hres', thh)] = pr.hw.token()
        pr.on('sync', hload)

        for mgd in range(16):
            def dblkld(q, thh=thh, mgd=mgd):
                pv = ck.get(('dmm', thh, mgd - 1))
                if pv is not None:
                    q.wait_ge(tsem.h, pv)
                elif thh == 1:
                    q.wait_ge(tsem.h, ck[('dmm', 0, 15)])
                src = wdp[DBLK * mgd:DBLK * (mgd + 1), :].rearrange("(k p) n -> p k n", p=128)
                q.dma_start(out=sb['wblk'][:, 0:22528].rearrange("p (k n) -> p k n", k=88),
                            in_=src)
                ck[('dblk', thh, mgd)] = pr.hw.token()
            pr.on('sync', dblkld)

            def dmm(tt, thh=thh, mgd=mgd):
                wait_tok(tt, ck[('dblk', thh, mgd)])
                wait_tok(tt, ck[('hres', thh)])
                pv = ck.get(('devl', thh, mgd - 1)) or (ck.get(('devl', thh - 1, 15)) if thh else None)
                if pv is not None:
                    tt.wait_ge(vsem.h, pv)
                if (thh, mgd) == (0, 0):
                    tt.wait_ge(vsem.h, ck[('guevl', NBLK_GU - 1)])
                for i, (j, rb, rbn) in enumerate(KT):
                    i2 = (1408 * j + 128 * rb) // 128
                    for m in range(2):
                        ins = tt.matmul(
                            sb[banks[m]][:, 0:512],
                            sb['wblk'][0:rbn, 256 * i2 + 128 * m:256 * i2 + 128 * (m + 1)],
                            sb['big'][0:rbn, 512 * i:512 * (i + 1)],
                            start=(i == 0), stop=(i == len(KT) - 1))
                tsem.inc(ins)
                ck[('dmm', thh, mgd)] = tsem.val
            pr.on('tensor', dmm)

            for m in range(2):
                mrow = 256 * mgd + 128 * m

                def dev(v, thh=thh, mgd=mgd, m=m, mrow=mrow):
                    v.wait_ge(tsem.h, ck[('dmm', thh, mgd)])
                    pv = ck.get(('dodma', thh, mgd, m - 1)) or (
                        ck.get(('dodma', thh, mgd - 1, 1)) if mgd else (
                            ck.get(('dodma', thh - 1, 15, 1)) if thh else None))
                    if pv is not None:
                        wait_tok(v, pv)
                    v.tensor_copy(sb['ev'][:, 0:512], sb[banks[m]][:, 0:512])
                    ck[('dev', thh, mgd, m)] = vsem.val
                    if m == 1:
                        ck[('devl', thh, mgd)] = vsem.val
                pr.on('vector', dev)

                def dsq(s, thh=thh, mgd=mgd, m=m):
                    s.wait_ge(vsem.h, ck[('dev', thh, mgd, m)])
                    s.activation(sb['sq'][:, 0:512], sb['ev'][:, 0:512], AF.Square)
                    ck[('dsq', thh, mgd, m)] = ssem.val
                pr.on('scalar', dsq)

                def dsmm(tt, thh=thh, mgd=mgd, m=m):
                    tt.wait_ge(ssem.h, ck[('dsq', thh, mgd, m)])
                    mvisit = 2 * mgd + m
                    tt.matmul(sb['psE'][0:1, 0:512], sb['ones_f'][:, 0:1],
                              sb['ev'][:, 0:512], start=(mvisit == 0), stop=(mvisit == 31))
                    ins = tt.matmul(sb['psF'][0:1, 0:512], sb['ones_f'][:, 0:1],
                                    sb['sq'][:, 0:512], start=(mvisit == 0), stop=(mvisit == 31))
                    tsem.inc(ins)
                    ck[('dsmm', thh, mgd, m)] = tsem.val
                pr.on('tensor', dsmm)

                def dodma(g, thh=thh, mgd=mgd, m=m, mrow=mrow):
                    g.wait_ge(tsem.h, ck[('dsmm', thh, mgd, m)])
                    g.dma_start(out=do_s[mrow:mrow + 128, 512 * thh:512 * (thh + 1)],
                                in_=sb['ev'][:, 0:512])
                    ck[('dodma', thh, mgd, m)] = pr.sw.token()
                pr.on('gpsimd', dodma)

        def dstr(v, thh=thh):
            v.wait_ge(tsem.h, ck[('dsmm', thh, 15, 1)])
            R = sb['srows']
            v.tensor_copy(R[0:1, 512 * thh:512 * (thh + 1)], sb['psE'][0:1, 0:512])
            v.tensor_copy(R[0:1, TPC + 512 * thh:TPC + 512 * (thh + 1)], sb['psF'][0:1, 0:512])
            ck[('dst', thh)] = vsem.val
        pr.on('vector', dstr)

    def fstr(v):
        v.wait_ge(vsem.h, ck[('dst', 1)])
        R = sb['srows']
        v.tensor_scalar(R[0:1, 2 * TPC:3 * TPC], R[0:1, 0:TPC], 1.0 / HID, None, ALU.mult)
        v.tensor_scalar(R[0:1, 3 * TPC:4 * TPC], R[0:1, TPC:2 * TPC], 1.0 / HID, None, ALU.mult)
        v.tensor_tensor(R[0:1, 4 * TPC:5 * TPC], R[0:1, 2 * TPC:3 * TPC],
                        R[0:1, 2 * TPC:3 * TPC], ALU.mult)
        v.tensor_scalar(R[0:1, 4 * TPC:5 * TPC], R[0:1, 4 * TPC:5 * TPC], -1.0, EPS,
                        ALU.mult, ALU.add)
        v.tensor_tensor(R[0:1, 3 * TPC:4 * TPC], R[0:1, 3 * TPC:4 * TPC],
                        R[0:1, 4 * TPC:5 * TPC], ALU.add)
        v.reciprocal(R[0:1, 4 * TPC:5 * TPC], R[0:1, 3 * TPC:4 * TPC])
        ck['fst1'] = vsem.val
    pr.on('vector', fstr)

    def fstr2(s):
        s.wait_ge(vsem.h, ck['fst1'])
        R = sb['srows']
        s.activation(R[0:1, 5 * TPC:6 * TPC], R[0:1, 4 * TPC:5 * TPC], AF.Sqrt)
        ck['fst2'] = ssem.val
    pr.on('scalar', fstr2)

    def fstr3(v):
        v.wait_ge(ssem.h, ck['fst2'])
        R = sb['srows']
        v.tensor_tensor(R[0:1, 2 * TPC:3 * TPC], R[0:1, 2 * TPC:3 * TPC],
                        R[0:1, 5 * TPC:6 * TPC], ALU.mult)
        v.tensor_scalar(R[0:1, 2 * TPC:3 * TPC], R[0:1, 2 * TPC:3 * TPC], -1.0, None, ALU.mult)
        ck['fst3'] = vsem.val
    pr.on('vector', fstr3)

    for t in range(2):
        for which, rowoff, dst in [(0, 5 * TPC, 'arep'), (1, 2 * TPC, 'brep')]:
            def rmm2(tt, t=t, rowoff=rowoff, which=which):
                tt.wait_ge(vsem.h, ck['fst3'])
                pv = ck.get(('frepev', t, which - 1)) or (ck.get(('frepev', t - 1, 1)) if t else None)
                if pv is not None:
                    tt.wait_ge(vsem.h, pv)
                ins = tt.matmul(sb['psA'][:, 0:512], sb['ones_f'][0:1, 0:128],
                                sb['srows'][0:1, rowoff + 512 * t:rowoff + 512 * (t + 1)],
                                start=True, stop=True)
                tsem.inc(ins)
                ck[('frepmm', t, which)] = tsem.val
            pr.on('tensor', rmm2)

            def rev2(v, t=t, which=which, dst=dst):
                v.wait_ge(tsem.h, ck[('frepmm', t, which)])
                v.tensor_copy(sb[dst][:, 512 * t:512 * (t + 1)], sb['psA'][:, 0:512])
                ck[('frepev', t, which)] = vsem.val
            pr.on('vector', rev2)

    for f in range(32):
        def fld(q, f=f):
            if f == 0:
                q.wait_ge(vsem.h, ck[('frepev', 1, 1)])
            pv = ck.get(('r2d', f - 1))
            if pv is not None:
                q.wait_ge(vsem.h, ck[('r2u', f - 1)])
            q.dma_start(out=sb['r1t'][:, :], in_=r1[128 * f:128 * (f + 1), :])
            q.dma_start(out=sb['ld'][:, :], in_=do_s[128 * f:128 * (f + 1), :])
            ck[('fl', f)] = pr.hw.token()
        pr.on('sync', fld)

        def fap(v, f=f):
            wait_tok(v, ck[('fl', f)])
            pv = ck.get(('r2d', f - 1))
            if pv is not None:
                wait_tok(v, pv)
            v.tensor_tensor(sb['r2t'][:, :], sb['ld'][:, :], sb['arep'][:, :], ALU.mult)
            v.tensor_tensor(sb['r2t'][:, :], sb['r2t'][:, :], sb['brep'][:, :], ALU.add)
            v.tensor_tensor(sb['r2t'][:, :], sb['r2t'][:, :], sb['r1t'][:, :], ALU.add)
            ck[('r2u', f)] = vsem.val
        pr.on('vector', fap)

        def fdma(g, f=f):
            g.wait_ge(vsem.h, ck[('r2u', f)])
            g.dma_start(out=o_r2[128 * f:128 * (f + 1), :], in_=sb['r2t'][:, :])
            ck[('r2d', f)] = pr.sw.token()
        pr.on('gpsimd', fdma)



def build_ffn0():
    nc = bass.Bass("TRN2", target_bir_lowering=False, debug=False)
    r1 = nc.dram_tensor("r1", [HID, TPC], F32, kind="ExternalInput").ap()
    wgu = nc.dram_tensor("wgu", [NBLK_GU * HID, 512], BF16, kind="ExternalInput").ap()
    wgs = nc.dram_tensor("wgs", [1, FFN_I], F32, kind="ExternalInput").ap()
    wus = nc.dram_tensor("wus", [1, FFN_I], F32, kind="ExternalInput").ap()
    wdp = nc.dram_tensor("wdp", [16 * DBLK, 256], BF16, kind="ExternalInput").ap()
    o_r2 = nc.dram_tensor("o_r2", [HID, TPC], F32, kind="ExternalOutput").ap()
    o_dbg = nc.dram_tensor("o_dbg", [1, 2048], F32, kind="ExternalOutput").ap()
    h_s = nc.dram_tensor("h_s", [FFN_I, TPC], BF16, kind="Internal").ap()
    do_s = nc.dram_tensor("do_s", [HID, TPC], F32, kind="Internal").ap()

    tensors = CONSTS + FFN0_TENSORS

    with nc.Block() as block:
        hw_sems = [nc.semaphore(f"hw{i}").__enter__() for i in range(8)]
        sw_sems = [nc.semaphore(f"sw{i}").__enter__() for i in range(8)]
        th = nc.semaphore("tsem").__enter__()
        vh = nc.semaphore("vsem").__enter__()
        sh = nc.semaphore("ssem").__enter__()
        gh = nc.semaphore("gsem").__enter__()
        sb = alloc(nc, tensors)
        pr = Prog(nc, block, hw_sems, sw_sems, th, vh, sh, gh)
        tsem, vsem, ssem, gsem = pr.tsem, pr.vsem, pr.ssem, pr.gsem
        emit_consts(pr, sb)
        ck = {}
        emit_ffn0_body(nc, pr, sb, ck, r1=r1, wgu=wgu, wgs=wgs, wus=wus, wdp=wdp, o_r2=o_r2, h_s=h_s, do_s=do_s, o_dbg=o_dbg)
        def fin(g):
            wait_tok(g, pr.dtok())
            g.wait_ge(vsem.h, vsem.val)
            g.wait_ge(ssem.h, ssem.val)
            g.wait_ge(tsem.h, tsem.val)
        pr.on('gpsimd', fin)

    return nc


def ffn0_in_maps(prep, r1s):
    wgu = np.concatenate([np.asarray(prep[c]['o_wgu'])[:HID * GU_BLOCKS[c]]
                          for c in range(NCORE)], axis=0)
    wgs = np.concatenate([np.asarray(prep[c]['o_wgs'])[:, :256 * GU_BLOCKS[c]]
                          for c in range(NCORE)], axis=1)
    wus = np.concatenate([np.asarray(prep[c]['o_wus'])[:, :256 * GU_BLOCKS[c]]
                          for c in range(NCORE)], axis=1)
    wdp = np.concatenate([np.asarray(prep[c]['o_wdp']) for c in range(NCORE)], axis=0)
    return [dict(r1=np.asarray(r1s[c]), wgu=wgu, wgs=wgs, wus=wus, wdp=wdp)
            for c in range(NCORE)]


ATTN1_TENSORS = [
    ('big', [128, 32768], BF16, 'sbuf'),
    ('wblk', [128, 16384], BF16, 'sbuf'),
    ('asc', [128, 32], F32, 'sbuf'),
    ('ld', [128, 1024], F32, 'sbuf'),
    ('ld2', [128, 1024], F32, 'sbuf'),
    ('sq', [128, 1024], F32, 'sbuf'),
    ('ev', [128, 1024], F32, 'sbuf'),
    ('go', [128, 1024], F32, 'sbuf'),
    ('arep', [128, 1024], F32, 'sbuf'),
    ('brep', [128, 1024], F32, 'sbuf'),
    ('r1t', [128, 1024], F32, 'sbuf'),
    ('r2t', [128, 1024], F32, 'sbuf'),
    ('srows', [1, 6 * TPC], F32, 'sbuf'),
    ('asr', [1, HID], F32, 'sbuf'),
    ('b2c', [128, 32], F32, 'sbuf'),
    ('psA', [128, 512], F32, 'psum'),
    ('psB', [128, 512], F32, 'psum'),
    ('psC', [128, 512], F32, 'psum'),
    ('psD', [128, 512], F32, 'psum'),
    ('psE', [128, 512], F32, 'psum'),
    ('psF', [128, 512], F32, 'psum'),
    ('psG', [128, 512], F32, 'psum'),
    ('psH', [128, 512], F32, 'psum'),
]

def emit_attn1_body(nc, pr, sb, ck, *, r2, attnT, asum, b2, o_r3, go_s):
    tsem, vsem, ssem, gsem = pr.tsem, pr.vsem, pr.ssem, pr.gsem

    def ld0(q):
        q.dma_start(out=sb['asc'][:, :],
                    in_=bass.AP(asum.tensor, 0, [[1, 128], [128, 32]]),
                    allow_slow_non_contiguous=True)
        q.dma_start(out=sb['b2c'][:, :],
                    in_=bass.AP(b2.tensor, 0, [[1, 128], [128, 32]]),
                    allow_slow_non_contiguous=True)
        ck['ld0'] = pr.hw.token()
    pr.on('sync', ld0)

    for f in range(32):
        def ldr(q, f=f):
            pv = ck.get(('stmm', f - 2))
            if pv is not None:
                q.wait_ge(tsem.h, pv)
                q.wait_ge(vsem.h, ck[('xc', f - 2)])
            q.dma_start(out=sb['ld' if f % 2 == 0 else 'ld2'][:, :],
                        in_=r2[128 * f:128 * (f + 1), :])
            ck[('rl', f)] = pr.hw.token()
        pr.on('sync', ldr)

        def casx(v, f=f):
            wait_tok(v, ck[('rl', f)])
            v.tensor_copy(sb['big'][:, 1024 * f:1024 * (f + 1)],
                          sb['ld' if f % 2 == 0 else 'ld2'][:, :])
            ck[('xc', f)] = vsem.val
        pr.on('vector', casx)

        def sqp(s, f=f):
            wait_tok(s, ck[('rl', f)])
            if f >= 1:
                s.wait_ge(tsem.h, ck[('stmm', f - 1)])
            s.activation(sb['sq'][:, :], sb['ld' if f % 2 == 0 else 'ld2'][:, :], AF.Square)
            ck[('rsq', f)] = ssem.val
        pr.on('scalar', sqp)

        def stmm(tt, f=f):
            tt.wait_ge(ssem.h, ck[('rsq', f)])
            tt.wait_ge(gsem.h, gsem.val)
            for t in range(2):
                tt.matmul(sb['psE' if t == 0 else 'psF'][0:1, 0:512], sb['ones_f'][:, 0:1],
                          sb['ld' if f % 2 == 0 else 'ld2'][:, 512 * t:512 * (t + 1)],
                          start=(f == 0), stop=(f == 31))
                ins = tt.matmul(sb['psG' if t == 0 else 'psH'][0:1, 0:512], sb['ones_f'][:, 0:1],
                                sb['sq'][:, 512 * t:512 * (t + 1)], start=(f == 0), stop=(f == 31))
            tsem.inc(ins)
            ck[('stmm', f)] = tsem.val
        pr.on('tensor', stmm)

    def strows(v):
        v.wait_ge(tsem.h, ck[('stmm', 31)])
        R = sb['srows']
        v.tensor_copy(R[0:1, 0:512], sb['psE'][0:1, 0:512])
        v.tensor_copy(R[0:1, 512:1024], sb['psF'][0:1, 0:512])
        v.tensor_copy(R[0:1, TPC:TPC + 512], sb['psG'][0:1, 0:512])
        v.tensor_copy(R[0:1, TPC + 512:TPC + 1024], sb['psH'][0:1, 0:512])
        v.tensor_scalar(R[0:1, 2 * TPC:3 * TPC], R[0:1, 0:TPC], 1.0 / HID, None, ALU.mult)
        v.tensor_scalar(R[0:1, 3 * TPC:4 * TPC], R[0:1, TPC:2 * TPC], 1.0 / HID, None, ALU.mult)
        v.tensor_tensor(R[0:1, 4 * TPC:5 * TPC], R[0:1, 2 * TPC:3 * TPC],
                        R[0:1, 2 * TPC:3 * TPC], ALU.mult)
        v.tensor_scalar(R[0:1, 4 * TPC:5 * TPC], R[0:1, 4 * TPC:5 * TPC], -1.0, EPS,
                        ALU.mult, ALU.add)
        v.tensor_tensor(R[0:1, 3 * TPC:4 * TPC], R[0:1, 3 * TPC:4 * TPC],
                        R[0:1, 4 * TPC:5 * TPC], ALU.add)
        v.reciprocal(R[0:1, 4 * TPC:5 * TPC], R[0:1, 3 * TPC:4 * TPC])
        v.tensor_scalar(R[0:1, 2 * TPC:3 * TPC], R[0:1, 2 * TPC:3 * TPC], -1.0, None, ALU.mult)
        ck['st1'] = vsem.val
    pr.on('vector', strows)

    def strows2(s):
        s.wait_ge(vsem.h, ck['st1'])
        s.activation(sb['srows'][0:1, 5 * TPC:6 * TPC], sb['srows'][0:1, 4 * TPC:5 * TPC], AF.Sqrt)
        ck['st2'] = ssem.val
    pr.on('scalar', strows2)

    for t in range(2):
        def rmm(tt, t=t):
            tt.wait_ge(ssem.h, ck['st2'])
            pv = ck.get(('repev2', t - 1))
            if pv is not None:
                tt.wait_ge(vsem.h, pv)
            ins = tt.matmul(sb['psA'][:, 0:512], sb['ones_f'][0:1, 0:128],
                            sb['srows'][0:1, 5 * TPC + 512 * t:5 * TPC + 512 * (t + 1)],
                            start=True, stop=True)
            tsem.inc(ins)
            ck[('repmm', t)] = tsem.val
        pr.on('tensor', rmm)

        def rev(v, t=t):
            v.wait_ge(tsem.h, ck[('repmm', t)])
            v.tensor_copy(sb['arep'][:, 512 * t:512 * (t + 1)], sb['psA'][:, 0:512])
            ck[('repev', t)] = vsem.val
        pr.on('vector', rev)

        def rmm2(tt, t=t):
            tt.wait_ge(vsem.h, ck[('repev', t)])
            ins = tt.matmul(sb['psA'][:, 0:512], sb['ones_f'][0:1, 0:128],
                            sb['srows'][0:1, 2 * TPC + 512 * t:2 * TPC + 512 * (t + 1)],
                            start=True, stop=True)
            tsem.inc(ins)
            ck[('repmm2', t)] = tsem.val
        pr.on('tensor', rmm2)

        def rev2(v, t=t):
            v.wait_ge(tsem.h, ck[('repmm2', t)])
            v.tensor_copy(sb['brep'][:, 512 * t:512 * (t + 1)], sb['psA'][:, 0:512])
            ck[('repev2', t)] = vsem.val
        pr.on('vector', rev2)

    banks = ['psA', 'psB', 'psC', 'psD']
    for m2 in range(16):
        def wld(q, m2=m2):
            if m2 == 0:
                q.wait_ge(vsem.h, ck[('repev2', 1)])
            pv = ck.get(('ymm', m2 - 2, 31))
            if pv is not None:
                q.wait_ge(tsem.h, pv)
            q.dma_start(
                out=sb['wblk'][:, 8192 * (m2 % 2):8192 * (m2 % 2 + 1)]
                .rearrange("p (k n) -> p k n", k=32),
                in_=attnT[(m2 // 2) * HID:(m2 // 2 + 1) * HID,
                          256 * (m2 % 2):256 * (m2 % 2) + 256]
                .rearrange("(k p) n -> p k n", p=128))
            ck[('yw', m2)] = pr.hw.token()
        pr.on('sync', wld)

        def ymm(tt, m2=m2):
            wait_tok(tt, ck[('yw', m2)])
            if m2 == 0:
                tt.wait_ge(vsem.h, ck[('xc', 31)])
            pv = ck.get(('goev', m2 - 1, 1))
            if pv is not None:
                tt.wait_ge(ssem.h, ck[('gos', m2 - 1, 1)])
            W0 = 8192 * (m2 % 2)
            for k in range(32):
                for mi in range(2):
                    for t in range(2):
                        ins = tt.matmul(sb[banks[mi * 2 + t]][:, 0:512],
                                        sb['wblk'][:, W0 + 256 * k + 128 * mi:
                                                   W0 + 256 * k + 128 * (mi + 1)],
                                        sb['big'][:, 1024 * k + 512 * t:1024 * k + 512 * (t + 1)],
                                        start=(k == 0), stop=(k == 31))
                tsem.inc(ins)
                ck[('ymm', m2, k)] = tsem.val
        pr.on('tensor', ymm)

        for mi in range(2):
            m = 2 * m2 + mi

            def yev(v, m2=m2, mi=mi, m=m):
                v.wait_ge(tsem.h, ck[('ymm', m2, 31)])
                if mi == 1:
                    v.wait_ge(ssem.h, ck[('gos', m2, 0)])
                pv = ck.get(('godma', m - 1))
                if pv is not None:
                    wait_tok(v, pv)
                for t in range(2):
                    v.scalar_tensor_tensor(sb['ev'][:, 512 * t:512 * (t + 1)],
                                           sb['brep'][:, 512 * t:512 * (t + 1)],
                                           sb['asc'][:, m:m + 1],
                                           sb[banks[mi * 2 + t]][:, 0:512],
                                           ALU.mult, ALU.add)
                    v.tensor_tensor(sb['ev'][:, 512 * t:512 * (t + 1)],
                                    sb['ev'][:, 512 * t:512 * (t + 1)],
                                    sb['arep'][:, 512 * t:512 * (t + 1)], ALU.mult)
                ck[('yev', m)] = vsem.val
                if mi == 1:
                    ck[('goev', m2, 1)] = vsem.val
            pr.on('vector', yev)

            def gelu(s, m2=m2, mi=mi, m=m):
                s.wait_ge(vsem.h, ck[('yev', m)])
                if m >= 1:
                    s.wait_ge(tsem.h, ck[('gsmm', m - 1)])
                wait_tok(s, ck['ld0'])
                s.activation(sb['go'][:, :], sb['ev'][:, :], AF.Gelu,
                             bias=sb['b2c'][:, m:m + 1], scale=1.0)
                s.activation(sb['sq'][:, :], sb['go'][:, :], AF.Square)
                ck[('gos', m2, mi)] = ssem.val
                ck[('gow', m)] = ssem.val
            pr.on('scalar', gelu)

            def gsmm(tt, m=m):
                tt.wait_ge(ssem.h, ck[('gow', m)])
                for t in range(2):
                    tt.matmul(sb['psE' if t == 0 else 'psF'][0:1, 0:512], sb['ones_f'][:, 0:1],
                              sb['go'][:, 512 * t:512 * (t + 1)], start=(m == 0), stop=(m == 31))
                    ins = tt.matmul(sb['psG' if t == 0 else 'psH'][0:1, 0:512],
                                    sb['ones_f'][:, 0:1],
                                    sb['sq'][:, 512 * t:512 * (t + 1)], start=(m == 0), stop=(m == 31))
                tsem.inc(ins)
                ck[('gsmm', m)] = tsem.val
            pr.on('tensor', gsmm)

            def godma(g, m=m):
                g.wait_ge(tsem.h, ck[('gsmm', m)])
                g.dma_start(out=go_s[128 * m:128 * (m + 1), :], in_=sb['go'][:, :])
                ck[('godma', m)] = pr.sw.token()
            pr.on('gpsimd', godma)

    def fstr(v):
        v.wait_ge(tsem.h, ck[('gsmm', 31)])
        R = sb['srows']
        v.tensor_copy(R[0:1, 0:512], sb['psE'][0:1, 0:512])
        v.tensor_copy(R[0:1, 512:1024], sb['psF'][0:1, 0:512])
        v.tensor_copy(R[0:1, TPC:TPC + 512], sb['psG'][0:1, 0:512])
        v.tensor_copy(R[0:1, TPC + 512:TPC + 1024], sb['psH'][0:1, 0:512])
        v.tensor_scalar(R[0:1, 2 * TPC:3 * TPC], R[0:1, 0:TPC], 1.0 / HID, None, ALU.mult)
        v.tensor_scalar(R[0:1, 3 * TPC:4 * TPC], R[0:1, TPC:2 * TPC], 1.0 / HID, None, ALU.mult)
        v.tensor_tensor(R[0:1, 4 * TPC:5 * TPC], R[0:1, 2 * TPC:3 * TPC],
                        R[0:1, 2 * TPC:3 * TPC], ALU.mult)
        v.tensor_scalar(R[0:1, 4 * TPC:5 * TPC], R[0:1, 4 * TPC:5 * TPC], -1.0, EPS,
                        ALU.mult, ALU.add)
        v.tensor_tensor(R[0:1, 3 * TPC:4 * TPC], R[0:1, 3 * TPC:4 * TPC],
                        R[0:1, 4 * TPC:5 * TPC], ALU.add)
        v.reciprocal(R[0:1, 4 * TPC:5 * TPC], R[0:1, 3 * TPC:4 * TPC])
        ck['fst1'] = vsem.val
    pr.on('vector', fstr)

    def fstr2(s):
        s.wait_ge(vsem.h, ck['fst1'])
        s.activation(sb['srows'][0:1, 5 * TPC:6 * TPC],
                     sb['srows'][0:1, 4 * TPC:5 * TPC], AF.Sqrt)
        ck['fst2'] = ssem.val
    pr.on('scalar', fstr2)

    def fstr3(v):
        v.wait_ge(ssem.h, ck['fst2'])
        R = sb['srows']
        v.tensor_tensor(R[0:1, 2 * TPC:3 * TPC], R[0:1, 2 * TPC:3 * TPC],
                        R[0:1, 5 * TPC:6 * TPC], ALU.mult)
        v.tensor_scalar(R[0:1, 2 * TPC:3 * TPC], R[0:1, 2 * TPC:3 * TPC], -1.0, None, ALU.mult)
        ck['fst3'] = vsem.val
    pr.on('vector', fstr3)

    for t in range(2):
        for which, rowoff, dst in [(0, 5 * TPC, 'arep'), (1, 2 * TPC, 'brep')]:
            def rmm2(tt, t=t, rowoff=rowoff, which=which):
                tt.wait_ge(vsem.h, ck['fst3'])
                pv = ck.get(('frepev', t, which - 1)) or (ck.get(('frepev', t - 1, 1)) if t else None)
                if pv is not None:
                    tt.wait_ge(vsem.h, pv)
                ins = tt.matmul(sb['psA'][:, 0:512], sb['ones_f'][0:1, 0:128],
                                sb['srows'][0:1, rowoff + 512 * t:rowoff + 512 * (t + 1)],
                                start=True, stop=True)
                tsem.inc(ins)
                ck[('frepmm', t, which)] = tsem.val
            pr.on('tensor', rmm2)

            def rev2(v, t=t, which=which, dst=dst):
                v.wait_ge(tsem.h, ck[('frepmm', t, which)])
                v.tensor_copy(sb[dst][:, 512 * t:512 * (t + 1)], sb['psA'][:, 0:512])
                ck[('frepev', t, which)] = vsem.val
            pr.on('vector', rev2)

    for f in range(32):
        def fld(q, f=f):
            if f == 0:
                q.wait_ge(vsem.h, ck[('frepev', 1, 1)])
            pv = ck.get(('r3d', f - 1))
            if pv is not None:
                q.wait_ge(vsem.h, ck[('r3u', f - 1)])
            q.dma_start(out=sb['r1t'][:, :], in_=r2[128 * f:128 * (f + 1), :])
            q.dma_start(out=sb['ld'][:, :], in_=go_s[128 * f:128 * (f + 1), :])
            ck[('fl', f)] = pr.hw.token()
        pr.on('sync', fld)

        def fap(v, f=f):
            wait_tok(v, ck[('fl', f)])
            pv = ck.get(('r3d', f - 1))
            if pv is not None:
                wait_tok(v, pv)
            v.tensor_tensor(sb['r2t'][:, :], sb['ld'][:, :], sb['arep'][:, :], ALU.mult)
            v.tensor_tensor(sb['r2t'][:, :], sb['r2t'][:, :], sb['brep'][:, :], ALU.add)
            v.tensor_tensor(sb['r2t'][:, :], sb['r2t'][:, :], sb['r1t'][:, :], ALU.add)
            ck[('r3u', f)] = vsem.val
        pr.on('vector', fap)

        def fdma(g, f=f):
            g.wait_ge(vsem.h, ck[('r3u', f)])
            g.dma_start(out=o_r3[128 * f:128 * (f + 1), :], in_=sb['r2t'][:, :])
            ck[('r3d', f)] = pr.sw.token()
        pr.on('gpsimd', fdma)



def build_attn1():
    nc = bass.Bass("TRN2", target_bir_lowering=False, debug=False)
    r2 = nc.dram_tensor("r2", [HID, TPC], F32, kind="ExternalInput").ap()
    attnT = nc.dram_tensor("attnT", [8 * HID, 512], BF16, kind="ExternalInput").ap()
    asum = nc.dram_tensor("asum", [1, HID], F32, kind="ExternalInput").ap()
    b2 = nc.dram_tensor("b2", [1, HID], F32, kind="ExternalInput").ap()
    o_r3 = nc.dram_tensor("o_r3", [HID, TPC], F32, kind="ExternalOutput").ap()
    go_s = nc.dram_tensor("go_s", [HID, TPC], F32, kind="Internal").ap()

    tensors = CONSTS + ATTN1_TENSORS

    with nc.Block() as block:
        hw_sems = [nc.semaphore(f"hw{i}").__enter__() for i in range(8)]
        sw_sems = [nc.semaphore(f"sw{i}").__enter__() for i in range(8)]
        th = nc.semaphore("tsem").__enter__()
        vh = nc.semaphore("vsem").__enter__()
        sh = nc.semaphore("ssem").__enter__()
        gh = nc.semaphore("gsem").__enter__()
        sb = alloc(nc, tensors)
        pr = Prog(nc, block, hw_sems, sw_sems, th, vh, sh, gh)
        tsem, vsem, ssem, gsem = pr.tsem, pr.vsem, pr.ssem, pr.gsem
        emit_consts(pr, sb)
        ck = {}
        emit_attn1_body(nc, pr, sb, ck, r2=r2, attnT=attnT, asum=asum, b2=b2, o_r3=o_r3, go_s=go_s)
        def fin(g):
            wait_tok(g, pr.dtok())
            g.wait_ge(vsem.h, vsem.val)
            g.wait_ge(ssem.h, ssem.val)
            g.wait_ge(tsem.h, tsem.val)
        pr.on('gpsimd', fin)

    return nc


def attn1_in_maps(inp, prep, r2s):
    attnT = np.concatenate([np.asarray(prep[c]['o_attnT']) for c in range(NCORE)], axis=0)
    asum = np.concatenate([np.asarray(prep[c]['o_asum']) for c in range(NCORE)], axis=1)
    b2 = np.ascontiguousarray(inp['L1_attn_b2'].reshape(1, HID))
    return [dict(r2=np.asarray(r2s[c]), attnT=attnT, asum=asum, b2=b2) for c in range(NCORE)]


FFN1_TENSORS = [
    ('big', [128, 32768], BF16, 'sbuf'),
    ('labT', [128, 4096], BF16, 'sbuf'),
    ('wrT', [128, 256], BF16, 'sbuf'),
    ('rep16', [8, 128], BF16, 'sbuf'),
    ('wblk', [128, 8192], BF16, 'sbuf'),
    ('onesr', [1, 512], F32, 'sbuf'),
    ('ld2', [128, 1024], F32, 'sbuf'),
    ('ld', [128, 1024], F32, 'sbuf'),
    ('sq', [128, 1024], F32, 'sbuf'),
    ('ev', [128, 1024], F32, 'sbuf'),
    ('hl', [128, 1024], F32, 'sbuf'),
    ('hw', [128, 1024], BF16, 'sbuf'),
    ('wfm', [8, 1024], BF16, 'sbuf'),
    ('ltm', [128, 64], F32, 'sbuf'),
    ('lsc', [128, 32], F32, 'sbuf'),
    ('acol', [128, 8], F32, 'sbuf'),
    ('wcols', [128, 8], F32, 'sbuf'),
    ('arep', [128, 1024], F32, 'sbuf'),
    ('brep', [128, 1024], F32, 'sbuf'),
    ('r1t', [128, 1024], F32, 'sbuf'),
    ('r2t', [128, 1024], F32, 'sbuf'),
    ('tto', [128, 128], F32, 'sbuf'),
    ('srows', [1, 6 * TPC], F32, 'sbuf'),
    ('bbc', [128, 32], F32, 'sbuf'),
    ('wbsc', [128, 32], F32, 'sbuf'),
    ('rsr', [1, 8], F32, 'sbuf'),
    ('acsr', [1, 128], F32, 'sbuf'),
    ('wrsb', [8, 4096], BF16, 'sbuf'),
    ('bcbs', [128, 4096], BF16, 'sbuf'),
    ('psA', [128, 512], F32, 'psum'),
    ('psB', [128, 512], F32, 'psum'),
    ('psC', [128, 512], F32, 'psum'),
    ('psD', [128, 512], F32, 'psum'),
    ('psE', [128, 512], F32, 'psum'),
    ('psF', [128, 512], F32, 'psum'),
    ('psG', [128, 512], F32, 'psum'),
    ('psH', [128, 512], F32, 'psum'),
]

def emit_ffn1_body(nc, pr, sb, ck, *, r3, wbb, wbsum, wrb, rsum, lab, acs, bcb, bb, o_out, fo_s):
    tsem, vsem, ssem, gsem = pr.tsem, pr.vsem, pr.ssem, pr.gsem

    def consts2(g):
        g.memset(sb['rep16'][:, :], 1.0)
        g.affine_select(sb['rep16'][:, :], sb['rep16'][:, :], [[1, 8], [0, 16]],
                        ALU.is_equal, 0.0, base=0, channel_multiplier=-1)
        g.memset(sb['onesr'][:, :], 1.0)
        ck['c2'] = gsem.val
    pr.on('gpsimd', consts2)

    def ld0(q):
        q.dma_start(out=sb['bbc'][:, :],
                    in_=bass.AP(bb.tensor, 0, [[1, 128], [128, 32]]),
                    allow_slow_non_contiguous=True)
        q.dma_start(out=sb['wbsc'][:, :],
                    in_=bass.AP(wbsum.tensor, 0, [[1, 128], [128, 32]]),
                    allow_slow_non_contiguous=True)
        q.dma_start(out=sb['rsr'][0:1, :], in_=rsum)
        q.dma_start(out=sb['acsr'][0:1, :], in_=acs)
        q.dma_start(out=sb['wrsb'][:, :], in_=wrb)
        q.dma_start(out=sb['bcbs'][:, :], in_=bcb)
        ck['ld0'] = pr.hw.token()
    pr.on('sync', ld0)

    for f in range(32):
        def ldr(q, f=f):
            pv = ck.get(('stmm', f - 2))
            if pv is not None:
                q.wait_ge(tsem.h, pv)
                q.wait_ge(vsem.h, ck[('xc', f - 2)])
            q.dma_start(out=sb['ld' if f % 2 == 0 else 'ld2'][:, :],
                        in_=r3[128 * f:128 * (f + 1), :])
            ck[('rl', f)] = pr.hw.token()
        pr.on('sync', ldr)

        def casx(v, f=f):
            wait_tok(v, ck[('rl', f)])
            v.tensor_copy(sb['big'][:, 1024 * f:1024 * (f + 1)],
                          sb['ld' if f % 2 == 0 else 'ld2'][:, :])
            ck[('xc', f)] = vsem.val
        pr.on('vector', casx)

        def sqp(s, f=f):
            wait_tok(s, ck[('rl', f)])
            if f >= 1:
                s.wait_ge(tsem.h, ck[('stmm', f - 1)])
            s.activation(sb['sq'][:, :], sb['ld' if f % 2 == 0 else 'ld2'][:, :], AF.Square)
            ck[('rsq', f)] = ssem.val
        pr.on('scalar', sqp)

        def stmm(tt, f=f):
            tt.wait_ge(ssem.h, ck[('rsq', f)])
            tt.wait_ge(gsem.h, gsem.val)
            for t in range(2):
                tt.matmul(sb['psE' if t == 0 else 'psF'][0:1, 0:512], sb['ones_f'][:, 0:1],
                          sb['ld' if f % 2 == 0 else 'ld2'][:, 512 * t:512 * (t + 1)],
                          start=(f == 0), stop=(f == 31))
                ins = tt.matmul(sb['psG' if t == 0 else 'psH'][0:1, 0:512], sb['ones_f'][:, 0:1],
                                sb['sq'][:, 512 * t:512 * (t + 1)], start=(f == 0), stop=(f == 31))
            tsem.inc(ins)
            ck[('stmm', f)] = tsem.val
        pr.on('tensor', stmm)

    def strows(v):
        v.wait_ge(tsem.h, ck[('stmm', 31)])
        R = sb['srows']
        v.tensor_copy(R[0:1, 0:512], sb['psE'][0:1, 0:512])
        v.tensor_copy(R[0:1, 512:1024], sb['psF'][0:1, 0:512])
        v.tensor_copy(R[0:1, TPC:TPC + 512], sb['psG'][0:1, 0:512])
        v.tensor_copy(R[0:1, TPC + 512:TPC + 1024], sb['psH'][0:1, 0:512])
        v.tensor_scalar(R[0:1, 2 * TPC:3 * TPC], R[0:1, 0:TPC], 1.0 / HID, None, ALU.mult)
        v.tensor_scalar(R[0:1, 3 * TPC:4 * TPC], R[0:1, TPC:2 * TPC], 1.0 / HID, None, ALU.mult)
        v.tensor_tensor(R[0:1, 4 * TPC:5 * TPC], R[0:1, 2 * TPC:3 * TPC],
                        R[0:1, 2 * TPC:3 * TPC], ALU.mult)
        v.tensor_scalar(R[0:1, 4 * TPC:5 * TPC], R[0:1, 4 * TPC:5 * TPC], -1.0, EPS,
                        ALU.mult, ALU.add)
        v.tensor_tensor(R[0:1, 3 * TPC:4 * TPC], R[0:1, 3 * TPC:4 * TPC],
                        R[0:1, 4 * TPC:5 * TPC], ALU.add)
        v.reciprocal(R[0:1, 4 * TPC:5 * TPC], R[0:1, 3 * TPC:4 * TPC])
        v.tensor_scalar(R[0:1, 2 * TPC:3 * TPC], R[0:1, 2 * TPC:3 * TPC], -1.0, None, ALU.mult)
        ck['st1'] = vsem.val
    pr.on('vector', strows)

    def strows2(s):
        s.wait_ge(vsem.h, ck['st1'])
        s.activation(sb['srows'][0:1, 5 * TPC:6 * TPC],
                     sb['srows'][0:1, 4 * TPC:5 * TPC], AF.Sqrt)
        ck['st2'] = ssem.val
    pr.on('scalar', strows2)

    for t in range(2):
        def rmm(tt, t=t):
            tt.wait_ge(ssem.h, ck['st2'])
            pv = ck.get(('repev2', t - 1))
            if pv is not None:
                tt.wait_ge(vsem.h, pv)
            ins = tt.matmul(sb['psA'][:, 0:512], sb['ones_f'][0:1, 0:128],
                            sb['srows'][0:1, 5 * TPC + 512 * t:5 * TPC + 512 * (t + 1)],
                            start=True, stop=True)
            tsem.inc(ins)
            ck[('repmm', t)] = tsem.val
        pr.on('tensor', rmm)

        def rev(v, t=t):
            v.wait_ge(tsem.h, ck[('repmm', t)])
            v.tensor_copy(sb['arep'][:, 512 * t:512 * (t + 1)], sb['psA'][:, 0:512])
            ck[('repev', t)] = vsem.val
        pr.on('vector', rev)

        def rmm2(tt, t=t):
            tt.wait_ge(vsem.h, ck[('repev', t)])
            ins = tt.matmul(sb['psA'][:, 0:512], sb['ones_f'][0:1, 0:128],
                            sb['srows'][0:1, 2 * TPC + 512 * t:2 * TPC + 512 * (t + 1)],
                            start=True, stop=True)
            tsem.inc(ins)
            ck[('repmm2', t)] = tsem.val
        pr.on('tensor', rmm2)

        def rev2(v, t=t):
            v.wait_ge(tsem.h, ck[('repmm2', t)])
            v.tensor_copy(sb['brep'][:, 512 * t:512 * (t + 1)], sb['psA'][:, 0:512])
            ck[('repev2', t)] = vsem.val
        pr.on('vector', rev2)

    for t8 in range(8):
        def acl(tt, t8=t8):
            tt.wait_ge(ssem.h, ck['st2'])
            pv = ck.get(('aclv', t8 - 1))
            if pv is not None:
                tt.wait_ge(vsem.h, pv)
            else:
                tt.wait_ge(vsem.h, ck.get(('repev', 1), 0))
            ins = tt.transpose(sb['psB'][0:128, 0:1],
                               sb['srows'][0:1, 5 * TPC + 128 * t8:5 * TPC + 128 * (t8 + 1)],
                               sb['idf'][0:1, 0:1])
            tsem.inc(ins)
            ck[('aclt', t8)] = tsem.val
        pr.on('tensor', acl)

        def aclv(v, t8=t8):
            v.wait_ge(tsem.h, ck[('aclt', t8)])
            v.tensor_copy(sb['acol'][:, t8:t8 + 1], sb['psB'][0:128, 0:1])
            ck[('aclv', t8)] = vsem.val
        pr.on('vector', aclv)

    def wrl(q):
        wait_tok(q, ck['ld0'])
        with nc.allow_non_contiguous_dma(reason="tiny 8-row router transpose"):
            for k in range(32):
                q.dma_start_transpose(sb['wrT'][:, 8 * k:8 * (k + 1)],
                                      wrb[0:8, 128 * k:128 * (k + 1)])
        for k in range(32):
            q.dma_start_transpose(sb['labT'][:, 128 * k:128 * (k + 1)],
                                  lab[0:128, 128 * k:128 * (k + 1)])
        ck['wrl'] = pr.hw.token()
    pr.on('sync', wrl)

    def lgmm(tt):
        tt.wait_ge(vsem.h, ck[('xc', 31)])
        wait_tok(tt, ck['wrl'])
        tt.wait_ge(vsem.h, ck[('aclv', 7)])
        for k in range(32):
            for t in range(2):
                ins = tt.matmul(sb['psA' if t == 0 else 'psB'][0:8, 0:512],
                                sb['wrT'][:, 8 * k:8 * (k + 1)],
                                sb['big'][:, 1024 * k + 512 * t:1024 * k + 512 * (t + 1)],
                                start=(k == 0), stop=False)
        for t in range(2):
            ins = tt.matmul(sb['psA' if t == 0 else 'psB'][0:8, 0:512],
                            sb['rsr'][0:1, 0:8],
                            sb['srows'][0:1, 2 * TPC + 512 * t:2 * TPC + 512 * (t + 1)],
                            start=False, stop=True)
        tsem.inc(ins)
        ck['lg'] = tsem.val
    pr.on('tensor', lgmm)

    def lgev(v):
        v.wait_ge(tsem.h, ck['lg'])
        v.tensor_copy(sb['ld'][0:8, 0:512], sb['psA'][0:8, 0:512])
        v.tensor_copy(sb['ld'][0:8, 512:1024], sb['psB'][0:8, 0:512])
        ck['lgev'] = vsem.val
    pr.on('vector', lgev)

    def hmm(tt):
        tt.wait_ge(vsem.h, ck['lgev'])
        for k in range(32):
            for t in range(2):
                ins = tt.matmul(sb['psC' if t == 0 else 'psD'][:, 0:512],
                                sb['labT'][:, 128 * k:128 * (k + 1)],
                                sb['big'][:, 1024 * k + 512 * t:1024 * k + 512 * (t + 1)],
                                start=(k == 0), stop=False)
        for t in range(2):
            ins = tt.matmul(sb['psC' if t == 0 else 'psD'][:, 0:512],
                            sb['acsr'][0:1, 0:128],
                            sb['srows'][0:1, 2 * TPC + 512 * t:2 * TPC + 512 * (t + 1)],
                            start=False, stop=True)
        tsem.inc(ins)
        ck['hmm'] = tsem.val
    pr.on('tensor', hmm)

    def hev(v):
        v.wait_ge(tsem.h, ck['hmm'])
        for t in range(2):
            v.tensor_tensor(sb['hl'][:, 512 * t:512 * (t + 1)],
                            sb['psC' if t == 0 else 'psD'][:, 0:512],
                            sb['arep'][:, 512 * t:512 * (t + 1)], ALU.mult)
        ck['hev'] = vsem.val
    pr.on('vector', hev)

    for t8 in range(8):
        def ltt(tt, t8=t8):
            tt.wait_ge(vsem.h, ck['hev'] if t8 == 0 else ck[('w_fmv', t8 - 1)])
            ins = tt.transpose(sb['psT'].bitcast(F32)[0:128, 0:8],
                               sb['ld'][0:8, 128 * t8:128 * (t8 + 1)],
                               sb['idf'][0:8, 0:8]) if False else None
            ins = tt.transpose(sb['psG'][0:128, 0:8],
                               sb['ld'][0:8, 128 * t8:128 * (t8 + 1)],
                               sb['idf'][0:8, 0:8])
            tsem.inc(ins)
            ck[('ltt', t8)] = tsem.val
        pr.on('tensor', ltt)

        def smax(v, t8=t8):
            v.wait_ge(tsem.h, ck[('ltt', t8)])
            L = sb['lsc']
            v.tensor_scalar(L[:, 0:8], sb['psG'][0:128, 0:8], sb['acol'][:, t8:t8 + 1],
                            None, ALU.mult)
            v.reduce_max(L[:, 8:9], L[:, 0:8], AXX)
            v.tensor_scalar(L[:, 0:8], L[:, 0:8], L[:, 8:9], None, ALU.subtract)
            ck[('sm1', t8)] = vsem.val
        pr.on('vector', smax)

        def sexp(s, t8=t8):
            s.wait_ge(vsem.h, ck[('sm1', t8)])
            s.activation(sb['lsc'][:, 0:8], sb['lsc'][:, 0:8], AF.Exp)
            ck[('sm2', t8)] = ssem.val
        pr.on('scalar', sexp)

        def top2(v, t8=t8):
            v.wait_ge(ssem.h, ck[('sm2', t8)])
            L = sb['lsc']
            v.reduce_sum(L[:, 8:9], L[:, 0:8], AXX)
            v.reciprocal(L[:, 9:10], L[:, 8:9])
            v.tensor_scalar(L[:, 0:8], L[:, 0:8], L[:, 9:10], None, ALU.mult)
            v.reduce_max(L[:, 10:11], L[:, 0:8], AXX)
            v.tensor_scalar(L[:, 16:24], L[:, 0:8], L[:, 10:11], None, ALU.is_equal)
            v.scalar_tensor_tensor(L[:, 24:32], L[:, 16:24], -2.0, L[:, 0:8],
                                   ALU.mult, ALU.add)
            v.reduce_max(L[:, 11:12], L[:, 24:32], AXX)
            v.tensor_scalar(L[:, 16:24], L[:, 0:8], L[:, 11:12], None, ALU.is_ge)
            v.tensor_tensor(L[:, 12:13], L[:, 10:11], L[:, 11:12], ALU.add)
            v.reciprocal(L[:, 13:14], L[:, 12:13])
            v.tensor_tensor(L[:, 24:32], L[:, 0:8], L[:, 16:24], ALU.mult)
            v.tensor_scalar(sb['wcols'][:, 0:8], L[:, 24:32], L[:, 13:14], None, ALU.mult)
            ck[('w', t8)] = vsem.val
        pr.on('vector', top2)

        def wtr(tt, t8=t8):
            tt.wait_ge(vsem.h, ck[('w', t8)])
            ins = tt.transpose(sb['psG'][0:8, 0:128], sb['wcols'][:, 0:8],
                               sb['idf'][:, :])
            tsem.inc(ins)
            ck[('wtr', t8)] = tsem.val
        pr.on('tensor', wtr)

        def wfmv(v, t8=t8):
            v.wait_ge(tsem.h, ck[('wtr', t8)])
            v.tensor_copy(sb['wfm'][0:8, 128 * t8:128 * (t8 + 1)], sb['psG'][0:8, 0:128])
            ck[('w_fmv', t8)] = vsem.val
        pr.on('vector', wfmv)

    for t in range(2):
        def wexp(tt, t=t):
            tt.wait_ge(vsem.h, ck[('w_fmv', 7)])
            tt.wait_ge(gsem.h, gsem.val)
            pv = ck.get(('hwv', t - 1))
            if pv is not None:
                tt.wait_ge(vsem.h, pv)
            ins = tt.matmul(sb['psE'][:, 0:512], sb['rep16'][:, :],
                            sb['wfm'][0:8, 512 * t:512 * (t + 1)], start=True, stop=True)
            tsem.inc(ins)
            ck[('wx', t)] = tsem.val
        pr.on('tensor', wexp)

        def hwv(v, t=t):
            v.wait_ge(tsem.h, ck[('wx', t)])
            v.tensor_tensor(sb['hw'][:, 512 * t:512 * (t + 1)], sb['hl'][:, 512 * t:512 * (t + 1)],
                            sb['psE'][:, 0:512], ALU.mult)
            ck[('hwv', t)] = vsem.val
        pr.on('vector', hwv)

    for m in range(32):
        def wld(q, m=m):
            if m == 0:
                q.wait_ge(vsem.h, ck[('hwv', 1)])
            pv = ck.get(('bmm', m - 2, 31))
            if pv is not None:
                q.wait_ge(tsem.h, pv)
            q.dma_start(
                out=sb['wblk'][:, 4096 * (m % 2):4096 * (m % 2 + 1)]
                .rearrange("p (k n) -> p k n", k=32),
                in_=wbb[(m // 4) * HID:(m // 4 + 1) * HID,
                        128 * (m % 4):128 * (m % 4) + 128]
                .rearrange("(k p) n -> p k n", p=128))
            ck[('bw', m)] = pr.hw.token()
        pr.on('sync', wld)

        def bmm(tt, m=m):
            wait_tok(tt, ck[('bw', m)])
            pv2 = ck.get(('bev', m - 1))
            if pv2 is not None:
                tt.wait_ge(vsem.h, pv2)
            W0 = 4096 * (m % 2)
            for k in range(32):
                for t in range(2):
                    ins = tt.matmul(sb['psA' if t == 0 else 'psB'][:, 0:512],
                                    sb['wblk'][:, W0 + 128 * k:W0 + 128 * (k + 1)],
                                    sb['big'][:, 1024 * k + 512 * t:1024 * k + 512 * (t + 1)],
                                    start=(k == 0), stop=(k == 31))
                tsem.inc(ins)
                ck[('bmm', m, k)] = tsem.val
        pr.on('tensor', bmm)

        def bfold(tt, m=m):
            for t in range(2):
                ins = tt.matmul(sb['psC' if t == 0 else 'psD'][:, 0:512],
                                sb['bcbs'][:, 128 * m:128 * (m + 1)],
                                sb['hw'][:, 512 * t:512 * (t + 1)], start=True, stop=True)
            tsem.inc(ins)
            ck[('bfold', m)] = tsem.val
        pr.on('tensor', bfold)

        def bev(v, m=m):
            v.wait_ge(tsem.h, ck[('bfold', m)])
            pv = ck.get(('fodma', m - 1))
            if pv is not None:
                wait_tok(v, pv)
            for t in range(2):
                v.scalar_tensor_tensor(sb['ev'][:, 512 * t:512 * (t + 1)],
                                       sb['brep'][:, 512 * t:512 * (t + 1)],
                                       sb['wbsc'][:, m:m + 1],
                                       sb['psA' if t == 0 else 'psB'][:, 0:512],
                                       ALU.mult, ALU.add)
                v.tensor_tensor(sb['ev'][:, 512 * t:512 * (t + 1)],
                                sb['ev'][:, 512 * t:512 * (t + 1)],
                                sb['arep'][:, 512 * t:512 * (t + 1)], ALU.mult)
                v.scalar_tensor_tensor(sb['ev'][:, 512 * t:512 * (t + 1)],
                                       sb['psC' if t == 0 else 'psD'][:, 0:512],
                                       sb['bbc'][:, m:m + 1],
                                       sb['ev'][:, 512 * t:512 * (t + 1)],
                                       ALU.add, ALU.add)
            ck[('bev', m)] = vsem.val
        pr.on('vector', bev)

        def fsq(s, m=m):
            s.wait_ge(vsem.h, ck[('bev', m)])
            s.activation(sb['sq'][:, :], sb['ev'][:, :], AF.Square)
            ck[('fsq', m)] = ssem.val
        pr.on('scalar', fsq)

        def fsmm(tt, m=m):
            tt.wait_ge(ssem.h, ck[('fsq', m)])
            for t in range(2):
                tt.matmul(sb['psE' if t == 0 else 'psF'][0:1, 0:512], sb['ones_f'][:, 0:1],
                          sb['ev'][:, 512 * t:512 * (t + 1)], start=(m == 0), stop=(m == 31))
                ins = tt.matmul(sb['psG' if t == 0 else 'psH'][0:1, 0:512], sb['ones_f'][:, 0:1],
                                sb['sq'][:, 512 * t:512 * (t + 1)], start=(m == 0), stop=(m == 31))
            tsem.inc(ins)
            ck[('fsmm', m, 1)] = tsem.val
        pr.on('tensor', fsmm)

        def fodma(g, m=m):
            g.wait_ge(tsem.h, ck[('fsmm', m, 1)])
            g.dma_start(out=fo_s[128 * m:128 * (m + 1), :], in_=sb['ev'][:, :])
            ck[('fodma', m)] = pr.sw.token()
        pr.on('gpsimd', fodma)

    def fstr(v):
        v.wait_ge(tsem.h, ck[('fsmm', 31, 1)])
        R = sb['srows']
        v.tensor_copy(R[0:1, 0:512], sb['psE'][0:1, 0:512])
        v.tensor_copy(R[0:1, 512:1024], sb['psF'][0:1, 0:512])
        v.tensor_copy(R[0:1, TPC:TPC + 512], sb['psG'][0:1, 0:512])
        v.tensor_copy(R[0:1, TPC + 512:TPC + 1024], sb['psH'][0:1, 0:512])
        v.tensor_scalar(R[0:1, 2 * TPC:3 * TPC], R[0:1, 0:TPC], 1.0 / HID, None, ALU.mult)
        v.tensor_scalar(R[0:1, 3 * TPC:4 * TPC], R[0:1, TPC:2 * TPC], 1.0 / HID, None, ALU.mult)
        v.tensor_tensor(R[0:1, 4 * TPC:5 * TPC], R[0:1, 2 * TPC:3 * TPC],
                        R[0:1, 2 * TPC:3 * TPC], ALU.mult)
        v.tensor_scalar(R[0:1, 4 * TPC:5 * TPC], R[0:1, 4 * TPC:5 * TPC], -1.0, EPS,
                        ALU.mult, ALU.add)
        v.tensor_tensor(R[0:1, 3 * TPC:4 * TPC], R[0:1, 3 * TPC:4 * TPC],
                        R[0:1, 4 * TPC:5 * TPC], ALU.add)
        v.reciprocal(R[0:1, 4 * TPC:5 * TPC], R[0:1, 3 * TPC:4 * TPC])
        ck['fst1'] = vsem.val
    pr.on('vector', fstr)

    def fstr2(s):
        s.wait_ge(vsem.h, ck['fst1'])
        s.activation(sb['srows'][0:1, 5 * TPC:6 * TPC],
                     sb['srows'][0:1, 4 * TPC:5 * TPC], AF.Sqrt)
        ck['fst2'] = ssem.val
    pr.on('scalar', fstr2)

    def fstr3(v):
        v.wait_ge(ssem.h, ck['fst2'])
        R = sb['srows']
        v.tensor_tensor(R[0:1, 2 * TPC:3 * TPC], R[0:1, 2 * TPC:3 * TPC],
                        R[0:1, 5 * TPC:6 * TPC], ALU.mult)
        v.tensor_scalar(R[0:1, 2 * TPC:3 * TPC], R[0:1, 2 * TPC:3 * TPC], -1.0, None, ALU.mult)
        ck['fst3'] = vsem.val
    pr.on('vector', fstr3)

    for t in range(2):
        for which, rowoff, dst in [(0, 5 * TPC, 'arep'), (1, 2 * TPC, 'brep')]:
            def rmm2(tt, t=t, rowoff=rowoff, which=which):
                tt.wait_ge(vsem.h, ck['fst3'])
                pv = ck.get(('frepev', t, which - 1)) or (ck.get(('frepev', t - 1, 1)) if t else None)
                if pv is not None:
                    tt.wait_ge(vsem.h, pv)
                ins = tt.matmul(sb['psA'][:, 0:512], sb['ones_f'][0:1, 0:128],
                                sb['srows'][0:1, rowoff + 512 * t:rowoff + 512 * (t + 1)],
                                start=True, stop=True)
                tsem.inc(ins)
                ck[('frepmm', t, which)] = tsem.val
            pr.on('tensor', rmm2)

            def rev2(v, t=t, which=which, dst=dst):
                v.wait_ge(tsem.h, ck[('frepmm', t, which)])
                v.tensor_copy(sb[dst][:, 512 * t:512 * (t + 1)], sb['psA'][:, 0:512])
                ck[('frepev', t, which)] = vsem.val
            pr.on('vector', rev2)

    for f in range(32):
        def fld(q, f=f):
            if f == 0:
                q.wait_ge(vsem.h, ck[('frepev', 1, 1)])
            else:
                q.wait_ge(vsem.h, ck[('ru', f - 1)])
            q.dma_start(out=sb['r1t'][:, :], in_=r3[128 * f:128 * (f + 1), :])
            q.dma_start(out=sb['ld'][:, :], in_=fo_s[128 * f:128 * (f + 1), :])
            ck[('fl', f)] = pr.hw.token()
        pr.on('sync', fld)

        def fap(v, f=f):
            wait_tok(v, ck[('fl', f)])
            pv = ck.get(('otr', f - 1, 7))
            if pv is not None:
                v.wait_ge(tsem.h, pv)
            v.tensor_tensor(sb['r2t'][:, :], sb['ld'][:, :], sb['arep'][:, :], ALU.mult)
            v.tensor_tensor(sb['r2t'][:, :], sb['r2t'][:, :], sb['brep'][:, :], ALU.add)
            v.tensor_tensor(sb['r2t'][:, :], sb['r2t'][:, :], sb['r1t'][:, :], ALU.add)
            ck[('ru', f)] = vsem.val
        pr.on('vector', fap)

        for t8 in range(8):
            def otr(tt, f=f, t8=t8):
                tt.wait_ge(vsem.h, ck[('ru', f)])
                pv = ck.get(('ote', f, t8 - 1)) or (ck.get(('ote', f - 1, 7)) if f else None)
                if pv is not None:
                    tt.wait_ge(vsem.h, pv)
                ins = tt.transpose(sb['psB'][:, 0:128],
                                   sb['r2t'][:, 128 * t8:128 * (t8 + 1)], sb['idf'][:, :])
                tsem.inc(ins)
                ck[('otr', f, t8)] = tsem.val
            pr.on('tensor', otr)

            def ote(v, f=f, t8=t8):
                v.wait_ge(tsem.h, ck[('otr', f, t8)])
                pv = ck.get(('od', f, t8 - 1)) or (ck.get(('od', f - 1, 7)) if f else None)
                if pv is not None:
                    wait_tok(v, pv)
                v.tensor_copy(sb['tto'][:, :], sb['psB'][:, 0:128])
                ck[('ote', f, t8)] = vsem.val
            pr.on('vector', ote)

            def od(g, f=f, t8=t8):
                g.wait_ge(vsem.h, ck[('ote', f, t8)])
                g.dma_start(out=o_out[128 * t8:128 * (t8 + 1), 128 * f:128 * (f + 1)],
                            in_=sb['tto'][:, :])
                ck[('od', f, t8)] = pr.sw.token()
            pr.on('gpsimd', od)



def build_ffn1():
    nc = bass.Bass("TRN2", target_bir_lowering=False, debug=False)
    r3 = nc.dram_tensor("r3", [HID, TPC], F32, kind="ExternalInput").ap()
    wbb = nc.dram_tensor("wbb", [8 * HID, 512], BF16, kind="ExternalInput").ap()
    wbsum = nc.dram_tensor("wbsum", [1, HID], F32, kind="ExternalInput").ap()
    wrb = nc.dram_tensor("wrb", [8, HID], BF16, kind="ExternalInput").ap()
    rsum = nc.dram_tensor("rsum", [1, 8], F32, kind="ExternalInput").ap()
    lab = nc.dram_tensor("lab", [128, HID], BF16, kind="ExternalInput").ap()
    acs = nc.dram_tensor("acs", [1, 128], F32, kind="ExternalInput").ap()
    bcb = nc.dram_tensor("bcb", [128, HID], BF16, kind="ExternalInput").ap()
    bb = nc.dram_tensor("bb", [1, HID], F32, kind="ExternalInput").ap()
    o_out = nc.dram_tensor("o_out", [TPC, HID], F32, kind="ExternalOutput").ap()
    fo_s = nc.dram_tensor("fo_s", [HID, TPC], F32, kind="Internal").ap()

    tensors = CONSTS + FFN1_TENSORS

    with nc.Block() as block:
        hw_sems = [nc.semaphore(f"hw{i}").__enter__() for i in range(8)]
        sw_sems = [nc.semaphore(f"sw{i}").__enter__() for i in range(8)]
        th = nc.semaphore("tsem").__enter__()
        vh = nc.semaphore("vsem").__enter__()
        sh = nc.semaphore("ssem").__enter__()
        gh = nc.semaphore("gsem").__enter__()
        sb = alloc(nc, tensors)
        pr = Prog(nc, block, hw_sems, sw_sems, th, vh, sh, gh)
        tsem, vsem, ssem, gsem = pr.tsem, pr.vsem, pr.ssem, pr.gsem
        emit_consts(pr, sb)
        ck = {}
        emit_ffn1_body(nc, pr, sb, ck, r3=r3, wbb=wbb, wbsum=wbsum, wrb=wrb, rsum=rsum, lab=lab, acs=acs, bcb=bcb, bb=bb, o_out=o_out, fo_s=fo_s)
        def fin(g):
            wait_tok(g, pr.dtok())
            g.wait_ge(vsem.h, vsem.val)
            g.wait_ge(ssem.h, ssem.val)
            g.wait_ge(tsem.h, tsem.val)
        pr.on('gpsimd', fin)

    return nc


def ffn1_in_maps(inp, prep, r3s):
    wbbf = np.concatenate([np.asarray(prep[c]['o_wbb']) for c in range(NCORE)], axis=0)
    wbs = np.concatenate([np.asarray(prep[c]['o_wbsum']) for c in range(NCORE)], axis=1)
    return [dict(r3=np.asarray(r3s[c]), wbb=wbbf, wbsum=wbs,
                 wrb=np.asarray(prep[0]['o_wrb']), rsum=np.asarray(prep[0]['o_rsum']),
                 lab=np.asarray(prep[0]['o_lab']), acs=np.asarray(prep[0]['o_acs']),
                 bcb=np.asarray(prep[0]['o_bcb']),
                 bb=np.ascontiguousarray(inp['L1_ffn_b_base'].reshape(1, HID)))
            for c in range(NCORE)]


def kernel(**inputs):
    for nm in ('gate', 'up', 'down'):
        gidx = np.asarray(inputs[f'L0_ffn_{nm}_g_idx']).reshape(-1)
        assert (gidx == np.arange(gidx.size) // 128).all(), f'{nm} g_idx not contiguous-grouped'
    return kernel_fused(**inputs)


def kernel_unfused(**inputs):
    cores = list(range(NCORE))
    prep = run_bass_kernel_spmd(build_prep(), prep_in_maps(inputs), core_ids=cores).results
    prep = [{k: np.asarray(v) for k, v in r.items()} for r in prep]

    res = run_bass_kernel_spmd(build_attn0(), attn0_in_maps(inputs, prep), core_ids=cores)
    ghats = [np.asarray(res.results[c]['o_ghat']) for c in cores]

    res = run_bass_kernel_spmd(build_outproj(), outproj_in_maps(inputs, prep, ghats),
                               core_ids=cores)
    r1s = [np.asarray(res.results[c]['o_r1']) for c in cores]

    res = run_bass_kernel_spmd(build_ffn0(), ffn0_in_maps(prep, r1s), core_ids=cores)
    r2s = [np.asarray(res.results[c]['o_r2']) for c in cores]

    res = run_bass_kernel_spmd(build_attn1(), attn1_in_maps(inputs, prep, r2s), core_ids=cores)
    r3s = [np.asarray(res.results[c]['o_r3']) for c in cores]

    res = run_bass_kernel_spmd(build_ffn1(), ffn1_in_maps(inputs, prep, r3s), core_ids=cores)
    outs = [np.asarray(res.results[c]['o_out']) for c in cores]

    return np.concatenate(outs, axis=0).reshape(2, 4096, HID).astype(np.float32)


class APShim:

    def __init__(self, fn):
        self._fn = fn

    def __getitem__(self, idx):
        return self._fn(idx)


GU_C = {}
GU_L = {}
for _c in range(NCORE):
    for _l in range(GU_BLOCKS[_c]):
        GU_C[GU_B0[_c] + _l] = _c
        GU_L[GU_B0[_c] + _l] = _l


def build_fused():
    import contextlib
    nc = bass.Bass("TRN2", target_bir_lowering=False, debug=False)
    Iap = lambda name, shape, dt=F32: nc.dram_tensor(name, shape, dt, kind="ExternalInput").ap()
    Dr = lambda name, shape, dt=F32: nc.dram_tensor(name, shape, dt, kind="Internal").ap()

    xs = Iap("xs", [TPC, HID])
    wi = Iap("wi", [RPC, HID])
    wo = Iap("wo", [IPC, HID])
    gq = Iap("gq", [512, NB * 256], I32)
    gsc = Iap("gsc", [32, NB * 256])
    gz = Iap("gz", [32, NB * 32], I32)
    uq = Iap("uq", [512, NB * 256], I32)
    usc = Iap("usc", [32, NB * 256])
    uz = Iap("uz", [32, NB * 32], I32)
    dq = Iap("dq", [FFN_I // 8, 512], I32)
    dsc = Iap("dsc", [86, 512])
    dz = Iap("dz", [86, 64], I32)
    w1 = Iap("w1", [HID, HID])
    w2s = Iap("w2s", [512, HID])
    masks = Iap("masks", [HID, 512])
    wbs = Iap("wbs", [HID, 512])
    wr = Iap("wr", [8, HID])
    lA = Iap("lA", [128, HID])
    lB = Iap("lB", [8 * HID, 16])
    nwc = Iap("nwc", [1, IPC])
    cw = Iap("cw", [CCPC, 4])
    cb = Iap("cb", [1, CCPC])
    dtb = Iap("dtb", [1, HPC])
    alog = Iap("alog", [1, HPC])
    dd = Iap("dd", [1, HPC])
    b2 = Iap("b2", [1, HID])
    bb = Iap("bb", [1, HID])
    o_out = nc.dram_tensor("o_out", [TPC, HID], F32, kind="ExternalOutput").ap()

    w2c = Dr("w2c", [512, HID], BF16)
    mkc = Dr("mkc", [HID, 512], BF16)
    xbt_l = Dr("xbt_l", [TPC, HID], BF16)
    lnrow_l = Dr("lnrow_l", [2, TPC])
    lncol_l = Dr("lncol_l", [128, 16])
    wib = Dr("wib_l", [RPC, HID], BF16)
    wsum = Dr("wsum_l", [1, RPC])
    wob_l = Dr("wob_l", [IPC, HID], BF16)
    wgu_l = Dr("wgu_l", [NB * HID, 512], BF16)
    wgs_l = Dr("wgs_l", [1, NB * 256])
    wus_l = Dr("wus_l", [1, NB * 256])
    wdp_l = Dr("wdp_l", [2 * DBLK, 256], BF16)
    attnT_l = Dr("attnT_l", [HID, 512], BF16)
    asum_l = Dr("asum_l", [1, 512])
    wbb_l = Dr("wbb_l", [HID, 512], BF16)
    wbsum_l = Dr("wbsum_l", [1, 512])
    wrb = Dr("wrb_l", [8, HID], BF16)
    rsum = Dr("rsum_l", [1, 8])
    lab = Dr("lab_l", [128, HID], BF16)
    acs = Dr("acs_l", [1, 128])
    bcb = Dr("bcb_l", [128, HID], BF16)
    dbg = Dr("dbg_l", [1, 2048])

    xbt_g = Dr("xbt_g", [TOK, HID], BF16)
    lnrow_g = Dr("lnrow_g", [16, TPC])
    lncol_g = Dr("lncol_g", [1024, 16])
    wob_g = Dr("wob_g", [8 * IPC, HID], BF16)
    wgu_g = Dr("wgu_g", [8 * NB * HID, 512], BF16)
    wgs_g = Dr("wgs_g", [8, NB * 256])
    wus_g = Dr("wus_g", [8, NB * 256])
    wdp_g = Dr("wdp_g", [16 * DBLK, 256], BF16)
    attnT_g = Dr("attnT_g", [8 * HID, 512], BF16)
    asum_g = Dr("asum_g", [8, 512])
    wbb_g = Dr("wbb_g", [8 * HID, 512], BF16)
    wbsum_g = Dr("wbsum_g", [8, 512])
    ghat_l = Dr("ghat_l", [IPC, TOK], BF16)
    ghat_g = Dr("ghat_g", [INTER, TOK], BF16)
    lnrow_f = Dr("lnrow_f", [2, TOK])
    lncola_f = Dr("lncola_f", [128, 64])
    asum_f = Dr("asum_f", [1, HID])
    wbsum_f = Dr("wbsum_f", [1, HID])
    xbc_s = Dr("xbc_s", [CCPC, TOK], BF16)
    gate_s = Dr("gate_s", [TOK, IPC], BF16)
    cvs_s = Dr("cvs_s", [CCPC, TOK], BF16)
    ys_s = Dr("ys_s", [TOK, IPC], BF16)
    ao_s = Dr("ao_s", [HID, TPC])
    r1 = Dr("r1_h", [HID, TPC])
    h_s = Dr("h_s", [FFN_I, TPC], BF16)
    do_s = Dr("do_s", [HID, TPC])
    r2 = Dr("r2_h", [HID, TPC])
    go_s = Dr("go_s", [HID, TPC])
    r3 = Dr("r3_h", [HID, TPC])
    fo_s = Dr("fo_s", [HID, TPC])

    def wgu_fn(idx):
        rs, cs = idx
        mg = rs.start // HID
        base = GU_C[mg] * (NB * HID) + GU_L[mg] * HID
        return wgu_g[base:base + HID, :]
    wgu_sh = APShim(wgu_fn)

    def wgs_fn(idx):
        rs, cs = idx
        mg = cs.start // 256
        c, l = GU_C[mg], GU_L[mg]
        return wgs_g[c:c + 1, 256 * l:256 * (l + 1)]
    wgs_sh = APShim(wgs_fn)

    def wus_fn(idx):
        rs, cs = idx
        mg = cs.start // 256
        c, l = GU_C[mg], GU_L[mg]
        return wus_g[c:c + 1, 256 * l:256 * (l + 1)]
    wus_sh = APShim(wus_fn)

    rg = [list(range(NCORE))]

    with nc.Block() as block:
        hw_sems = [nc.semaphore(f"hw{i}").__enter__() for i in range(8)]
        sw_sems = [nc.semaphore(f"sw{i}").__enter__() for i in range(8)]
        th = nc.semaphore("tsem").__enter__()
        vh = nc.semaphore("vsem").__enter__()
        sh = nc.semaphore("ssem").__enter__()
        gh = nc.semaphore("gsem").__enter__()
        cch = nc.semaphore("ccsem").__enter__()
        consts = alloc(nc, CONSTS)
        pr = Prog(nc, block, hw_sems, sw_sems, th, vh, sh, gh)
        tsem, vsem, ssem, gsem = pr.tsem, pr.vsem, pr.ssem, pr.gsem
        emit_consts(pr, consts)
        ncc = [0]

        def phase(tens, emit, **kw):
            with contextlib.ExitStack() as st:
                sb = alloc(nc, tens, stack=st, prefix=f"p{phase.n}_")
                sb.update(consts)
                ck = {}
                emit(nc, pr, sb, ck, **kw)
                emit_barrier(pr)
            phase.n += 1
        phase.n = 0

        def agather(pairs):
            def f(g):
                for (i_ap, o_ap) in pairs:
                    g.collective_compute("AllGather", ALU.bypass, replica_groups=rg,
                                         ins=[i_ap.opt()], outs=[o_ap.opt()]
                                         ).then_inc(cch, 1)
                    ncc[0] += 1
            pr.on('gpsimd', f)

        def ccwait(n):
            def mk():
                def f(e):
                    raw = e._e if isinstance(e, ChainEng) else e
                    raw.wait_ge(cch, n)
                return f
            for eng in ('sync', 'tensor', 'vector', 'scalar', 'gpsimd'):
                pr.on(eng, mk())

        with contextlib.ExitStack() as _stP:
            sbp = alloc(nc, PREP_TENSORS, stack=_stP, prefix='pp_')
            sbp.update(consts)
            ckp = {}
            emit_prep_p1(nc, pr, sbp, ckp, xs=xs, o_xbt=xbt_l,
                         o_lnrow=lnrow_l, o_lncol=lncol_l)

            def agx(g):
                wait_tok(g, ckp['lnc'])
                for (i_ap, o_ap) in [(xbt_l, xbt_g), (lnrow_l, lnrow_g),
                                     (lncol_l, lncol_g)]:
                    g.collective_compute("AllGather", ALU.bypass, replica_groups=rg,
                                         ins=[i_ap.opt()], outs=[o_ap.opt()]
                                         ).then_inc(cch, 1)
                    ncc[0] += 1
            pr.on('gpsimd', agx)

            emit_prep_rest(nc, pr, sbp, ckp,
                           wi=wi, wo=wo, gq=gq, gsc=gsc, gz=gz, uq=uq, usc=usc,
                           uz=uz, dq=dq, dsc=dsc, dz=dz, w1=w1, w2s=w2s,
                           masks=masks, wbs=wbs, wr=wr, lA=lA, lB=lB,
                           o_wib=wib, o_wsum=wsum, o_wob=wob_l, o_wgu=wgu_l,
                           o_wgs=wgs_l, o_wus=wus_l, o_wdp=wdp_l,
                           o_attnT=attnT_l, o_asum=asum_l, o_wbb=wbb_l,
                           o_wbsum=wbsum_l, o_wrb=wrb, o_rsum=rsum, o_lab=lab,
                           o_acs=acs, o_bcb=bcb, w2c=w2c, mkc=mkc)
            emit_barrier(pr)

        agather([(wob_l, wob_g), (attnT_l, attnT_g), (asum_l, asum_g),
                 (wbb_l, wbb_g), (wbsum_l, wbsum_g)])

        rps = contextlib.ExitStack()
        rp = alloc(nc, [('rpk', [128, 1024], F32, 'sbuf')], stack=rps)

        def repack(g):
            g.wait_ge(cch, 3)
            for r in range(8):
                g.dma_start(out=rp['rpk'][0:1, 0:1024], in_=lnrow_g[2 * r:2 * r + 1, :])
                g.dma_start(out=rp['rpk'][1:2, 0:1024], in_=lnrow_g[2 * r + 1:2 * r + 2, :])
                wait_tok(g, pr.sw.token())
                g.dma_start(out=lnrow_f[0:1, TPC * r:TPC * (r + 1)], in_=rp['rpk'][0:1, 0:1024])
                g.dma_start(out=lnrow_f[1:2, TPC * r:TPC * (r + 1)], in_=rp['rpk'][1:2, 0:1024])
                wait_tok(g, pr.sw.token())
            for r in range(8):
                g.dma_start(out=rp['rpk'][:, 0:8], in_=lncol_g[128 * r:128 * (r + 1), 0:8])
                g.dma_start(out=rp['rpk'][:, 8:16], in_=lncol_g[128 * r:128 * (r + 1), 8:16])
                wait_tok(g, pr.sw.token())
                g.dma_start(out=lncola_f[:, 8 * r:8 * (r + 1)], in_=rp['rpk'][:, 0:8])
                wait_tok(g, pr.sw.token())
        pr.on('gpsimd', repack)
        ccwait(3)
        emit_barrier(pr)
        rps.close()

        def agdef():
            agather([(wgu_l, wgu_g), (wgs_l, wgs_g),
                     (wus_l, wus_g), (wdp_l, wdp_g)])

        phase(ATTN0_TENSORS, emit_attn0_body,
              xbt=xbt_g, lnrow=lnrow_f, lncola=lncola_f, wib=wib, wsum=wsum,
              nwc=nwc, cw=cw, cb=cb, dtb=dtb, alog=alog, dd=dd,
              o_ghat=ghat_l, xbc_s=xbc_s, gate_s=gate_s, cvs_s=cvs_s, ys_s=ys_s,
              post_a=agdef)

        agather([(ghat_l, ghat_g)])
        ccwait(13)
        rps2 = contextlib.ExitStack()
        rp2 = alloc(nc, [('rpk2', [8, 1024], F32, 'sbuf')], stack=rps2)

        def repack2(g):
            g.wait_ge(cch, 8)
            g.dma_start(out=rp2['rpk2'][0:8, 0:512], in_=asum_g)
            g.dma_start(out=rp2['rpk2'][0:8, 512:1024], in_=wbsum_g)
            wait_tok(g, pr.sw.token())
            for r in range(8):
                g.dma_start(out=asum_f[0:1, 512 * r:512 * (r + 1)],
                            in_=rp2['rpk2'][r:r + 1, 0:512])
                g.dma_start(out=wbsum_f[0:1, 512 * r:512 * (r + 1)],
                            in_=rp2['rpk2'][r:r + 1, 512:1024])
            wait_tok(g, pr.sw.token())
        pr.on('gpsimd', repack2)
        emit_barrier(pr)
        rps2.close()

        phase(OUTPROJ_TENSORS, emit_outproj_body,
              ghat=ghat_g, wob=wob_g, x0=xs, o_r1=r1, ao_s=ao_s, cid=True)

        phase(FFN0_TENSORS, emit_ffn0_body,
              r1=r1, wgu=wgu_sh, wgs=wgs_sh, wus=wus_sh, wdp=wdp_g,
              o_r2=r2, h_s=h_s, do_s=do_s, o_dbg=dbg)

        phase(ATTN1_TENSORS, emit_attn1_body,
              r2=r2, attnT=attnT_g, asum=asum_f, b2=b2, o_r3=r3, go_s=go_s)

        phase(FFN1_TENSORS, emit_ffn1_body,
              r3=r3, wbb=wbb_g, wbsum=wbsum_f, wrb=wrb, rsum=rsum, lab=lab,
              acs=acs, bcb=bcb, bb=bb, o_out=o_out, fo_s=fo_s)

        def fin(g):
            wait_tok(g, pr.dtok())
            g.wait_ge(vsem.h, vsem.val)
            g.wait_ge(ssem.h, ssem.val)
            g.wait_ge(tsem.h, tsem.val)
        pr.on('gpsimd', fin)

    return nc


def fused_in_maps(inputs):
    pm = prep_in_maps(inputs)
    cw2 = inputs['L0_attn_conv_w'][:, 0, :]
    cbv = inputs['L0_attn_conv_b']
    maps = []
    for c in range(NCORE):
        chsl = np.r_[IPC * c:IPC * (c + 1),
                     INTER + 128 * c:INTER + 128 * (c + 1),
                     INTER + G * NS + 128 * c:INTER + G * NS + 128 * (c + 1)]
        m = dict(pm[c])
        m.update(
            nwc=np.ascontiguousarray(inputs['L0_attn_norm_w'][IPC * c:IPC * (c + 1)].reshape(1, IPC)),
            cw=np.ascontiguousarray(cw2[chsl]),
            cb=np.ascontiguousarray(cbv[chsl].reshape(1, CCPC)),
            dtb=np.ascontiguousarray(inputs['L0_attn_dt_bias'][HPC * c:HPC * (c + 1)].reshape(1, HPC)),
            alog=np.ascontiguousarray(inputs['L0_attn_A_log'][HPC * c:HPC * (c + 1)].reshape(1, HPC)),
            dd=np.ascontiguousarray(inputs['L0_attn_D'][HPC * c:HPC * (c + 1)].reshape(1, HPC)),
            b2=np.ascontiguousarray(inputs['L1_attn_b2'].reshape(1, HID)),
            bb=np.ascontiguousarray(inputs['L1_ffn_b_base'].reshape(1, HID)),
        )
        maps.append(m)
    return maps


def kernel_fused(**inputs):
    nc = build_fused()
    res = run_bass_kernel_spmd(nc, fused_in_maps(inputs), core_ids=list(range(NCORE)))
    outs = [np.asarray(res.results[c]['o_out']) for c in range(NCORE)]
    return np.concatenate(outs, axis=0).reshape(2, 4096, HID).astype(np.float32)


# version 09
# speedup vs baseline: 1.0237x; 1.0237x over previous
import numpy as np
import ml_dtypes
import concourse.bass as bass
import concourse.mybir as mybir
from concourse.bass_utils import run_bass_kernel_spmd

F32 = mybir.dt.float32
BF16 = mybir.dt.bfloat16
I32 = mybir.dt.int32
AF = mybir.ActivationFunctionType
ALU = mybir.AluOpType
AXX = mybir.AxisListType.X
BFNP = ml_dtypes.bfloat16

NCORE = 8
HID = 4096
TOK = 8192
TPC = TOK // NCORE
H, P, NS, G = 128, 64, 128, 8
HPC = H // NCORE
INTER = H * P
IPC = INTER // NCORE
CONVCH = INTER + 2 * G * NS
CCPC = CONVCH // NCORE
PROJ = INTER + CONVCH + H
RPC = PROJ // NCORE
FFN_I = 11008
CS = 128
EPS = 1e-5
NB = 6
NBLK_GU = 43
GU_BLOCKS = [6, 6, 6, 5, 5, 5, 5, 5]
GU_B0 = [0, 6, 12, 18, 23, 28, 33, 38]


def cdiv(a, b):
    return (a + b - 1) // b


class Sem:
    def __init__(self, handle, step):
        self.h = handle
        self.step = step
        self.val = 0

    def inc(self, inst, n=1):
        inst.then_inc(self.h, n * self.step)
        self.val += n * self.step
        return self.val


_CHAIN_OPS = {
    'memset', 'memzero', 'tensor_copy', 'tensor_tensor', 'tensor_add',
    'tensor_sub', 'tensor_mul', 'tensor_max', 'tensor_scalar',
    'tensor_single_scalar', 'tensor_scalar_add', 'tensor_scalar_sub',
    'tensor_scalar_mul', 'tensor_scalar_max', 'tensor_scalar_min',
    'tensor_relu', 'tensor_reduce', 'reduce_max', 'reduce_sum',
    'scalar_tensor_tensor', 'tensor_tensor_scan', 'reciprocal', 'select',
    'copy_predicated', 'transpose', 'bn_stats', 'bn_aggr', 'pool', 'pool_avg',
    'pool_max', 'stream_shuffle', 'activation', 'sign', 'square', 'sqrt',
    'mul', 'add', 'copy', 'iota', 'affine_select', 'partition_broadcast',
    'partition_all_reduce',
}
_DMA_OPS = {'dma_start', 'dma_start_transpose', 'indirect_dma_start'}


class DmaFamily:

    def __init__(self, sems):
        self.sems = sems
        self.cnt = [0] * len(sems)
        self.i = 0

    def issue(self, raw_eng, fn, *a, **k):
        j = self.i % len(self.sems)
        if self.cnt[j] > 0:
            raw_eng.wait_ge(self.sems[j], 16 * self.cnt[j])
        inst = fn(*a, **k)
        inst.then_inc(self.sems[j], 16)
        self.cnt[j] += 1
        self.i += 1
        return inst

    def token(self):
        return tuple((s, 16 * c) for s, c in zip(self.sems, self.cnt) if c > 0)


class ChainEng:

    def __init__(self, eng, sem, fam):
        self._e = eng
        self._sem = sem
        self._fam = fam
        self._lastwait = -1

    def __getattr__(self, name):
        fn = getattr(self._e, name)
        if name in _DMA_OPS:
            def wdma(*a, **k):
                return self._fam.issue(self._e, fn, *a, **k)
            return wdma
        if name in _CHAIN_OPS:
            def wop(*a, **k):
                if self._sem.val > self._lastwait:
                    self._e.wait_ge(self._sem.h, self._sem.val)
                    self._lastwait = self._sem.val
                inst = fn(*a, **k)
                self._sem.inc(inst)
                return inst
            return wop
        return fn

    def wait_tok(self, tok):
        for s, v in tok:
            self._e.wait_ge(s, v)


class Prog:
    def __init__(self, nc, block, hw_sems, sw_sems, th, vh, sh, gh):
        self.nc = nc
        self.block = block
        self.hw = DmaFamily(hw_sems)
        self.sw = DmaFamily(sw_sems)
        self.tsem = Sem(th, 1)
        self.vsem = Sem(vh, 1)
        self.ssem = Sem(sh, 1)
        self.gsem = Sem(gh, 1)
        self.engsem = {'vector': self.vsem, 'scalar': self.ssem,
                       'gpsimd': self.gsem, 'sync': Sem(th, 0)}

    def on(self, eng, fn):
        def wrap(raw):
            if eng == 'tensor':
                fn(raw)
            else:
                fam = self.sw if eng == 'gpsimd' else self.hw
                fn(ChainEng(raw, self.engsem[eng], fam))
        getattr(self.block, eng)(wrap)

    def dtok(self):
        return self.hw.token() + self.sw.token()


def wait_tok(eng, tok):
    e = eng._e if isinstance(eng, ChainEng) else eng
    for s, v in tok:
        e.wait_ge(s, v)


CONSTS = [
    ('ones_bf', [128, 128], BF16, 'sbuf'),
    ('ones_f', [128, 128], F32, 'sbuf'),
    ('idbf', [128, 128], BF16, 'sbuf'),
    ('idf', [128, 128], F32, 'sbuf'),
    ('triu', [128, 128], F32, 'sbuf'),
    ('mneg', [128, 128], F32, 'sbuf'),
]


def alloc(nc, want, stack=None, prefix=''):
    out = {}
    for name, shape, dt, kind in want:
        g = (nc.sbuf_tensor(prefix + name, shape, dt) if kind == 'sbuf'
             else nc.psum_tensor(prefix + name, shape, dt))
        out[name] = stack.enter_context(g) if stack is not None else g.__enter__()
    return out


def emit_barrier(pr):
    vals = [(pr.tsem.h, pr.tsem.val), (pr.vsem.h, pr.vsem.val),
            (pr.ssem.h, pr.ssem.val), (pr.gsem.h, pr.gsem.val)]
    toks = pr.hw.token() + pr.sw.token()

    def mk(eng):
        def f(e):
            raw = e._e if isinstance(e, ChainEng) else e
            for h, v in vals:
                if v > 0:
                    raw.wait_ge(h, v)
            for h, v in toks:
                raw.wait_ge(h, v)
        return f
    for eng in ('sync', 'tensor', 'vector', 'scalar', 'gpsimd'):
        pr.on(eng, mk(eng))


def emit_consts(pr, sb):
    def f(g):
        g.memset(sb['ones_bf'][:, :], 1.0)
        g.memset(sb['ones_f'][:, :], 1.0)
        g.affine_select(sb['idbf'][:, :], sb['ones_bf'][:, 0:128], [[1, 128]],
                        ALU.is_equal, 0.0, base=0, channel_multiplier=-1)
        g.affine_select(sb['idf'][:, :], sb['ones_f'][:, 0:128], [[1, 128]],
                        ALU.is_equal, 0.0, base=0, channel_multiplier=-1)
        g.affine_select(sb['triu'][:, :], sb['ones_f'][:, 0:128], [[1, 128]],
                        ALU.is_ge, 0.0, base=0, channel_multiplier=-1)
        g.memset(sb['mneg'][:, :], 0.0)
        g.affine_select(sb['mneg'][:, :], sb['mneg'][:, :], [[1, 128]],
                        ALU.is_ge, -1e30, base=0, channel_multiplier=-1)
    pr.on('gpsimd', f)



PREP_TENSORS = [
    ('lda', [128, 4096], F32, 'sbuf'),
    ('ldb', [128, 4096], F32, 'sbuf'),
    ('cba', [128, 4096], BF16, 'sbuf'),
    ('cbb', [128, 4096], BF16, 'sbuf'),
    ('sq', [128, 4096], F32, 'sbuf'),
    ('cols', [128, 64], F32, 'sbuf'),
    ('rowa', [1, 4096], F32, 'sbuf'),
    ('rowb', [1, 4096], F32, 'sbuf'),
    ('qwa', [128, 512], I32, 'sbuf'),
    ('qwb', [128, 512], I32, 'sbuf'),
    ('nib', [128, 512], I32, 'sbuf'),
    ('dqt', [128, 512], F32, 'sbuf'),
    ('dqw', [128, 512], BF16, 'sbuf'),
    ('dqw2', [128, 512], BF16, 'sbuf'),
    ('srep', [128, 512], F32, 'sbuf'),
    ('srb', [128, 512], F32, 'sbuf'),
    ('brep', [128, 512], F32, 'sbuf'),
    ('brb', [128, 512], F32, 'sbuf'),
    ('scz', [86, 1536], F32, 'sbuf'),
    ('zqz', [86, 192], I32, 'sbuf'),
    ('zzi', [86, 1536], I32, 'sbuf'),
    ('zz', [86, 1536], F32, 'sbuf'),
    ('beta', [86, 1536], F32, 'sbuf'),
    ('w2T', [128, 32 * 512], BF16, 'sbuf'),
    ('mk', [128, 512], BF16, 'sbuf'),
    ('att', [128, 512], BF16, 'sbuf'),
    ('att2', [128, 256], BF16, 'sbuf'),
    ('psA', [128, 512], F32, 'psum'),
    ('psB', [128, 512], F32, 'psum'),
    ('psC', [128, 512], F32, 'psum'),
    ('psD', [128, 512], F32, 'psum'),
    ('psT', [128, 128], BF16, 'psum'),
    ('psT2', [128, 128], BF16, 'psum'),
    ('psTF', [128, 128], F32, 'psum'),
    ('psR', [1, 512], F32, 'psum'),
]

def emit_prep_p1(nc, pr, sb, ck, *, xs, o_xbt, o_lnrow, o_lncol):
    tsem, vsem, ssem, gsem = pr.tsem, pr.vsem, pr.ssem, pr.gsem

    for t in range(8):
        def load(q, t=t):
            if t >= 2:
                q.wait_ge(vsem.h, ck[('xcast', t - 2)])
            q.dma_start(out=sb['lda' if t % 2 == 0 else 'ldb'][:, :],
                        in_=xs[128 * t:128 * (t + 1), :])
            ck[('xld', t)] = pr.hw.token()
        pr.on('sync', load)

        def square(s, t=t):
            wait_tok(s, ck[('xld', t)])
            s.activation(sb['sq'][:, :], sb['lda' if t % 2 == 0 else 'ldb'][:, :],
                         AF.Square, accum_out=sb['cols'][:, 32:33])
            ck[('xsq', t)] = ssem.val
        pr.on('scalar', square)

        def stats(v, t=t):
            fl = sb['lda' if t % 2 == 0 else 'ldb']
            wait_tok(v, ck[('xld', t)])
            v.reduce_sum(sb['cols'][:, 33:34], fl[:, :], AXX)
            v.wait_ge(ssem.h, ck[('xsq', t)])
            c = sb['cols']
            v.tensor_scalar(c[:, 34:35], c[:, 33:34], 1.0 / HID, None, ALU.mult)
            v.tensor_scalar(c[:, 35:36], c[:, 32:33], 1.0 / HID, None, ALU.mult)
            v.tensor_tensor(c[:, 36:37], c[:, 34:35], c[:, 34:35], ALU.mult)
            v.tensor_scalar(c[:, 36:37], c[:, 36:37], -1.0, EPS, ALU.mult, ALU.add)
            v.tensor_tensor(c[:, 36:37], c[:, 35:36], c[:, 36:37], ALU.add)
            v.reciprocal(c[:, 37:38], c[:, 36:37])
            v.tensor_scalar(c[:, 8 + t:9 + t], c[:, 34:35], -1.0, None, ALU.mult)
            if t >= 2:
                wait_tok(v, ck[('xob', t - 2)])
            v.tensor_copy(sb['cba' if t % 2 == 0 else 'cbb'][:, :], fl[:, :])
            ck[('xcast', t)] = vsem.val
        pr.on('vector', stats)

        def rstd(s, t=t):
            s.wait_ge(vsem.h, ck[('xcast', t)])
            s.activation(sb['cols'][:, 0 + t:1 + t], sb['cols'][:, 37:38], AF.Sqrt)
            ck[('xr', t)] = ssem.val
        pr.on('scalar', rstd)

        def out(g, t=t):
            g.wait_ge(vsem.h, ck[('xcast', t)])
            g.dma_start(out=o_xbt[128 * t:128 * (t + 1), :],
                        in_=sb['cba' if t % 2 == 0 else 'cbb'][:, :])
            ck[('xob', t)] = pr.sw.token()
        pr.on('gpsimd', out)

    def lnr_t(tt):
        tt.wait_ge(ssem.h, ck[('xr', 7)])
        tt.wait_ge(gsem.h, gsem.val)
        ins = tt.transpose(sb['psTF'][0:16, 0:128], sb['cols'][:, 0:16], sb['idf'][:, :])
        tsem.inc(ins)
        ck['lnr'] = tsem.val
    pr.on('tensor', lnr_t)

    def lnr_v(v):
        v.wait_ge(tsem.h, ck['lnr'])
        v.tensor_copy(sb['sq'][0:16, 0:128], sb['psTF'][0:16, 0:128])
        ck['lnrv'] = vsem.val
    pr.on('vector', lnr_v)

    def lnr_g(g):
        g.wait_ge(vsem.h, ck['lnrv'])
        for t in range(8):
            g.dma_start(out=o_lnrow[0:1, 128 * t:128 * (t + 1)], in_=sb['sq'][t:t + 1, 0:128])
            g.dma_start(out=o_lnrow[1:2, 128 * t:128 * (t + 1)], in_=sb['sq'][8 + t:9 + t, 0:128])
        g.dma_start(out=o_lncol[:, :], in_=sb['cols'][:, 0:16])
        ck['lnc'] = pr.sw.token()
    pr.on('gpsimd', lnr_g)


def emit_prep_rest(nc, pr, sb, ck, *, wi, wo, gq, gsc, gz, uq, usc, uz, dq, dsc, dz, w1, w2s, masks, wbs, wr, lA, lB, o_wib, o_wsum, o_wob, o_wgu, o_wgs, o_wus, o_wdp, o_attnT, o_asum, o_wbb, o_wbsum, o_wrb, o_rsum, o_lab, o_acs, o_bcb, w2c, mkc):
    tsem, vsem, ssem, gsem = pr.tsem, pr.vsem, pr.ssem, pr.gsem

    def cast_pass(src, dst, R, C, sum_row_dst, tag):
        nt = cdiv(R, 128)
        for i in range(nt):
            r0 = 128 * i
            rr = min(R, r0 + 128) - r0

            def load(q, i=i, r0=r0, rr=rr):
                if i >= 2:
                    q.wait_ge(ssem.h, ck[(tag, 'c', i - 2)])
                if i == 0:
                    q.wait_ge(vsem.h, vsem.val)
                    q.wait_ge(ssem.h, ssem.val)
                    q.wait_ge(tsem.h, tsem.val)
                    wait_tok(q, pr.dtok())
                q.dma_start(out=sb['lda' if i % 2 == 0 else 'ldb'][0:rr, 0:C],
                            in_=src[r0:r0 + rr, :])
                ck[(tag, 'l', i)] = pr.hw.token()
            pr.on('sync', load)

            def cast(s, i=i, rr=rr):
                wait_tok(s, ck[(tag, 'l', i)])
                if i >= 2:
                    wait_tok(s, ck[(tag, 'd', i - 2)])
                    if sum_row_dst is not None:
                        s.wait_ge(tsem.h, ck[(tag, 't', i - 2)])
                acc = (sb['cols'][0:rr, 34 + (i % 2):35 + (i % 2)]
                       if sum_row_dst is not None else None)
                s.activation(sb['cba' if i % 2 == 0 else 'cbb'][0:rr, 0:C],
                             sb['lda' if i % 2 == 0 else 'ldb'][0:rr, 0:C],
                             AF.Identity, accum_out=acc)
                ck[(tag, 'c', i)] = ssem.val
            pr.on('scalar', cast)

            if sum_row_dst is not None:
                def trow(tt, i=i, rr=rr, vg=None):
                    tt.wait_ge(ssem.h, ck[(tag, 'c', i)])
                    tt.wait_ge(vsem.h, ck[(tag, 'r', i - 1)] if i >= 1 else vg)
                    ins = tt.transpose(sb['psTF'][0:1, 0:rr],
                                       sb['cols'][0:rr, 34 + (i % 2):35 + (i % 2)],
                                       sb['idf'][0:rr, 0:rr])
                    tsem.inc(ins)
                    ck[(tag, 't', i)] = tsem.val
                pr.on('tensor', lambda tt, f=trow, vg=vsem.val: f(tt, vg=vg))

                def vrow(v, i=i, r0=r0, rr=rr):
                    v.wait_ge(tsem.h, ck[(tag, 't', i)])
                    v.tensor_copy(sb['rowa'][0:1, r0:r0 + rr], sb['psTF'][0:1, 0:rr])
                    ck[(tag, 'r', i)] = vsem.val
                pr.on('vector', vrow)

            def outd(g, i=i, r0=r0, rr=rr):
                g.wait_ge(ssem.h, ck[(tag, 'c', i)])
                g.dma_start(out=dst[r0:r0 + rr, :],
                            in_=sb['cba' if i % 2 == 0 else 'cbb'][0:rr, 0:C])
                ck[(tag, 'd', i)] = pr.sw.token()
            pr.on('gpsimd', outd)

        if sum_row_dst is not None:
            def sout(g, nt=nt):
                g.wait_ge(vsem.h, ck[(tag, 'r', nt - 1)])
                g.dma_start(out=sum_row_dst, in_=sb['rowa'][0:1, 0:R])
                ck[(tag, 'sum')] = pr.sw.token()
            pr.on('gpsimd', sout)

    def cast_pass_colsum(src, dst, R, C, sum_dst, tag):
        nt = cdiv(R, 128)
        for i in range(nt):
            r0 = 128 * i

            def load(q, i=i, r0=r0):
                if i >= 2:
                    q.wait_ge(ssem.h, ck[(tag, 'c', i - 2)])
                if i == 0:
                    q.wait_ge(vsem.h, vsem.val)
                    q.wait_ge(ssem.h, ssem.val)
                    q.wait_ge(tsem.h, tsem.val)
                    wait_tok(q, pr.dtok())
                q.dma_start(out=sb['lda' if i % 2 == 0 else 'ldb'][:, 0:C],
                            in_=src[r0:r0 + 128, :])
                ck[(tag, 'l', i)] = pr.hw.token()
            pr.on('sync', load)

            def cast(s, i=i):
                wait_tok(s, ck[(tag, 'l', i)])
                if i >= 2:
                    wait_tok(s, ck[(tag, 'd', i - 2)])
                    s.wait_ge(tsem.h, ck[(tag, 't', i - 2)])
                s.activation(sb['cba' if i % 2 == 0 else 'cbb'][:, 0:C],
                             sb['lda' if i % 2 == 0 else 'ldb'][:, 0:C], AF.Identity)
                ck[(tag, 'c', i)] = ssem.val
            pr.on('scalar', cast)

            def csmm(tt, i=i):
                tt.wait_ge(ssem.h, ck[(tag, 'c', i)])
                ins = tt.matmul(sb['psR'][0:1, 0:C], sb['ones_bf'][:, 0:1],
                                sb['cba' if i % 2 == 0 else 'cbb'][:, 0:C],
                                start=(i == 0), stop=(i == nt - 1))
                tsem.inc(ins)
                ck[(tag, 't', i)] = tsem.val
            pr.on('tensor', csmm)

            def outd(g, i=i, r0=r0):
                g.wait_ge(ssem.h, ck[(tag, 'c', i)])
                g.dma_start(out=dst[r0:r0 + 128, :],
                            in_=sb['cba' if i % 2 == 0 else 'cbb'][:, 0:C])
                ck[(tag, 'd', i)] = pr.sw.token()
            pr.on('gpsimd', outd)

        def sev(v, nt=nt):
            v.wait_ge(tsem.h, ck[(tag, 't', nt - 1)])
            v.tensor_copy(sb['rowa'][0:1, 0:C], sb['psR'][0:1, 0:C])
            ck[(tag, 'sv')] = vsem.val
        pr.on('vector', sev)

        def sout(g):
            g.wait_ge(vsem.h, ck[(tag, 'sv')])
            g.dma_start(out=sum_dst, in_=sb['rowa'][0:1, 0:C])
            ck[(tag, 'sum')] = pr.sw.token()
        pr.on('gpsimd', sout)

    cast_pass(w2s, w2c, 512, HID, None, 'w2')
    cast_pass(masks, mkc, HID, 512, None, 'mk')
    cast_pass(wi, o_wib, RPC, HID, o_wsum, 'wi')
    cast_pass(wo, o_wob, IPC, HID, None, 'wo')
    cast_pass_colsum(wbs, o_wbb, HID, 512, o_wbsum, 'wb')
    cast_pass(wr, o_wrb, 8, HID, o_rsum, 'wr')
    cast_pass(lA, o_lab, 128, HID, o_acs, 'la')

    for e in range(8):
        for ot in range(32):
            i = e * 32 + ot

            def load(q, e=e, ot=ot, i=i):
                if i == 0:
                    q.wait_ge(ssem.h, ssem.val)
                    q.wait_ge(vsem.h, vsem.val)
                    q.wait_ge(tsem.h, tsem.val)
                    wait_tok(q, pr.dtok())
                if i >= 2:
                    q.wait_ge(tsem.h, ck[('bc', 't', i - 2)])
                q.dma_start(out=sb['lda' if i % 2 == 0 else 'ldb'][:, 0:16],
                            in_=lB[4096 * e + 128 * ot:4096 * e + 128 * (ot + 1), :])
                ck[('bc', 'l', i)] = pr.hw.token()
            pr.on('sync', load)

            def casts(s, i=i):
                wait_tok(s, ck[('bc', 'l', i)])
                if i >= 1:
                    s.wait_ge(tsem.h, ck[('bc', 't', i - 1)])
                s.activation(sb['cbb'][:, 0:16],
                             sb['lda' if i % 2 == 0 else 'ldb'][:, 0:16],
                             AF.Identity, scale=2.0)
                ck[('bc', 'c', i)] = ssem.val
            pr.on('scalar', casts)

            def tr(tt, i=i):
                tt.wait_ge(ssem.h, ck[('bc', 'c', i)])
                if i >= 1:
                    tt.wait_ge(vsem.h, ck[('bc', 'v', i - 1)])
                ins = tt.transpose(sb['psT'][0:16, 0:128], sb['cbb'][:, 0:16],
                                   sb['idbf'][:, :])
                tsem.inc(ins)
                ck[('bc', 't', i)] = tsem.val
            pr.on('tensor', tr)

            def ev(v, i=i):
                v.wait_ge(tsem.h, ck[('bc', 't', i)])
                if i >= 2:
                    wait_tok(v, ck[('bc', 'd', i - 2)])
                buf = sb['cba'] if i % 2 == 0 else sb['cbb']
                v.tensor_copy(buf[0:16, 2048:2176], sb['psT'][0:16, 0:128])
                ck[('bc', 'v', i)] = vsem.val
            pr.on('vector', ev)

            def outp(g, e=e, ot=ot, i=i):
                g.wait_ge(vsem.h, ck[('bc', 'v', i)])
                buf = sb['cba'] if i % 2 == 0 else sb['cbb']
                g.dma_start(out=o_bcb[16 * e:16 * (e + 1), 128 * ot:128 * (ot + 1)],
                            in_=buf[0:16, 2048:2176])
                ck[('bc', 'd', i)] = pr.sw.token()
            pr.on('gpsimd', outp)

    entry_tok = pr.sw.token()
    dq_units = []
    dq_units += emit_dequant_u(pr, sb, ck, 'dqg', gq, gsc, gz, 512, NB * 256,
                               lambda j, rb, n0: (o_wgu, HID * (n0 // 256) + 512 * j + 128 * rb, n0 % 256),
                               o_wgs, prevq=None, prevw=None)
    dq_units += emit_dequant_u(pr, sb, ck, 'dqu', uq, usc, uz, 512, NB * 256,
                               lambda j, rb, n0: (o_wgu, HID * (n0 // 256) + 512 * j + 128 * rb, 256 + n0 % 256),
                               o_wus, prevq=('dqg', 'q', 23), prevw=('dqg', 'wsd'))
    dq_units += emit_dequant_u(pr, sb, ck, 'dqd', dq, dsc, dz, FFN_I // 8, 512,
                               lambda j, rb, n0: (o_wdp, 11264 * (n0 // 256) + 1408 * j + 128 * rb, n0 % 256),
                               None, prevq=('dqu', 'q', 23), prevw=None)
    at_units = emit_attn_prep_u(pr, sb, ck, w1, w2c, mkc, o_attnT, o_asum, entry_tok)
    for i in range(max(len(dq_units), len(at_units))):
        if i < len(dq_units):
            dq_units[i]()
        if i < len(at_units):
            at_units[i]()


def emit_prep_body(nc, pr, sb, ck, **kw):
    emit_prep_p1(nc, pr, sb, ck, xs=kw.pop('xs'), o_xbt=kw.pop('o_xbt'),
                 o_lnrow=kw.pop('o_lnrow'), o_lncol=kw.pop('o_lncol'))
    emit_prep_rest(nc, pr, sb, ck, **kw)



def build_prep():
    nc = bass.Bass("TRN2", target_bir_lowering=False, debug=False)
    Iap = lambda name, shape, dt=F32: nc.dram_tensor(name, shape, dt, kind="ExternalInput").ap()
    Oap = lambda name, shape, dt=F32: nc.dram_tensor(name, shape, dt, kind="ExternalOutput").ap()

    xs = Iap("xs", [TPC, HID])
    wi = Iap("wi", [RPC, HID])
    wo = Iap("wo", [IPC, HID])
    gq = Iap("gq", [512, NB * 256], I32)
    gsc = Iap("gsc", [32, NB * 256])
    gz = Iap("gz", [32, NB * 32], I32)
    uq = Iap("uq", [512, NB * 256], I32)
    usc = Iap("usc", [32, NB * 256])
    uz = Iap("uz", [32, NB * 32], I32)
    dq = Iap("dq", [FFN_I // 8, 512], I32)
    dsc = Iap("dsc", [86, 512])
    dz = Iap("dz", [86, 64], I32)
    w1 = Iap("w1", [HID, HID])
    w2s = Iap("w2s", [512, HID])
    masks = Iap("masks", [HID, 512])
    wbs = Iap("wbs", [HID, 512])
    wr = Iap("wr", [8, HID])
    lA = Iap("lA", [128, HID])
    lB = Iap("lB", [8 * HID, 16])

    o_xbt = Oap("o_xbt", [TPC, HID], BF16)
    o_lnrow = Oap("o_lnrow", [2, TPC])
    o_lncol = Oap("o_lncol", [128, 16])
    o_wib = Oap("o_wib", [RPC, HID], BF16)
    o_wsum = Oap("o_wsum", [1, RPC])
    o_wob = Oap("o_wob", [IPC, HID], BF16)
    o_wgu = Oap("o_wgu", [NB * HID, 512], BF16)
    o_wgs = Oap("o_wgs", [1, NB * 256])
    o_wus = Oap("o_wus", [1, NB * 256])
    o_wdp = Oap("o_wdp", [2 * 11264, 256], BF16)
    o_attnT = Oap("o_attnT", [HID, 512], BF16)
    o_asum = Oap("o_asum", [1, 512])
    o_wbb = Oap("o_wbb", [HID, 512], BF16)
    o_wbsum = Oap("o_wbsum", [1, 512])
    o_wrb = Oap("o_wrb", [8, HID], BF16)
    o_rsum = Oap("o_rsum", [1, 8])
    o_lab = Oap("o_lab", [128, HID], BF16)
    o_acs = Oap("o_acs", [1, 128])
    o_bcb = Oap("o_bcb", [128, HID], BF16)
    w2c = nc.dram_tensor("w2c", [512, HID], BF16, kind="Internal").ap()
    mkc = nc.dram_tensor("mkc", [HID, 512], BF16, kind="Internal").ap()

    tensors = CONSTS + PREP_TENSORS

    with nc.Block() as block:
        hw_sems = [nc.semaphore(f"hw{i}").__enter__() for i in range(8)]
        sw_sems = [nc.semaphore(f"sw{i}").__enter__() for i in range(8)]
        th = nc.semaphore("tsem").__enter__()
        vh = nc.semaphore("vsem").__enter__()
        sh = nc.semaphore("ssem").__enter__()
        gh = nc.semaphore("gsem").__enter__()
        sb = alloc(nc, tensors)
        pr = Prog(nc, block, hw_sems, sw_sems, th, vh, sh, gh)
        tsem, vsem, ssem, gsem = pr.tsem, pr.vsem, pr.ssem, pr.gsem
        emit_consts(pr, sb)
        ck = {}
        emit_prep_body(nc, pr, sb, ck, xs=xs, wi=wi, wo=wo, gq=gq, gsc=gsc, gz=gz, uq=uq, usc=usc, uz=uz, dq=dq, dsc=dsc, dz=dz, w1=w1, w2s=w2s, masks=masks, wbs=wbs, wr=wr, lA=lA, lB=lB, o_xbt=o_xbt, o_lnrow=o_lnrow, o_lncol=o_lncol, o_wib=o_wib, o_wsum=o_wsum, o_wob=o_wob, o_wgu=o_wgu, o_wgs=o_wgs, o_wus=o_wus, o_wdp=o_wdp, o_attnT=o_attnT, o_asum=o_asum, o_wbb=o_wbb, o_wbsum=o_wbsum, o_wrb=o_wrb, o_rsum=o_rsum, o_lab=o_lab, o_acs=o_acs, o_bcb=o_bcb, w2c=w2c, mkc=mkc)
        def fin(g):
            wait_tok(g, pr.dtok())
            g.wait_ge(vsem.h, vsem.val)
            g.wait_ge(ssem.h, ssem.val)
            g.wait_ge(tsem.h, tsem.val)
        pr.on('gpsimd', fin)

    return nc


def emit_dequant(pr, sb, ck, tag, qw, sc, zq, QROWS, NC, dst_fn, o_ws):
    tsem, vsem, ssem, gsem = pr.tsem, pr.vsem, pr.ssem, pr.gsem
    NGRP = QROWS // 16
    nzc = cdiv(NC, 8)

    def ldsz(q):
        q.wait_ge(vsem.h, vsem.val)
        q.wait_ge(ssem.h, ssem.val)
        q.wait_ge(tsem.h, tsem.val)
        wait_tok(q, pr.dtok())
        q.dma_start(out=sb['scz'][0:NGRP, 0:NC], in_=sc)
        q.dma_start(out=sb['zqz'][0:NGRP, 0:nzc], in_=zq)
        ck[(tag, 'sz')] = pr.hw.token()
    pr.on('sync', ldsz)

    def zprep(v):
        wait_tok(v, ck[(tag, 'sz')])
        zview = sb['zzi'][0:NGRP, 0:NC].rearrange("g (c l) -> g c l", l=8)
        for l in range(8):
            v.tensor_scalar(zview[:, :, l], sb['zqz'][0:NGRP, 0:nzc], 4 * l, 15,
                            ALU.logical_shift_right, ALU.bitwise_and)
        v.tensor_copy(sb['zz'][0:NGRP, 0:NC], sb['zzi'][0:NGRP, 0:NC])
        v.tensor_scalar(sb['beta'][0:NGRP, 0:NC], sb['zz'][0:NGRP, 0:NC],
                        1.0, -1.0, ALU.add, ALU.mult)
        v.tensor_tensor(sb['beta'][0:NGRP, 0:NC], sb['beta'][0:NGRP, 0:NC],
                        sb['scz'][0:NGRP, 0:NC], ALU.mult)
        ck[(tag, 'beta')] = vsem.val
    pr.on('vector', zprep)

    if o_ws is not None:
        def wsz(g):
            g.wait_ge(vsem.h, ck[(tag, 'beta')])
            g.memset(sb['rowb'][0:1, 0:NC], 0.0)
            ck[(tag, 'ws0')] = gsem.val
        pr.on('gpsimd', wsz)

    nblk = cdiv(QROWS, 128)
    nsubs = [(256 * i, min(NC, 256 * (i + 1))) for i in range(cdiv(NC, 256))]
    step = 0
    for b in range(nblk):
        rb0 = 128 * b
        rbn = min(QROWS, rb0 + 128) - rb0
        for (n0, n1) in nsubs:
            nn = n1 - n0

            def ldq(q, b=b, rb0=rb0, rbn=rbn, n0=n0, nn=nn, step=step):
                if step > 0:
                    q.wait_ge(vsem.h, ck[(tag, 'u', step - 1, 7)])
                q.wait_ge(vsem.h, ck[(tag, 'beta')])
                q.dma_start(out=sb['qwa'][0:rbn, 0:nn], in_=qw[rb0:rb0 + rbn, n0:n1])
                g0 = rb0 // 16
                ng = cdiv(rbn, 16)
                sap = sb['scz'][g0:g0 + ng, n0:n1].unsqueeze(1).to_broadcast([ng, 16, nn])
                bap = sb['beta'][g0:g0 + ng, n0:n1].unsqueeze(1).to_broadcast([ng, 16, nn])
                q.dma_start(out=sb['srep'][0:rbn, 0:nn], in_=sap)
                q.dma_start(out=sb['brep'][0:rbn, 0:nn], in_=bap)
                ck[(tag, 'q', step)] = pr.hw.token()
            pr.on('sync', ldq)

            for j in range(8):
                def unpack(v, j=j, rbn=rbn, nn=nn, step=step):
                    wait_tok(v, ck[(tag, 'q', step)])
                    if j >= 1:
                        wait_tok(v, ck[(tag, 'd', step, j - 1)])
                    elif step > 0:
                        wait_tok(v, ck[(tag, 'd', step - 1, 7)])
                    v.tensor_scalar(sb['nib'][0:rbn, 0:nn], sb['qwa'][0:rbn, 0:nn],
                                    4 * j, 15, ALU.logical_shift_right, ALU.bitwise_and)
                    v.tensor_tensor(sb['dqt'][0:rbn, 0:nn], sb['nib'][0:rbn, 0:nn],
                                    sb['srep'][0:rbn, 0:nn], ALU.mult)
                    v.tensor_tensor(sb['dqw'][0:rbn, 0:nn], sb['dqt'][0:rbn, 0:nn],
                                    sb['brep'][0:rbn, 0:nn], ALU.add)
                    ck[(tag, 'u', step, j)] = vsem.val
                pr.on('vector', unpack)

                if o_ws is not None:
                    def wsmm(tt, j=j, rbn=rbn, nn=nn, step=step):
                        tt.wait_ge(vsem.h, ck[(tag, 'u', step, j)])
                        if (tag, 'm', step, j - 1) in ck:
                            tt.wait_ge(vsem.h, ck[(tag, 'm', step, j - 1)])
                        elif step > 0:
                            tt.wait_ge(vsem.h, ck[(tag, 'm', step - 1, 7)])
                        else:
                            tt.wait_ge(gsem.h, ck[(tag, 'ws0')])
                        ins = tt.matmul(sb['psR'][0:1, 0:nn], sb['ones_bf'][0:rbn, 0:1],
                                        sb['dqw'][0:rbn, 0:nn], start=True, stop=True)
                        tsem.inc(ins)
                        ck[(tag, 'mm', step, j)] = tsem.val
                    pr.on('tensor', wsmm)

                    def wsacc(v, n0=n0, nn=nn, step=step, j=j):
                        v.wait_ge(tsem.h, ck[(tag, 'mm', step, j)])
                        v.tensor_tensor(sb['rowb'][0:1, n0:n0 + nn],
                                        sb['rowb'][0:1, n0:n0 + nn],
                                        sb['psR'][0:1, 0:nn], ALU.add)
                        ck[(tag, 'm', step, j)] = vsem.val
                    pr.on('vector', wsacc)

                def wout(g, b=b, j=j, rbn=rbn, n0=n0, nn=nn, step=step):
                    g.wait_ge(vsem.h, ck[(tag, 'u', step, j)])
                    dst, row0, col0 = dst_fn(j, b, n0)
                    g.dma_start(out=dst[row0:row0 + rbn, col0:col0 + nn],
                                in_=sb['dqw'][0:rbn, 0:nn])
                    ck[(tag, 'd', step, j)] = pr.sw.token()
                pr.on('gpsimd', wout)
            step += 1

    if o_ws is not None:
        laststep = step - 1

        def wsout(g):
            g.wait_ge(vsem.h, ck[(tag, 'm', laststep, 7)])
            g.dma_start(out=o_ws, in_=sb['rowb'][0:1, 0:NC])
            ck[(tag, 'wsd')] = pr.sw.token()
        pr.on('gpsimd', wsout)


def emit_attn_prep(pr, sb, ck, w1, w2c, mkc, o_attnT, o_asum):
    tsem, vsem, ssem, gsem = pr.tsem, pr.vsem, pr.ssem, pr.gsem

    def ldw2(q):
        q.wait_ge(vsem.h, vsem.val)
        q.wait_ge(ssem.h, ssem.val)
        q.wait_ge(tsem.h, tsem.val)
        wait_tok(q, pr.dtok())
        for d in range(32):
            q.dma_start_transpose(sb['w2T'][:, 512 * d:512 * (d + 1)],
                                  w2c[0:512, 128 * d:128 * (d + 1)])
        ck['at_w2'] = pr.hw.token()
    pr.on('sync', ldw2)

    def az(g):
        wait_tok(g, ck['at_w2'])
        g.memset(sb['cols'][:, 56:60], 0.0)
        ck['at_az'] = gsem.val
    pr.on('gpsimd', az)

    banks = ['psA', 'psB', 'psC', 'psD']
    for sn in range(8):
        for d in range(32):
            def ldp(g, sn=sn, d=d):
                i = sn * 32 + d
                if i >= 2:
                    g.wait_ge(tsem.h, ck[('at', 'mm', i - 2)])
                if i == 0:
                    g.wait_ge(gsem.h, ck['at_az'])
                buf = sb['cba' if d % 2 == 0 else 'cbb']
                g.dma_start(out=buf[:, 0:512],
                            in_=w1[128 * d:128 * (d + 1), 512 * sn:512 * (sn + 1)])
                ck[('at', 'p', i)] = pr.sw.token()
            pr.on('gpsimd', ldp)

            def mm(tt, sn=sn, d=d):
                i = sn * 32 + d
                wait_tok(tt, ck[('at', 'p', i)])
                if i == 0:
                    wait_tok(tt, ck['at_w2'])
                if d == 0 and sn > 0:
                    tt.wait_ge(vsem.h, ck[('at', 'ev', sn - 1, 3)])
                buf = sb['cba' if d % 2 == 0 else 'cbb']
                for m in range(4):
                    ins = tt.matmul(sb[banks[m]][:, 0:512],
                                    sb['w2T'][:, 512 * d + 128 * m:512 * d + 128 * (m + 1)],
                                    buf[:, 0:512], start=(d == 0), stop=(d == 31))
                tsem.inc(ins)
                ck[('at', 'mm', i)] = tsem.val
            pr.on('tensor', mm)

        for m in range(4):
            def ldmk(q, sn=sn, m=m):
                q.wait_ge(tsem.h, ck[('at', 'mm', sn * 32 + 31)])
                if (sn, m) != (0, 0):
                    pv = ('at', 'ev', sn, m - 1) if m > 0 else ('at', 'ev', sn - 1, 3)
                    q.wait_ge(vsem.h, ck[pv])
                q.dma_start_transpose(sb['mk'][:, 0:512],
                                      mkc[512 * sn:512 * (sn + 1), 128 * m:128 * (m + 1)])
                ck[('at', 'mk', sn, m)] = pr.hw.token()
            pr.on('sync', ldmk)

            def ev(v, sn=sn, m=m):
                wait_tok(v, ck[('at', 'mk', sn, m)])
                v.wait_ge(gsem.h, ck['at_az'])
                if (sn, m) != (0, 0):
                    pv = ('at', 'tr', sn, m - 1, 3) if m > 0 else ('at', 'tr', sn - 1, 3, 3)
                    tv = ck.get(pv)
                    if tv is not None:
                        v.wait_ge(tsem.h, tv)
                v.scalar_tensor_tensor(sb['att'][:, 0:512], sb[banks[m]][:, 0:512],
                                       1.0, sb['mk'][:, 0:512], ALU.mult, ALU.mult)
                v.reduce_sum(sb['cols'][:, 60:61], sb['att'][:, 0:512], AXX)
                v.tensor_tensor(sb['cols'][:, 56 + m:57 + m], sb['cols'][:, 56 + m:57 + m],
                                sb['cols'][:, 60:61], ALU.add)
                ck[('at', 'ev', sn, m)] = vsem.val
            pr.on('vector', ev)

            for j in range(4):
                def atr(tt, sn=sn, m=m, j=j):
                    tt.wait_ge(vsem.h, ck[('at', 'ev', sn, m)])
                    pv = ck.get(('at', 'te', sn, m, j - 2),
                                ck.get(('at', 'te', sn, m - 1, j + 2),
                                       ck.get(('at', 'te', sn - 1, 3, j + 2))))
                    if pv is not None:
                        tt.wait_ge(vsem.h, pv)
                    ins = tt.transpose(sb['psT' if j % 2 == 0 else 'psT2'][:, 0:128],
                                       sb['att'][:, 128 * j:128 * (j + 1)],
                                       sb['idbf'][:, :])
                    tsem.inc(ins)
                    ck[('at', 'tr', sn, m, j)] = tsem.val
                pr.on('tensor', atr)

                def ate(v, sn=sn, m=m, j=j):
                    v.wait_ge(tsem.h, ck[('at', 'tr', sn, m, j)])
                    pv = ck.get(('at', 'ad', sn, m, j - 2),
                                ck.get(('at', 'ad', sn, m - 1, j + 2),
                                       ck.get(('at', 'ad', sn - 1, 3, j + 2))))
                    if pv is not None:
                        wait_tok(v, pv)
                    v.tensor_copy(sb['att2'][:, 128 * (j % 2):128 * (j % 2 + 1)],
                                  sb['psT' if j % 2 == 0 else 'psT2'][:, 0:128])
                    ck[('at', 'te', sn, m, j)] = vsem.val
                pr.on('vector', ate)

                def outd(g, sn=sn, m=m, j=j):
                    g.wait_ge(vsem.h, ck[('at', 'te', sn, m, j)])
                    g.dma_start(out=o_attnT[512 * sn + 128 * j:512 * sn + 128 * (j + 1),
                                            128 * m:128 * (m + 1)],
                                in_=sb['att2'][:, 128 * (j % 2):128 * (j % 2 + 1)])
                    ck[('at', 'ad', sn, m, j)] = pr.sw.token()
                pr.on('gpsimd', outd)

    def asum_t(tt):
        tt.wait_ge(vsem.h, ck[('at', 'ev', 7, 3)])
        ins = tt.transpose(sb['psTF'][0:4, 0:128], sb['cols'][:, 56:60], sb['idf'][:, :])
        tsem.inc(ins)
        ck['at_str'] = tsem.val
    pr.on('tensor', asum_t)

    def asum_v(v):
        v.wait_ge(tsem.h, ck['at_str'])
        v.tensor_copy(sb['sq'][0:4, 0:128], sb['psTF'][0:4, 0:128])
        ck['at_srow'] = vsem.val
    pr.on('vector', asum_v)

    def asum_g(g):
        g.wait_ge(vsem.h, ck['at_srow'])
        for m in range(4):
            g.dma_start(out=o_asum[0:1, 128 * m:128 * (m + 1)], in_=sb['sq'][m:m + 1, 0:128])
        ck['at_sd'] = pr.sw.token()
    pr.on('gpsimd', asum_g)


def emit_dequant_u(pr, sb, ck, tag, qw, sc, zq, QROWS, NC, dst_fn, o_ws,
                   prevq=None, prevw=None):
    tsem, vsem, ssem, gsem = pr.tsem, pr.vsem, pr.ssem, pr.gsem
    NGRP = QROWS // 16
    nzc = cdiv(NC, 8)
    units = []

    def u_head():
        def ldsz(q):
            if prevq is not None:
                wait_tok(q, ck[prevq])
            q.dma_start(out=sb['scz'][0:NGRP, 0:NC], in_=sc)
            q.dma_start(out=sb['zqz'][0:NGRP, 0:nzc], in_=zq)
            ck[(tag, 'sz')] = pr.hw.token()
        pr.on('sync', ldsz)

        def zprep(v):
            wait_tok(v, ck[(tag, 'sz')])
            zview = sb['zzi'][0:NGRP, 0:NC].rearrange("g (c l) -> g c l", l=8)
            for l in range(8):
                v.tensor_scalar(zview[:, :, l], sb['zqz'][0:NGRP, 0:nzc], 4 * l, 15,
                                ALU.logical_shift_right, ALU.bitwise_and)
            v.tensor_copy(sb['zz'][0:NGRP, 0:NC], sb['zzi'][0:NGRP, 0:NC])
            v.tensor_scalar(sb['beta'][0:NGRP, 0:NC], sb['zz'][0:NGRP, 0:NC],
                            1.0, -1.0, ALU.add, ALU.mult)
            v.tensor_tensor(sb['beta'][0:NGRP, 0:NC], sb['beta'][0:NGRP, 0:NC],
                            sb['scz'][0:NGRP, 0:NC], ALU.mult)
            ck[(tag, 'beta')] = vsem.val
        pr.on('vector', zprep)

        if o_ws is not None:
            def wsz(g):
                if prevw is not None:
                    wait_tok(g, ck[prevw])
                g.memset(sb['rowb'][0:1, 0:NC], 0.0)
                ck[(tag, 'ws0')] = gsem.val
            pr.on('gpsimd', wsz)
    units.append(u_head)

    nblk = cdiv(QROWS, 128)
    nsubs = [(256 * i, min(NC, 256 * (i + 1))) for i in range(cdiv(NC, 256))]
    plan = [(b, n0, n1) for b in range(nblk) for (n0, n1) in nsubs]
    for step, (b, n0, n1) in enumerate(plan):
        rb0 = 128 * b
        rbn = min(QROWS, rb0 + 128) - rb0
        nn = n1 - n0

        def u_step(b=b, rb0=rb0, rbn=rbn, n0=n0, nn=nn, step=step):
            qb = 'qwa' if step % 2 == 0 else 'qwb'
            srb = 'srep' if step % 2 == 0 else 'srb'
            brb = 'brep' if step % 2 == 0 else 'brb'

            def ldq(q):
                if step >= 2:
                    q.wait_ge(vsem.h, ck[(tag, 'u', step - 2, 7)])
                q.wait_ge(vsem.h, ck[(tag, 'beta')])
                q.dma_start(out=sb[qb][0:rbn, 0:nn], in_=qw[rb0:rb0 + rbn, n0:n0 + nn])
                g0 = rb0 // 16
                ng = cdiv(rbn, 16)
                sap = sb['scz'][g0:g0 + ng, n0:n0 + nn].unsqueeze(1).to_broadcast([ng, 16, nn])
                bap = sb['beta'][g0:g0 + ng, n0:n0 + nn].unsqueeze(1).to_broadcast([ng, 16, nn])
                q.dma_start(out=sb[srb][0:rbn, 0:nn], in_=sap)
                q.dma_start(out=sb[brb][0:rbn, 0:nn], in_=bap)
                ck[(tag, 'q', step)] = pr.hw.token()
            pr.on('sync', ldq)

            for j in range(8):
                dwb = 'dqw' if j % 2 == 0 else 'dqw2'

                def unpack(v, j=j, dwb=dwb):
                    wait_tok(v, ck[(tag, 'q', step)])
                    pd = ((tag, 'd', step, j - 2) if j >= 2 else
                          ((tag, 'd', step - 1, 6 + j) if step > 0 else None))
                    if pd is not None:
                        wait_tok(v, ck[pd])
                    if o_ws is not None:
                        pm = ((tag, 'mm', step, j - 2) if j >= 2 else
                              ((tag, 'mm', step - 1, 6 + j) if step > 0 else None))
                        if pm is not None:
                            v.wait_ge(tsem.h, ck[pm])
                    v.tensor_scalar(sb['nib'][0:rbn, 0:nn], sb[qb][0:rbn, 0:nn],
                                    4 * j, 15, ALU.logical_shift_right, ALU.bitwise_and)
                    v.tensor_tensor(sb['dqt'][0:rbn, 0:nn], sb['nib'][0:rbn, 0:nn],
                                    sb[srb][0:rbn, 0:nn], ALU.mult)
                    v.tensor_tensor(sb[dwb][0:rbn, 0:nn], sb['dqt'][0:rbn, 0:nn],
                                    sb[brb][0:rbn, 0:nn], ALU.add)
                    ck[(tag, 'u', step, j)] = vsem.val
                pr.on('vector', unpack)

                if o_ws is not None:
                    def wsmm(tt, j=j, dwb=dwb):
                        tt.wait_ge(vsem.h, ck[(tag, 'u', step, j)])
                        if (tag, 'm', step, j - 1) in ck:
                            tt.wait_ge(vsem.h, ck[(tag, 'm', step, j - 1)])
                        elif step > 0:
                            tt.wait_ge(vsem.h, ck[(tag, 'm', step - 1, 7)])
                        else:
                            tt.wait_ge(gsem.h, ck[(tag, 'ws0')])
                        ins = tt.matmul(sb['psR'][0:1, 0:nn], sb['ones_bf'][0:rbn, 0:1],
                                        sb[dwb][0:rbn, 0:nn], start=True, stop=True)
                        tsem.inc(ins)
                        ck[(tag, 'mm', step, j)] = tsem.val
                    pr.on('tensor', wsmm)

                    def wsacc(v, j=j):
                        v.wait_ge(tsem.h, ck[(tag, 'mm', step, j)])
                        v.tensor_tensor(sb['rowb'][0:1, n0:n0 + nn],
                                        sb['rowb'][0:1, n0:n0 + nn],
                                        sb['psR'][0:1, 0:nn], ALU.add)
                        ck[(tag, 'm', step, j)] = vsem.val
                    pr.on('vector', wsacc)

                def wout(g, b=b, j=j, dwb=dwb):
                    g.wait_ge(vsem.h, ck[(tag, 'u', step, j)])
                    dst, row0, col0 = dst_fn(j, b, n0)
                    g.dma_start(out=dst[row0:row0 + rbn, col0:col0 + nn],
                                in_=sb[dwb][0:rbn, 0:nn])
                    ck[(tag, 'd', step, j)] = pr.sw.token()
                pr.on('gpsimd', wout)
        units.append(u_step)

    if o_ws is not None:
        laststep = len(plan) - 1

        def u_tail():
            def wsout(g):
                g.wait_ge(vsem.h, ck[(tag, 'm', laststep, 7)])
                g.dma_start(out=o_ws, in_=sb['rowb'][0:1, 0:NC])
                ck[(tag, 'wsd')] = pr.sw.token()
            pr.on('gpsimd', wsout)
        units.append(u_tail)
    return units


def emit_attn_prep_u(pr, sb, ck, w1, w2c, mkc, o_attnT, o_asum, entry_tok):
    tsem, vsem, ssem, gsem = pr.tsem, pr.vsem, pr.ssem, pr.gsem
    units = []

    def u_head():
        def ldw2(q):
            wait_tok(q, entry_tok)
            for d in range(32):
                q.dma_start_transpose(sb['w2T'][:, 512 * d:512 * (d + 1)],
                                      w2c[0:512, 128 * d:128 * (d + 1)])
            ck['at_w2'] = pr.hw.token()
        pr.on('sync', ldw2)

        def az(g):
            wait_tok(g, ck['at_w2'])
            g.memset(sb['cols'][:, 56:60], 0.0)
            ck['at_az'] = gsem.val
        pr.on('gpsimd', az)
    units.append(u_head)

    banks = ['psA', 'psB', 'psC', 'psD']
    for sn in range(8):
        for dg in range(4):
            def u_mm(sn=sn, dg=dg):
                for d in range(8 * dg, 8 * dg + 8):
                    def ldp(g, sn=sn, d=d):
                        i = sn * 32 + d
                        if i >= 2:
                            g.wait_ge(tsem.h, ck[('at', 'mm', i - 2)])
                        if i == 0:
                            g.wait_ge(gsem.h, ck['at_az'])
                        buf = sb['cba' if d % 2 == 0 else 'cbb']
                        g.dma_start(out=buf[:, 0:512],
                                    in_=w1[128 * d:128 * (d + 1), 512 * sn:512 * (sn + 1)])
                        ck[('at', 'p', i)] = pr.sw.token()
                    pr.on('gpsimd', ldp)

                    def mm(tt, sn=sn, d=d):
                        i = sn * 32 + d
                        wait_tok(tt, ck[('at', 'p', i)])
                        if i == 0:
                            wait_tok(tt, ck['at_w2'])
                        if d == 0 and sn > 0:
                            tt.wait_ge(vsem.h, ck[('at', 'ev', sn - 1, 3)])
                        buf = sb['cba' if d % 2 == 0 else 'cbb']
                        for m in range(4):
                            ins = tt.matmul(sb[banks[m]][:, 0:512],
                                            sb['w2T'][:, 512 * d + 128 * m:512 * d + 128 * (m + 1)],
                                            buf[:, 0:512], start=(d == 0), stop=(d == 31))
                        tsem.inc(ins)
                        ck[('at', 'mm', i)] = tsem.val
                    pr.on('tensor', mm)
            units.append(u_mm)

        for m in range(4):
            def u_ev(sn=sn, m=m):
                def ldmk(q, sn=sn, m=m):
                    q.wait_ge(tsem.h, ck[('at', 'mm', sn * 32 + 31)])
                    if (sn, m) != (0, 0):
                        pv = ('at', 'ev', sn, m - 1) if m > 0 else ('at', 'ev', sn - 1, 3)
                        q.wait_ge(vsem.h, ck[pv])
                    q.dma_start_transpose(sb['mk'][:, 0:512],
                                          mkc[512 * sn:512 * (sn + 1), 128 * m:128 * (m + 1)])
                    ck[('at', 'mk', sn, m)] = pr.hw.token()
                pr.on('sync', ldmk)

                def ev(v, sn=sn, m=m):
                    wait_tok(v, ck[('at', 'mk', sn, m)])
                    v.wait_ge(gsem.h, ck['at_az'])
                    if (sn, m) != (0, 0):
                        pv = ('at', 'tr', sn, m - 1, 3) if m > 0 else ('at', 'tr', sn - 1, 3, 3)
                        tv = ck.get(pv)
                        if tv is not None:
                            v.wait_ge(tsem.h, tv)
                    v.scalar_tensor_tensor(sb['att'][:, 0:512], sb[banks[m]][:, 0:512],
                                           1.0, sb['mk'][:, 0:512], ALU.mult, ALU.mult)
                    v.reduce_sum(sb['cols'][:, 60:61], sb['att'][:, 0:512], AXX)
                    v.tensor_tensor(sb['cols'][:, 56 + m:57 + m], sb['cols'][:, 56 + m:57 + m],
                                    sb['cols'][:, 60:61], ALU.add)
                    ck[('at', 'ev', sn, m)] = vsem.val
                pr.on('vector', ev)

                for j in range(4):
                    def atr(tt, sn=sn, m=m, j=j):
                        tt.wait_ge(vsem.h, ck[('at', 'ev', sn, m)])
                        pv = ck.get(('at', 'te', sn, m, j - 2),
                                    ck.get(('at', 'te', sn, m - 1, j + 2),
                                           ck.get(('at', 'te', sn - 1, 3, j + 2))))
                        if pv is not None:
                            tt.wait_ge(vsem.h, pv)
                        ins = tt.transpose(sb['psT' if j % 2 == 0 else 'psT2'][:, 0:128],
                                           sb['att'][:, 128 * j:128 * (j + 1)],
                                           sb['idbf'][:, :])
                        tsem.inc(ins)
                        ck[('at', 'tr', sn, m, j)] = tsem.val
                    pr.on('tensor', atr)

                    def ate(v, sn=sn, m=m, j=j):
                        v.wait_ge(tsem.h, ck[('at', 'tr', sn, m, j)])
                        pv = ck.get(('at', 'ad', sn, m, j - 2),
                                    ck.get(('at', 'ad', sn, m - 1, j + 2),
                                           ck.get(('at', 'ad', sn - 1, 3, j + 2))))
                        if pv is not None:
                            wait_tok(v, pv)
                        v.tensor_copy(sb['att2'][:, 128 * (j % 2):128 * (j % 2 + 1)],
                                      sb['psT' if j % 2 == 0 else 'psT2'][:, 0:128])
                        ck[('at', 'te', sn, m, j)] = vsem.val
                    pr.on('vector', ate)

                    def outd(g, sn=sn, m=m, j=j):
                        g.wait_ge(vsem.h, ck[('at', 'te', sn, m, j)])
                        g.dma_start(out=o_attnT[512 * sn + 128 * j:512 * sn + 128 * (j + 1),
                                                128 * m:128 * (m + 1)],
                                    in_=sb['att2'][:, 128 * (j % 2):128 * (j % 2 + 1)])
                        ck[('at', 'ad', sn, m, j)] = pr.sw.token()
                    pr.on('gpsimd', outd)
            units.append(u_ev)

    def u_tail():
        def asum_t(tt):
            tt.wait_ge(vsem.h, ck[('at', 'ev', 7, 3)])
            ins = tt.transpose(sb['psTF'][0:4, 0:128], sb['cols'][:, 56:60], sb['idf'][:, :])
            tsem.inc(ins)
            ck['at_str'] = tsem.val
        pr.on('tensor', asum_t)

        def asum_v(v):
            v.wait_ge(tsem.h, ck['at_str'])
            v.tensor_copy(sb['sq'][0:4, 0:128], sb['psTF'][0:4, 0:128])
            ck['at_srow'] = vsem.val
        pr.on('vector', asum_v)

        def asum_g(g):
            g.wait_ge(vsem.h, ck['at_srow'])
            for m in range(4):
                g.dma_start(out=o_asum[0:1, 128 * m:128 * (m + 1)], in_=sb['sq'][m:m + 1, 0:128])
            ck['at_sd'] = pr.sw.token()
        pr.on('gpsimd', asum_g)
    units.append(u_tail)
    return units



def prep_in_maps(inp):
    x2 = np.ascontiguousarray(inp['L0_attn_input_states'].reshape(TOK, HID))
    Wi = inp['L0_attn_in_proj_w']
    Wo = inp['L0_attn_out_w']
    maps = []
    for c in range(NCORE):
        b0, nb = GU_B0[c], GU_BLOCKS[c]

        def pad_cols(a, colw):
            sl = a[:, b0 * colw:(b0 + nb) * colw]
            if nb < NB:
                sl = np.concatenate([sl, a[:, :colw * (NB - nb)]], axis=1)
            return np.ascontiguousarray(sl)
        m = dict(
            xs=x2[TPC * c:TPC * (c + 1)],
            wi=np.ascontiguousarray(np.concatenate([
                Wi[IPC * c:IPC * (c + 1)],
                Wi[INTER + IPC * c:INTER + IPC * (c + 1)],
                Wi[2 * INTER + 128 * c:2 * INTER + 128 * (c + 1)],
                Wi[2 * INTER + G * NS + 128 * c:2 * INTER + G * NS + 128 * (c + 1)],
                Wi[PROJ - H + HPC * c:PROJ - H + HPC * (c + 1)],
            ], axis=0)),
            wo=np.ascontiguousarray(Wo[:, IPC * c:IPC * (c + 1)].T),
            gq=pad_cols(inp['L0_ffn_gate_qweight'], 256),
            gsc=pad_cols(inp['L0_ffn_gate_scales'], 256),
            gz=pad_cols(inp['L0_ffn_gate_qzeros'], 32),
            uq=pad_cols(inp['L0_ffn_up_qweight'], 256),
            usc=pad_cols(inp['L0_ffn_up_scales'], 256),
            uz=pad_cols(inp['L0_ffn_up_qzeros'], 32),
            dq=np.ascontiguousarray(inp['L0_ffn_down_qweight'][:, 512 * c:512 * (c + 1)]),
            dsc=np.ascontiguousarray(inp['L0_ffn_down_scales'][:, 512 * c:512 * (c + 1)]),
            dz=np.ascontiguousarray(inp['L0_ffn_down_qzeros'][:, 64 * c:64 * (c + 1)]),
            w1=inp['L1_attn_w1'],
            w2s=np.ascontiguousarray(inp['L1_attn_w2'][512 * c:512 * (c + 1), :]),
            masks=np.ascontiguousarray(inp['L1_attn_sparse_mask'][:, 512 * c:512 * (c + 1)]),
            wbs=np.ascontiguousarray(inp['L1_ffn_W_base'][512 * c:512 * (c + 1), :].T),
            wr=np.ascontiguousarray(inp['L1_ffn_W_router'].reshape(8, HID)),
            lA=np.ascontiguousarray(inp['L1_ffn_lora_A'].reshape(128, HID)),
            lB=np.ascontiguousarray(inp['L1_ffn_lora_B'].reshape(8 * HID, 16)),
        )
        maps.append(m)
    return maps


def run_prep(inp):
    nc = build_prep()
    maps = prep_in_maps(inp)
    res = run_bass_kernel_spmd(nc, maps, core_ids=list(range(NCORE)))
    return res.results




ATTN0_TENSORS = [
    ('dts', [128, 1024], F32, 'sbuf'),
    ('hrep', [128, 192], F32, 'sbuf'),
    ('acol', [128, 64], F32, 'sbuf'),
    ('nwrep', [128, 1024], F32, 'sbuf'),
    ('nwrow', [1, 1024], F32, 'sbuf'),
    ('cwc', [128, 40], F32, 'sbuf'),
    ('cbc', [128, 10], F32, 'sbuf'),
    ('grow', [1, 16], F32, 'sbuf'),
]

ATTN0A_TENSORS = [
    ('xseg', [128, 32768], BF16, 'sbuf'),
    ('wgT', [128, 33280], BF16, 'sbuf'),
    ('wTa', [128, 512], BF16, 'sbuf'),
    ('wTb', [128, 512], BF16, 'sbuf'),
    ('arepa', [128, 512], F32, 'sbuf'),
    ('arepb', [128, 512], F32, 'sbuf'),
    ('evb', [128, 1024], BF16, 'sbuf'),
    ('evb2', [128, 1024], BF16, 'sbuf'),
    ('rows', [1, 1024], F32, 'sbuf'),
    ('rown', [1, 1024], F32, 'sbuf'),
    ('wsr', [1, RPC], F32, 'sbuf'),
    ('psA', [128, 512], F32, 'psum'),
    ('psB', [128, 512], F32, 'psum'),
    ('psC', [128, 512], F32, 'psum'),
    ('psD', [128, 512], F32, 'psum'),
    ('psE', [128, 512], F32, 'psum'),
    ('psF', [128, 512], F32, 'psum'),
    ('psG', [128, 512], F32, 'psum'),
    ('psT', [128, 512], BF16, 'psum'),
]

ATTN0A2_TENSORS = [
    ('xin', [128, 2640], BF16, 'sbuf'),
    ('cva', [128, 2560], F32, 'sbuf'),
    ('cvb', [128, 2560], F32, 'sbuf'),
    ('svo', [128, 2560], BF16, 'sbuf'),
]

ATTN0B_TENSORS = [
    ('bcb16', [128, 512], BF16, 'sbuf'),
    ('btm', [128, 256], BF16, 'sbuf'),
    ('xctm', [128, 2048], BF16, 'sbuf'),
    ('hcols', [128, 192], F32, 'sbuf'),
    ('hctp', [128, 32], F32, 'sbuf'),
    ('hctq', [128, 32], F32, 'sbuf'),
    ('csrow', [1, 32], F32, 'sbuf'),
    ('tp4', [128, 1024], F32, 'sbuf'),
    ('texp4', [128, 1024], F32, 'sbuf'),
    ('sct', [128, 256], BF16, 'sbuf'),
    ('xcw', [128, 128], BF16, 'sbuf'),
    ('y1t', [128, 128], F32, 'sbuf'),
    ('pfa', [128, 1024], F32, 'sbuf'),
    ('pfb', [128, 1024], F32, 'sbuf'),
    ('pbf', [128, 1024], BF16, 'sbuf'),
    ('ybf', [128, 2048], BF16, 'sbuf'),
    ('pCB', [128, 512], F32, 'psum'),
    ('pQ0', [128, 512], F32, 'psum'),
    ('pQ1', [128, 512], F32, 'psum'),
    ('pQ2', [128, 512], F32, 'psum'),
    ('pF', [128, 512], F32, 'psum'),
    ('pC', [128, 512], F32, 'psum'),
    ('pD', [128, 512], F32, 'psum'),
    ('pE', [128, 512], F32, 'psum'),
]

ATTN0C_TENSORS = [
    ('gld', [128, 4096], BF16, 'sbuf'),
    ('yld', [128, 4096], BF16, 'sbuf'),
    ('sg4', [128, 4096], F32, 'sbuf'),
    ('gg4', [128, 4096], F32, 'sbuf'),
    ('stat', [128, 16], F32, 'sbuf'),
    ('evc', [128, 2048], BF16, 'sbuf'),
    ('tev', [128, 512], BF16, 'sbuf'),
    ('psT0', [128, 512], BF16, 'psum'),
    ('psT1', [128, 512], BF16, 'psum'),
    ('psT2', [128, 512], BF16, 'psum'),
    ('psT3', [128, 512], BF16, 'psum'),
]

def emit_attn0_body(nc, pr, sb, ck, *, xbt, lnrow, lncola, wib, wsum, nwc, cw, cb, dtb, alog, dd, o_ghat, xbc_s, gate_s, cvs_s, ys_s, post_a=None):
    import contextlib
    tsem, vsem, ssem, gsem = pr.tsem, pr.vsem, pr.ssem, pr.gsem

    with contextlib.ExitStack() as _stA:
        sba = alloc(nc, ATTN0A_TENSORS + ATTN0A2_TENSORS, stack=_stA, prefix='a0a_')
        sba.update(sb)

        def ilv(seg):
            _emit_attn0_A2(nc, pr, sba, ck, xbc_s=xbc_s, cvs_s=cvs_s,
                           t0=8 * seg, t1=8 * seg + 8)

        _emit_attn0_A(nc, pr, sba, ck, xbt=xbt, lnrow=lnrow, lncola=lncola,
                      wib=wib, wsum=wsum, nwc=nwc, cw=cw, cb=cb, dtb=dtb,
                      alog=alog, dd=dd, xbc_s=xbc_s, gate_s=gate_s,
                      interleave=ilv)
        emit_barrier(pr)
    if post_a is not None:
        post_a()

    with contextlib.ExitStack() as _stB:
        sbb = alloc(nc, ATTN0B_TENSORS, stack=_stB, prefix='a0b_')
        sbb.update(sb)
        _emit_attn0_B(nc, pr, sbb, ck, cvs_s=cvs_s, ys_s=ys_s)
        emit_barrier(pr)

    with contextlib.ExitStack() as _stC:
        sbc = alloc(nc, ATTN0C_TENSORS, stack=_stC, prefix='a0g_')
        sbc.update(sb)
        _emit_attn0_C(nc, pr, sbc, ck, gate_s=gate_s, ys_s=ys_s, o_ghat=o_ghat)


def _emit_attn0_A(nc, pr, sb, ck, *, xbt, lnrow, lncola, wib, wsum, nwc, cw, cb, dtb, alog, dd, xbc_s, gate_s, interleave=None):
    tsem, vsem, ssem, gsem = pr.tsem, pr.vsem, pr.ssem, pr.gsem
    banks = ['psA', 'psB', 'psC', 'psD', 'psE', 'psF', 'psG', 'psT']

    def ld0(q):
        q.dma_start(out=sb['wsr'][0:1, :], in_=wsum)
        q.dma_start(out=sb['acol'][:, :], in_=lncola)
        q.dma_start(out=sb['cbc'][:, 0:10],
                    in_=bass.AP(cb.tensor, 0, [[1, 128], [128, 10]]),
                    allow_slow_non_contiguous=True)
        for ct in range(10):
            q.dma_start(out=sb['cwc'][:, 4 * ct:4 * (ct + 1)],
                        in_=cw[128 * ct:128 * (ct + 1), :])
        q.dma_start(out=sb['grow'][0:1, 0:16], in_=alog)
        q.dma_start(out=sb['hrep'][:, 16:32],
                    in_=bass.AP(dtb.tensor, 0, [[0, 128], [1, 16]]))
        q.dma_start(out=sb['hrep'][:, 32:48],
                    in_=bass.AP(dd.tensor, 0, [[0, 128], [1, 16]]))
        q.dma_start(out=sb['nwrow'][0:1, 0:1024], in_=nwc)
        ck['ld0'] = pr.hw.token()
        for k in range(32):
            q.dma_start_transpose(sb['wgT'][:, 1040 * k:1040 * k + 512],
                                  wib[0:512, 128 * k:128 * (k + 1)])
            q.dma_start_transpose(sb['wgT'][:, 1040 * k + 512:1040 * k + 1024],
                                  wib[512:1024, 128 * k:128 * (k + 1)])
            q.dma_start_transpose(sb['wgT'][:, 1040 * k + 1024:1040 * k + 1040],
                                  wib[2304:2320, 128 * k:128 * (k + 1)])
        ck['wgT'] = pr.hw.token()
    pr.on('sync', ld0)

    def c_t1(tt):
        wait_tok(tt, ck['ld0'])
        tt.wait_ge(gsem.h, gsem.val)
        ins = tt.matmul(sb['psF'][:, 0:16], sb['ones_f'][0:1, 0:128],
                        sb['grow'][0:1, 0:16], start=True, stop=True)
        tsem.inc(ins)
        ck['c_mm1'] = tsem.val
    pr.on('tensor', c_t1)

    def c_s1(s):
        s.wait_ge(tsem.h, ck['c_mm1'])
        s.activation(sb['hrep'][:, 0:16], sb['psF'][:, 0:16], AF.Exp)
        ck['c_exp'] = ssem.val
    pr.on('scalar', c_s1)

    def c_v1(v):
        v.wait_ge(ssem.h, ck['c_exp'])
        v.tensor_scalar(sb['hrep'][:, 0:16], sb['hrep'][:, 0:16], -1.0, None, ALU.mult)
        ck['c_arep'] = vsem.val
    pr.on('vector', c_v1)

    for half in range(2):
        def nwt(tt, half=half):
            if half == 0:
                tt.wait_ge(vsem.h, ck['c_arep'])
            else:
                tt.wait_ge(vsem.h, ck[('nwev', 0)])
            ins = tt.matmul(sb['psF'][:, 0:512], sb['ones_f'][0:1, 0:128],
                            sb['nwrow'][0:1, 512 * half:512 * (half + 1)],
                            start=True, stop=True)
            tsem.inc(ins)
            ck[('nwmm', half)] = tsem.val
        pr.on('tensor', nwt)

        def nwv(v, half=half):
            v.wait_ge(tsem.h, ck[('nwmm', half)])
            v.tensor_copy(sb['nwrep'][:, 512 * half:512 * (half + 1)], sb['psF'][:, 0:512])
            ck[('nwev', half)] = vsem.val
        pr.on('vector', nwv)

    for seg in range(8):
        t0 = 1024 * seg

        def xload(q, seg=seg, t0=t0):
            if seg == 0:
                q.wait_ge(vsem.h, ck[('nwev', 1)])
            else:
                q.wait_ge(tsem.h, ck[('a2mm', 8 * seg - 1)])
            q.dma_start(out=sb['rows'][0:1, :], in_=lnrow[0:1, t0:t0 + 1024])
            q.dma_start(out=sb['rown'][0:1, :], in_=lnrow[1:2, t0:t0 + 1024])
            for kf in range(32):
                for p2 in range(2):
                    q.dma_start_transpose(
                        sb['xseg'][:, 1024 * kf + 512 * p2:1024 * kf + 512 * (p2 + 1)],
                        xbt[t0 + 512 * p2:t0 + 512 * (p2 + 1), 128 * kf:128 * (kf + 1)])
            ck[('xs', seg)] = pr.hw.token()
        pr.on('sync', xload)

        for tq in range(2):
            def armm(tt, seg=seg, tq=tq):
                pv = ck.get(('ar_ev', seg - 1, tq) if tq == 0 else ('ar_ev', seg, tq - 1))
                if pv is None:
                    tt.wait_ge(vsem.h, ck[('nwev', 1)])
                else:
                    tt.wait_ge(vsem.h, pv)
                wait_tok(tt, ck[('xs', seg)])
                ins = tt.matmul(sb['psG'][:, 0:512], sb['ones_f'][0:1, 0:128],
                                sb['rows'][0:1, 512 * tq:512 * (tq + 1)],
                                start=True, stop=True)
                tsem.inc(ins)
                ck[('ar_mm', seg, tq)] = tsem.val
            pr.on('tensor', armm)

            def arev(v, seg=seg, tq=tq):
                v.wait_ge(tsem.h, ck[('ar_mm', seg, tq)])
                v.tensor_copy((sb['arepa'] if tq == 0 else sb['arepb'])[:, 0:512],
                              sb['psG'][:, 0:512])
                ck[('ar_ev', seg, tq)] = vsem.val
            pr.on('vector', arev)

        for gi, (mg0, mgn) in enumerate([(0, 3), (3, 3), (6, 2), (8, 2)]):
            for k in range(32):
                def wld(q, seg=seg, gi=gi, mg0=mg0, mgn=mgn, k=k):
                    wait_tok(q, ck[('xs', seg)])
                    pv = ck.get(('a1mme', seg, gi, k - 2))
                    if pv is not None:
                        q.wait_ge(tsem.h, pv)
                    elif k < 2 and (seg, gi) != (0, 0):
                        pseg, pgi = (seg, gi - 1) if gi > 0 else (seg - 1, 3)
                        q.wait_ge(tsem.h, ck[('a1mme', pseg, pgi, 30 + k)])
                    buf = sb['wTa' if k % 2 == 0 else 'wTb']
                    q.dma_start_transpose(
                        buf[:, 0:128 * mgn],
                        wib[1024 + 128 * mg0:1024 + 128 * (mg0 + mgn), 128 * k:128 * (k + 1)])
                    ck[('a1w', seg, gi, k)] = pr.hw.token()
                pr.on('sync', wld)

                def a1mm(tt, seg=seg, gi=gi, mg0=mg0, mgn=mgn, k=k):
                    wait_tok(tt, ck[('a1w', seg, gi, k)])
                    if k == 0:
                        pv = (ck.get(('a1evl', seg, gi - 1)) if gi > 0
                              else ck.get(('a1evl', seg - 1, 3)))
                        if pv is not None:
                            tt.wait_ge(vsem.h, pv)
                        if gi == 0 and seg > 0:
                            tt.wait_ge(vsem.h, ck[('a2evl', 8 * seg - 1)])
                    buf = sb['wTa' if k % 2 == 0 else 'wTb']
                    for mi in range(mgn):
                        for tq in range(2):
                            ins = tt.matmul(
                                sb[banks[mi * 2 + tq]][:, 0:512],
                                buf[:, 128 * mi:128 * (mi + 1)],
                                sb['xseg'][:, 1024 * k + 512 * tq:1024 * k + 512 * (tq + 1)],
                                start=(k == 0), stop=False)
                    tsem.inc(ins)
                    ck[('a1mme', seg, gi, k)] = tsem.val
                pr.on('tensor', a1mm)

            def a1bf(tt, seg=seg, gi=gi, mg0=mg0, mgn=mgn):
                for mi in range(mgn):
                    for tq in range(2):
                        ins = tt.matmul(
                            sb[banks[mi * 2 + tq]][:, 0:512],
                            sb['wsr'][0:1, 1024 + 128 * (mg0 + mi):1024 + 128 * (mg0 + mi + 1)],
                            sb['rown'][0:1, 512 * tq:512 * (tq + 1)],
                            start=False, stop=True)
                tsem.inc(ins)
                ck[('a1bf', seg, gi)] = tsem.val
            pr.on('tensor', a1bf)

            for mi in range(mgn):
                def a1ev(v, seg=seg, gi=gi, mg0=mg0, mgn=mgn, mi=mi):
                    v.wait_ge(tsem.h, ck[('a1bf', seg, gi)])
                    for tq in range(2):
                        v.wait_ge(vsem.h, ck[('ar_ev', seg, tq)])
                    pv = ck.get(('a1dma_last',))
                    if pv is not None:
                        wait_tok(v, pv)
                    for tq in range(2):
                        arep = sb['arepa'] if tq == 0 else sb['arepb']
                        v.tensor_tensor(sb['evb'][:, 512 * tq:512 * (tq + 1)],
                                        sb[banks[mi * 2 + tq]][:, 0:512],
                                        arep[:, 0:512], ALU.mult)
                    ck[('a1ev', seg, gi, mi)] = vsem.val
                    if mi == mgn - 1:
                        ck[('a1evl', seg, gi)] = vsem.val
                pr.on('vector', a1ev)

                def a1dma(g, seg=seg, gi=gi, mg0=mg0, mi=mi):
                    g.wait_ge(vsem.h, ck[('a1ev', seg, gi, mi)])
                    g.dma_start(out=xbc_s[128 * (mg0 + mi):128 * (mg0 + mi + 1),
                                          1024 * seg:1024 * (seg + 1)],
                                in_=sb['evb'][:, 0:1024])
                    ck[('a1dma_last',)] = pr.sw.token()
                pr.on('gpsimd', a1dma)

        ck[('xbcw', seg)] = pr.sw.token()

        for tt8 in range(8):
            tau = 8 * seg + tt8
            idx = tau

            def a2mm(tt, seg=seg, tt8=tt8, tau=tau, idx=idx):
                wait_tok(tt, ck[('xs', seg)])
                if (seg, tt8) == (0, 0):
                    wait_tok(tt, ck['wgT'])
                pv = ck.get(('a2evl', idx - 1))
                if pv is not None:
                    tt.wait_ge(vsem.h, pv)
                if tt8 == 0:
                    tt.wait_ge(vsem.h, ck[('a1evl', seg, 3)])
                for k in range(32):
                    xap = sb['xseg'][:, 1024 * k + 128 * tt8:1024 * k + 128 * (tt8 + 1)]
                    tt.matmul(sb['psA'][:, 0:512], xap, sb['wgT'][:, 1040 * k:1040 * k + 512],
                              start=(k == 0), stop=False)
                    tt.matmul(sb['psB'][:, 0:512], xap,
                              sb['wgT'][:, 1040 * k + 512:1040 * k + 1024],
                              start=(k == 0), stop=False)
                    ins = tt.matmul(sb['psC'][:, 0:16], xap,
                                    sb['wgT'][:, 1040 * k + 1024:1040 * k + 1040],
                                    start=(k == 0), stop=False)
                tt.matmul(sb['psA'][:, 0:512],
                          sb['rown'][0:1, 128 * tt8:128 * (tt8 + 1)],
                          sb['wsr'][0:1, 0:512], start=False, stop=True)
                tt.matmul(sb['psB'][:, 0:512],
                          sb['rown'][0:1, 128 * tt8:128 * (tt8 + 1)],
                          sb['wsr'][0:1, 512:1024], start=False, stop=True)
                ins = tt.matmul(sb['psC'][:, 0:16],
                                sb['rown'][0:1, 128 * tt8:128 * (tt8 + 1)],
                                sb['wsr'][0:1, 2304:2320], start=False, stop=True)
                tsem.inc(ins)
                ck[('a2mm', idx)] = tsem.val
            pr.on('tensor', a2mm)

            def a2ev(v, tau=tau, idx=idx):
                v.wait_ge(tsem.h, ck[('a2mm', idx)])
                pv = ck.get(('a2dma', idx - 1))
                if pv is not None:
                    wait_tok(v, pv)
                ac = sb['acol'][:, tau:tau + 1]
                v.tensor_scalar(sb['evb2'][:, 0:512], sb['psA'][:, 0:512], ac, None, ALU.mult)
                v.tensor_scalar(sb['evb2'][:, 512:1024], sb['psB'][:, 0:512], ac, None, ALU.mult)
                v.tensor_scalar(sb['dts'][:, 16 * (tau % 64):16 * (tau % 64) + 16],
                                sb['psC'][:, 0:16], ac, None, ALU.mult)
                ck[('a2evl', idx)] = vsem.val
            pr.on('vector', a2ev)

            def a2dma(g, tau=tau, idx=idx):
                g.wait_ge(vsem.h, ck[('a2evl', idx)])
                g.dma_start(out=gate_s[128 * tau:128 * (tau + 1), :], in_=sb['evb2'][:, :])
                ck[('a2dma', idx)] = pr.sw.token()
            pr.on('gpsimd', a2dma)

        if interleave is not None:
            interleave(seg)


def _emit_attn0_A2(nc, pr, sb, ck, *, xbc_s, cvs_s, t0=0, t1=64):
    tsem, vsem, ssem, gsem = pr.tsem, pr.vsem, pr.ssem, pr.gsem
    for tau in range(t0, t1):
        pi = tau % 2
        X0, C0 = 1320 * pi, 1280 * pi
        bstart = (tau % 32 == 0)

        def cld(q, tau=tau, pi=pi, X0=X0, bstart=bstart):
            wait_tok(q, ck[('xbcw', tau // 8)])
            pv = ck.get(('cv_tap', tau - 2))
            if pv is not None:
                q.wait_ge(vsem.h, pv)
                q.wait_ge(ssem.h, ck[('cv_t3', tau - 2)])
            for ct in range(10):
                if bstart:
                    q.dma_start(out=sb['xin'][:, X0 + 132 * ct + 3:X0 + 132 * ct + 131],
                                in_=xbc_s[128 * ct:128 * (ct + 1), 128 * tau:128 * (tau + 1)])
                else:
                    q.dma_start(out=sb['xin'][:, X0 + 132 * ct:X0 + 132 * ct + 131],
                                in_=xbc_s[128 * ct:128 * (ct + 1), 128 * tau - 3:128 * (tau + 1)])
            ck[('cv_ld', tau)] = pr.hw.token()
        pr.on('sync', cld)

        if bstart:
            def czero(g, tau=tau, X0=X0):
                pv = ck.get(('cv_tap', tau - 2))
                if pv is not None:
                    g.wait_ge(vsem.h, pv)
                    g.wait_ge(ssem.h, ck[('cv_t3', tau - 2)])
                for ct in range(10):
                    g.memset(sb['xin'][:, X0 + 132 * ct:X0 + 132 * ct + 3], 0.0)
                ck[('cv_z', tau)] = gsem.val
            pr.on('gpsimd', czero)

        def tap3(s, tau=tau, pi=pi, X0=X0, C0=C0, bstart=bstart):
            wait_tok(s, ck[('cv_ld', tau)])
            if bstart:
                s.wait_ge(gsem.h, ck[('cv_z', tau)])
            pv = ck.get(('cv_tap', tau - 2))
            if pv is not None:
                s.wait_ge(vsem.h, pv)
            for ct in range(10):
                s.activation(sb['cva'][:, C0 + 128 * ct:C0 + 128 * (ct + 1)],
                             sb['xin'][:, X0 + 132 * ct + 3:X0 + 132 * ct + 131],
                             AF.Identity, bias=sb['cbc'][:, ct:ct + 1],
                             scale=sb['cwc'][:, 4 * ct + 3:4 * ct + 4])
            ck[('cv_t3', tau)] = ssem.val
        pr.on('scalar', tap3)

        def taps(v, tau=tau, pi=pi, X0=X0, C0=C0):
            v.wait_ge(ssem.h, ck[('cv_t3', tau)])
            pv = ck.get(('cv_silu', tau - 2))
            if pv is not None:
                v.wait_ge(ssem.h, pv)
            for ct in range(10):
                a = sb['cva'][:, C0 + 128 * ct:C0 + 128 * (ct + 1)]
                b = sb['cvb'][:, C0 + 128 * ct:C0 + 128 * (ct + 1)]
                x = sb['xin'][:, X0 + 132 * ct:X0 + 132 * ct + 132]
                v.scalar_tensor_tensor(b, x[:, 2:130], sb['cwc'][:, 4 * ct + 2:4 * ct + 3],
                                       a, ALU.mult, ALU.add)
                v.scalar_tensor_tensor(a, x[:, 1:129], sb['cwc'][:, 4 * ct + 1:4 * ct + 2],
                                       b, ALU.mult, ALU.add)
                v.scalar_tensor_tensor(b, x[:, 0:128], sb['cwc'][:, 4 * ct:4 * ct + 1],
                                       a, ALU.mult, ALU.add)
            ck[('cv_tap', tau)] = vsem.val
        pr.on('vector', taps)

        def silu(s, tau=tau, pi=pi, C0=C0):
            s.wait_ge(vsem.h, ck[('cv_tap', tau)])
            pv = ck.get(('cv_dma', tau - 2))
            if pv is not None:
                wait_tok(s, pv)
            s.activation(sb['svo'][:, C0:C0 + 1280], sb['cvb'][:, C0:C0 + 1280], AF.Silu)
            ck[('cv_silu', tau)] = ssem.val
        pr.on('scalar', silu)

        def cstore(g, tau=tau, pi=pi, C0=C0):
            g.wait_ge(ssem.h, ck[('cv_silu', tau)])
            g.dma_start(
                out=cvs_s[:, 128 * tau:128 * (tau + 1)].rearrange("(k p) n -> p k n", p=128),
                in_=sb['svo'][:, C0:C0 + 1280].rearrange("p (k n) -> p k n", k=10))
            ck[('cv_dma', tau)] = pr.sw.token()
        pr.on('gpsimd', cstore)


def _emit_attn0_B(nc, pr, sb, ck, *, cvs_s, ys_s):
    tsem, vsem, ssem, gsem = pr.tsem, pr.vsem, pr.ssem, pr.gsem
    pQ = ['pQ0', 'pQ1', 'pQ2']

    def init(g):
        g.memset(sb['pfa'][:, :], 0.0)
        ck['b_pfz'] = gsem.val
    pr.on('gpsimd', init)

    for tau in range(64):
        pi = tau % 2
        H0, HC0 = 96 * pi, 16 * pi
        bstart = (tau % 32 == 0)
        pcur = 'pfa' if tau % 2 == 0 else 'pfb'
        pnew = 'pfb' if tau % 2 == 0 else 'pfa'

        def bld(q, tau=tau, pi=pi):
            pv = ck.get(('b_ymm', tau - 2, 15))
            if pv is not None:
                q.wait_ge(tsem.h, pv)
                q.wait_ge(vsem.h, ck[('b_evu', tau - 2, 15)])
            q.dma_start(out=sb['bcb16'][:, 256 * pi:256 * pi + 128],
                        in_=cvs_s[1024:1152, 128 * tau:128 * (tau + 1)])
            q.dma_start(out=sb['bcb16'][:, 256 * pi + 128:256 * pi + 256],
                        in_=cvs_s[1152:1280, 128 * tau:128 * (tau + 1)])
            q.dma_start_transpose(sb['btm'][:, 128 * pi:128 * (pi + 1)],
                                  cvs_s[1024:1152, 128 * tau:128 * (tau + 1)])
            for f in range(8):
                q.dma_start_transpose(
                    sb['xctm'][:, 1024 * pi + 128 * f:1024 * pi + 128 * (f + 1)],
                    cvs_s[128 * f:128 * (f + 1), 128 * tau:128 * (tau + 1)])
            ck[('b_ld', tau)] = pr.hw.token()
        pr.on('sync', bld)

        def dtm_v(v, tau=tau, pi=pi, HC0=HC0):
            pv = ck.get(('b_evu', tau - 2, 15))
            if pv is not None:
                v.wait_ge(vsem.h, pv)
            sl = sb['dts'][:, 16 * tau:16 * tau + 16]
            v.tensor_tensor(sb['hctp'][:, HC0:HC0 + 16], sl, sb['hrep'][:, 16:32], ALU.add)
            v.tensor_scalar(sb['hctq'][:, HC0:HC0 + 16], sb['hctp'][:, HC0:HC0 + 16],
                            -1.0, None, ALU.mult)
            v.tensor_tensor(sb['hctq'][:, HC0:HC0 + 16], sb['hctp'][:, HC0:HC0 + 16],
                            sb['hctq'][:, HC0:HC0 + 16], ALU.max)
            ck[('b_dt1', tau)] = vsem.val
        pr.on('vector', dtm_v)

        def dtm_s(s, tau=tau, pi=pi, H0=H0, HC0=HC0):
            s.wait_ge(vsem.h, ck[('b_dt1', tau)])
            s.activation(sb['hcols'][:, H0 + 80:H0 + 96], sb['hctq'][:, HC0:HC0 + 16],
                         AF.Exp, scale=-1.0)
            s.activation(sb['hcols'][:, H0 + 80:H0 + 96], sb['hcols'][:, H0 + 80:H0 + 96],
                         AF.Ln, bias=1.0)
            ck[('b_dt2', tau)] = ssem.val
        pr.on('scalar', dtm_s)

        def dtm_v2(v, tau=tau, pi=pi, H0=H0, HC0=HC0):
            v.wait_ge(ssem.h, ck[('b_dt2', tau)])
            v.tensor_scalar(sb['hctq'][:, HC0:HC0 + 16], sb['hctp'][:, HC0:HC0 + 16],
                            0.0, None, ALU.max)
            v.tensor_tensor(sb['hctp'][:, HC0:HC0 + 16], sb['hctq'][:, HC0:HC0 + 16],
                            sb['hcols'][:, H0 + 80:H0 + 96], ALU.add)
            v.tensor_scalar(sb['hctp'][:, HC0:HC0 + 16], sb['hctp'][:, HC0:HC0 + 16],
                            1e-30, None, ALU.max)
            v.tensor_tensor(sb['hctq'][:, HC0:HC0 + 16], sb['hctp'][:, HC0:HC0 + 16],
                            sb['hrep'][:, 0:16], ALU.mult)
            ck[('b_dt3', tau)] = vsem.val
        pr.on('vector', dtm_v2)

        def cum_t(tt, tau=tau, pi=pi, HC0=HC0):
            tt.wait_ge(vsem.h, ck[('b_dt3', tau)])
            pv = ck.get(('b_cdec', tau - 1))
            if pv is not None:
                tt.wait_ge(ssem.h, pv)
                tt.wait_ge(vsem.h, ck[('b_w1', tau - 1)])
            tt.matmul(sb['pF'][:, 0:16], sb['triu'][:, :], sb['hctq'][:, HC0:HC0 + 16],
                      start=True, stop=True)
            ins = tt.matmul(sb['pQ2'][0:1, 16:32], sb['ones_f'][:, 0:1],
                            sb['hctq'][:, HC0:HC0 + 16], start=True, stop=True)
            tsem.inc(ins)
            ck[('b_cumt', tau)] = tsem.val
        pr.on('tensor', cum_t)

        def cum_v(v, tau=tau, pi=pi, H0=H0):
            v.wait_ge(tsem.h, ck[('b_cumt', tau)])
            v.tensor_copy(sb['hcols'][:, H0:H0 + 16], sb['pF'][:, 0:16])
            v.tensor_copy(sb['csrow'][0:1, 16 * pi:16 * pi + 16], sb['pQ2'][0:1, 16:32])
            ck[('b_cum', tau)] = vsem.val
        pr.on('vector', cum_v)

        def cl_t(tt, tau=tau, pi=pi):
            tt.wait_ge(vsem.h, ck[('b_cum', tau)])
            ins = tt.matmul(sb['pF'][:, 32:48], sb['ones_f'][0:1, 0:128],
                            sb['csrow'][0:1, 16 * pi:16 * pi + 16], start=True, stop=True)
            tsem.inc(ins)
            ck[('b_clt', tau)] = tsem.val
        pr.on('tensor', cl_t)

        def cb_t(tt, tau=tau, pi=pi):
            wait_tok(tt, ck[('b_ld', tau)])
            pv = ck.get(('b_sct_done', tau - 1, 15))
            if pv is not None:
                tt.wait_ge(vsem.h, pv)
            ins = tt.matmul(sb['pCB'][:, 0:128], sb['bcb16'][:, 256 * pi:256 * pi + 128],
                            sb['bcb16'][:, 256 * pi + 128:256 * pi + 256],
                            start=True, stop=True)
            tsem.inc(ins)
            ck[('b_cb', tau)] = tsem.val
        pr.on('tensor', cb_t)

        def post_s(s, tau=tau, pi=pi, H0=H0, HC0=HC0):
            s.wait_ge(vsem.h, ck[('b_cum', tau)])
            s.activation(sb['hcols'][:, H0 + 16:H0 + 32], sb['hcols'][:, H0:H0 + 16], AF.Exp)
            s.activation(sb['hcols'][:, H0 + 80:H0 + 96], sb['hctp'][:, HC0:HC0 + 16], AF.Ln)
            s.wait_ge(tsem.h, ck[('b_clt', tau)])
            s.activation(sb['hcols'][:, H0 + 64:H0 + 80], sb['pF'][:, 32:48], AF.Exp)
            ck[('b_cdec', tau)] = ssem.val
        pr.on('scalar', post_s)

        def post_v(v, tau=tau, pi=pi, H0=H0, HC0=HC0):
            v.wait_ge(ssem.h, ck[('b_cdec', tau)])
            v.tensor_scalar(sb['hcols'][:, H0 + 80:H0 + 96], sb['hcols'][:, H0 + 80:H0 + 96],
                            -60.0, None, ALU.max)
            v.tensor_tensor(sb['hcols'][:, H0 + 32:H0 + 48], sb['hcols'][:, H0:H0 + 16],
                            sb['hcols'][:, H0 + 80:H0 + 96], ALU.subtract)
            v.scalar_tensor_tensor(sb['hctq'][:, HC0:HC0 + 16], sb['hcols'][:, H0:H0 + 16],
                                   -1.0, sb['pF'][:, 32:48], ALU.mult, ALU.add)
            ck[('b_w1', tau)] = vsem.val
        pr.on('vector', post_v)

        def post_s2(s, tau=tau, pi=pi, HC0=HC0):
            s.wait_ge(vsem.h, ck[('b_w1', tau)])
            s.activation(sb['hctq'][:, HC0:HC0 + 16], sb['hctq'][:, HC0:HC0 + 16], AF.Exp)
            ck[('b_w2', tau)] = ssem.val
        pr.on('scalar', post_s2)

        def post_v2(v, tau=tau, pi=pi, H0=H0, HC0=HC0, pcur=pcur, bstart=bstart):
            v.wait_ge(ssem.h, ck[('b_w2', tau)])
            v.tensor_tensor(sb['hcols'][:, H0 + 48:H0 + 64], sb['hctq'][:, HC0:HC0 + 16],
                            sb['hctp'][:, HC0:HC0 + 16], ALU.mult)
            pv = ck.get(('b_ymm', tau - 1, 15))
            if pv is not None:
                v.wait_ge(tsem.h, pv)
            if tau == 0:
                v.wait_ge(gsem.h, ck['b_pfz'])
            if bstart and tau > 0:
                v.memset(sb[pcur][:, :], 0.0)
            v.tensor_copy(sb['pbf'][:, :], sb[pcur][:, :])
            ck[('b_wdone', tau)] = vsem.val
        pr.on('vector', post_v2)

        def em_cumtp(g4, tau=tau, pi=pi, H0=H0):
            for i in range(4):
                h = 4 * g4 + i

                def fct(tt, h=h, g4=g4):
                    if h == 0:
                        tt.wait_ge(vsem.h, ck[('b_cum', tau)])
                    pv = ck.get(('b_tpu', tau, h - 3), ck.get(('b_tpu', tau - 1, h + 13)))
                    if pv is not None:
                        tt.wait_ge(vsem.h, pv)
                    ins = tt.matmul(sb[pQ[h % 3]][:, 0:128],
                                    sb['hcols'][:, H0 + h:H0 + h + 1].to_broadcast([128, 128]),
                                    sb['idf'][:, :], start=True, stop=True)
                    tsem.inc(ins)
                    ck[('b_cumi', tau, h)] = tsem.val
                pr.on('tensor', fct)

                def ftp(v, h=h, i=i, g4=g4):
                    if i == 0:
                        pv = ck.get(('b_texp', tau, g4 - 2), ck.get(('b_texp', tau - 1, g4 + 2)))
                        if pv is not None:
                            v.wait_ge(ssem.h, pv)
                    v.wait_ge(tsem.h, ck[('b_cumi', tau, h)])
                    v.scalar_tensor_tensor(sb['tp4'][:, 512 * (g4 % 2) + 128 * i:
                                                     512 * (g4 % 2) + 128 * (i + 1)],
                                           sb[pQ[h % 3]][:, 0:128],
                                           sb['hcols'][:, H0 + 32 + h:H0 + 33 + h],
                                           sb['mneg'][:, :], ALU.subtract, ALU.add)
                    ck[('b_tpu', tau, h)] = vsem.val
                pr.on('vector', ftp)

        def em_texp(g4, tau=tau):
            def f(s):
                s.wait_ge(vsem.h, ck[('b_tpu', tau, 4 * g4 + 3)])
                pv = ck.get(('b_sct_done', tau, 4 * g4 - 5),
                            ck.get(('b_sct_done', tau - 1, 4 * g4 + 11)))
                if pv is not None:
                    s.wait_ge(vsem.h, pv)
                s.activation(sb['texp4'][:, 512 * (g4 % 2):512 * (g4 % 2 + 1)],
                             sb['tp4'][:, 512 * (g4 % 2):512 * (g4 % 2 + 1)], AF.Exp)
                ck[('b_texp', tau, g4)] = ssem.val
            pr.on('scalar', f)

        def em_scty(g4, tau=tau, pi=pi, H0=H0, pcur=pcur, pnew=pnew):
            for i in range(4):
                h = 4 * g4 + i
                hb = h % 2

                def fsct(v, h=h, i=i, g4=g4):
                    if i == 0:
                        v.wait_ge(ssem.h, ck[('b_texp', tau, g4)])
                        v.wait_ge(tsem.h, ck[('b_cb', tau)])
                        if g4 == 0:
                            wait_tok(v, ck[('b_ld', tau)])
                    pv = ck.get(('b_ymm', tau, h - 2), ck.get(('b_ymm', tau - 1, h + 14)))
                    if pv is not None:
                        v.wait_ge(tsem.h, pv)
                    v.tensor_tensor(sb['sct'][:, 128 * (h % 2):128 * (h % 2 + 1)],
                                    sb['texp4'][:, 512 * (g4 % 2) + 128 * i:
                                                512 * (g4 % 2) + 128 * (i + 1)],
                                    sb['pCB'][:, 0:128], ALU.mult)
                    v.tensor_scalar(sb['xcw'][:, 64 * (h % 2):64 * (h % 2 + 1)],
                                    sb['xctm'][:, 1024 * pi + 64 * h:1024 * pi + 64 * (h + 1)],
                                    sb['hcols'][:, H0 + 48 + h:H0 + 49 + h], None, ALU.mult)
                    ck[('b_sct_done', tau, h)] = vsem.val
                pr.on('vector', fsct)

                def fy(tt, h=h, hb=hb):
                    tt.wait_ge(vsem.h, ck[('b_sct_done', tau, h)])
                    if h == 0:
                        tt.wait_ge(vsem.h, ck[('b_wdone', tau)])
                    if hb == 0:
                        pv = ck.get(('b_evu', tau, h - 1), ck.get(('b_evu', tau - 1, 15)))
                        if pv is not None:
                            tt.wait_ge(vsem.h, pv)
                    tt.matmul(sb['pC'][:, 64 * hb:64 * (hb + 1)],
                              sb['sct'][:, 128 * (h % 2):128 * (h % 2 + 1)],
                              sb['xctm'][:, 1024 * pi + 64 * h:1024 * pi + 64 * (h + 1)],
                              start=(hb == 0), stop=(hb == 1))
                    tt.matmul(sb['pD'][:, 64 * hb:64 * (hb + 1)],
                              sb['bcb16'][:, 256 * pi + 128:256 * pi + 256],
                              sb['pbf'][:, 64 * h:64 * (h + 1)],
                              start=(hb == 0), stop=(hb == 1))
                    ins = tt.matmul(sb['pE'][:, 64 * hb:64 * (hb + 1)],
                                    sb['btm'][:, 128 * pi:128 * (pi + 1)],
                                    sb['xcw'][:, 64 * (h % 2):64 * (h % 2 + 1)],
                                    start=(hb == 0), stop=(hb == 1))
                    tsem.inc(ins)
                    ck[('b_ymm', tau, h)] = tsem.val
                pr.on('tensor', fy)

                if hb == 1:
                    def fyev(v, h=h):
                        for h2 in (h - 1, h):
                            hb2 = h2 % 2
                            v.wait_ge(tsem.h, ck[('b_ymm', tau, h2 | 1)])
                            if h2 == 0:
                                pv = ck.get(('b_ydma', tau - 2))
                                if pv is not None:
                                    wait_tok(v, pv)
                            pp = (h2 // 2) % 2
                            v.scalar_tensor_tensor(sb['y1t'][:, 64 * pp:64 * (pp + 1)],
                                                   sb['xctm'][:, 1024 * pi + 64 * h2:
                                                               1024 * pi + 64 * (h2 + 1)],
                                                   sb['hrep'][:, 32 + h2:33 + h2],
                                                   sb['pC'][:, 64 * hb2:64 * (hb2 + 1)],
                                                   ALU.mult, ALU.add)
                            v.scalar_tensor_tensor(sb['ybf'][:, 1024 * pi + 64 * h2:
                                                             1024 * pi + 64 * (h2 + 1)],
                                                   sb['pD'][:, 64 * hb2:64 * (hb2 + 1)],
                                                   sb['hcols'][:, H0 + 16 + h2:H0 + 17 + h2],
                                                   sb['y1t'][:, 64 * pp:64 * (pp + 1)],
                                                   ALU.mult, ALU.add)
                            v.scalar_tensor_tensor(sb[pnew][:, 64 * h2:64 * (h2 + 1)],
                                                   sb[pcur][:, 64 * h2:64 * (h2 + 1)],
                                                   sb['hcols'][:, H0 + 64 + h2:H0 + 65 + h2],
                                                   sb['pE'][:, 64 * hb2:64 * (hb2 + 1)],
                                                   ALU.mult, ALU.add)
                            ck[('b_evu', tau, h2)] = vsem.val
                            ck[('b_stu', tau, h2)] = vsem.val
                    pr.on('vector', fyev)

        em_cumtp(0)
        em_texp(0)
        for g4 in range(1, 4):
            em_cumtp(g4)
            em_scty(g4 - 1)
            em_texp(g4)
        em_scty(3)

        def ydma(g, tau=tau, pi=pi):
            g.wait_ge(vsem.h, ck[('b_stu', tau, 15)])
            g.dma_start(out=ys_s[128 * tau:128 * (tau + 1), :],
                        in_=sb['ybf'][:, 1024 * pi:1024 * (pi + 1)])
            ck[('b_ydma', tau)] = pr.sw.token()
        pr.on('gpsimd', ydma)


def _emit_attn0_C(nc, pr, sb, ck, *, gate_s, ys_s, o_ghat):
    tsem, vsem, ssem, gsem = pr.tsem, pr.vsem, pr.ssem, pr.gsem
    for b in range(16):
        def gload(q, b=b):
            pv = ck.get(('g_gg', b - 1))
            if pv is not None:
                q.wait_ge(vsem.h, pv)
            for j in range(4):
                tau = 4 * b + j
                q.dma_start(out=sb['gld'][:, 1024 * j:1024 * (j + 1)],
                            in_=gate_s[128 * tau:128 * (tau + 1), :])
                q.dma_start(out=sb['yld'][:, 1024 * j:1024 * (j + 1)],
                            in_=ys_s[128 * tau:128 * (tau + 1), :])
            ck[('g_ld', b)] = pr.hw.token()
        pr.on('sync', gload)

        def gsilu(s, b=b):
            wait_tok(s, ck[('g_ld', b)])
            pv = ck.get(('g_sc', b - 1))
            if pv is not None:
                s.wait_ge(vsem.h, pv)
            for j in range(4):
                s.activation(sb['sg4'][:, 1024 * j:1024 * (j + 1)],
                             sb['gld'][:, 1024 * j:1024 * (j + 1)], AF.Silu)
            ck[('g_silu', b)] = ssem.val
        pr.on('scalar', gsilu)

        def ggate(v, b=b):
            v.wait_ge(ssem.h, ck[('g_silu', b)])
            pv = ck.get(('g_sc', b - 1))
            if pv is not None:
                v.wait_ge(vsem.h, pv)
            for j in range(4):
                v.tensor_tensor(sb['gg4'][:, 1024 * j:1024 * (j + 1)],
                                sb['yld'][:, 1024 * j:1024 * (j + 1)],
                                sb['sg4'][:, 1024 * j:1024 * (j + 1)], ALU.mult)
            ck[('g_gg', b)] = vsem.val
        pr.on('vector', ggate)

        def gsq(s, b=b):
            s.wait_ge(vsem.h, ck[('g_gg', b)])
            for j in range(4):
                s.activation(sb['sg4'][:, 1024 * j:1024 * (j + 1)],
                             sb['gg4'][:, 1024 * j:1024 * (j + 1)], AF.Square,
                             accum_out=sb['stat'][:, j:j + 1])
            ck[('g_sq', b)] = ssem.val
        pr.on('scalar', gsq)

        def gms(v, b=b):
            v.wait_ge(ssem.h, ck[('g_sq', b)])
            for j in range(4):
                v.tensor_scalar(sb['stat'][:, 4 + j:5 + j], sb['stat'][:, j:j + 1],
                                1.0 / IPC, EPS, ALU.mult, ALU.add)
                v.reciprocal(sb['stat'][:, 8 + j:9 + j], sb['stat'][:, 4 + j:5 + j])
            ck[('g_ms', b)] = vsem.val
        pr.on('vector', gms)

        def grs(s, b=b):
            s.wait_ge(vsem.h, ck[('g_ms', b)])
            for j in range(4):
                s.activation(sb['stat'][:, 12 + j:13 + j], sb['stat'][:, 8 + j:9 + j], AF.Sqrt)
            ck[('g_rs', b)] = ssem.val
        pr.on('scalar', grs)

        for j in range(4):
            tau = 4 * b + j
            pi = tau % 2

            def gscale(v, b=b, j=j, tau=tau, pi=pi):
                v.wait_ge(ssem.h, ck[('g_rs', b)])
                pv = ck.get(('g_tr', tau - 2, 7))
                if pv is not None:
                    v.wait_ge(tsem.h, pv)
                v.tensor_scalar(sb['sg4'][:, 1024 * j:1024 * (j + 1)],
                                sb['gg4'][:, 1024 * j:1024 * (j + 1)],
                                sb['stat'][:, 12 + j:13 + j], None, ALU.mult)
                v.tensor_tensor(sb['evc'][:, 1024 * pi:1024 * (pi + 1)],
                                sb['sg4'][:, 1024 * j:1024 * (j + 1)],
                                sb['nwrep'][:, :], ALU.mult)
                ck[('g_sc2', tau)] = vsem.val
                if j == 3:
                    ck[('g_sc', b)] = vsem.val
            pr.on('vector', gscale)

            for f in range(8):
                def gtr(tt, tau=tau, pi=pi, f=f):
                    tt.wait_ge(vsem.h, ck[('g_sc2', tau)])
                    pv = ck.get(('g_te', tau, f - 4), ck.get(('g_te', tau - 1, f + 4)))
                    if pv is not None:
                        tt.wait_ge(vsem.h, pv)
                    ins = tt.transpose(sb['psT%d' % (f % 4)][:, 0:128],
                                       sb['evc'][:, 1024 * pi + 128 * f:
                                                 1024 * pi + 128 * (f + 1)],
                                       sb['idbf'][:, :])
                    tsem.inc(ins)
                    ck[('g_tr', tau, f)] = tsem.val
                pr.on('tensor', gtr)

                def gte(v, tau=tau, f=f):
                    v.wait_ge(tsem.h, ck[('g_tr', tau, f)])
                    pv = ck.get(('g_td', tau, f - 4), ck.get(('g_td', tau - 1, f + 4)))
                    if pv is not None:
                        wait_tok(v, pv)
                    v.tensor_copy(sb['tev'][:, 128 * (f % 4):128 * (f % 4 + 1)],
                                  sb['psT%d' % (f % 4)][:, 0:128])
                    ck[('g_te', tau, f)] = vsem.val
                pr.on('vector', gte)

                def gdma(g, tau=tau, f=f):
                    g.wait_ge(vsem.h, ck[('g_te', tau, f)])
                    g.dma_start(out=o_ghat[128 * f:128 * (f + 1), 128 * tau:128 * (tau + 1)],
                                in_=sb['tev'][:, 128 * (f % 4):128 * (f % 4 + 1)])
                    ck[('g_td', tau, f)] = pr.sw.token()
                pr.on('gpsimd', gdma)


def build_attn0():
    nc = bass.Bass("TRN2", target_bir_lowering=False, debug=False)
    Iap = lambda name, shape, dt=F32: nc.dram_tensor(name, shape, dt, kind="ExternalInput").ap()

    xbt = Iap("xbt", [TOK, HID], BF16)
    lnrow = Iap("lnrow", [2, TOK])
    lncola = Iap("lncola", [128, 64])
    wib = Iap("wib", [RPC, HID], BF16)
    wsum = Iap("wsum", [1, RPC])
    nwc = Iap("nwc", [1, IPC])
    cw = Iap("cw", [CCPC, 4])
    cb = Iap("cb", [1, CCPC])
    dtb = Iap("dtb", [1, HPC])
    alog = Iap("alog", [1, HPC])
    dd = Iap("dd", [1, HPC])
    o_ghat = nc.dram_tensor("o_ghat", [IPC, TOK], BF16, kind="ExternalOutput").ap()
    xbc_s = nc.dram_tensor("xbc_s", [CCPC, TOK], BF16, kind="Internal").ap()
    gate_s = nc.dram_tensor("gate_s", [TOK, IPC], BF16, kind="Internal").ap()
    cvs_s = nc.dram_tensor("cvs_s", [CCPC, TOK], BF16, kind="Internal").ap()
    ys_s = nc.dram_tensor("ys_s", [TOK, IPC], BF16, kind="Internal").ap()

    tensors = CONSTS + ATTN0_TENSORS

    with nc.Block() as block:
        hw_sems = [nc.semaphore(f"hw{i}").__enter__() for i in range(8)]
        sw_sems = [nc.semaphore(f"sw{i}").__enter__() for i in range(8)]
        th = nc.semaphore("tsem").__enter__()
        vh = nc.semaphore("vsem").__enter__()
        sh = nc.semaphore("ssem").__enter__()
        gh = nc.semaphore("gsem").__enter__()
        sb = alloc(nc, tensors)
        pr = Prog(nc, block, hw_sems, sw_sems, th, vh, sh, gh)
        tsem, vsem, ssem, gsem = pr.tsem, pr.vsem, pr.ssem, pr.gsem
        emit_consts(pr, sb)
        ck = {}
        emit_attn0_body(nc, pr, sb, ck, xbt=xbt, lnrow=lnrow, lncola=lncola, wib=wib, wsum=wsum, nwc=nwc, cw=cw, cb=cb, dtb=dtb, alog=alog, dd=dd, o_ghat=o_ghat, xbc_s=xbc_s, gate_s=gate_s, cvs_s=cvs_s, ys_s=ys_s)
        def fin(g):
            wait_tok(g, pr.dtok())
            g.wait_ge(vsem.h, vsem.val)
            g.wait_ge(ssem.h, ssem.val)
            g.wait_ge(tsem.h, tsem.val)
        pr.on('gpsimd', fin)

    return nc


def attn0_in_maps(inp, prep):
    xbt = np.concatenate([np.asarray(prep[c]['o_xbt']) for c in range(NCORE)], axis=0)
    lnrow = np.concatenate([np.asarray(prep[c]['o_lnrow']) for c in range(NCORE)], axis=1)
    lncola = np.concatenate([np.asarray(prep[c]['o_lncol'][:, 0:8]) for c in range(NCORE)], axis=1)
    cw2 = inp['L0_attn_conv_w'][:, 0, :]
    cbv = inp['L0_attn_conv_b']
    maps = []
    for c in range(NCORE):
        chsl = np.r_[IPC * c:IPC * (c + 1),
                     INTER + 128 * c:INTER + 128 * (c + 1),
                     INTER + G * NS + 128 * c:INTER + G * NS + 128 * (c + 1)]
        m = dict(
            xbt=xbt, lnrow=lnrow, lncola=lncola,
            wib=np.asarray(prep[c]['o_wib']),
            wsum=np.asarray(prep[c]['o_wsum']),
            nwc=np.ascontiguousarray(inp['L0_attn_norm_w'][IPC * c:IPC * (c + 1)].reshape(1, IPC)),
            cw=np.ascontiguousarray(cw2[chsl]),
            cb=np.ascontiguousarray(cbv[chsl].reshape(1, CCPC)),
            dtb=np.ascontiguousarray(inp['L0_attn_dt_bias'][HPC * c:HPC * (c + 1)].reshape(1, HPC)),
            alog=np.ascontiguousarray(inp['L0_attn_A_log'][HPC * c:HPC * (c + 1)].reshape(1, HPC)),
            dd=np.ascontiguousarray(inp['L0_attn_D'][HPC * c:HPC * (c + 1)].reshape(1, HPC)),
        )
        maps.append(m)
    return maps


OUTPROJ_TENSORS = [
    ('gres', [128, INTER * 512 // 128], BF16, 'sbuf'),
    ('wblk', [128, 16384], BF16, 'sbuf'),
    ('ev', [128, 512], F32, 'sbuf'),
    ('sq', [128, 512], F32, 'sbuf'),
    ('arep', [128, 1024], F32, 'sbuf'),
    ('brep', [128, 1024], F32, 'sbuf'),
    ('xfm', [128, 1024], F32, 'sbuf'),
    ('aof', [128, 1024], F32, 'sbuf'),
    ('r1f', [128, 1024], F32, 'sbuf'),
    ('x0t', [128, 128], F32, 'sbuf'),
    ('srows', [1, 6 * TPC], F32, 'sbuf'),
    ('psA', [128, 512], F32, 'psum'),
    ('psB', [128, 512], F32, 'psum'),
    ('psC', [128, 512], F32, 'psum'),
    ('psD', [128, 512], F32, 'psum'),
    ('psS0', [1, 512], F32, 'psum'),
    ('psS1', [1, 512], F32, 'psum'),
    ('psS2', [1, 512], F32, 'psum'),
    ('psS3', [1, 512], F32, 'psum'),
]

def emit_outproj_body(nc, pr, sb, ck, *, ghat, wob, x0, o_r1, ao_s, cid=None):
    tsem, vsem, ssem, gsem = pr.tsem, pr.vsem, pr.ssem, pr.gsem

    banks = ['psA', 'psB', 'psC', 'psD']
    for th in range(2):
        def gload(q, th=th):
            if th == 1:
                q.wait_ge(tsem.h, ck[('mml', 0, 31)])
            if cid is None:
                for k in range(64):
                    q.dma_start(out=sb['gres'][:, 512 * k:512 * (k + 1)],
                                in_=ghat[128 * k:128 * (k + 1), 512 * th:512 * (th + 1)])
            else:
                c2 = q.snap(q.partition_id() * 2 + th, min_val=0, max_val=2 * NCORE - 1)
                q.dma_start(
                    out=sb['gres'][:, :].rearrange("p (k n) -> p k n", k=64),
                    in_=ghat[:, bass.ts(c2, 512)].rearrange("(k p) n -> p k n", p=128))
            ck[('g', th)] = pr.hw.token()
        pr.on('sync', gload)

        for m1 in range(32):
            idx = th * 32 + m1

            def wld(q, th=th, m1=m1, idx=idx):
                pv = ck.get(('mml', (idx - 2) // 32, (idx - 2) % 32))
                if pv is not None:
                    q.wait_ge(tsem.h, pv)
                q.dma_start(
                    out=sb['wblk'][:, 8192 * (idx % 2):8192 * (idx % 2 + 1)]
                    .rearrange("p (k n) -> p k n", k=64),
                    in_=wob[0:INTER, 128 * m1:128 * (m1 + 1)]
                    .rearrange("(k p) n -> p k n", p=128))
                ck[('w', idx)] = pr.hw.token()
            pr.on('sync', wld)

            def mm(tt, th=th, m1=m1, idx=idx):
                wait_tok(tt, ck[('w', idx)])
                if m1 == 0:
                    wait_tok(tt, ck[('g', th)])
                pv = ck.get(('ev', idx - 2))
                if pv is not None:
                    tt.wait_ge(vsem.h, pv)
                W0 = 8192 * (idx % 2)
                for k in range(64):
                    ins = tt.matmul(sb[banks[idx % 4]][:, 0:512],
                                    sb['wblk'][:, W0 + 128 * k:W0 + 128 * (k + 1)],
                                    sb['gres'][:, 512 * k:512 * (k + 1)],
                                    start=(k == 0), stop=(k == 63))
                tsem.inc(ins)
                ck[('mml', th, m1)] = tsem.val
                ck[('mme', idx)] = tsem.val
            pr.on('tensor', mm)

            def ev(v, th=th, m1=m1, idx=idx):
                v.wait_ge(tsem.h, ck[('mme', idx)])
                pv = ck.get(('aodma', idx - 1))
                if pv is not None:
                    wait_tok(v, pv)
                v.tensor_copy(sb['ev'][:, 0:512], sb[banks[idx % 4]][:, 0:512])
                ck[('ev', idx)] = vsem.val
            pr.on('vector', ev)

            def sqp(s, idx=idx):
                s.wait_ge(vsem.h, ck[('ev', idx)])
                if idx >= 1:
                    s.wait_ge(tsem.h, ck[('smm', idx - 1)])
                s.activation(sb['sq'][:, 0:512], sb['ev'][:, 0:512], AF.Square)
                ck[('sq', idx)] = ssem.val
            pr.on('scalar', sqp)

            def smm(tt, th=th, m1=m1, idx=idx):
                tt.wait_ge(ssem.h, ck[('sq', idx)])
                tt.matmul(sb['psS' + str(th)][0:1, 0:512], sb['ones_f'][:, 0:1],
                          sb['ev'][:, 0:512], start=(m1 == 0), stop=(m1 == 31))
                ins = tt.matmul(sb['psS' + str(2 + th)][0:1, 0:512], sb['ones_f'][:, 0:1],
                                sb['sq'][:, 0:512], start=(m1 == 0), stop=(m1 == 31))
                tsem.inc(ins)
                ck[('smm', idx)] = tsem.val
            pr.on('tensor', smm)

            def aodma(g, th=th, m1=m1, idx=idx):
                g.wait_ge(tsem.h, ck[('smm', idx)])
                g.dma_start(out=ao_s[128 * m1:128 * (m1 + 1), 512 * th:512 * (th + 1)],
                            in_=sb['ev'][:, 0:512])
                ck[('aodma', idx)] = pr.sw.token()
            pr.on('gpsimd', aodma)

    def strows(v):
        v.wait_ge(tsem.h, ck[('smm', 63)])
        R = sb['srows']
        for t in range(2):
            v.tensor_copy(R[0:1, 512 * t:512 * (t + 1)], sb['psS' + str(t)][0:1, 0:512])
            v.tensor_copy(R[0:1, TPC + 512 * t:TPC + 512 * (t + 1)],
                          sb['psS' + str(2 + t)][0:1, 0:512])
        v.tensor_scalar(R[0:1, 2 * TPC:3 * TPC], R[0:1, 0:TPC], 1.0 / HID, None, ALU.mult)
        v.tensor_scalar(R[0:1, 3 * TPC:4 * TPC], R[0:1, TPC:2 * TPC], 1.0 / HID, None, ALU.mult)
        v.tensor_tensor(R[0:1, 4 * TPC:5 * TPC], R[0:1, 2 * TPC:3 * TPC],
                        R[0:1, 2 * TPC:3 * TPC], ALU.mult)
        v.tensor_scalar(R[0:1, 4 * TPC:5 * TPC], R[0:1, 4 * TPC:5 * TPC], -1.0, EPS,
                        ALU.mult, ALU.add)
        v.tensor_tensor(R[0:1, 3 * TPC:4 * TPC], R[0:1, 3 * TPC:4 * TPC],
                        R[0:1, 4 * TPC:5 * TPC], ALU.add)
        v.reciprocal(R[0:1, 4 * TPC:5 * TPC], R[0:1, 3 * TPC:4 * TPC])
        ck['st1'] = vsem.val
    pr.on('vector', strows)

    def strows2(s):
        s.wait_ge(vsem.h, ck['st1'])
        R = sb['srows']
        s.activation(R[0:1, 5 * TPC:6 * TPC], R[0:1, 4 * TPC:5 * TPC], AF.Sqrt)
        ck['st2'] = ssem.val
    pr.on('scalar', strows2)

    def strows3(v):
        v.wait_ge(ssem.h, ck['st2'])
        R = sb['srows']
        v.tensor_tensor(R[0:1, 2 * TPC:3 * TPC], R[0:1, 2 * TPC:3 * TPC],
                        R[0:1, 5 * TPC:6 * TPC], ALU.mult)
        v.tensor_scalar(R[0:1, 2 * TPC:3 * TPC], R[0:1, 2 * TPC:3 * TPC], -1.0, None, ALU.mult)
        ck['st3'] = vsem.val
    pr.on('vector', strows3)

    for t in range(2):
        def repmm(tt, t=t, which=0):
            tt.wait_ge(vsem.h, ck['st3'])
            pv = ck.get(('repev', t - 1, 1))
            if pv is not None:
                tt.wait_ge(vsem.h, pv)
            ins = tt.matmul(sb['psA'][:, 0:512], sb['ones_f'][0:1, 0:128],
                            sb['srows'][0:1, 5 * TPC + 512 * t:5 * TPC + 512 * (t + 1)],
                            start=True, stop=True)
            tsem.inc(ins)
            ck[('repmm', t, 0)] = tsem.val
        pr.on('tensor', repmm)

        def repev(v, t=t):
            v.wait_ge(tsem.h, ck[('repmm', t, 0)])
            v.tensor_copy(sb['arep'][:, 512 * t:512 * (t + 1)], sb['psA'][:, 0:512])
            ck[('repev', t, 0)] = vsem.val
        pr.on('vector', repev)

        def repmm2(tt, t=t):
            tt.wait_ge(vsem.h, ck[('repev', t, 0)])
            ins = tt.matmul(sb['psA'][:, 0:512], sb['ones_f'][0:1, 0:128],
                            sb['srows'][0:1, 2 * TPC + 512 * t:2 * TPC + 512 * (t + 1)],
                            start=True, stop=True)
            tsem.inc(ins)
            ck[('repmm', t, 1)] = tsem.val
        pr.on('tensor', repmm2)

        def repev2(v, t=t):
            v.wait_ge(tsem.h, ck[('repmm', t, 1)])
            v.tensor_copy(sb['brep'][:, 512 * t:512 * (t + 1)], sb['psA'][:, 0:512])
            ck[('repev', t, 1)] = vsem.val
        pr.on('vector', repev2)

    for f in range(32):
        for t in range(8):
            def x0ld(q, f=f, t=t):
                i = f * 8 + t
                pv = ck.get(('x0t', f, t - 1)) if t >= 1 else (
                    ck.get(('x0t', f - 1, 7)) if f > 0 else None)
                if pv is not None:
                    q.wait_ge(tsem.h, pv)
                if (f, t) == (0, 0):
                    q.wait_ge(vsem.h, ck[('repev', 1, 1)])
                q.dma_start(out=sb['x0t'][:, :],
                            in_=x0[128 * t:128 * (t + 1), 128 * f:128 * (f + 1)])
                ck[('x0l', f, t)] = pr.hw.token()
            pr.on('sync', x0ld)

            def x0tr(tt, f=f, t=t):
                wait_tok(tt, ck[('x0l', f, t)])
                pv = ck.get(('xfme', f, t - 1)) if t > 0 else (
                    ck.get(('xfme', f - 1, 7)) if f > 0 else None)
                if pv is not None:
                    tt.wait_ge(vsem.h, pv)
                tt.wait_ge(gsem.h, gsem.val)
                ins = tt.transpose(sb['psB'][:, 0:128], sb['x0t'][:, :], sb['idf'][:, :])
                tsem.inc(ins)
                ck[('x0t', f, t)] = tsem.val
            pr.on('tensor', x0tr)

            def xfme(v, f=f, t=t):
                v.wait_ge(tsem.h, ck[('x0t', f, t)])
                v.tensor_copy(sb['xfm'][:, 128 * t:128 * (t + 1)], sb['psB'][:, 0:128])
                ck[('xfme', f, t)] = vsem.val
            pr.on('vector', xfme)

        def aold(q, f=f):
            wait_tok(q, ck[('aodma', 63)])
            pv = ck.get(('r1u', f - 1))
            if pv is not None:
                q.wait_ge(vsem.h, pv)
            q.dma_start(out=sb['aof'][:, :], in_=ao_s[128 * f:128 * (f + 1), :])
            ck[('aol', f)] = pr.hw.token()
        pr.on('sync', aold)

        def applyv(v, f=f):
            wait_tok(v, ck[('aol', f)])
            v.wait_ge(vsem.h, ck[('xfme', f, 7)])
            pv = ck.get(('r1d', f - 1))
            if pv is not None:
                wait_tok(v, pv)
            v.tensor_tensor(sb['r1f'][:, :], sb['aof'][:, :], sb['arep'][:, :], ALU.mult)
            v.tensor_tensor(sb['r1f'][:, :], sb['r1f'][:, :], sb['brep'][:, :], ALU.add)
            v.tensor_tensor(sb['r1f'][:, :], sb['r1f'][:, :], sb['xfm'][:, :], ALU.add)
            ck[('r1u', f)] = vsem.val
        pr.on('vector', applyv)

        def r1dma(g, f=f):
            g.wait_ge(vsem.h, ck[('r1u', f)])
            g.dma_start(out=o_r1[128 * f:128 * (f + 1), :], in_=sb['r1f'][:, :])
            ck[('r1d', f)] = pr.sw.token()
        pr.on('gpsimd', r1dma)



def build_outproj():
    nc = bass.Bass("TRN2", target_bir_lowering=False, debug=False)
    ghat = nc.dram_tensor("ghat", [INTER, TPC], BF16, kind="ExternalInput").ap()
    wob = nc.dram_tensor("wob", [INTER, HID], BF16, kind="ExternalInput").ap()
    x0 = nc.dram_tensor("x0", [TPC, HID], F32, kind="ExternalInput").ap()
    o_r1 = nc.dram_tensor("o_r1", [HID, TPC], F32, kind="ExternalOutput").ap()
    ao_s = nc.dram_tensor("ao_s", [HID, TPC], F32, kind="Internal").ap()

    tensors = CONSTS + OUTPROJ_TENSORS

    with nc.Block() as block:
        hw_sems = [nc.semaphore(f"hw{i}").__enter__() for i in range(8)]
        sw_sems = [nc.semaphore(f"sw{i}").__enter__() for i in range(8)]
        th = nc.semaphore("tsem").__enter__()
        vh = nc.semaphore("vsem").__enter__()
        sh = nc.semaphore("ssem").__enter__()
        gh = nc.semaphore("gsem").__enter__()
        sb = alloc(nc, tensors)
        pr = Prog(nc, block, hw_sems, sw_sems, th, vh, sh, gh)
        tsem, vsem, ssem, gsem = pr.tsem, pr.vsem, pr.ssem, pr.gsem
        emit_consts(pr, sb)
        ck = {}
        emit_outproj_body(nc, pr, sb, ck, ghat=ghat, wob=wob, x0=x0, o_r1=o_r1, ao_s=ao_s)
        def fin(g):
            wait_tok(g, pr.dtok())
            g.wait_ge(vsem.h, vsem.val)
            g.wait_ge(ssem.h, ssem.val)
            g.wait_ge(tsem.h, tsem.val)
        pr.on('gpsimd', fin)

    return nc


def outproj_in_maps(inp, prep, ghats):
    x2 = np.ascontiguousarray(inp['L0_attn_input_states'].reshape(TOK, HID))
    wob = np.concatenate([np.asarray(prep[c]['o_wob']) for c in range(NCORE)], axis=0)
    maps = []
    for c in range(NCORE):
        gin = np.ascontiguousarray(np.concatenate(
            [np.asarray(ghats[j])[:, TPC * c:TPC * (c + 1)] for j in range(NCORE)], axis=0))
        maps.append(dict(ghat=gin, wob=wob, x0=x2[TPC * c:TPC * (c + 1)]))
    return maps


DBLK = 11264


FFN0_TENSORS = [
    ('big', [128, 45056], BF16, 'sbuf'),
    ('wblk', [128, 22528], BF16, 'sbuf'),
    ('ld', [128, 1024], F32, 'sbuf'),
    ('sq', [128, 1024], F32, 'sbuf'),
    ('ev', [128, 1024], F32, 'sbuf'),
    ('ev2', [128, 1024], F32, 'sbuf'),
    ('hbf', [128, 1024], BF16, 'sbuf'),
    ('arep', [128, 1024], F32, 'sbuf'),
    ('brep', [128, 1024], F32, 'sbuf'),
    ('r1t', [128, 1024], F32, 'sbuf'),
    ('r2t', [128, 1024], F32, 'sbuf'),
    ('srows', [1, 6 * TPC], F32, 'sbuf'),
    ('wsblk', [1, 512], F32, 'sbuf'),
    ('psA', [128, 512], F32, 'psum'),
    ('psB', [128, 512], F32, 'psum'),
    ('psC', [128, 512], F32, 'psum'),
    ('psD', [128, 512], F32, 'psum'),
    ('psE', [128, 512], F32, 'psum'),
    ('psF', [128, 512], F32, 'psum'),
    ('psG', [128, 512], F32, 'psum'),
    ('psH', [128, 512], F32, 'psum'),
]

def emit_ffn0_body(nc, pr, sb, ck, *, r1, wgu, wgs, wus, wdp, o_r2, h_s, do_s, o_dbg):
    tsem, vsem, ssem, gsem = pr.tsem, pr.vsem, pr.ssem, pr.gsem



    for f in range(32):
        def ldr(q, f=f):
            if f >= 1:
                q.wait_ge(tsem.h, ck[('stmm', f - 1)])
            q.dma_start(out=sb['ld'][:, :], in_=r1[128 * f:128 * (f + 1), :])
            ck[('r1l', f)] = pr.hw.token()
        pr.on('sync', ldr)

        def sqp(s, f=f):
            wait_tok(s, ck[('r1l', f)])
            if f >= 1:
                s.wait_ge(tsem.h, ck[('stmm', f - 1)])
            s.activation(sb['sq'][:, :], sb['ld'][:, :], AF.Square)
            ck[('r1sq', f)] = ssem.val
        pr.on('scalar', sqp)

        def stmm(tt, f=f):
            tt.wait_ge(ssem.h, ck[('r1sq', f)])
            tt.wait_ge(gsem.h, gsem.val)
            for t in range(2):
                tt.matmul(sb['psE' if t == 0 else 'psF'][0:1, 0:512], sb['ones_f'][:, 0:1],
                          sb['ld'][:, 512 * t:512 * (t + 1)], start=(f == 0), stop=(f == 31))
                ins = tt.matmul(sb['psG' if t == 0 else 'psH'][0:1, 0:512], sb['ones_f'][:, 0:1],
                                sb['sq'][:, 512 * t:512 * (t + 1)], start=(f == 0), stop=(f == 31))
            tsem.inc(ins)
            ck[('stmm', f)] = tsem.val
        pr.on('tensor', stmm)

    def xpl(g):
        for i in range(32):
            j, rb = i // 4, i % 4
            src = bass.AP(r1.tensor, (8 * 128 * rb + j) * TPC, [[8 * TPC, 128], [1, TPC]])
            g.dma_start(out=sb['big'][:, 1024 * i:1024 * (i + 1)], in_=src)
        ck['xp'] = pr.sw.token()
    pr.on('gpsimd', xpl)

    def strows(v):
        v.wait_ge(tsem.h, ck[('stmm', 31)])
        R = sb['srows']
        v.tensor_copy(R[0:1, 0:512], sb['psE'][0:1, 0:512])
        v.tensor_copy(R[0:1, 512:1024], sb['psF'][0:1, 0:512])
        v.tensor_copy(R[0:1, TPC:TPC + 512], sb['psG'][0:1, 0:512])
        v.tensor_copy(R[0:1, TPC + 512:TPC + 1024], sb['psH'][0:1, 0:512])
        v.tensor_scalar(R[0:1, 2 * TPC:3 * TPC], R[0:1, 0:TPC], 1.0 / HID, None, ALU.mult)
        v.tensor_scalar(R[0:1, 3 * TPC:4 * TPC], R[0:1, TPC:2 * TPC], 1.0 / HID, None, ALU.mult)
        v.tensor_tensor(R[0:1, 4 * TPC:5 * TPC], R[0:1, 2 * TPC:3 * TPC],
                        R[0:1, 2 * TPC:3 * TPC], ALU.mult)
        v.tensor_scalar(R[0:1, 4 * TPC:5 * TPC], R[0:1, 4 * TPC:5 * TPC], -1.0, EPS,
                        ALU.mult, ALU.add)
        v.tensor_tensor(R[0:1, 3 * TPC:4 * TPC], R[0:1, 3 * TPC:4 * TPC],
                        R[0:1, 4 * TPC:5 * TPC], ALU.add)
        v.reciprocal(R[0:1, 4 * TPC:5 * TPC], R[0:1, 3 * TPC:4 * TPC])
        ck['st1'] = vsem.val
    pr.on('vector', strows)

    def dbgout(g):
        g.wait_ge(vsem.h, ck['st1'])
        g.dma_start(out=o_dbg, in_=sb['srows'][0:1, 0:2048])
        ck['dbg'] = pr.sw.token()
    pr.on('gpsimd', dbgout)

    def strows2(s):
        s.wait_ge(vsem.h, ck['st1'])
        R = sb['srows']
        s.activation(R[0:1, 5 * TPC:6 * TPC], R[0:1, 4 * TPC:5 * TPC], AF.Sqrt)
        ck['st2'] = ssem.val
    pr.on('scalar', strows2)

    def strows3(v):
        v.wait_ge(ssem.h, ck['st2'])
        R = sb['srows']
        v.tensor_scalar(R[0:1, 2 * TPC:3 * TPC], R[0:1, 2 * TPC:3 * TPC], -1.0, None, ALU.mult)
        ck['st3'] = vsem.val
    pr.on('vector', strows3)

    for t in range(2):
        for which, rowoff, dst in [(0, 5 * TPC, 'arep'), (1, 2 * TPC, 'brep')]:
            def rmm(tt, t=t, rowoff=rowoff, which=which):
                tt.wait_ge(vsem.h, ck['st3'])
                pv = ck.get(('repev', t, which - 1)) or (ck.get(('repev', t - 1, 1)) if t else None)
                if pv is not None:
                    tt.wait_ge(vsem.h, pv)
                ins = tt.matmul(sb['psA'][:, 0:512], sb['ones_f'][0:1, 0:128],
                                sb['srows'][0:1, rowoff + 512 * t:rowoff + 512 * (t + 1)],
                                start=True, stop=True)
                tsem.inc(ins)
                ck[('repmm', t, which)] = tsem.val
            pr.on('tensor', rmm)

            def rev(v, t=t, which=which, dst=dst):
                v.wait_ge(tsem.h, ck[('repmm', t, which)])
                v.tensor_copy(sb[dst][:, 512 * t:512 * (t + 1)], sb['psA'][:, 0:512])
                ck[('repev', t, which)] = vsem.val
            pr.on('vector', rev)

    banks = ['psA', 'psB', 'psC', 'psD', 'psE', 'psF', 'psG', 'psH']
    for mg in range(NBLK_GU):
        def blkld(q, mg=mg):
            if mg == 0:
                q.wait_ge(vsem.h, ck[('repev', 1, 1)])
            else:
                q.wait_ge(tsem.h, ck[('gumm', mg - 1)])
            src = wgu[HID * mg:HID * (mg + 1), :].rearrange("(k p) n -> p k n", p=128)
            q.dma_start(out=sb['wblk'][:, 0:16384].rearrange("p (k n) -> p k n", k=32), in_=src)
            q.dma_start(out=sb['wsblk'][0:1, 0:256], in_=wgs[0:1, 256 * mg:256 * (mg + 1)])
            q.dma_start(out=sb['wsblk'][0:1, 256:512], in_=wus[0:1, 256 * mg:256 * (mg + 1)])
            ck[('blk', mg)] = pr.hw.token()
        pr.on('sync', blkld)

        def gumm(tt, mg=mg):
            wait_tok(tt, ck[('blk', mg)])
            wait_tok(tt, ck['xp'])
            pv = ck.get(('guevl', mg - 1))
            if pv is not None:
                tt.wait_ge(vsem.h, pv)
            for k in range(32):
                for sub in range(4):
                    half, mi = sub // 2, sub % 2
                    lhs = sb['wblk'][:, 512 * k + 256 * half + 128 * mi:
                                     512 * k + 256 * half + 128 * (mi + 1)]
                    for t in range(2):
                        tt.matmul(sb[banks[sub * 2 + t]][:, 0:512], lhs,
                                  sb['big'][:, 1024 * k + 512 * t:1024 * k + 512 * (t + 1)],
                                  start=(k == 0), stop=False)
            for sub in range(4):
                half, mi = sub // 2, sub % 2
                gcol = 256 * half + 128 * mi
                for t in range(2):
                    ins = tt.matmul(sb[banks[sub * 2 + t]][:, 0:512],
                                    sb['wsblk'][0:1, gcol:gcol + 128],
                                    sb['srows'][0:1, 2 * TPC + 512 * t:2 * TPC + 512 * (t + 1)],
                                    start=False, stop=True)
            tsem.inc(ins)
            ck[('gumm', mg)] = tsem.val
        pr.on('tensor', gumm)

        for mi in range(2):
            def guev(v, mg=mg, mi=mi):
                v.wait_ge(tsem.h, ck[('gumm', mg)])
                pv = ck.get(('hdma', mg, mi - 1)) or (ck.get(('hdma', mg - 1, 1)) if mg else None)
                if pv is not None:
                    wait_tok(v, pv)
                for t in range(2):
                    v.tensor_tensor(sb['ev'][:, 512 * t:512 * (t + 1)],
                                    sb[banks[mi * 2 + t]][:, 0:512],
                                    sb['arep'][:, 512 * t:512 * (t + 1)], ALU.mult)
                    v.tensor_tensor(sb['ev2'][:, 512 * t:512 * (t + 1)],
                                    sb[banks[4 + mi * 2 + t]][:, 0:512],
                                    sb['arep'][:, 512 * t:512 * (t + 1)], ALU.mult)
                ck[('guev', mg, mi)] = vsem.val
                if mi == 1:
                    ck[('guevl', mg)] = vsem.val
            pr.on('vector', guev)

            def gusl(s, mg=mg, mi=mi):
                s.wait_ge(vsem.h, ck[('guev', mg, mi)])
                s.activation(sb['sq'][:, :], sb['ev'][:, :], AF.Silu)
                ck[('gusl', mg, mi)] = ssem.val
            pr.on('scalar', gusl)

            def humul(v, mg=mg, mi=mi):
                v.wait_ge(ssem.h, ck[('gusl', mg, mi)])
                v.tensor_tensor(sb['hbf'][:, :], sb['sq'][:, :], sb['ev2'][:, :], ALU.mult)
                ck[('hh', mg, mi)] = vsem.val
            pr.on('vector', humul)

            def hdma(g, mg=mg, mi=mi):
                g.wait_ge(vsem.h, ck[('hh', mg, mi)])
                g.dma_start(out=h_s[256 * mg + 128 * mi:256 * mg + 128 * (mi + 1), :],
                            in_=sb['hbf'][:, :])
                ck[('hdma', mg, mi)] = pr.sw.token()
            pr.on('gpsimd', hdma)

    KT = []
    for j in range(8):
        for rb in range(11):
            KT.append((j, rb, 96 if rb == 10 else 128))
    for thh in range(2):
        def hload(q, thh=thh):
            wait_tok(q, ck[('hdma', NBLK_GU - 1, 1)])
            if thh == 1:
                q.wait_ge(tsem.h, ck[('dmm', 0, 15)])
            for i, (j, rb, rbn) in enumerate(KT):
                src = bass.AP(h_s.tensor, (8 * 128 * rb + j) * TPC + 512 * thh,
                              [[8 * TPC, rbn], [1, 512]])
                q.dma_start(out=sb['big'][0:rbn, 512 * i:512 * (i + 1)], in_=src)
            ck[('hres', thh)] = pr.hw.token()
        pr.on('sync', hload)

        for mgd in range(16):
            def dblkld(q, thh=thh, mgd=mgd):
                pv = ck.get(('dmm', thh, mgd - 1))
                if pv is not None:
                    q.wait_ge(tsem.h, pv)
                elif thh == 1:
                    q.wait_ge(tsem.h, ck[('dmm', 0, 15)])
                src = wdp[DBLK * mgd:DBLK * (mgd + 1), :].rearrange("(k p) n -> p k n", p=128)
                q.dma_start(out=sb['wblk'][:, 0:22528].rearrange("p (k n) -> p k n", k=88),
                            in_=src)
                ck[('dblk', thh, mgd)] = pr.hw.token()
            pr.on('sync', dblkld)

            def dmm(tt, thh=thh, mgd=mgd):
                wait_tok(tt, ck[('dblk', thh, mgd)])
                wait_tok(tt, ck[('hres', thh)])
                pv = ck.get(('devl', thh, mgd - 1)) or (ck.get(('devl', thh - 1, 15)) if thh else None)
                if pv is not None:
                    tt.wait_ge(vsem.h, pv)
                if (thh, mgd) == (0, 0):
                    tt.wait_ge(vsem.h, ck[('guevl', NBLK_GU - 1)])
                for i, (j, rb, rbn) in enumerate(KT):
                    i2 = (1408 * j + 128 * rb) // 128
                    for m in range(2):
                        ins = tt.matmul(
                            sb[banks[m]][:, 0:512],
                            sb['wblk'][0:rbn, 256 * i2 + 128 * m:256 * i2 + 128 * (m + 1)],
                            sb['big'][0:rbn, 512 * i:512 * (i + 1)],
                            start=(i == 0), stop=(i == len(KT) - 1))
                tsem.inc(ins)
                ck[('dmm', thh, mgd)] = tsem.val
            pr.on('tensor', dmm)

            for m in range(2):
                mrow = 256 * mgd + 128 * m

                def dev(v, thh=thh, mgd=mgd, m=m, mrow=mrow):
                    v.wait_ge(tsem.h, ck[('dmm', thh, mgd)])
                    pv = ck.get(('dodma', thh, mgd, m - 1)) or (
                        ck.get(('dodma', thh, mgd - 1, 1)) if mgd else (
                            ck.get(('dodma', thh - 1, 15, 1)) if thh else None))
                    if pv is not None:
                        wait_tok(v, pv)
                    v.tensor_copy(sb['ev'][:, 0:512], sb[banks[m]][:, 0:512])
                    ck[('dev', thh, mgd, m)] = vsem.val
                    if m == 1:
                        ck[('devl', thh, mgd)] = vsem.val
                pr.on('vector', dev)

                def dsq(s, thh=thh, mgd=mgd, m=m):
                    s.wait_ge(vsem.h, ck[('dev', thh, mgd, m)])
                    s.activation(sb['sq'][:, 0:512], sb['ev'][:, 0:512], AF.Square)
                    ck[('dsq', thh, mgd, m)] = ssem.val
                pr.on('scalar', dsq)

                def dsmm(tt, thh=thh, mgd=mgd, m=m):
                    tt.wait_ge(ssem.h, ck[('dsq', thh, mgd, m)])
                    mvisit = 2 * mgd + m
                    tt.matmul(sb['psE'][0:1, 0:512], sb['ones_f'][:, 0:1],
                              sb['ev'][:, 0:512], start=(mvisit == 0), stop=(mvisit == 31))
                    ins = tt.matmul(sb['psF'][0:1, 0:512], sb['ones_f'][:, 0:1],
                                    sb['sq'][:, 0:512], start=(mvisit == 0), stop=(mvisit == 31))
                    tsem.inc(ins)
                    ck[('dsmm', thh, mgd, m)] = tsem.val
                pr.on('tensor', dsmm)

                def dodma(g, thh=thh, mgd=mgd, m=m, mrow=mrow):
                    g.wait_ge(tsem.h, ck[('dsmm', thh, mgd, m)])
                    g.dma_start(out=do_s[mrow:mrow + 128, 512 * thh:512 * (thh + 1)],
                                in_=sb['ev'][:, 0:512])
                    ck[('dodma', thh, mgd, m)] = pr.sw.token()
                pr.on('gpsimd', dodma)

        def dstr(v, thh=thh):
            v.wait_ge(tsem.h, ck[('dsmm', thh, 15, 1)])
            R = sb['srows']
            v.tensor_copy(R[0:1, 512 * thh:512 * (thh + 1)], sb['psE'][0:1, 0:512])
            v.tensor_copy(R[0:1, TPC + 512 * thh:TPC + 512 * (thh + 1)], sb['psF'][0:1, 0:512])
            ck[('dst', thh)] = vsem.val
        pr.on('vector', dstr)

    def fstr(v):
        v.wait_ge(vsem.h, ck[('dst', 1)])
        R = sb['srows']
        v.tensor_scalar(R[0:1, 2 * TPC:3 * TPC], R[0:1, 0:TPC], 1.0 / HID, None, ALU.mult)
        v.tensor_scalar(R[0:1, 3 * TPC:4 * TPC], R[0:1, TPC:2 * TPC], 1.0 / HID, None, ALU.mult)
        v.tensor_tensor(R[0:1, 4 * TPC:5 * TPC], R[0:1, 2 * TPC:3 * TPC],
                        R[0:1, 2 * TPC:3 * TPC], ALU.mult)
        v.tensor_scalar(R[0:1, 4 * TPC:5 * TPC], R[0:1, 4 * TPC:5 * TPC], -1.0, EPS,
                        ALU.mult, ALU.add)
        v.tensor_tensor(R[0:1, 3 * TPC:4 * TPC], R[0:1, 3 * TPC:4 * TPC],
                        R[0:1, 4 * TPC:5 * TPC], ALU.add)
        v.reciprocal(R[0:1, 4 * TPC:5 * TPC], R[0:1, 3 * TPC:4 * TPC])
        ck['fst1'] = vsem.val
    pr.on('vector', fstr)

    def fstr2(s):
        s.wait_ge(vsem.h, ck['fst1'])
        R = sb['srows']
        s.activation(R[0:1, 5 * TPC:6 * TPC], R[0:1, 4 * TPC:5 * TPC], AF.Sqrt)
        ck['fst2'] = ssem.val
    pr.on('scalar', fstr2)

    def fstr3(v):
        v.wait_ge(ssem.h, ck['fst2'])
        R = sb['srows']
        v.tensor_tensor(R[0:1, 2 * TPC:3 * TPC], R[0:1, 2 * TPC:3 * TPC],
                        R[0:1, 5 * TPC:6 * TPC], ALU.mult)
        v.tensor_scalar(R[0:1, 2 * TPC:3 * TPC], R[0:1, 2 * TPC:3 * TPC], -1.0, None, ALU.mult)
        ck['fst3'] = vsem.val
    pr.on('vector', fstr3)

    for t in range(2):
        for which, rowoff, dst in [(0, 5 * TPC, 'arep'), (1, 2 * TPC, 'brep')]:
            def rmm2(tt, t=t, rowoff=rowoff, which=which):
                tt.wait_ge(vsem.h, ck['fst3'])
                pv = ck.get(('frepev', t, which - 1)) or (ck.get(('frepev', t - 1, 1)) if t else None)
                if pv is not None:
                    tt.wait_ge(vsem.h, pv)
                ins = tt.matmul(sb['psA'][:, 0:512], sb['ones_f'][0:1, 0:128],
                                sb['srows'][0:1, rowoff + 512 * t:rowoff + 512 * (t + 1)],
                                start=True, stop=True)
                tsem.inc(ins)
                ck[('frepmm', t, which)] = tsem.val
            pr.on('tensor', rmm2)

            def rev2(v, t=t, which=which, dst=dst):
                v.wait_ge(tsem.h, ck[('frepmm', t, which)])
                v.tensor_copy(sb[dst][:, 512 * t:512 * (t + 1)], sb['psA'][:, 0:512])
                ck[('frepev', t, which)] = vsem.val
            pr.on('vector', rev2)

    for f in range(32):
        def fld(q, f=f):
            if f == 0:
                q.wait_ge(vsem.h, ck[('frepev', 1, 1)])
            pv = ck.get(('r2d', f - 1))
            if pv is not None:
                q.wait_ge(vsem.h, ck[('r2u', f - 1)])
            q.dma_start(out=sb['r1t'][:, :], in_=r1[128 * f:128 * (f + 1), :])
            q.dma_start(out=sb['ld'][:, :], in_=do_s[128 * f:128 * (f + 1), :])
            ck[('fl', f)] = pr.hw.token()
        pr.on('sync', fld)

        def fap(v, f=f):
            wait_tok(v, ck[('fl', f)])
            pv = ck.get(('r2d', f - 1))
            if pv is not None:
                wait_tok(v, pv)
            v.tensor_tensor(sb['r2t'][:, :], sb['ld'][:, :], sb['arep'][:, :], ALU.mult)
            v.tensor_tensor(sb['r2t'][:, :], sb['r2t'][:, :], sb['brep'][:, :], ALU.add)
            v.tensor_tensor(sb['r2t'][:, :], sb['r2t'][:, :], sb['r1t'][:, :], ALU.add)
            ck[('r2u', f)] = vsem.val
        pr.on('vector', fap)

        def fdma(g, f=f):
            g.wait_ge(vsem.h, ck[('r2u', f)])
            g.dma_start(out=o_r2[128 * f:128 * (f + 1), :], in_=sb['r2t'][:, :])
            ck[('r2d', f)] = pr.sw.token()
        pr.on('gpsimd', fdma)



def build_ffn0():
    nc = bass.Bass("TRN2", target_bir_lowering=False, debug=False)
    r1 = nc.dram_tensor("r1", [HID, TPC], F32, kind="ExternalInput").ap()
    wgu = nc.dram_tensor("wgu", [NBLK_GU * HID, 512], BF16, kind="ExternalInput").ap()
    wgs = nc.dram_tensor("wgs", [1, FFN_I], F32, kind="ExternalInput").ap()
    wus = nc.dram_tensor("wus", [1, FFN_I], F32, kind="ExternalInput").ap()
    wdp = nc.dram_tensor("wdp", [16 * DBLK, 256], BF16, kind="ExternalInput").ap()
    o_r2 = nc.dram_tensor("o_r2", [HID, TPC], F32, kind="ExternalOutput").ap()
    o_dbg = nc.dram_tensor("o_dbg", [1, 2048], F32, kind="ExternalOutput").ap()
    h_s = nc.dram_tensor("h_s", [FFN_I, TPC], BF16, kind="Internal").ap()
    do_s = nc.dram_tensor("do_s", [HID, TPC], F32, kind="Internal").ap()

    tensors = CONSTS + FFN0_TENSORS

    with nc.Block() as block:
        hw_sems = [nc.semaphore(f"hw{i}").__enter__() for i in range(8)]
        sw_sems = [nc.semaphore(f"sw{i}").__enter__() for i in range(8)]
        th = nc.semaphore("tsem").__enter__()
        vh = nc.semaphore("vsem").__enter__()
        sh = nc.semaphore("ssem").__enter__()
        gh = nc.semaphore("gsem").__enter__()
        sb = alloc(nc, tensors)
        pr = Prog(nc, block, hw_sems, sw_sems, th, vh, sh, gh)
        tsem, vsem, ssem, gsem = pr.tsem, pr.vsem, pr.ssem, pr.gsem
        emit_consts(pr, sb)
        ck = {}
        emit_ffn0_body(nc, pr, sb, ck, r1=r1, wgu=wgu, wgs=wgs, wus=wus, wdp=wdp, o_r2=o_r2, h_s=h_s, do_s=do_s, o_dbg=o_dbg)
        def fin(g):
            wait_tok(g, pr.dtok())
            g.wait_ge(vsem.h, vsem.val)
            g.wait_ge(ssem.h, ssem.val)
            g.wait_ge(tsem.h, tsem.val)
        pr.on('gpsimd', fin)

    return nc


def ffn0_in_maps(prep, r1s):
    wgu = np.concatenate([np.asarray(prep[c]['o_wgu'])[:HID * GU_BLOCKS[c]]
                          for c in range(NCORE)], axis=0)
    wgs = np.concatenate([np.asarray(prep[c]['o_wgs'])[:, :256 * GU_BLOCKS[c]]
                          for c in range(NCORE)], axis=1)
    wus = np.concatenate([np.asarray(prep[c]['o_wus'])[:, :256 * GU_BLOCKS[c]]
                          for c in range(NCORE)], axis=1)
    wdp = np.concatenate([np.asarray(prep[c]['o_wdp']) for c in range(NCORE)], axis=0)
    return [dict(r1=np.asarray(r1s[c]), wgu=wgu, wgs=wgs, wus=wus, wdp=wdp)
            for c in range(NCORE)]


ATTN1_TENSORS = [
    ('big', [128, 32768], BF16, 'sbuf'),
    ('wblk', [128, 16384], BF16, 'sbuf'),
    ('asc', [128, 32], F32, 'sbuf'),
    ('ld', [128, 1024], F32, 'sbuf'),
    ('ld2', [128, 1024], F32, 'sbuf'),
    ('sq', [128, 1024], F32, 'sbuf'),
    ('ev', [128, 1024], F32, 'sbuf'),
    ('go', [128, 1024], F32, 'sbuf'),
    ('arep', [128, 1024], F32, 'sbuf'),
    ('brep', [128, 1024], F32, 'sbuf'),
    ('r1t', [128, 1024], F32, 'sbuf'),
    ('r2t', [128, 1024], F32, 'sbuf'),
    ('srows', [1, 6 * TPC], F32, 'sbuf'),
    ('asr', [1, HID], F32, 'sbuf'),
    ('b2c', [128, 32], F32, 'sbuf'),
    ('psA', [128, 512], F32, 'psum'),
    ('psB', [128, 512], F32, 'psum'),
    ('psC', [128, 512], F32, 'psum'),
    ('psD', [128, 512], F32, 'psum'),
    ('psE', [128, 512], F32, 'psum'),
    ('psF', [128, 512], F32, 'psum'),
    ('psG', [128, 512], F32, 'psum'),
    ('psH', [128, 512], F32, 'psum'),
]

def emit_attn1_body(nc, pr, sb, ck, *, r2, attnT, asum, b2, o_r3, go_s):
    tsem, vsem, ssem, gsem = pr.tsem, pr.vsem, pr.ssem, pr.gsem

    def ld0(q):
        q.dma_start(out=sb['asc'][:, :],
                    in_=bass.AP(asum.tensor, 0, [[1, 128], [128, 32]]),
                    allow_slow_non_contiguous=True)
        q.dma_start(out=sb['b2c'][:, :],
                    in_=bass.AP(b2.tensor, 0, [[1, 128], [128, 32]]),
                    allow_slow_non_contiguous=True)
        ck['ld0'] = pr.hw.token()
    pr.on('sync', ld0)

    for f in range(32):
        def ldr(q, f=f):
            pv = ck.get(('stmm', f - 2))
            if pv is not None:
                q.wait_ge(tsem.h, pv)
                q.wait_ge(vsem.h, ck[('xc', f - 2)])
            q.dma_start(out=sb['ld' if f % 2 == 0 else 'ld2'][:, :],
                        in_=r2[128 * f:128 * (f + 1), :])
            ck[('rl', f)] = pr.hw.token()
        pr.on('sync', ldr)

        def casx(v, f=f):
            wait_tok(v, ck[('rl', f)])
            v.tensor_copy(sb['big'][:, 1024 * f:1024 * (f + 1)],
                          sb['ld' if f % 2 == 0 else 'ld2'][:, :])
            ck[('xc', f)] = vsem.val
        pr.on('vector', casx)

        def sqp(s, f=f):
            wait_tok(s, ck[('rl', f)])
            if f >= 1:
                s.wait_ge(tsem.h, ck[('stmm', f - 1)])
            s.activation(sb['sq'][:, :], sb['ld' if f % 2 == 0 else 'ld2'][:, :], AF.Square)
            ck[('rsq', f)] = ssem.val
        pr.on('scalar', sqp)

        def stmm(tt, f=f):
            tt.wait_ge(ssem.h, ck[('rsq', f)])
            tt.wait_ge(gsem.h, gsem.val)
            for t in range(2):
                tt.matmul(sb['psE' if t == 0 else 'psF'][0:1, 0:512], sb['ones_f'][:, 0:1],
                          sb['ld' if f % 2 == 0 else 'ld2'][:, 512 * t:512 * (t + 1)],
                          start=(f == 0), stop=(f == 31))
                ins = tt.matmul(sb['psG' if t == 0 else 'psH'][0:1, 0:512], sb['ones_f'][:, 0:1],
                                sb['sq'][:, 512 * t:512 * (t + 1)], start=(f == 0), stop=(f == 31))
            tsem.inc(ins)
            ck[('stmm', f)] = tsem.val
        pr.on('tensor', stmm)

    def strows(v):
        v.wait_ge(tsem.h, ck[('stmm', 31)])
        R = sb['srows']
        v.tensor_copy(R[0:1, 0:512], sb['psE'][0:1, 0:512])
        v.tensor_copy(R[0:1, 512:1024], sb['psF'][0:1, 0:512])
        v.tensor_copy(R[0:1, TPC:TPC + 512], sb['psG'][0:1, 0:512])
        v.tensor_copy(R[0:1, TPC + 512:TPC + 1024], sb['psH'][0:1, 0:512])
        v.tensor_scalar(R[0:1, 2 * TPC:3 * TPC], R[0:1, 0:TPC], 1.0 / HID, None, ALU.mult)
        v.tensor_scalar(R[0:1, 3 * TPC:4 * TPC], R[0:1, TPC:2 * TPC], 1.0 / HID, None, ALU.mult)
        v.tensor_tensor(R[0:1, 4 * TPC:5 * TPC], R[0:1, 2 * TPC:3 * TPC],
                        R[0:1, 2 * TPC:3 * TPC], ALU.mult)
        v.tensor_scalar(R[0:1, 4 * TPC:5 * TPC], R[0:1, 4 * TPC:5 * TPC], -1.0, EPS,
                        ALU.mult, ALU.add)
        v.tensor_tensor(R[0:1, 3 * TPC:4 * TPC], R[0:1, 3 * TPC:4 * TPC],
                        R[0:1, 4 * TPC:5 * TPC], ALU.add)
        v.reciprocal(R[0:1, 4 * TPC:5 * TPC], R[0:1, 3 * TPC:4 * TPC])
        v.tensor_scalar(R[0:1, 2 * TPC:3 * TPC], R[0:1, 2 * TPC:3 * TPC], -1.0, None, ALU.mult)
        ck['st1'] = vsem.val
    pr.on('vector', strows)

    def strows2(s):
        s.wait_ge(vsem.h, ck['st1'])
        s.activation(sb['srows'][0:1, 5 * TPC:6 * TPC], sb['srows'][0:1, 4 * TPC:5 * TPC], AF.Sqrt)
        ck['st2'] = ssem.val
    pr.on('scalar', strows2)

    for t in range(2):
        def rmm(tt, t=t):
            tt.wait_ge(ssem.h, ck['st2'])
            pv = ck.get(('repev2', t - 1))
            if pv is not None:
                tt.wait_ge(vsem.h, pv)
            ins = tt.matmul(sb['psA'][:, 0:512], sb['ones_f'][0:1, 0:128],
                            sb['srows'][0:1, 5 * TPC + 512 * t:5 * TPC + 512 * (t + 1)],
                            start=True, stop=True)
            tsem.inc(ins)
            ck[('repmm', t)] = tsem.val
        pr.on('tensor', rmm)

        def rev(v, t=t):
            v.wait_ge(tsem.h, ck[('repmm', t)])
            v.tensor_copy(sb['arep'][:, 512 * t:512 * (t + 1)], sb['psA'][:, 0:512])
            ck[('repev', t)] = vsem.val
        pr.on('vector', rev)

        def rmm2(tt, t=t):
            tt.wait_ge(vsem.h, ck[('repev', t)])
            ins = tt.matmul(sb['psA'][:, 0:512], sb['ones_f'][0:1, 0:128],
                            sb['srows'][0:1, 2 * TPC + 512 * t:2 * TPC + 512 * (t + 1)],
                            start=True, stop=True)
            tsem.inc(ins)
            ck[('repmm2', t)] = tsem.val
        pr.on('tensor', rmm2)

        def rev2(v, t=t):
            v.wait_ge(tsem.h, ck[('repmm2', t)])
            v.tensor_copy(sb['brep'][:, 512 * t:512 * (t + 1)], sb['psA'][:, 0:512])
            ck[('repev2', t)] = vsem.val
        pr.on('vector', rev2)

    banks = ['psA', 'psB', 'psC', 'psD']
    for m2 in range(16):
        def wld(q, m2=m2):
            if m2 == 0:
                q.wait_ge(vsem.h, ck[('repev2', 1)])
            pv = ck.get(('ymm', m2 - 2, 31))
            if pv is not None:
                q.wait_ge(tsem.h, pv)
            q.dma_start(
                out=sb['wblk'][:, 8192 * (m2 % 2):8192 * (m2 % 2 + 1)]
                .rearrange("p (k n) -> p k n", k=32),
                in_=attnT[(m2 // 2) * HID:(m2 // 2 + 1) * HID,
                          256 * (m2 % 2):256 * (m2 % 2) + 256]
                .rearrange("(k p) n -> p k n", p=128))
            ck[('yw', m2)] = pr.hw.token()
        pr.on('sync', wld)

        def ymm(tt, m2=m2):
            wait_tok(tt, ck[('yw', m2)])
            if m2 == 0:
                tt.wait_ge(vsem.h, ck[('xc', 31)])
            pv = ck.get(('goev', m2 - 1, 1))
            if pv is not None:
                tt.wait_ge(ssem.h, ck[('gos', m2 - 1, 1)])
            W0 = 8192 * (m2 % 2)
            for k in range(32):
                for mi in range(2):
                    for t in range(2):
                        ins = tt.matmul(sb[banks[mi * 2 + t]][:, 0:512],
                                        sb['wblk'][:, W0 + 256 * k + 128 * mi:
                                                   W0 + 256 * k + 128 * (mi + 1)],
                                        sb['big'][:, 1024 * k + 512 * t:1024 * k + 512 * (t + 1)],
                                        start=(k == 0), stop=(k == 31))
                tsem.inc(ins)
                ck[('ymm', m2, k)] = tsem.val
        pr.on('tensor', ymm)

        for mi in range(2):
            m = 2 * m2 + mi

            def yev(v, m2=m2, mi=mi, m=m):
                v.wait_ge(tsem.h, ck[('ymm', m2, 31)])
                if mi == 1:
                    v.wait_ge(ssem.h, ck[('gos', m2, 0)])
                pv = ck.get(('godma', m - 1))
                if pv is not None:
                    wait_tok(v, pv)
                for t in range(2):
                    v.scalar_tensor_tensor(sb['ev'][:, 512 * t:512 * (t + 1)],
                                           sb['brep'][:, 512 * t:512 * (t + 1)],
                                           sb['asc'][:, m:m + 1],
                                           sb[banks[mi * 2 + t]][:, 0:512],
                                           ALU.mult, ALU.add)
                    v.tensor_tensor(sb['ev'][:, 512 * t:512 * (t + 1)],
                                    sb['ev'][:, 512 * t:512 * (t + 1)],
                                    sb['arep'][:, 512 * t:512 * (t + 1)], ALU.mult)
                ck[('yev', m)] = vsem.val
                if mi == 1:
                    ck[('goev', m2, 1)] = vsem.val
            pr.on('vector', yev)

            def gelu(s, m2=m2, mi=mi, m=m):
                s.wait_ge(vsem.h, ck[('yev', m)])
                if m >= 1:
                    s.wait_ge(tsem.h, ck[('gsmm', m - 1)])
                wait_tok(s, ck['ld0'])
                s.activation(sb['go'][:, :], sb['ev'][:, :], AF.Gelu,
                             bias=sb['b2c'][:, m:m + 1], scale=1.0)
                s.activation(sb['sq'][:, :], sb['go'][:, :], AF.Square)
                ck[('gos', m2, mi)] = ssem.val
                ck[('gow', m)] = ssem.val
            pr.on('scalar', gelu)

            def gsmm(tt, m=m):
                tt.wait_ge(ssem.h, ck[('gow', m)])
                for t in range(2):
                    tt.matmul(sb['psE' if t == 0 else 'psF'][0:1, 0:512], sb['ones_f'][:, 0:1],
                              sb['go'][:, 512 * t:512 * (t + 1)], start=(m == 0), stop=(m == 31))
                    ins = tt.matmul(sb['psG' if t == 0 else 'psH'][0:1, 0:512],
                                    sb['ones_f'][:, 0:1],
                                    sb['sq'][:, 512 * t:512 * (t + 1)], start=(m == 0), stop=(m == 31))
                tsem.inc(ins)
                ck[('gsmm', m)] = tsem.val
            pr.on('tensor', gsmm)

            def godma(g, m=m):
                g.wait_ge(tsem.h, ck[('gsmm', m)])
                g.dma_start(out=go_s[128 * m:128 * (m + 1), :], in_=sb['go'][:, :])
                ck[('godma', m)] = pr.sw.token()
            pr.on('gpsimd', godma)

    def fstr(v):
        v.wait_ge(tsem.h, ck[('gsmm', 31)])
        R = sb['srows']
        v.tensor_copy(R[0:1, 0:512], sb['psE'][0:1, 0:512])
        v.tensor_copy(R[0:1, 512:1024], sb['psF'][0:1, 0:512])
        v.tensor_copy(R[0:1, TPC:TPC + 512], sb['psG'][0:1, 0:512])
        v.tensor_copy(R[0:1, TPC + 512:TPC + 1024], sb['psH'][0:1, 0:512])
        v.tensor_scalar(R[0:1, 2 * TPC:3 * TPC], R[0:1, 0:TPC], 1.0 / HID, None, ALU.mult)
        v.tensor_scalar(R[0:1, 3 * TPC:4 * TPC], R[0:1, TPC:2 * TPC], 1.0 / HID, None, ALU.mult)
        v.tensor_tensor(R[0:1, 4 * TPC:5 * TPC], R[0:1, 2 * TPC:3 * TPC],
                        R[0:1, 2 * TPC:3 * TPC], ALU.mult)
        v.tensor_scalar(R[0:1, 4 * TPC:5 * TPC], R[0:1, 4 * TPC:5 * TPC], -1.0, EPS,
                        ALU.mult, ALU.add)
        v.tensor_tensor(R[0:1, 3 * TPC:4 * TPC], R[0:1, 3 * TPC:4 * TPC],
                        R[0:1, 4 * TPC:5 * TPC], ALU.add)
        v.reciprocal(R[0:1, 4 * TPC:5 * TPC], R[0:1, 3 * TPC:4 * TPC])
        ck['fst1'] = vsem.val
    pr.on('vector', fstr)

    def fstr2(s):
        s.wait_ge(vsem.h, ck['fst1'])
        s.activation(sb['srows'][0:1, 5 * TPC:6 * TPC],
                     sb['srows'][0:1, 4 * TPC:5 * TPC], AF.Sqrt)
        ck['fst2'] = ssem.val
    pr.on('scalar', fstr2)

    def fstr3(v):
        v.wait_ge(ssem.h, ck['fst2'])
        R = sb['srows']
        v.tensor_tensor(R[0:1, 2 * TPC:3 * TPC], R[0:1, 2 * TPC:3 * TPC],
                        R[0:1, 5 * TPC:6 * TPC], ALU.mult)
        v.tensor_scalar(R[0:1, 2 * TPC:3 * TPC], R[0:1, 2 * TPC:3 * TPC], -1.0, None, ALU.mult)
        ck['fst3'] = vsem.val
    pr.on('vector', fstr3)

    for t in range(2):
        for which, rowoff, dst in [(0, 5 * TPC, 'arep'), (1, 2 * TPC, 'brep')]:
            def rmm2(tt, t=t, rowoff=rowoff, which=which):
                tt.wait_ge(vsem.h, ck['fst3'])
                pv = ck.get(('frepev', t, which - 1)) or (ck.get(('frepev', t - 1, 1)) if t else None)
                if pv is not None:
                    tt.wait_ge(vsem.h, pv)
                ins = tt.matmul(sb['psA'][:, 0:512], sb['ones_f'][0:1, 0:128],
                                sb['srows'][0:1, rowoff + 512 * t:rowoff + 512 * (t + 1)],
                                start=True, stop=True)
                tsem.inc(ins)
                ck[('frepmm', t, which)] = tsem.val
            pr.on('tensor', rmm2)

            def rev2(v, t=t, which=which, dst=dst):
                v.wait_ge(tsem.h, ck[('frepmm', t, which)])
                v.tensor_copy(sb[dst][:, 512 * t:512 * (t + 1)], sb['psA'][:, 0:512])
                ck[('frepev', t, which)] = vsem.val
            pr.on('vector', rev2)

    for f in range(32):
        def fld(q, f=f):
            if f == 0:
                q.wait_ge(vsem.h, ck[('frepev', 1, 1)])
            pv = ck.get(('r3d', f - 1))
            if pv is not None:
                q.wait_ge(vsem.h, ck[('r3u', f - 1)])
            q.dma_start(out=sb['r1t'][:, :], in_=r2[128 * f:128 * (f + 1), :])
            q.dma_start(out=sb['ld'][:, :], in_=go_s[128 * f:128 * (f + 1), :])
            ck[('fl', f)] = pr.hw.token()
        pr.on('sync', fld)

        def fap(v, f=f):
            wait_tok(v, ck[('fl', f)])
            pv = ck.get(('r3d', f - 1))
            if pv is not None:
                wait_tok(v, pv)
            v.tensor_tensor(sb['r2t'][:, :], sb['ld'][:, :], sb['arep'][:, :], ALU.mult)
            v.tensor_tensor(sb['r2t'][:, :], sb['r2t'][:, :], sb['brep'][:, :], ALU.add)
            v.tensor_tensor(sb['r2t'][:, :], sb['r2t'][:, :], sb['r1t'][:, :], ALU.add)
            ck[('r3u', f)] = vsem.val
        pr.on('vector', fap)

        def fdma(g, f=f):
            g.wait_ge(vsem.h, ck[('r3u', f)])
            g.dma_start(out=o_r3[128 * f:128 * (f + 1), :], in_=sb['r2t'][:, :])
            ck[('r3d', f)] = pr.sw.token()
        pr.on('gpsimd', fdma)



def build_attn1():
    nc = bass.Bass("TRN2", target_bir_lowering=False, debug=False)
    r2 = nc.dram_tensor("r2", [HID, TPC], F32, kind="ExternalInput").ap()
    attnT = nc.dram_tensor("attnT", [8 * HID, 512], BF16, kind="ExternalInput").ap()
    asum = nc.dram_tensor("asum", [1, HID], F32, kind="ExternalInput").ap()
    b2 = nc.dram_tensor("b2", [1, HID], F32, kind="ExternalInput").ap()
    o_r3 = nc.dram_tensor("o_r3", [HID, TPC], F32, kind="ExternalOutput").ap()
    go_s = nc.dram_tensor("go_s", [HID, TPC], F32, kind="Internal").ap()

    tensors = CONSTS + ATTN1_TENSORS

    with nc.Block() as block:
        hw_sems = [nc.semaphore(f"hw{i}").__enter__() for i in range(8)]
        sw_sems = [nc.semaphore(f"sw{i}").__enter__() for i in range(8)]
        th = nc.semaphore("tsem").__enter__()
        vh = nc.semaphore("vsem").__enter__()
        sh = nc.semaphore("ssem").__enter__()
        gh = nc.semaphore("gsem").__enter__()
        sb = alloc(nc, tensors)
        pr = Prog(nc, block, hw_sems, sw_sems, th, vh, sh, gh)
        tsem, vsem, ssem, gsem = pr.tsem, pr.vsem, pr.ssem, pr.gsem
        emit_consts(pr, sb)
        ck = {}
        emit_attn1_body(nc, pr, sb, ck, r2=r2, attnT=attnT, asum=asum, b2=b2, o_r3=o_r3, go_s=go_s)
        def fin(g):
            wait_tok(g, pr.dtok())
            g.wait_ge(vsem.h, vsem.val)
            g.wait_ge(ssem.h, ssem.val)
            g.wait_ge(tsem.h, tsem.val)
        pr.on('gpsimd', fin)

    return nc


def attn1_in_maps(inp, prep, r2s):
    attnT = np.concatenate([np.asarray(prep[c]['o_attnT']) for c in range(NCORE)], axis=0)
    asum = np.concatenate([np.asarray(prep[c]['o_asum']) for c in range(NCORE)], axis=1)
    b2 = np.ascontiguousarray(inp['L1_attn_b2'].reshape(1, HID))
    return [dict(r2=np.asarray(r2s[c]), attnT=attnT, asum=asum, b2=b2) for c in range(NCORE)]


FFN1_TENSORS = [
    ('big', [128, 32768], BF16, 'sbuf'),
    ('labT', [128, 4096], BF16, 'sbuf'),
    ('wrT', [128, 256], BF16, 'sbuf'),
    ('rep16', [8, 128], BF16, 'sbuf'),
    ('wblk', [128, 8192], BF16, 'sbuf'),
    ('onesr', [1, 512], F32, 'sbuf'),
    ('ld2', [128, 1024], F32, 'sbuf'),
    ('ld', [128, 1024], F32, 'sbuf'),
    ('sq', [128, 1024], F32, 'sbuf'),
    ('ev', [128, 1024], F32, 'sbuf'),
    ('hl', [128, 1024], F32, 'sbuf'),
    ('hw', [128, 1024], BF16, 'sbuf'),
    ('wfm', [8, 1024], BF16, 'sbuf'),
    ('ltm', [128, 64], F32, 'sbuf'),
    ('lsc', [128, 32], F32, 'sbuf'),
    ('acol', [128, 8], F32, 'sbuf'),
    ('wcols', [128, 8], F32, 'sbuf'),
    ('arep', [128, 1024], F32, 'sbuf'),
    ('brep', [128, 1024], F32, 'sbuf'),
    ('r1t', [128, 1024], F32, 'sbuf'),
    ('r2t', [128, 1024], F32, 'sbuf'),
    ('tto', [128, 128], F32, 'sbuf'),
    ('srows', [1, 6 * TPC], F32, 'sbuf'),
    ('bbc', [128, 32], F32, 'sbuf'),
    ('wbsc', [128, 32], F32, 'sbuf'),
    ('rsr', [1, 8], F32, 'sbuf'),
    ('acsr', [1, 128], F32, 'sbuf'),
    ('wrsb', [8, 4096], BF16, 'sbuf'),
    ('bcbs', [128, 4096], BF16, 'sbuf'),
    ('psA', [128, 512], F32, 'psum'),
    ('psB', [128, 512], F32, 'psum'),
    ('psC', [128, 512], F32, 'psum'),
    ('psD', [128, 512], F32, 'psum'),
    ('psE', [128, 512], F32, 'psum'),
    ('psF', [128, 512], F32, 'psum'),
    ('psG', [128, 512], F32, 'psum'),
    ('psH', [128, 512], F32, 'psum'),
]

def emit_ffn1_body(nc, pr, sb, ck, *, r3, wbb, wbsum, wrb, rsum, lab, acs, bcb, bb, o_out, fo_s):
    tsem, vsem, ssem, gsem = pr.tsem, pr.vsem, pr.ssem, pr.gsem

    def consts2(g):
        g.memset(sb['rep16'][:, :], 1.0)
        g.affine_select(sb['rep16'][:, :], sb['rep16'][:, :], [[1, 8], [0, 16]],
                        ALU.is_equal, 0.0, base=0, channel_multiplier=-1)
        g.memset(sb['onesr'][:, :], 1.0)
        ck['c2'] = gsem.val
    pr.on('gpsimd', consts2)

    def ld0(q):
        q.dma_start(out=sb['bbc'][:, :],
                    in_=bass.AP(bb.tensor, 0, [[1, 128], [128, 32]]),
                    allow_slow_non_contiguous=True)
        q.dma_start(out=sb['wbsc'][:, :],
                    in_=bass.AP(wbsum.tensor, 0, [[1, 128], [128, 32]]),
                    allow_slow_non_contiguous=True)
        q.dma_start(out=sb['rsr'][0:1, :], in_=rsum)
        q.dma_start(out=sb['acsr'][0:1, :], in_=acs)
        q.dma_start(out=sb['wrsb'][:, :], in_=wrb)
        q.dma_start(out=sb['bcbs'][:, :], in_=bcb)
        ck['ld0'] = pr.hw.token()
    pr.on('sync', ld0)

    for f in range(32):
        def ldr(q, f=f):
            pv = ck.get(('stmm', f - 2))
            if pv is not None:
                q.wait_ge(tsem.h, pv)
                q.wait_ge(vsem.h, ck[('xc', f - 2)])
            q.dma_start(out=sb['ld' if f % 2 == 0 else 'ld2'][:, :],
                        in_=r3[128 * f:128 * (f + 1), :])
            ck[('rl', f)] = pr.hw.token()
        pr.on('sync', ldr)

        def casx(v, f=f):
            wait_tok(v, ck[('rl', f)])
            v.tensor_copy(sb['big'][:, 1024 * f:1024 * (f + 1)],
                          sb['ld' if f % 2 == 0 else 'ld2'][:, :])
            ck[('xc', f)] = vsem.val
        pr.on('vector', casx)

        def sqp(s, f=f):
            wait_tok(s, ck[('rl', f)])
            if f >= 1:
                s.wait_ge(tsem.h, ck[('stmm', f - 1)])
            s.activation(sb['sq'][:, :], sb['ld' if f % 2 == 0 else 'ld2'][:, :], AF.Square)
            ck[('rsq', f)] = ssem.val
        pr.on('scalar', sqp)

        def stmm(tt, f=f):
            tt.wait_ge(ssem.h, ck[('rsq', f)])
            tt.wait_ge(gsem.h, gsem.val)
            for t in range(2):
                tt.matmul(sb['psE' if t == 0 else 'psF'][0:1, 0:512], sb['ones_f'][:, 0:1],
                          sb['ld' if f % 2 == 0 else 'ld2'][:, 512 * t:512 * (t + 1)],
                          start=(f == 0), stop=(f == 31))
                ins = tt.matmul(sb['psG' if t == 0 else 'psH'][0:1, 0:512], sb['ones_f'][:, 0:1],
                                sb['sq'][:, 512 * t:512 * (t + 1)], start=(f == 0), stop=(f == 31))
            tsem.inc(ins)
            ck[('stmm', f)] = tsem.val
        pr.on('tensor', stmm)

    def strows(v):
        v.wait_ge(tsem.h, ck[('stmm', 31)])
        R = sb['srows']
        v.tensor_copy(R[0:1, 0:512], sb['psE'][0:1, 0:512])
        v.tensor_copy(R[0:1, 512:1024], sb['psF'][0:1, 0:512])
        v.tensor_copy(R[0:1, TPC:TPC + 512], sb['psG'][0:1, 0:512])
        v.tensor_copy(R[0:1, TPC + 512:TPC + 1024], sb['psH'][0:1, 0:512])
        v.tensor_scalar(R[0:1, 2 * TPC:3 * TPC], R[0:1, 0:TPC], 1.0 / HID, None, ALU.mult)
        v.tensor_scalar(R[0:1, 3 * TPC:4 * TPC], R[0:1, TPC:2 * TPC], 1.0 / HID, None, ALU.mult)
        v.tensor_tensor(R[0:1, 4 * TPC:5 * TPC], R[0:1, 2 * TPC:3 * TPC],
                        R[0:1, 2 * TPC:3 * TPC], ALU.mult)
        v.tensor_scalar(R[0:1, 4 * TPC:5 * TPC], R[0:1, 4 * TPC:5 * TPC], -1.0, EPS,
                        ALU.mult, ALU.add)
        v.tensor_tensor(R[0:1, 3 * TPC:4 * TPC], R[0:1, 3 * TPC:4 * TPC],
                        R[0:1, 4 * TPC:5 * TPC], ALU.add)
        v.reciprocal(R[0:1, 4 * TPC:5 * TPC], R[0:1, 3 * TPC:4 * TPC])
        v.tensor_scalar(R[0:1, 2 * TPC:3 * TPC], R[0:1, 2 * TPC:3 * TPC], -1.0, None, ALU.mult)
        ck['st1'] = vsem.val
    pr.on('vector', strows)

    def strows2(s):
        s.wait_ge(vsem.h, ck['st1'])
        s.activation(sb['srows'][0:1, 5 * TPC:6 * TPC],
                     sb['srows'][0:1, 4 * TPC:5 * TPC], AF.Sqrt)
        ck['st2'] = ssem.val
    pr.on('scalar', strows2)

    for t in range(2):
        def rmm(tt, t=t):
            tt.wait_ge(ssem.h, ck['st2'])
            pv = ck.get(('repev2', t - 1))
            if pv is not None:
                tt.wait_ge(vsem.h, pv)
            ins = tt.matmul(sb['psA'][:, 0:512], sb['ones_f'][0:1, 0:128],
                            sb['srows'][0:1, 5 * TPC + 512 * t:5 * TPC + 512 * (t + 1)],
                            start=True, stop=True)
            tsem.inc(ins)
            ck[('repmm', t)] = tsem.val
        pr.on('tensor', rmm)

        def rev(v, t=t):
            v.wait_ge(tsem.h, ck[('repmm', t)])
            v.tensor_copy(sb['arep'][:, 512 * t:512 * (t + 1)], sb['psA'][:, 0:512])
            ck[('repev', t)] = vsem.val
        pr.on('vector', rev)

        def rmm2(tt, t=t):
            tt.wait_ge(vsem.h, ck[('repev', t)])
            ins = tt.matmul(sb['psA'][:, 0:512], sb['ones_f'][0:1, 0:128],
                            sb['srows'][0:1, 2 * TPC + 512 * t:2 * TPC + 512 * (t + 1)],
                            start=True, stop=True)
            tsem.inc(ins)
            ck[('repmm2', t)] = tsem.val
        pr.on('tensor', rmm2)

        def rev2(v, t=t):
            v.wait_ge(tsem.h, ck[('repmm2', t)])
            v.tensor_copy(sb['brep'][:, 512 * t:512 * (t + 1)], sb['psA'][:, 0:512])
            ck[('repev2', t)] = vsem.val
        pr.on('vector', rev2)

    for t8 in range(8):
        def acl(tt, t8=t8):
            tt.wait_ge(ssem.h, ck['st2'])
            pv = ck.get(('aclv', t8 - 1))
            if pv is not None:
                tt.wait_ge(vsem.h, pv)
            else:
                tt.wait_ge(vsem.h, ck.get(('repev', 1), 0))
            ins = tt.transpose(sb['psB'][0:128, 0:1],
                               sb['srows'][0:1, 5 * TPC + 128 * t8:5 * TPC + 128 * (t8 + 1)],
                               sb['idf'][0:1, 0:1])
            tsem.inc(ins)
            ck[('aclt', t8)] = tsem.val
        pr.on('tensor', acl)

        def aclv(v, t8=t8):
            v.wait_ge(tsem.h, ck[('aclt', t8)])
            v.tensor_copy(sb['acol'][:, t8:t8 + 1], sb['psB'][0:128, 0:1])
            ck[('aclv', t8)] = vsem.val
        pr.on('vector', aclv)

    def wrl(q):
        wait_tok(q, ck['ld0'])
        with nc.allow_non_contiguous_dma(reason="tiny 8-row router transpose"):
            for k in range(32):
                q.dma_start_transpose(sb['wrT'][:, 8 * k:8 * (k + 1)],
                                      wrb[0:8, 128 * k:128 * (k + 1)])
        for k in range(32):
            q.dma_start_transpose(sb['labT'][:, 128 * k:128 * (k + 1)],
                                  lab[0:128, 128 * k:128 * (k + 1)])
        ck['wrl'] = pr.hw.token()
    pr.on('sync', wrl)

    def lgmm(tt):
        tt.wait_ge(vsem.h, ck[('xc', 31)])
        wait_tok(tt, ck['wrl'])
        tt.wait_ge(vsem.h, ck[('aclv', 7)])
        for k in range(32):
            for t in range(2):
                ins = tt.matmul(sb['psA' if t == 0 else 'psB'][0:8, 0:512],
                                sb['wrT'][:, 8 * k:8 * (k + 1)],
                                sb['big'][:, 1024 * k + 512 * t:1024 * k + 512 * (t + 1)],
                                start=(k == 0), stop=False)
        for t in range(2):
            ins = tt.matmul(sb['psA' if t == 0 else 'psB'][0:8, 0:512],
                            sb['rsr'][0:1, 0:8],
                            sb['srows'][0:1, 2 * TPC + 512 * t:2 * TPC + 512 * (t + 1)],
                            start=False, stop=True)
        tsem.inc(ins)
        ck['lg'] = tsem.val
    pr.on('tensor', lgmm)

    def lgev(v):
        v.wait_ge(tsem.h, ck['lg'])
        v.tensor_copy(sb['ld'][0:8, 0:512], sb['psA'][0:8, 0:512])
        v.tensor_copy(sb['ld'][0:8, 512:1024], sb['psB'][0:8, 0:512])
        ck['lgev'] = vsem.val
    pr.on('vector', lgev)

    def hmm(tt):
        tt.wait_ge(vsem.h, ck['lgev'])
        for k in range(32):
            for t in range(2):
                ins = tt.matmul(sb['psC' if t == 0 else 'psD'][:, 0:512],
                                sb['labT'][:, 128 * k:128 * (k + 1)],
                                sb['big'][:, 1024 * k + 512 * t:1024 * k + 512 * (t + 1)],
                                start=(k == 0), stop=False)
        for t in range(2):
            ins = tt.matmul(sb['psC' if t == 0 else 'psD'][:, 0:512],
                            sb['acsr'][0:1, 0:128],
                            sb['srows'][0:1, 2 * TPC + 512 * t:2 * TPC + 512 * (t + 1)],
                            start=False, stop=True)
        tsem.inc(ins)
        ck['hmm'] = tsem.val
    pr.on('tensor', hmm)

    def hev(v):
        v.wait_ge(tsem.h, ck['hmm'])
        for t in range(2):
            v.tensor_tensor(sb['hl'][:, 512 * t:512 * (t + 1)],
                            sb['psC' if t == 0 else 'psD'][:, 0:512],
                            sb['arep'][:, 512 * t:512 * (t + 1)], ALU.mult)
        ck['hev'] = vsem.val
    pr.on('vector', hev)

    for t8 in range(8):
        def ltt(tt, t8=t8):
            tt.wait_ge(vsem.h, ck['hev'] if t8 == 0 else ck[('w_fmv', t8 - 1)])
            ins = tt.transpose(sb['psT'].bitcast(F32)[0:128, 0:8],
                               sb['ld'][0:8, 128 * t8:128 * (t8 + 1)],
                               sb['idf'][0:8, 0:8]) if False else None
            ins = tt.transpose(sb['psG'][0:128, 0:8],
                               sb['ld'][0:8, 128 * t8:128 * (t8 + 1)],
                               sb['idf'][0:8, 0:8])
            tsem.inc(ins)
            ck[('ltt', t8)] = tsem.val
        pr.on('tensor', ltt)

        def smax(v, t8=t8):
            v.wait_ge(tsem.h, ck[('ltt', t8)])
            L = sb['lsc']
            v.tensor_scalar(L[:, 0:8], sb['psG'][0:128, 0:8], sb['acol'][:, t8:t8 + 1],
                            None, ALU.mult)
            v.reduce_max(L[:, 8:9], L[:, 0:8], AXX)
            v.tensor_scalar(L[:, 0:8], L[:, 0:8], L[:, 8:9], None, ALU.subtract)
            ck[('sm1', t8)] = vsem.val
        pr.on('vector', smax)

        def sexp(s, t8=t8):
            s.wait_ge(vsem.h, ck[('sm1', t8)])
            s.activation(sb['lsc'][:, 0:8], sb['lsc'][:, 0:8], AF.Exp)
            ck[('sm2', t8)] = ssem.val
        pr.on('scalar', sexp)

        def top2(v, t8=t8):
            v.wait_ge(ssem.h, ck[('sm2', t8)])
            L = sb['lsc']
            v.reduce_sum(L[:, 8:9], L[:, 0:8], AXX)
            v.reciprocal(L[:, 9:10], L[:, 8:9])
            v.tensor_scalar(L[:, 0:8], L[:, 0:8], L[:, 9:10], None, ALU.mult)
            v.reduce_max(L[:, 10:11], L[:, 0:8], AXX)
            v.tensor_scalar(L[:, 16:24], L[:, 0:8], L[:, 10:11], None, ALU.is_equal)
            v.scalar_tensor_tensor(L[:, 24:32], L[:, 16:24], -2.0, L[:, 0:8],
                                   ALU.mult, ALU.add)
            v.reduce_max(L[:, 11:12], L[:, 24:32], AXX)
            v.tensor_scalar(L[:, 16:24], L[:, 0:8], L[:, 11:12], None, ALU.is_ge)
            v.tensor_tensor(L[:, 12:13], L[:, 10:11], L[:, 11:12], ALU.add)
            v.reciprocal(L[:, 13:14], L[:, 12:13])
            v.tensor_tensor(L[:, 24:32], L[:, 0:8], L[:, 16:24], ALU.mult)
            v.tensor_scalar(sb['wcols'][:, 0:8], L[:, 24:32], L[:, 13:14], None, ALU.mult)
            ck[('w', t8)] = vsem.val
        pr.on('vector', top2)

        def wtr(tt, t8=t8):
            tt.wait_ge(vsem.h, ck[('w', t8)])
            ins = tt.transpose(sb['psG'][0:8, 0:128], sb['wcols'][:, 0:8],
                               sb['idf'][:, :])
            tsem.inc(ins)
            ck[('wtr', t8)] = tsem.val
        pr.on('tensor', wtr)

        def wfmv(v, t8=t8):
            v.wait_ge(tsem.h, ck[('wtr', t8)])
            v.tensor_copy(sb['wfm'][0:8, 128 * t8:128 * (t8 + 1)], sb['psG'][0:8, 0:128])
            ck[('w_fmv', t8)] = vsem.val
        pr.on('vector', wfmv)

    for t in range(2):
        def wexp(tt, t=t):
            tt.wait_ge(vsem.h, ck[('w_fmv', 7)])
            tt.wait_ge(gsem.h, gsem.val)
            pv = ck.get(('hwv', t - 1))
            if pv is not None:
                tt.wait_ge(vsem.h, pv)
            ins = tt.matmul(sb['psE'][:, 0:512], sb['rep16'][:, :],
                            sb['wfm'][0:8, 512 * t:512 * (t + 1)], start=True, stop=True)
            tsem.inc(ins)
            ck[('wx', t)] = tsem.val
        pr.on('tensor', wexp)

        def hwv(v, t=t):
            v.wait_ge(tsem.h, ck[('wx', t)])
            v.tensor_tensor(sb['hw'][:, 512 * t:512 * (t + 1)], sb['hl'][:, 512 * t:512 * (t + 1)],
                            sb['psE'][:, 0:512], ALU.mult)
            ck[('hwv', t)] = vsem.val
        pr.on('vector', hwv)

    for m in range(32):
        def wld(q, m=m):
            if m == 0:
                q.wait_ge(vsem.h, ck[('hwv', 1)])
            pv = ck.get(('bmm', m - 2, 31))
            if pv is not None:
                q.wait_ge(tsem.h, pv)
            q.dma_start(
                out=sb['wblk'][:, 4096 * (m % 2):4096 * (m % 2 + 1)]
                .rearrange("p (k n) -> p k n", k=32),
                in_=wbb[(m // 4) * HID:(m // 4 + 1) * HID,
                        128 * (m % 4):128 * (m % 4) + 128]
                .rearrange("(k p) n -> p k n", p=128))
            ck[('bw', m)] = pr.hw.token()
        pr.on('sync', wld)

        def bmm(tt, m=m):
            wait_tok(tt, ck[('bw', m)])
            pv2 = ck.get(('bev', m - 1))
            if pv2 is not None:
                tt.wait_ge(vsem.h, pv2)
            W0 = 4096 * (m % 2)
            for k in range(32):
                for t in range(2):
                    ins = tt.matmul(sb['psA' if t == 0 else 'psB'][:, 0:512],
                                    sb['wblk'][:, W0 + 128 * k:W0 + 128 * (k + 1)],
                                    sb['big'][:, 1024 * k + 512 * t:1024 * k + 512 * (t + 1)],
                                    start=(k == 0), stop=(k == 31))
                tsem.inc(ins)
                ck[('bmm', m, k)] = tsem.val
        pr.on('tensor', bmm)

        def bfold(tt, m=m):
            for t in range(2):
                ins = tt.matmul(sb['psC' if t == 0 else 'psD'][:, 0:512],
                                sb['bcbs'][:, 128 * m:128 * (m + 1)],
                                sb['hw'][:, 512 * t:512 * (t + 1)], start=True, stop=True)
            tsem.inc(ins)
            ck[('bfold', m)] = tsem.val
        pr.on('tensor', bfold)

        def bev(v, m=m):
            v.wait_ge(tsem.h, ck[('bfold', m)])
            pv = ck.get(('fodma', m - 1))
            if pv is not None:
                wait_tok(v, pv)
            for t in range(2):
                v.scalar_tensor_tensor(sb['ev'][:, 512 * t:512 * (t + 1)],
                                       sb['brep'][:, 512 * t:512 * (t + 1)],
                                       sb['wbsc'][:, m:m + 1],
                                       sb['psA' if t == 0 else 'psB'][:, 0:512],
                                       ALU.mult, ALU.add)
                v.tensor_tensor(sb['ev'][:, 512 * t:512 * (t + 1)],
                                sb['ev'][:, 512 * t:512 * (t + 1)],
                                sb['arep'][:, 512 * t:512 * (t + 1)], ALU.mult)
                v.scalar_tensor_tensor(sb['ev'][:, 512 * t:512 * (t + 1)],
                                       sb['psC' if t == 0 else 'psD'][:, 0:512],
                                       sb['bbc'][:, m:m + 1],
                                       sb['ev'][:, 512 * t:512 * (t + 1)],
                                       ALU.add, ALU.add)
            ck[('bev', m)] = vsem.val
        pr.on('vector', bev)

        def fsq(s, m=m):
            s.wait_ge(vsem.h, ck[('bev', m)])
            s.activation(sb['sq'][:, :], sb['ev'][:, :], AF.Square)
            ck[('fsq', m)] = ssem.val
        pr.on('scalar', fsq)

        def fsmm(tt, m=m):
            tt.wait_ge(ssem.h, ck[('fsq', m)])
            for t in range(2):
                tt.matmul(sb['psE' if t == 0 else 'psF'][0:1, 0:512], sb['ones_f'][:, 0:1],
                          sb['ev'][:, 512 * t:512 * (t + 1)], start=(m == 0), stop=(m == 31))
                ins = tt.matmul(sb['psG' if t == 0 else 'psH'][0:1, 0:512], sb['ones_f'][:, 0:1],
                                sb['sq'][:, 512 * t:512 * (t + 1)], start=(m == 0), stop=(m == 31))
            tsem.inc(ins)
            ck[('fsmm', m, 1)] = tsem.val
        pr.on('tensor', fsmm)

        def fodma(g, m=m):
            g.wait_ge(tsem.h, ck[('fsmm', m, 1)])
            g.dma_start(out=fo_s[128 * m:128 * (m + 1), :], in_=sb['ev'][:, :])
            ck[('fodma', m)] = pr.sw.token()
        pr.on('gpsimd', fodma)

    def fstr(v):
        v.wait_ge(tsem.h, ck[('fsmm', 31, 1)])
        R = sb['srows']
        v.tensor_copy(R[0:1, 0:512], sb['psE'][0:1, 0:512])
        v.tensor_copy(R[0:1, 512:1024], sb['psF'][0:1, 0:512])
        v.tensor_copy(R[0:1, TPC:TPC + 512], sb['psG'][0:1, 0:512])
        v.tensor_copy(R[0:1, TPC + 512:TPC + 1024], sb['psH'][0:1, 0:512])
        v.tensor_scalar(R[0:1, 2 * TPC:3 * TPC], R[0:1, 0:TPC], 1.0 / HID, None, ALU.mult)
        v.tensor_scalar(R[0:1, 3 * TPC:4 * TPC], R[0:1, TPC:2 * TPC], 1.0 / HID, None, ALU.mult)
        v.tensor_tensor(R[0:1, 4 * TPC:5 * TPC], R[0:1, 2 * TPC:3 * TPC],
                        R[0:1, 2 * TPC:3 * TPC], ALU.mult)
        v.tensor_scalar(R[0:1, 4 * TPC:5 * TPC], R[0:1, 4 * TPC:5 * TPC], -1.0, EPS,
                        ALU.mult, ALU.add)
        v.tensor_tensor(R[0:1, 3 * TPC:4 * TPC], R[0:1, 3 * TPC:4 * TPC],
                        R[0:1, 4 * TPC:5 * TPC], ALU.add)
        v.reciprocal(R[0:1, 4 * TPC:5 * TPC], R[0:1, 3 * TPC:4 * TPC])
        ck['fst1'] = vsem.val
    pr.on('vector', fstr)

    def fstr2(s):
        s.wait_ge(vsem.h, ck['fst1'])
        s.activation(sb['srows'][0:1, 5 * TPC:6 * TPC],
                     sb['srows'][0:1, 4 * TPC:5 * TPC], AF.Sqrt)
        ck['fst2'] = ssem.val
    pr.on('scalar', fstr2)

    def fstr3(v):
        v.wait_ge(ssem.h, ck['fst2'])
        R = sb['srows']
        v.tensor_tensor(R[0:1, 2 * TPC:3 * TPC], R[0:1, 2 * TPC:3 * TPC],
                        R[0:1, 5 * TPC:6 * TPC], ALU.mult)
        v.tensor_scalar(R[0:1, 2 * TPC:3 * TPC], R[0:1, 2 * TPC:3 * TPC], -1.0, None, ALU.mult)
        ck['fst3'] = vsem.val
    pr.on('vector', fstr3)

    for t in range(2):
        for which, rowoff, dst in [(0, 5 * TPC, 'arep'), (1, 2 * TPC, 'brep')]:
            def rmm2(tt, t=t, rowoff=rowoff, which=which):
                tt.wait_ge(vsem.h, ck['fst3'])
                pv = ck.get(('frepev', t, which - 1)) or (ck.get(('frepev', t - 1, 1)) if t else None)
                if pv is not None:
                    tt.wait_ge(vsem.h, pv)
                ins = tt.matmul(sb['psA'][:, 0:512], sb['ones_f'][0:1, 0:128],
                                sb['srows'][0:1, rowoff + 512 * t:rowoff + 512 * (t + 1)],
                                start=True, stop=True)
                tsem.inc(ins)
                ck[('frepmm', t, which)] = tsem.val
            pr.on('tensor', rmm2)

            def rev2(v, t=t, which=which, dst=dst):
                v.wait_ge(tsem.h, ck[('frepmm', t, which)])
                v.tensor_copy(sb[dst][:, 512 * t:512 * (t + 1)], sb['psA'][:, 0:512])
                ck[('frepev', t, which)] = vsem.val
            pr.on('vector', rev2)

    for f in range(32):
        def fld(q, f=f):
            if f == 0:
                q.wait_ge(vsem.h, ck[('frepev', 1, 1)])
            else:
                q.wait_ge(vsem.h, ck[('ru', f - 1)])
            q.dma_start(out=sb['r1t'][:, :], in_=r3[128 * f:128 * (f + 1), :])
            q.dma_start(out=sb['ld'][:, :], in_=fo_s[128 * f:128 * (f + 1), :])
            ck[('fl', f)] = pr.hw.token()
        pr.on('sync', fld)

        def fap(v, f=f):
            wait_tok(v, ck[('fl', f)])
            pv = ck.get(('otr', f - 1, 7))
            if pv is not None:
                v.wait_ge(tsem.h, pv)
            v.tensor_tensor(sb['r2t'][:, :], sb['ld'][:, :], sb['arep'][:, :], ALU.mult)
            v.tensor_tensor(sb['r2t'][:, :], sb['r2t'][:, :], sb['brep'][:, :], ALU.add)
            v.tensor_tensor(sb['r2t'][:, :], sb['r2t'][:, :], sb['r1t'][:, :], ALU.add)
            ck[('ru', f)] = vsem.val
        pr.on('vector', fap)

        for t8 in range(8):
            def otr(tt, f=f, t8=t8):
                tt.wait_ge(vsem.h, ck[('ru', f)])
                pv = ck.get(('ote', f, t8 - 1)) or (ck.get(('ote', f - 1, 7)) if f else None)
                if pv is not None:
                    tt.wait_ge(vsem.h, pv)
                ins = tt.transpose(sb['psB'][:, 0:128],
                                   sb['r2t'][:, 128 * t8:128 * (t8 + 1)], sb['idf'][:, :])
                tsem.inc(ins)
                ck[('otr', f, t8)] = tsem.val
            pr.on('tensor', otr)

            def ote(v, f=f, t8=t8):
                v.wait_ge(tsem.h, ck[('otr', f, t8)])
                pv = ck.get(('od', f, t8 - 1)) or (ck.get(('od', f - 1, 7)) if f else None)
                if pv is not None:
                    wait_tok(v, pv)
                v.tensor_copy(sb['tto'][:, :], sb['psB'][:, 0:128])
                ck[('ote', f, t8)] = vsem.val
            pr.on('vector', ote)

            def od(g, f=f, t8=t8):
                g.wait_ge(vsem.h, ck[('ote', f, t8)])
                g.dma_start(out=o_out[128 * t8:128 * (t8 + 1), 128 * f:128 * (f + 1)],
                            in_=sb['tto'][:, :])
                ck[('od', f, t8)] = pr.sw.token()
            pr.on('gpsimd', od)



def build_ffn1():
    nc = bass.Bass("TRN2", target_bir_lowering=False, debug=False)
    r3 = nc.dram_tensor("r3", [HID, TPC], F32, kind="ExternalInput").ap()
    wbb = nc.dram_tensor("wbb", [8 * HID, 512], BF16, kind="ExternalInput").ap()
    wbsum = nc.dram_tensor("wbsum", [1, HID], F32, kind="ExternalInput").ap()
    wrb = nc.dram_tensor("wrb", [8, HID], BF16, kind="ExternalInput").ap()
    rsum = nc.dram_tensor("rsum", [1, 8], F32, kind="ExternalInput").ap()
    lab = nc.dram_tensor("lab", [128, HID], BF16, kind="ExternalInput").ap()
    acs = nc.dram_tensor("acs", [1, 128], F32, kind="ExternalInput").ap()
    bcb = nc.dram_tensor("bcb", [128, HID], BF16, kind="ExternalInput").ap()
    bb = nc.dram_tensor("bb", [1, HID], F32, kind="ExternalInput").ap()
    o_out = nc.dram_tensor("o_out", [TPC, HID], F32, kind="ExternalOutput").ap()
    fo_s = nc.dram_tensor("fo_s", [HID, TPC], F32, kind="Internal").ap()

    tensors = CONSTS + FFN1_TENSORS

    with nc.Block() as block:
        hw_sems = [nc.semaphore(f"hw{i}").__enter__() for i in range(8)]
        sw_sems = [nc.semaphore(f"sw{i}").__enter__() for i in range(8)]
        th = nc.semaphore("tsem").__enter__()
        vh = nc.semaphore("vsem").__enter__()
        sh = nc.semaphore("ssem").__enter__()
        gh = nc.semaphore("gsem").__enter__()
        sb = alloc(nc, tensors)
        pr = Prog(nc, block, hw_sems, sw_sems, th, vh, sh, gh)
        tsem, vsem, ssem, gsem = pr.tsem, pr.vsem, pr.ssem, pr.gsem
        emit_consts(pr, sb)
        ck = {}
        emit_ffn1_body(nc, pr, sb, ck, r3=r3, wbb=wbb, wbsum=wbsum, wrb=wrb, rsum=rsum, lab=lab, acs=acs, bcb=bcb, bb=bb, o_out=o_out, fo_s=fo_s)
        def fin(g):
            wait_tok(g, pr.dtok())
            g.wait_ge(vsem.h, vsem.val)
            g.wait_ge(ssem.h, ssem.val)
            g.wait_ge(tsem.h, tsem.val)
        pr.on('gpsimd', fin)

    return nc


def ffn1_in_maps(inp, prep, r3s):
    wbbf = np.concatenate([np.asarray(prep[c]['o_wbb']) for c in range(NCORE)], axis=0)
    wbs = np.concatenate([np.asarray(prep[c]['o_wbsum']) for c in range(NCORE)], axis=1)
    return [dict(r3=np.asarray(r3s[c]), wbb=wbbf, wbsum=wbs,
                 wrb=np.asarray(prep[0]['o_wrb']), rsum=np.asarray(prep[0]['o_rsum']),
                 lab=np.asarray(prep[0]['o_lab']), acs=np.asarray(prep[0]['o_acs']),
                 bcb=np.asarray(prep[0]['o_bcb']),
                 bb=np.ascontiguousarray(inp['L1_ffn_b_base'].reshape(1, HID)))
            for c in range(NCORE)]


def kernel(**inputs):
    for nm in ('gate', 'up', 'down'):
        gidx = np.asarray(inputs[f'L0_ffn_{nm}_g_idx']).reshape(-1)
        assert (gidx == np.arange(gidx.size) // 128).all(), f'{nm} g_idx not contiguous-grouped'
    return kernel_fused(**inputs)


def kernel_unfused(**inputs):
    cores = list(range(NCORE))
    prep = run_bass_kernel_spmd(build_prep(), prep_in_maps(inputs), core_ids=cores).results
    prep = [{k: np.asarray(v) for k, v in r.items()} for r in prep]

    res = run_bass_kernel_spmd(build_attn0(), attn0_in_maps(inputs, prep), core_ids=cores)
    ghats = [np.asarray(res.results[c]['o_ghat']) for c in cores]

    res = run_bass_kernel_spmd(build_outproj(), outproj_in_maps(inputs, prep, ghats),
                               core_ids=cores)
    r1s = [np.asarray(res.results[c]['o_r1']) for c in cores]

    res = run_bass_kernel_spmd(build_ffn0(), ffn0_in_maps(prep, r1s), core_ids=cores)
    r2s = [np.asarray(res.results[c]['o_r2']) for c in cores]

    res = run_bass_kernel_spmd(build_attn1(), attn1_in_maps(inputs, prep, r2s), core_ids=cores)
    r3s = [np.asarray(res.results[c]['o_r3']) for c in cores]

    res = run_bass_kernel_spmd(build_ffn1(), ffn1_in_maps(inputs, prep, r3s), core_ids=cores)
    outs = [np.asarray(res.results[c]['o_out']) for c in cores]

    return np.concatenate(outs, axis=0).reshape(2, 4096, HID).astype(np.float32)


class APShim:

    def __init__(self, fn):
        self._fn = fn

    def __getitem__(self, idx):
        return self._fn(idx)


GU_C = {}
GU_L = {}
for _c in range(NCORE):
    for _l in range(GU_BLOCKS[_c]):
        GU_C[GU_B0[_c] + _l] = _c
        GU_L[GU_B0[_c] + _l] = _l


def build_fused():
    import contextlib
    nc = bass.Bass("TRN2", target_bir_lowering=False, debug=False)
    Iap = lambda name, shape, dt=F32: nc.dram_tensor(name, shape, dt, kind="ExternalInput").ap()
    Dr = lambda name, shape, dt=F32: nc.dram_tensor(name, shape, dt, kind="Internal").ap()

    xs = Iap("xs", [TPC, HID])
    wi = Iap("wi", [RPC, HID])
    wo = Iap("wo", [IPC, HID])
    gq = Iap("gq", [512, NB * 256], I32)
    gsc = Iap("gsc", [32, NB * 256])
    gz = Iap("gz", [32, NB * 32], I32)
    uq = Iap("uq", [512, NB * 256], I32)
    usc = Iap("usc", [32, NB * 256])
    uz = Iap("uz", [32, NB * 32], I32)
    dq = Iap("dq", [FFN_I // 8, 512], I32)
    dsc = Iap("dsc", [86, 512])
    dz = Iap("dz", [86, 64], I32)
    w1 = Iap("w1", [HID, HID])
    w2s = Iap("w2s", [512, HID])
    masks = Iap("masks", [HID, 512])
    wbs = Iap("wbs", [HID, 512])
    wr = Iap("wr", [8, HID])
    lA = Iap("lA", [128, HID])
    lB = Iap("lB", [8 * HID, 16])
    nwc = Iap("nwc", [1, IPC])
    cw = Iap("cw", [CCPC, 4])
    cb = Iap("cb", [1, CCPC])
    dtb = Iap("dtb", [1, HPC])
    alog = Iap("alog", [1, HPC])
    dd = Iap("dd", [1, HPC])
    b2 = Iap("b2", [1, HID])
    bb = Iap("bb", [1, HID])
    o_out = nc.dram_tensor("o_out", [TPC, HID], F32, kind="ExternalOutput").ap()

    w2c = Dr("w2c", [512, HID], BF16)
    mkc = Dr("mkc", [HID, 512], BF16)
    xbt_l = Dr("xbt_l", [TPC, HID], BF16)
    lnrow_l = Dr("lnrow_l", [2, TPC])
    lncol_l = Dr("lncol_l", [128, 16])
    wib = Dr("wib_l", [RPC, HID], BF16)
    wsum = Dr("wsum_l", [1, RPC])
    wob_l = Dr("wob_l", [IPC, HID], BF16)
    wgu_l = Dr("wgu_l", [NB * HID, 512], BF16)
    wgs_l = Dr("wgs_l", [1, NB * 256])
    wus_l = Dr("wus_l", [1, NB * 256])
    wdp_l = Dr("wdp_l", [2 * DBLK, 256], BF16)
    attnT_l = Dr("attnT_l", [HID, 512], BF16)
    asum_l = Dr("asum_l", [1, 512])
    wbb_l = Dr("wbb_l", [HID, 512], BF16)
    wbsum_l = Dr("wbsum_l", [1, 512])
    wrb = Dr("wrb_l", [8, HID], BF16)
    rsum = Dr("rsum_l", [1, 8])
    lab = Dr("lab_l", [128, HID], BF16)
    acs = Dr("acs_l", [1, 128])
    bcb = Dr("bcb_l", [128, HID], BF16)
    dbg = Dr("dbg_l", [1, 2048])

    xbt_g = Dr("xbt_g", [TOK, HID], BF16)
    lnrow_g = Dr("lnrow_g", [16, TPC])
    lncol_g = Dr("lncol_g", [1024, 16])
    wob_g = Dr("wob_g", [8 * IPC, HID], BF16)
    wgu_g = Dr("wgu_g", [8 * NB * HID, 512], BF16)
    wgs_g = Dr("wgs_g", [8, NB * 256])
    wus_g = Dr("wus_g", [8, NB * 256])
    wdp_g = Dr("wdp_g", [16 * DBLK, 256], BF16)
    attnT_g = Dr("attnT_g", [8 * HID, 512], BF16)
    asum_g = Dr("asum_g", [8, 512])
    wbb_g = Dr("wbb_g", [8 * HID, 512], BF16)
    wbsum_g = Dr("wbsum_g", [8, 512])
    ghat_l = Dr("ghat_l", [IPC, TOK], BF16)
    ghat_g = Dr("ghat_g", [INTER, TOK], BF16)
    lnrow_f = Dr("lnrow_f", [2, TOK])
    lncola_f = Dr("lncola_f", [128, 64])
    asum_f = Dr("asum_f", [1, HID])
    wbsum_f = Dr("wbsum_f", [1, HID])
    xbc_s = Dr("xbc_s", [CCPC, TOK], BF16)
    gate_s = Dr("gate_s", [TOK, IPC], BF16)
    cvs_s = Dr("cvs_s", [CCPC, TOK], BF16)
    ys_s = Dr("ys_s", [TOK, IPC], BF16)
    ao_s = Dr("ao_s", [HID, TPC])
    r1 = Dr("r1_h", [HID, TPC])
    h_s = Dr("h_s", [FFN_I, TPC], BF16)
    do_s = Dr("do_s", [HID, TPC])
    r2 = Dr("r2_h", [HID, TPC])
    go_s = Dr("go_s", [HID, TPC])
    r3 = Dr("r3_h", [HID, TPC])
    fo_s = Dr("fo_s", [HID, TPC])

    def wgu_fn(idx):
        rs, cs = idx
        mg = rs.start // HID
        base = GU_C[mg] * (NB * HID) + GU_L[mg] * HID
        return wgu_g[base:base + HID, :]
    wgu_sh = APShim(wgu_fn)

    def wgs_fn(idx):
        rs, cs = idx
        mg = cs.start // 256
        c, l = GU_C[mg], GU_L[mg]
        return wgs_g[c:c + 1, 256 * l:256 * (l + 1)]
    wgs_sh = APShim(wgs_fn)

    def wus_fn(idx):
        rs, cs = idx
        mg = cs.start // 256
        c, l = GU_C[mg], GU_L[mg]
        return wus_g[c:c + 1, 256 * l:256 * (l + 1)]
    wus_sh = APShim(wus_fn)

    rg = [list(range(NCORE))]

    with nc.Block() as block:
        hw_sems = [nc.semaphore(f"hw{i}").__enter__() for i in range(8)]
        sw_sems = [nc.semaphore(f"sw{i}").__enter__() for i in range(8)]
        th = nc.semaphore("tsem").__enter__()
        vh = nc.semaphore("vsem").__enter__()
        sh = nc.semaphore("ssem").__enter__()
        gh = nc.semaphore("gsem").__enter__()
        cch = nc.semaphore("ccsem").__enter__()
        consts = alloc(nc, CONSTS)
        pr = Prog(nc, block, hw_sems, sw_sems, th, vh, sh, gh)
        tsem, vsem, ssem, gsem = pr.tsem, pr.vsem, pr.ssem, pr.gsem
        emit_consts(pr, consts)
        ncc = [0]

        def phase(tens, emit, **kw):
            with contextlib.ExitStack() as st:
                sb = alloc(nc, tens, stack=st, prefix=f"p{phase.n}_")
                sb.update(consts)
                ck = {}
                emit(nc, pr, sb, ck, **kw)
                emit_barrier(pr)
            phase.n += 1
        phase.n = 0

        def agather(pairs):
            def f(g):
                for (i_ap, o_ap) in pairs:
                    g.collective_compute("AllGather", ALU.bypass, replica_groups=rg,
                                         ins=[i_ap.opt()], outs=[o_ap.opt()]
                                         ).then_inc(cch, 1)
                    ncc[0] += 1
            pr.on('gpsimd', f)

        def ccwait(n):
            def mk():
                def f(e):
                    raw = e._e if isinstance(e, ChainEng) else e
                    raw.wait_ge(cch, n)
                return f
            for eng in ('sync', 'tensor', 'vector', 'scalar', 'gpsimd'):
                pr.on(eng, mk())

        with contextlib.ExitStack() as _stP:
            sbp = alloc(nc, PREP_TENSORS, stack=_stP, prefix='pp_')
            sbp.update(consts)
            ckp = {}
            emit_prep_p1(nc, pr, sbp, ckp, xs=xs, o_xbt=xbt_l,
                         o_lnrow=lnrow_l, o_lncol=lncol_l)

            def agx(g):
                wait_tok(g, ckp['lnc'])
                for (i_ap, o_ap) in [(xbt_l, xbt_g), (lnrow_l, lnrow_g),
                                     (lncol_l, lncol_g)]:
                    g.collective_compute("AllGather", ALU.bypass, replica_groups=rg,
                                         ins=[i_ap.opt()], outs=[o_ap.opt()]
                                         ).then_inc(cch, 1)
                    ncc[0] += 1
            pr.on('gpsimd', agx)

            emit_prep_rest(nc, pr, sbp, ckp,
                           wi=wi, wo=wo, gq=gq, gsc=gsc, gz=gz, uq=uq, usc=usc,
                           uz=uz, dq=dq, dsc=dsc, dz=dz, w1=w1, w2s=w2s,
                           masks=masks, wbs=wbs, wr=wr, lA=lA, lB=lB,
                           o_wib=wib, o_wsum=wsum, o_wob=wob_l, o_wgu=wgu_l,
                           o_wgs=wgs_l, o_wus=wus_l, o_wdp=wdp_l,
                           o_attnT=attnT_l, o_asum=asum_l, o_wbb=wbb_l,
                           o_wbsum=wbsum_l, o_wrb=wrb, o_rsum=rsum, o_lab=lab,
                           o_acs=acs, o_bcb=bcb, w2c=w2c, mkc=mkc)
            emit_barrier(pr)

        agather([(wob_l, wob_g), (attnT_l, attnT_g), (asum_l, asum_g),
                 (wbb_l, wbb_g), (wbsum_l, wbsum_g)])

        rps = contextlib.ExitStack()
        rp = alloc(nc, [('rpk', [128, 1024], F32, 'sbuf')], stack=rps)

        def repack(g):
            g.wait_ge(cch, 3)
            for r in range(8):
                g.dma_start(out=rp['rpk'][0:1, 0:1024], in_=lnrow_g[2 * r:2 * r + 1, :])
                g.dma_start(out=rp['rpk'][1:2, 0:1024], in_=lnrow_g[2 * r + 1:2 * r + 2, :])
                wait_tok(g, pr.sw.token())
                g.dma_start(out=lnrow_f[0:1, TPC * r:TPC * (r + 1)], in_=rp['rpk'][0:1, 0:1024])
                g.dma_start(out=lnrow_f[1:2, TPC * r:TPC * (r + 1)], in_=rp['rpk'][1:2, 0:1024])
                wait_tok(g, pr.sw.token())
            for r in range(8):
                g.dma_start(out=rp['rpk'][:, 0:8], in_=lncol_g[128 * r:128 * (r + 1), 0:8])
                g.dma_start(out=rp['rpk'][:, 8:16], in_=lncol_g[128 * r:128 * (r + 1), 8:16])
                wait_tok(g, pr.sw.token())
                g.dma_start(out=lncola_f[:, 8 * r:8 * (r + 1)], in_=rp['rpk'][:, 0:8])
                wait_tok(g, pr.sw.token())
        pr.on('gpsimd', repack)
        ccwait(3)
        emit_barrier(pr)
        rps.close()

        def agdef():
            agather([(wgu_l, wgu_g), (wgs_l, wgs_g),
                     (wus_l, wus_g), (wdp_l, wdp_g)])

        phase(ATTN0_TENSORS, emit_attn0_body,
              xbt=xbt_g, lnrow=lnrow_f, lncola=lncola_f, wib=wib, wsum=wsum,
              nwc=nwc, cw=cw, cb=cb, dtb=dtb, alog=alog, dd=dd,
              o_ghat=ghat_l, xbc_s=xbc_s, gate_s=gate_s, cvs_s=cvs_s, ys_s=ys_s,
              post_a=agdef)

        agather([(ghat_l, ghat_g)])
        ccwait(13)
        rps2 = contextlib.ExitStack()
        rp2 = alloc(nc, [('rpk2', [8, 1024], F32, 'sbuf')], stack=rps2)

        def repack2(g):
            g.wait_ge(cch, 8)
            g.dma_start(out=rp2['rpk2'][0:8, 0:512], in_=asum_g)
            g.dma_start(out=rp2['rpk2'][0:8, 512:1024], in_=wbsum_g)
            wait_tok(g, pr.sw.token())
            for r in range(8):
                g.dma_start(out=asum_f[0:1, 512 * r:512 * (r + 1)],
                            in_=rp2['rpk2'][r:r + 1, 0:512])
                g.dma_start(out=wbsum_f[0:1, 512 * r:512 * (r + 1)],
                            in_=rp2['rpk2'][r:r + 1, 512:1024])
            wait_tok(g, pr.sw.token())
        pr.on('gpsimd', repack2)
        emit_barrier(pr)
        rps2.close()

        phase(OUTPROJ_TENSORS, emit_outproj_body,
              ghat=ghat_g, wob=wob_g, x0=xs, o_r1=r1, ao_s=ao_s, cid=True)

        phase(FFN0_TENSORS, emit_ffn0_body,
              r1=r1, wgu=wgu_sh, wgs=wgs_sh, wus=wus_sh, wdp=wdp_g,
              o_r2=r2, h_s=h_s, do_s=do_s, o_dbg=dbg)

        phase(ATTN1_TENSORS, emit_attn1_body,
              r2=r2, attnT=attnT_g, asum=asum_f, b2=b2, o_r3=r3, go_s=go_s)

        phase(FFN1_TENSORS, emit_ffn1_body,
              r3=r3, wbb=wbb_g, wbsum=wbsum_f, wrb=wrb, rsum=rsum, lab=lab,
              acs=acs, bcb=bcb, bb=bb, o_out=o_out, fo_s=fo_s)

        def fin(g):
            wait_tok(g, pr.dtok())
            g.wait_ge(vsem.h, vsem.val)
            g.wait_ge(ssem.h, ssem.val)
            g.wait_ge(tsem.h, tsem.val)
        pr.on('gpsimd', fin)

    return nc


def fused_in_maps(inputs):
    pm = prep_in_maps(inputs)
    cw2 = inputs['L0_attn_conv_w'][:, 0, :]
    cbv = inputs['L0_attn_conv_b']
    maps = []
    for c in range(NCORE):
        chsl = np.r_[IPC * c:IPC * (c + 1),
                     INTER + 128 * c:INTER + 128 * (c + 1),
                     INTER + G * NS + 128 * c:INTER + G * NS + 128 * (c + 1)]
        m = dict(pm[c])
        m.update(
            nwc=np.ascontiguousarray(inputs['L0_attn_norm_w'][IPC * c:IPC * (c + 1)].reshape(1, IPC)),
            cw=np.ascontiguousarray(cw2[chsl]),
            cb=np.ascontiguousarray(cbv[chsl].reshape(1, CCPC)),
            dtb=np.ascontiguousarray(inputs['L0_attn_dt_bias'][HPC * c:HPC * (c + 1)].reshape(1, HPC)),
            alog=np.ascontiguousarray(inputs['L0_attn_A_log'][HPC * c:HPC * (c + 1)].reshape(1, HPC)),
            dd=np.ascontiguousarray(inputs['L0_attn_D'][HPC * c:HPC * (c + 1)].reshape(1, HPC)),
            b2=np.ascontiguousarray(inputs['L1_attn_b2'].reshape(1, HID)),
            bb=np.ascontiguousarray(inputs['L1_ffn_b_base'].reshape(1, HID)),
        )
        maps.append(m)
    return maps


def kernel_fused(**inputs):
    nc = build_fused()
    res = run_bass_kernel_spmd(nc, fused_in_maps(inputs), core_ids=list(range(NCORE)))
    outs = [np.asarray(res.results[c]['o_out']) for c in range(NCORE)]
    return np.concatenate(outs, axis=0).reshape(2, 4096, HID).astype(np.float32)
